# version 10
# baseline (speedup 1.0000x reference)
import numpy as np
import ml_dtypes
from contextlib import ExitStack
import concourse.bass as bass
import concourse.mybir as mybir
from concourse.bass_utils import run_bass_kernel_spmd

F32 = mybir.dt.float32
BF16 = mybir.dt.bfloat16
AF = mybir.ActivationFunctionType
ALU = mybir.AluOpType
AX = mybir.AxisListType.X
BF = ml_dtypes.bfloat16

T, TI, D, DG, NCORES = 2048, 16, 1024, 8, 8

BLOCKS = ["l0_attn", "l0_ffn", "l1_attn", "l1_ffn"]
A_PHASES = 9
M_PHASES = 9
M_GORDER = list(range(DG))



class Sem:
    def __init__(self, handle):
        self.h = handle
        self.n = 0

    def inc(self, instr, k):
        instr.then_inc(self.h, k)
        self.n += k
        return self.n

    def wait(self, eng, val=None):
        eng.wait_ge(self.h, self.n if val is None else val)


class Prog:
    def __init__(self, nc, stack):
        self.nc = nc
        self.stack = stack
        self.sems = {}
        self.v = {}
        self.dram = {}

    def uniq(self):
        self._u = getattr(self, "_u", 0) + 1
        return self._u

    def uniq2(self):
        return self.uniq()

    def sem(self, name):
        if name not in self.sems:
            h = self.stack.enter_context(self.nc.semaphore(f"s_{name}"))
            self.sems[name] = Sem(h)
        return self.sems[name]

    def gate_w(self, name, eng=None):
        if hasattr(self, "ccname") and name in self.ccname:
            for sem, val in self.ccname[name]:
                (eng or self.nc.sync).wait_ge(sem.h, val)

    def barrier(self):
        nc = self.nc
        s = self.sem("barrier")
        c = self.v["ccol"]
        s.inc(nc.vector.memset(c[:, 0:1], 0.0), 1)
        s.inc(nc.scalar.memzero(c[:, 1:2]), 1)
        s.inc(nc.tensor.nop(), 1)
        s.inc(nc.sync.nop(), 1)
        for e in (nc.tensor, nc.vector, nc.scalar, nc.sync):
            s.wait(e)


ARENA_COLS = 53188


def carve(P, mega):
    v = P.v
    m = mega[:]
    v["X"] = m[:, 0:16384].rearrange("p (i d) -> p i d", i=TI)
    v["FA"] = m[:, 16384:24580].bitcast(BF16)
    v["FB"] = m[:, 24580:32772].bitcast(BF16)
    v["FC"] = m[:, 32772:40964].bitcast(BF16)
    v["FD"] = m[:, 40964:49284].bitcast(BF16)
    v["GF"] = m[:, 49284:51972]
    v["WB"] = m[:, 51972:52996].bitcast(BF16)
    v["WBf32"] = m[:, 51972:52996]
    v["CS"] = m[:, 52996:53188].bitcast(BF16)

    v["FAf32"] = m[:, 16384:24580]
    v["FDf32"] = m[:, 40964:49284]
    v["STG"] = v["GF"][:, 0:2048]
    v["mcols"] = v["GF"][:, 2048:2064]
    v["colsA"] = v["GF"][:, 2064:2080]
    v["lncol"] = v["GF"][:, 2080:2088]
    v["bcol"] = v["GF"][:, 2088:2120]
    v["ccol"] = v["GF"][:, 2120:2176]
    brow = v["GF"][:, 2176:2688]
    v["brow_bf"] = brow[0:1, :].bitcast(BF16)
    v["frow"] = brow[32:33, :]
    v["ident"] = v["CS"][:, 0:128]
    v["perm"] = v["CS"][:, 128:256]
    v["ones_rowb"] = v["CS"][0:1, 256:384]
    v["ones_c128"] = v["CS"][:, 383:384]




def vfence(P):
    s = P.sem("vf")
    v = s.inc(P.nc.vector.engine_nop(), 1)
    P.nc.vector.wait_ge(s.h, v)


class WStager:

    def __init__(self, P):
        self.P = P
        self.ring = P.v["STG"]
        self.s_ld = [P.sem("wst_ld0"), P.sem("wst_ld1")]
        self.s_cast = P.sem("wst_cast")
        self.slot = 0

    def stage(self, src_ap, dst_bf16, cols, consumer_sem=None, consumer_val=0):
        P = self.P
        assert cols <= 1024
        sl = self.slot
        self.slot ^= 1
        if src_ap.dtype == BF16:
            dst = dst_bf16
            if len(src_ap.shape) == 3 and len(dst.shape) == 2:
                dst = dst.rearrange("p (a b) -> p a b", a=src_ap.shape[1])
            P.nc.sync.wait_ge(self.s_ld[sl].h, self.s_ld[sl].n)
            if consumer_sem is not None and consumer_val > 0:
                P.nc.sync.wait_ge(consumer_sem.h, consumer_val)
            dv = self.s_ld[sl].inc(P.nc.sync.dma_start(dst, src_ap), 16)
            self.last = (self.s_ld[sl], dv)
            return None
        rows = src_ap.shape[0]
        st = self.ring[0:rows, sl * 1024: sl * 1024 + cols]
        std = st
        if len(src_ap.shape) == 3:
            std = st.rearrange("p (a b) -> p a b", a=src_ap.shape[1])
        if self.s_cast.n >= 2:
            P.nc.sync.wait_ge(self.s_cast.h, self.s_cast.n - 1)
        ldv = self.s_ld[sl].inc(P.nc.sync.dma_start(std, src_ap), 16)
        P.nc.gpsimd.wait_ge(self.s_ld[sl].h, ldv)
        if consumer_sem is not None and consumer_val > 0:
            P.nc.gpsimd.wait_ge(consumer_sem.h, consumer_val)
        cv = self.s_cast.inc(P.nc.gpsimd.tensor_copy(dst_bf16, st), 1)
        self.last = (self.s_cast, cv)
        return cv

    def wait(self, eng):
        s, val = self.last
        eng.wait_ge(s.h, val)


def load_row_bf16(P, src_dram_1d, dst_bf_row, s):
    nc = P.nc
    n = src_dram_1d.shape[0]
    sd = P.sem("rowdma")
    nc.sync.wait_ge(s.h, s.n)
    dv = sd.inc(nc.sync.dma_start(P.v["frow"][:, 0:n], src_dram_1d[None, :]), 16)
    nc.vector.wait_ge(sd.h, dv)
    return s.inc(nc.vector.tensor_copy(dst_bf_row[:, 0:n], P.v["frow"][:, 0:n]), 1)




def emit_ln(P, Xv, sq_out, cols, s):
    nc = P.nc
    lc = P.v["lncol"]
    inv = 1.0 / float(cols)
    nc.vector.wait_ge(s.h, s.n)
    nc.vector.reduce_sum(lc[:, 0:1], Xv, axis=AX)
    vfence(P)
    nc.vector.tensor_scalar_mul(lc[:, 1:2], lc[:, 0:1], -inv)
    v = s.inc(nc.vector.engine_nop(), 1)
    s.wait(nc.scalar, v)
    v = s.inc(nc.scalar.activation(sq_out, Xv, AF.Square, bias=lc[:, 1:2],
                                   scale=1.0, accum_out=lc[:, 2:3]), 1)
    s.wait(nc.vector, v)
    nc.vector.tensor_scalar(lc[:, 3:4], lc[:, 2:3], inv, 1e-5, ALU.mult, ALU.add)
    v = s.inc(nc.vector.engine_nop(), 1)
    s.wait(nc.scalar, v)
    v = s.inc(nc.scalar.activation(lc[:, 4:5], lc[:, 3:4], AF.Sqrt), 1)
    s.wait(nc.vector, v)
    nc.vector.reciprocal(lc[:, 5:6], lc[:, 4:5])
    vfence(P)
    s.inc(nc.vector.tensor_scalar(Xv, Xv, lc[:, 1:2], lc[:, 5:6],
                                  ALU.add, ALU.mult), 1)




def emit_build_xf(P, Xv, XFv, off, tag, src_bf16=False):
    nc = P.nc
    s_c = P.sem("bxf_c")
    s_t = P.sem("bxf_t")
    s_e = P.sem("bxf_e")
    t0, e0 = s_t.n, s_e.n
    wb = P.v["WB"]
    if off:
        nc.vector.memset(XFv[:, :, 0:1].rearrange("p g c -> p (g c)"), 0.0)
    with (
        nc.psum_tensor(f"ps_bx0_{tag}", [128, 512], BF16) as p0,
        nc.psum_tensor(f"ps_bx1_{tag}", [128, 512], BF16) as p1,
    ):
        ps = [p0, p1]
        for i in range(TI):
            if src_bf16:
                buf = Xv[:, i, :]
            else:
                buf = wb[:, (i % 2) * 1024:(i % 2) * 1024 + 1024]
                if i >= 2:
                    nc.vector.wait_ge(s_t.h, t0 + 2 * (i - 1))
                cv = s_c.inc(nc.vector.tensor_copy(buf, Xv[:, i, :]), 1)
                nc.tensor.wait_ge(s_c.h, cv)
            for half in range(2):
                pset = i * 2 + half
                pt = ps[pset % 2]
                if pset >= 2:
                    nc.tensor.wait_ge(s_e.h, e0 + pset - 1)
                ins = None
                for k in range(4):
                    g = half * 4 + k
                    ins = nc.tensor.transpose(pt[:, k * 128:(k + 1) * 128],
                                              buf[:, g * 128:g * 128 + 128],
                                              P.v["ident"])
                tv = s_t.inc(ins, 1)
                nc.vector.wait_ge(s_t.h, tv)
                dst = XFv[:, 4 * half:4 * half + 4,
                          off + i * 128: off + i * 128 + 128]
                s_e.inc(nc.vector.tensor_copy(
                    dst, pt[:].rearrange("p (g c) -> p g c", g=4)), 1)
    P.barrier()




def emit_proj_fm(P, stg, tag, w_src_fn, n_m, rhs_fn, KG, evac, eng, n_tb=4,
                 extra_mm=None):
    nc = P.nc
    s_mm = P.sem("pf_mm")
    s_ev = P.sem("pf_ev")
    ev0 = s_ev.n
    wb = P.v["WB"]
    with (
        nc.psum_tensor(f"ps_{tag}{P.uniq()}0", [128, 512]) as p0,
        nc.psum_tensor(f"ps_{tag}{P.uniq2()}1", [128, 512]) as p1,
        nc.psum_tensor(f"ps_{tag}{P.uniq2()}2", [128, 512]) as p2,
        nc.psum_tensor(f"ps_{tag}{P.uniq2()}3", [128, 512]) as p3,
    ):
        ps = [p0, p1, p2, p3]
        for m in range(n_m):
            dst = wb[:, (m % 2) * 1024:(m % 2) * 1024 + KG * 128]
            cval = ev0 + (m - 1) * n_tb
            stg.stage(w_src_fn(m), dst, KG * 128,
                      consumer_sem=s_ev if m >= 2 else None,
                      consumer_val=cval)
            stg.wait(nc.tensor)
            for tb in range(n_tb):
                j = m * n_tb + tb
                pt = ps[j % 4]
                if j >= 4:
                    nc.tensor.wait_ge(s_ev.h, ev0 + j - 3)
                ins = None
                for g in range(KG):
                    ins = nc.tensor.matmul(pt[:], dst[:, g * 128:(g + 1) * 128],
                                           rhs_fn(g, tb), start=(g == 0),
                                           stop=(extra_mm is None and g == KG - 1))
                if extra_mm is not None:
                    ins = extra_mm(m, tb, pt)
                mv = s_mm.inc(ins, 1)
                eng.wait_ge(s_mm.h, mv)
                s_ev.inc(evac(m, tb, pt), 1)




def emit_proj_tm(P, stg, tag, lhsT_fn, w_src_fn, n_k, evac, eng, n_ob=2,
                 bias_row=None, tb_list=(0, 1, 2, 3)):
    nc = P.nc
    s_mm = P.sem("pt_mm")
    s_ev = P.sem("pt_ev")
    s_kd = P.sem("pt_kd")
    ev0, kd0 = s_ev.n, s_kd.n
    wb = P.v["WB"]
    with nc.psum_tensor(f"ps_{tag}{P.uniq()}", [128, 4096]) as pall:
        for tbi, tb in enumerate(tb_list):
            for k in range(n_k):
                dst = wb[:, (k % 2) * 1024:(k % 2) * 1024 + n_ob * 512]
                rel = tbi * n_k + k - 1
                stg.stage(w_src_fn(k), dst, n_ob * 512,
                          consumer_sem=s_kd if rel > 0 else None,
                          consumer_val=kd0 + rel)
                stg.wait(nc.tensor)
                if k == 0 and tbi > 0:
                    nc.tensor.wait_ge(s_ev.h, ev0 + tbi * 4 * n_ob)
                ins = None
                for ii in range(4):
                    lh = lhsT_fn(k, tb * 4 + ii)
                    for ob in range(n_ob):
                        pt = pall[:, (ii * n_ob + ob) * 512:
                                  (ii * n_ob + ob) * 512 + 512]
                        ins = nc.tensor.matmul(
                            pt, lh, dst[:, ob * 512:(ob + 1) * 512],
                            start=(k == 0),
                            stop=(bias_row is None and k == n_k - 1))
                s_kd.inc(ins, 1)
            if bias_row is not None:
                ins = None
                for ii in range(4):
                    for ob in range(n_ob):
                        pt = pall[:, (ii * n_ob + ob) * 512:
                                  (ii * n_ob + ob) * 512 + 512]
                        ins = nc.tensor.matmul(
                            pt, P.v["ones_rowb"],
                            bias_row[:, ob * 512:(ob + 1) * 512],
                            start=False, stop=True)
                mv = s_mm.inc(ins, 1)
                eng.wait_ge(s_mm.h, mv)
            else:
                eng.wait_ge(s_kd.h, s_kd.n)
            for ii in range(4):
                for ob in range(n_ob):
                    pt = pall[:, (ii * n_ob + ob) * 512:(ii * n_ob + ob) * 512 + 512]
                    s_ev.inc(evac(tb * 4 + ii, ob, pt), 1)




def emit_l0_attn(P, stg, Wd):
    nc = P.nc
    v = P.v
    X = v["X"]
    XF = v["FA"][:, 0:16384].rearrange("p (g c) -> p g c", g=DG)
    QT = v["FB"][:, 0:16384].rearrange("p (g c) -> p g c", g=DG)
    KT = v["FC"][:, 0:16384].rearrange("p (g c) -> p g c", g=DG)
    VA = v["FD"][:, 0:16640].rearrange("p (i h e) -> p i h e", i=TI, h=16)
    AVN = QT

    emit_build_xf(P, X, XF, 0, "a")
    if A_PHASES < 1:
        return

    s_dma = P.sem("a_dma")
    s_rows = P.sem("a_rows")
    bq = s_dma.inc(nc.sync.dma_start(v["bcol"][:, 0:16],
                                     Wd["L0_attn_bq_cols"]), 16)
    load_row_bf16(P, Wd["L0_attn_b_qkv"][2048:2560], v["brow_bf"][:, 0:512], s_rows)
    load_row_bf16(P, Wd["L0_attn_b_qkv"][2560:3072], v["brow_bf"][:, 512:1024], s_rows)
    s_rows.wait(nc.tensor)

    wq = Wd["L0_attn_W_qkv"].rearrange("(g p) f -> p g f", p=128)

    def qk_src(m):
        return wq[:, :, m * 128:(m + 1) * 128]

    def qk_rhs(g, tb):
        return XF[:, g, tb * 512:(tb + 1) * 512]

    def qk_evac(m, tb, pt):
        tgt, g = (QT, m) if m < 8 else (KT, m - 8)
        return nc.vector.tensor_scalar_add(tgt[:, g, tb * 512:(tb + 1) * 512],
                                           pt[:], v["bcol"][:, m:m + 1])

    P.gate_w("L0_attn_W_qkv")
    nc.vector.wait_ge(s_dma.h, bq)
    emit_proj_fm(P, stg, "aqk", qk_src, 16, qk_rhs, DG, qk_evac, nc.vector)
    P.barrier()

    def v_src(k):
        return wq[:, k, 2048:3072]

    def v_lhs(k, i):
        return XF[:, k, i * 128:(i + 1) * 128]

    def v_evac(i, vb, pt):
        dstv = VA[:, i, 8 * vb:8 * vb + 8, 0:64]
        return nc.vector.tensor_copy(dstv, pt.rearrange("p (h e) -> p h e", h=8))

    emit_proj_tm(P, stg, "av", v_lhs, v_src, DG, v_evac, nc.vector,
                 n_ob=2, bias_row=v["brow_bf"][:, 0:1024], tb_list=(2, 3))
    if hasattr(P, "convdone"):
        cs, cn_, cos, covs = P.convdone
        nc.vector.wait_ge(cs.h, cn_)
    sev = P.sem("pt_ev")
    nc.tensor.wait_ge(sev.h, sev.n)
    nc.sync.wait_ge(sev.h, sev.n)
    emit_proj_tm(P, stg, "av", v_lhs, v_src, DG, v_evac, nc.vector,
                 n_ob=2, bias_row=v["brow_bf"][:, 0:1024], tb_list=(0, 1))
    nc.vector.memset(VA[:, :, :, 64:65].rearrange("p i h e -> p (i h e)"), 1.0)
    P.barrier()
    if A_PHASES < 2:
        return

    dmax = P.dram["mrow"]
    s_red = P.sem("a_red")
    s_mmb = P.sem("a_mmb")
    mrt = v["STG"][0:16, 0:128].bitcast(BF16)
    hmax = v["STG"][:, 288:352]
    if hasattr(P, "convdone"):
        _, _, cos, covs = P.convdone
        nc.vector.wait_ge(cos[0].h, covs[0])
        nc.vector.wait_ge(cos[1].h, covs[1])
    dmax_written = 0
    red0 = s_red.n
    with (
        nc.psum_tensor("ps_s1_0", [128, 1024]) as q0,
        nc.psum_tensor("ps_s1_1", [128, 1024]) as q1,
        nc.psum_tensor("ps_s1_2", [128, 1024]) as q2,
        nc.psum_tensor("ps_s1t", [16, 256], BF16) as ptr,
    ):
        qs = [q0, q1, q2]
        f = 0
        red_val = {}
        for hp in range(8):
            for par in range(2):
                for i in range(TI):
                    lh = QT[64 * par:64 * par + 64, hp, i * 128:(i + 1) * 128]
                    for half in range(2):
                        pt = qs[f % 3]
                        if f >= 3:
                            nc.tensor.wait_ge(s_red.h, red_val[f - 3])
                        ins = None
                        for kk in range(2):
                            tkb = half * 2 + kk
                            rh = KT[64 * par:64 * par + 64, hp,
                                    tkb * 512:(tkb + 1) * 512]
                            ins = nc.tensor.matmul(pt[:, kk * 512:(kk + 1) * 512],
                                                   lh, rh, start=True, stop=True,
                                                   tile_position=(64 * par, 0))
                        mv = s_mmb.inc(ins, 1)
                        nc.vector.wait_ge(s_mmb.h, mv)
                        red_val[f] = s_red.inc(nc.vector.reduce_max(
                            hmax[:, par * 32 + i * 2 + half:
                                 par * 32 + i * 2 + half + 1], pt[:], axis=AX), 1)
                        f += 1
            vfence(P)
            hm = hmax.rearrange("p (i two) -> p i two", two=2)
            cm = nc.vector.tensor_tensor(
                v["GF"][:, 2048:2080].rearrange("p (i o) -> p i o", o=1),
                hm[:, :, 0:1], hm[:, :, 1:2], ALU.max)
            s_red.inc(cm, 1)
            vfence(P)
            mcb = v["STG"][:, 256:272].bitcast(BF16)
            cv = s_red.inc(nc.vector.tensor_copy(mcb, v["GF"][:, 2048:2080]), 1)
            nc.tensor.wait_ge(s_red.h, cv)
            t2 = None
            for q in range(2):
                t2 = nc.tensor.transpose(ptr[:, q * 128:q * 128 + 128],
                                         mcb[:, q * 16:(q + 1) * 16], v["ident"])
            mv = s_mmb.inc(t2, 1)
            nc.vector.wait_ge(s_mmb.h, mv)
            if hp > 0:
                nc.vector.wait_ge(s_dma.h, dmax_written)
            ev = s_red.inc(nc.vector.tensor_copy(mrt, ptr[:]), 1)
            nc.sync.wait_ge(s_red.h, ev)
            s_dma.inc(nc.sync.dma_start(
                dmax[2 * hp, :].rearrange("(i p) -> i p", p=128),
                mrt[:, 0:128]), 16)
            s_dma.inc(nc.sync.dma_start(
                dmax[2 * hp + 1, :].rearrange("(i p) -> i p", p=128),
                mrt[:, 128:256]), 16)
            dmax_written = s_dma.n
    P.barrier()
    if A_PHASES < 3:
        return

    PT = v["FA"][:, 0:8192].rearrange("p (j c) -> p j c", j=16)
    qaug = [v["FA"][:, 8192:10240], v["FA"][:, 12288:14336]]
    kaug = [v["FA"][:, 10240:12288], v["FA"][:, 14336:16384]]
    mrow = [v["STG"][0:1, 1024:2048].bitcast(BF16),
            v["WB"][0:1, 0:2048]]
    rs_row = v["STG"][0:1, 0:512]
    rs_rowb = v["STG"][0:1, 512:768].bitcast(BF16)
    av_sb = v["STG"][64:128, 0:256].bitcast(BF16)
    s_row = P.sem("a_row")
    s_exp = P.sem("a_exp")
    s_avp = P.sem("a_avp")
    s_ave = P.sem("a_ave")
    s_mmc = P.sem("a_mmc")
    s_nrm = P.sem("a_nrm")
    s_mr = [P.sem("a_mr0"), P.sem("a_mr1")]
    s_dma.wait(nc.sync)
    row0, e0, n0 = s_row.n, s_exp.n, s_nrm.n
    av_cnt = {}
    exp_cnt = {}
    mmc_s2 = {}
    with (
        nc.psum_tensor("ps_s2a", [128, 1024]) as p2a,
        nc.psum_tensor("ps_s2b", [128, 1024]) as p2b,
        nc.psum_tensor("ps_s2c", [128, 1024]) as p2c,
        nc.psum_tensor("ps_avo", [128, 512]) as pav,
        nc.psum_tensor("ps_rr", [128, 512]) as prr,
    ):
        p2 = [p2a, p2b, p2c]
        pend = None
        tt_prev = 0

        def emit_av(bg, grp):
            hh = bg // 4
            nc.tensor.wait_ge(s_exp.h, exp_cnt[bg * 8 + grp])
            if grp == 0 and bg >= 1:
                nc.tensor.wait_ge(s_nrm.h, n0 + bg)
            ins = None
            for k2 in range(2):
                jtk = grp * 2 + k2
                ins = nc.tensor.matmul(pav[0:65, :], VA[:, jtk, hh, :],
                                       PT[:, jtk, :], start=(jtk == 0),
                                       stop=(jtk == TI - 1))
            av_cnt[(bg, grp)] = s_avp.inc(ins, 1)

        def emit_norm2(pn):
            nonlocal tt_prev
            g2, rb2, b2, nv, ttp = pn
            if ttp:
                nc.tensor.wait_ge(s_ave.h, ttp)
            nc.tensor.wait_ge(s_nrm.h, nv)
            pins = nc.tensor.matmul(prr[0:64, :], v["ones_rowb"][:, 0:64],
                                    rs_rowb, start=True, stop=True)
            pv = s_mmc.inc(pins, 1)
            nc.vector.wait_ge(s_mmc.h, pv)
            dsta = AVN[rb2:rb2 + 64, g2, b2 * 512:(b2 + 1) * 512]
            tt_prev = s_ave.inc(nc.vector.tensor_tensor(
                dsta, av_sb[:, 0:512], prr[0:64, :], ALU.mult), 1)

        rowv = [0, 0]
        rowv[0] = s_mr[0].inc(nc.sync.dma_start(mrow[0], dmax[0, :][None, :]), 16)
        for h in range(16):
            sl = h % 2
            if h + 1 < 16:
                nsl = (h + 1) % 2
                if h >= 1:
                    nc.sync.wait_ge(s_row.h, row0 + h)
                rowv[nsl] = s_mr[nsl].inc(
                    nc.sync.dma_start(mrow[nsl], dmax[h + 1, :][None, :]), 16)
            g, rb = h // 2, (h % 2) * 64
            if h >= 2:
                nc.vector.wait_ge(s_mmc.h, mmc_s2[h - 2])
            if h < 2:
                nc.vector.memset(kaug[sl][64:65, :], 1.0)
            nc.vector.tensor_copy(kaug[sl][0:64, :], KT[rb:rb + 64, g, :])
            nc.vector.tensor_copy(qaug[sl][0:64, :], QT[rb:rb + 64, g, :])
            nc.vector.wait_ge(s_mr[sl].h, rowv[sl])
            rv = s_row.inc(nc.vector.tensor_scalar_mul(
                qaug[sl][64:65, :], mrow[sl], -8.0), 1)
            first_of_h = True
            for b in range(4):
                bg = h * 4 + b
                for grp in range(8):
                    j = bg * 8 + grp
                    pt2 = p2[j % 3]
                    if first_of_h and grp == 0:
                        nc.tensor.wait_ge(s_row.h, rv)
                        first_of_h = False
                    if j >= 3:
                        nc.tensor.wait_ge(s_exp.h, e0 + j - 2)
                    ins = None
                    for k2 in range(2):
                        jtk = grp * 2 + k2
                        ins = nc.tensor.matmul(
                            pt2[:, k2 * 512:(k2 + 1) * 512],
                            kaug[sl][0:65, jtk * 128:(jtk + 1) * 128],
                            qaug[sl][0:65, b * 512:(b + 1) * 512],
                            start=True, stop=True)
                    mv = s_mmc.inc(ins, 1)
                    if bg == h * 4 + 3 and grp == 7:
                        mmc_s2[h] = mv
                    nc.scalar.wait_ge(s_mmc.h, mv)
                    if bg >= 1:
                        nc.scalar.wait_ge(s_avp.h, av_cnt[(bg - 1, grp)])
                    dstp = PT[:, grp * 2:grp * 2 + 2, :] \
                        .rearrange("p j c -> p (j c)")
                    exp_cnt[j] = s_exp.inc(
                        nc.scalar.activation(dstp, pt2[:], AF.Exp,
                                             scale=0.125), 1)
                    if grp >= 1:
                        emit_av(bg, grp - 1)
                    if grp == 2 and pend is not None:
                        emit_norm2(pend)
                        pend = None
                emit_av(bg, 7)
                nc.vector.wait_ge(s_avp.h, av_cnt[(bg, 7)])
                nc.vector.reciprocal(rs_row, pav[64:65, :])
                vfence(P)
                nc.vector.tensor_copy(rs_rowb, rs_row)
                nv = s_nrm.inc(nc.vector.tensor_copy(av_sb[:, 0:512],
                                                     pav[0:64, :]), 1)
                pend = (g, rb, b, nv, tt_prev)
        emit_norm2(pend)
    P.barrier()
    if A_PHASES < 4:
        return

    P.gate_w("L0_attn_W_o")
    load_row_bf16(P, Wd["L0_attn_b_o"][0:512], v["brow_bf"][:, 0:512], s_rows)
    load_row_bf16(P, Wd["L0_attn_b_o"][512:1024], v["brow_bf"][:, 512:1024], s_rows)
    s_rows.wait(nc.tensor)
    wo = Wd["L0_attn_W_o"].rearrange("(k p) f -> p k f", p=128)
    AV3 = AVN

    def o_src(k):
        return wo[:, k, :]

    def o_lhs(k, i):
        return AV3[:, k, i * 128:(i + 1) * 128]

    def o_evac(i, ob, pt):
        Xsl = X[:, i, ob * 512:(ob + 1) * 512]
        return nc.vector.tensor_tensor(Xsl, Xsl, pt, ALU.add)

    emit_proj_tm(P, stg, "aout", o_lhs, o_src, DG, o_evac, nc.vector,
                 n_ob=2, bias_row=v["brow_bf"][:, 0:1024])
    P.barrier()
    SQ = v["FD"][:, 0:1024]
    s_ln = P.sem("a_ln")
    for i in range(TI):
        emit_ln(P, X[:, i, :], SQ, 1024, s_ln)
    P.barrier()



CONVERT_WEIGHTS = [
    "L0_attn_W_qkv", "L0_attn_W_o", "L0_ffn_wqkv", "L0_ffn_wo",
    "L0_ffn_w1p", "L0_ffn_w2p", "L1_attn_W_in", "L1_attn_dt_w",
    "L1_attn_B_w", "L1_attn_C_w", "L1_attn_W_out", "L1_ffn_gate_Wt",
    "L1_ffn_wi_Wt", "L1_ffn_wo_Wt", "L1_ffn_lora_At", "L1_ffn_lora_Bc",
]


def emit_convert_weights(P, Wd, Wbf):
    nc = P.nc
    if not hasattr(P, "ccname"):
        P.ccname = {}
    FDf = P.v["FDf32"]
    fin = [FDf[:, 0:2048], FDf[:, 2048:4096]]
    fout = [P.v["STG"][:, 0:1024].bitcast(BF16),
            P.v["STG"][:, 1024:2048].bitcast(BF16)]
    s_ci = [P.sem("cw_i0"), P.sem("cw_i1")]
    s_cc = P.sem("cw_c")
    s_co = [P.sem("cw_o0"), P.sem("cw_o1")]
    chunks = []
    for nm in CONVERT_WEIGHTS:
        w = Wd[nm]
        L = 1
        for d in w.shape:
            L *= d
        cols = L // 128
        spec = " ".join(chr(97 + i) for i in range(len(w.shape)))
        wf = w.rearrange(f"{spec} -> ({spec})")
        bf = Wbf[nm].rearrange(f"{spec} -> ({spec})")
        off = 0
        while off < cols:
            cn = min(2048, cols - off)
            chunks.append((nm, wf, bf, off, cn))
            off += cn

    def in_dma(ci):
        nm, wf, bf, off, cn = chunks[ci]
        sl = ci % 2
        srcap = wf[off * 128:(off + cn) * 128].rearrange("(p c) -> p c", c=cn)
        return s_ci[sl].inc(nc.gpsimd.dma_start(fin[sl][:, 0:cn], srcap), 16)

    n_ch = len(chunks)
    iv = {}
    for k in range(min(2, n_ch)):
        iv[k] = in_dma(k)
    co_v = {}
    cc_v = {}
    for ci, (nm, wf, bf, off, cn) in enumerate(chunks):
        sl4, sl2 = ci % 2, ci % 2
        dstap = bf[off * 128:(off + cn) * 128].rearrange("(p c) -> p c", c=cn)
        nc.gpsimd.wait_ge(s_ci[sl4].h, iv[ci])
        if ci >= 2:
            nc.gpsimd.wait_ge(s_co[sl2].h, co_v[ci - 2])
        cv = s_cc.inc(nc.gpsimd.tensor_copy(fout[sl2][:, 0:cn],
                                            fin[sl4][:, 0:cn]), 1)
        cc_v[ci] = cv
        nc.gpsimd.wait_ge(s_cc.h, cv)
        co_v[ci] = s_co[sl2].inc(nc.gpsimd.dma_start(dstap,
                                                     fout[sl2][:, 0:cn]), 16)
        P.ccname[nm] = [(s_co[0], s_co[0].n), (s_co[1], s_co[1].n)]
        if ci + 2 < n_ch:
            iv[ci + 2] = in_dma(ci + 2)
    P.ccsem = s_cc
    P.convdone = (s_cc, s_cc.n, s_co, [s_co[0].n, s_co[1].n])




def build_program(blocks):
    nc = bass.Bass("TRN2", target_bir_lowering=False)
    stack = ExitStack()
    P = Prog(nc, stack)

    Wd = {}
    names = [
        ("L0_attn_x", [T, D]),
        ("L0_attn_W_qkv", [D, 3072]), ("L0_attn_b_qkv", [3072]),
        ("L0_attn_W_o", [D, D]), ("L0_attn_b_o", [D]),
        ("L0_ffn_wqkv", [2, D, 1536]), ("L0_ffn_q_scale", [2, 64]),
        ("L0_ffn_k_scale", [2, 64]), ("L0_ffn_wo", [2, 512, D]),
        ("L0_ffn_ln_g", [2, D]), ("L0_ffn_ln_b", [2, D]),
        ("L0_ffn_w1p", [2, D, 5632]), ("L0_ffn_b1p", [2, 5632]),
        ("L0_ffn_w2p", [2, 2816, D]), ("L0_ffn_b2", [2, D]),
        ("L1_attn_ln_g", [D]), ("L1_attn_ln_b", [D]),
        ("L1_attn_W_in", [D, 2048]), ("L1_attn_conv_w", [D, 1, 4]),
        ("L1_attn_conv_b", [D]), ("L1_attn_dt_w", [D, D]),
        ("L1_attn_dt_b", [D]), ("L1_attn_B_w", [D, 16]),
        ("L1_attn_C_w", [D, 16]), ("L1_attn_log_A", [D]),
        ("L1_attn_D_param", [D]), ("L1_attn_W_out", [D, D]),
        ("L1_ffn_gate_Wt", [D, 8]), ("L1_ffn_gate_b", [8]),
        ("L1_ffn_wi_Wt", [D, 4096]), ("L1_ffn_wi_b", [4096]),
        ("L1_ffn_wo_Wt", [4096, D]), ("L1_ffn_wo_b", [D]),
        ("L1_ffn_lora_At", [D, 32]), ("L1_ffn_lora_Bc", [32, 4096]),
        ("L0_attn_bq_cols", [128, 16]),
        ("L1_ffn_wib_cols", [128, 32]),
        ("L1_attn_logA_cols", [128, 8]), ("L1_attn_dtb_cols", [128, 8]),
        ("L1_attn_D_cols", [128, 8]), ("L1_attn_convb_cols", [128, 8]),
        ("L1_attn_convw_cols", [128, 32]),
        ("c_bmask8", [128, 256]), ("c_cmask", [128, 256]), ("c_tid16", [16, 128]),
        ("c_rot_cs", [128, 2048]), ("c_rot_ss", [128, 2048]),
        ("c_rot_cds", [128, 2048]), ("c_rot_sds", [128, 2048]),
        ("c_locmask", [128, 4096]), ("c_perm", [128, 128]),
        ("L0_ffn_qk_cols", [128, 4]), ("L0_ffn_lngb_cols", [128, 32]),
        ("L0_ffn_b1_cols", [2, 128, 44]),
        ("c_ident", [128, 128]),
    ]
    for nm, shp in names:
        dt = BF16 if nm in ("c_ident", "c_bmask8", "c_cmask", "c_tid16",
                            "c_rot_cs", "c_rot_ss", "c_rot_cds", "c_rot_sds",
                            "c_locmask", "c_perm") else F32
        Wd[nm] = nc.dram_tensor(nm, shp, dt, kind="ExternalInput").ap()
    xout = nc.dram_tensor("x_out", [T, D], F32, kind="ExternalOutput").ap()

    P.dram["mrow"] = nc.dram_tensor("scr_mrow", [16, 2048], BF16,
                                    kind="Internal").ap()
    Wbf = {}
    for nm in CONVERT_WEIGHTS:
        shp = dict(names)[nm]
        Wbf[nm] = nc.dram_tensor(f"bf_{nm}", shp, BF16, kind="Internal").ap()
    P.dram["x0"] = nc.dram_tensor("scr_x0", [T, D], F32, kind="Internal").ap()
    P.dram["z"] = nc.dram_tensor("scr_z", [D, T], BF16, kind="Internal").ap()

    with stack:
        mega = stack.enter_context(nc.sbuf_tensor("mega", [128, ARENA_COLS], F32))
        with nc.Block() as _block:
            carve(P, mega)
            stg = WStager(P)
            s_io = P.sem("io")
            s_io.inc(nc.sync.dma_start(
                P.v["X"],
                Wd["L0_attn_x"].rearrange("(i p) d -> p i d", p=128)), 16)
            s_io.inc(nc.sync.dma_start(P.v["ident"], Wd["c_ident"]), 16)
            s_io.inc(nc.sync.dma_start(P.v["perm"], Wd["c_perm"]), 16)
            nc.vector.memset(P.v["ones_rowb"][:], 1.0)
            for e in (nc.tensor, nc.vector, nc.scalar, nc.sync):
                s_io.wait(e)
            emit_convert_weights(P, Wd, Wbf)
            for nm in CONVERT_WEIGHTS:
                Wd[nm] = Wbf[nm]

            for b in blocks:
                if b == "l0_attn":
                    emit_l0_attn(P, stg, Wd)
                elif b == "l0_ffn":
                    emit_l0_ffn(P, stg, Wd)
                elif b == "l1_attn":
                    emit_l1_attn(P, stg, Wd)
                elif b == "l1_ffn":
                    emit_l1_ffn(P, stg, Wd)

            s_out = P.sem("out")
            s_out.inc(nc.sync.dma_start(
                xout.rearrange("(i p) d -> p i d", p=128), P.v["X"]), 16)
            s_out.wait(nc.sync)
    return nc




def prep_inputs(inputs):
    f32 = np.float32
    g = {k: np.asarray(v, dtype=f32) for k, v in inputs.items()}

    HID, HIDP = 2730, 2816
    com = {}
    com["L0_attn_W_qkv"] = g["L0_attn_W_qkv"]
    com["L0_attn_b_qkv"] = g["L0_attn_b_qkv"]
    com["L0_attn_W_o"] = g["L0_attn_W_o"]
    com["L0_attn_b_o"] = g["L0_attn_b_o"]
    com["L0_ffn_wqkv"] = g["L0_ffn_wqkv"]
    com["L0_ffn_q_scale"] = g["L0_ffn_q_scale"]
    com["L0_ffn_k_scale"] = g["L0_ffn_k_scale"]
    com["L0_ffn_wo"] = g["L0_ffn_wo"]
    com["L0_ffn_ln_g"] = g["L0_ffn_ln_g"]
    com["L0_ffn_ln_b"] = g["L0_ffn_ln_b"]
    w1 = g["L0_ffn_w1"]
    b1 = g["L0_ffn_b1"]
    w1p = np.zeros((2, D, 2 * HIDP), f32)
    b1p = np.zeros((2, 2 * HIDP), f32)
    perm = np.zeros(2 * HIDP, np.int64)
    keep = np.zeros(2 * HIDP, bool)
    for blk in range(22):
        lo, hi = blk * 128, min((blk + 1) * 128, HID)
        n = hi - lo
        pa = 2 * blk * 128
        pg = (2 * blk + 1) * 128
        perm[pa:pa + n] = np.arange(lo, hi)
        keep[pa:pa + n] = True
        perm[pg:pg + n] = HID + np.arange(lo, hi)
        keep[pg:pg + n] = True
    w1p[:, :, keep[: 2 * HIDP]] = w1[:, :, perm[keep]]
    b1p[:, keep] = b1[:, perm[keep]]
    com["L0_ffn_w1p"] = w1p
    com["L0_ffn_b1p"] = b1p
    w2 = g["L0_ffn_w2"]
    w2p = np.zeros((2, HIDP, D), f32)
    w2p[:, :HID, :] = w2
    com["L0_ffn_w2p"] = w2p
    com["L0_ffn_b2"] = g["L0_ffn_b2"]
    for k in ["L1_attn_ln_g", "L1_attn_ln_b", "L1_attn_W_in",
              "L1_attn_conv_b", "L1_attn_dt_w", "L1_attn_dt_b",
              "L1_attn_B_w", "L1_attn_C_w", "L1_attn_log_A",
              "L1_attn_D_param", "L1_attn_W_out"]:
        com[k] = g[k]
    com["L1_attn_conv_w"] = g["L1_attn_conv_w"]
    com["L1_ffn_gate_Wt"] = np.ascontiguousarray(g["L1_ffn_gate_W"].T)
    com["L1_ffn_gate_b"] = g["L1_ffn_gate_b"]
    com["L1_ffn_wi_Wt"] = np.ascontiguousarray(g["L1_ffn_wi_W"].T)
    com["L1_ffn_wi_b"] = g["L1_ffn_wi_b"]
    com["L1_ffn_wo_Wt"] = np.ascontiguousarray(g["L1_ffn_wo_W"].T)
    com["L1_ffn_wo_b"] = g["L1_ffn_wo_b"]
    la = g["L1_ffn_lora_A"]
    com["L1_ffn_lora_At"] = np.ascontiguousarray(
        la.reshape(32, D).T)
    lb = g["L1_ffn_lora_B"]
    com["L1_ffn_lora_Bc"] = np.ascontiguousarray(
        lb.transpose(0, 2, 1).reshape(32, 4096))
    com["L0_attn_bq_cols"] = np.ascontiguousarray(
        g["L0_attn_b_qkv"][:2048].reshape(16, 128).T)
    com["L1_ffn_wib_cols"] = np.ascontiguousarray(
        g["L1_ffn_wi_b"].reshape(32, 128).T)
    com["L1_attn_logA_cols"] = np.ascontiguousarray(
        g["L1_attn_log_A"].reshape(8, 128).T)
    com["L1_attn_dtb_cols"] = np.ascontiguousarray(
        g["L1_attn_dt_b"].reshape(8, 128).T)
    com["L1_attn_D_cols"] = np.ascontiguousarray(
        g["L1_attn_D_param"].reshape(8, 128).T)
    com["L1_attn_convb_cols"] = np.ascontiguousarray(
        g["L1_attn_conv_b"].reshape(8, 128).T)
    com["L1_attn_convw_cols"] = np.ascontiguousarray(
        g["L1_attn_conv_w"][:, 0, :].reshape(8, 128, 4).transpose(1, 0, 2)
        .reshape(128, 32))
    bm = np.zeros((128, 16, 16), BF)
    for p in range(128):
        bm[p, p // 8, :] = 1
    com["c_bmask8"] = bm.reshape(128, 256)
    cm = np.zeros((128, 2, 128), BF)
    for q in range(128):
        c_l_in_half = q // 16
        for h2 in range(2):
            c_l = h2 * 8 + c_l_in_half
            for p in range(128):
                if c_l == p // 8:
                    cm[q, h2, p] = 1
    com["c_cmask"] = cm.reshape(128, 256)
    tid = np.zeros((16, 128), BF)
    for qq in range(128):
        tid[qq % 16, qq] = 1
    com["c_tid16"] = tid
    DH, WIN, NN = 64, 512, 2048
    half_idx = np.arange(0, DH, 2, dtype=np.float64)
    inv_freq = 1.0 / 10000.0 ** (half_idx / DH)
    tt = np.arange(NN, dtype=np.float64)
    freqs = tt[:, None] * inv_freq[None, :]
    freqs = np.concatenate([freqs, freqs], axis=-1)
    basev = (half_idx + 0.4 * DH) / (1.4 * DH)
    power = (tt - NN // 2) / float(WIN // 2)
    scalev = basev[None, :] ** power[:, None]
    scalev = np.concatenate([scalev, scalev], axis=-1)
    cosv, sinv = np.cos(freqs), np.sin(freqs)
    def rep2(m):
        r = np.empty((128, NN))
        r[0:64, :] = m.T
        r[64:128, :] = m.T
        return r.astype(BF)
    com["c_rot_cs"] = rep2(cosv * scalev)
    com["c_rot_ss"] = rep2(sinv * scalev)
    com["c_rot_cds"] = rep2(cosv / scalev)
    com["c_rot_sds"] = rep2(sinv / scalev)
    lm = np.zeros((128, 8, 512), BF)
    for tkt in range(8):
        for p in range(128):
            tk = tkt * 128 + p
            c0, c1 = max(0, tk - 512), min(511, tk)
            if c0 <= c1:
                lm[p, tkt, c0:c1 + 1] = 1
    com["c_locmask"] = lm.reshape(128, 4096)
    pm = np.zeros((128, 128), BF)
    for mcol in range(128):
        base0 = 64 * (mcol // 64)
        ml = mcol % 64
        if ml < 64 // 2:
            pm[base0 + ml + 32, mcol] = -1
        else:
            pm[base0 + ml - 32, mcol] = 1
    com["c_perm"] = pm
    com["L0_ffn_qk_cols"] = np.stack(
        [g["L0_ffn_q_scale"][0], g["L0_ffn_k_scale"][0],
         g["L0_ffn_q_scale"][1], g["L0_ffn_k_scale"][1]],
        axis=1)[np.tile(np.arange(64), 2)].astype(f32)
    lngb = np.zeros((128, 32), f32)
    for l in range(2):
        for gg in range(8):
            lngb[:, l * 8 + gg] = g["L0_ffn_ln_g"][l][gg * 128: (gg + 1) * 128]
            lngb[:, 16 + l * 8 + gg] = g["L0_ffn_ln_b"][l][gg * 128: (gg + 1) * 128]
    com["L0_ffn_lngb_cols"] = lngb
    com["L0_ffn_b1_cols"] = np.ascontiguousarray(
        b1p.reshape(2, 44, 128).transpose(0, 2, 1))
    com["c_ident"] = np.eye(128, dtype=BF)

    x = g["L0_attn_x"]
    in_maps = []
    for c in range(NCORES):
        m = dict(com)
        m["L0_attn_x"] = np.ascontiguousarray(x[c])
        in_maps.append(m)
    return in_maps


def kernel(**inputs):
    in_maps = prep_inputs(inputs)
    nc = build_program(BLOCKS)
    res = run_bass_kernel_spmd(nc, in_maps, core_ids=list(range(NCORES)))
    out = np.stack([np.asarray(r["x_out"], dtype=np.float32)
                    for r in res.results], axis=0)
    return out


def emit_l0_ffn(P, stg, Wd):
    nc = P.nc
    v = P.v
    X = v["X"]
    P.gate_w("L0_ffn_w2p")
    FA, FB, FC, FD = v["FA"], v["FB"], v["FC"], v["FD"]
    XF = FA[:, 0:16392].rearrange("p (g c) -> p g c", g=DG)
    QK = FC[:, 0:16384].rearrange("p (g c) -> p g c", g=DG)
    VA = FD[:, 0:8320].rearrange("p (i h e) -> p i h e", i=TI, h=8)
    ROT = FD[:, 8320:16512]
    AVN = FB[:, 0:8192].rearrange("p (g c) -> p g c", g=4)
    PT = FB[:, 8192:12288].rearrange("p (j c) -> p j c", j=8)
    MSK = FB[:, 12288:16384].rearrange("p (j c) -> p j c", j=8)
    katk = [FA[:, 0:2048], FA[:, 2560:4608]]
    qaug = [FA[:, 2048:2560], FA[:, 4608:5120]]
    dmax = P.dram["mrow"]
    s_dma = P.sem("f0_dma")
    s_rows = P.sem("f0_rows")
    s_pe = P.sem("f0_pe")
    s_ln = P.sem("f0_ln")

    s_sp = P.sem("f0_spill")
    s_sp.inc(nc.sync.dma_start(
        P.dram["x0"].rearrange("(i p) d -> p i d", p=128), X), 16)
    s_dma.inc(nc.sync.dma_start(ROT[:, 0:2048], Wd["c_rot_cs"]), 16)
    s_dma.inc(nc.sync.dma_start(ROT[:, 2048:4096], Wd["c_rot_ss"]), 16)
    s_dma.inc(nc.sync.dma_start(ROT[:, 4096:6144], Wd["c_rot_cds"]), 16)
    s_dma.inc(nc.sync.dma_start(ROT[:, 6144:8192], Wd["c_rot_sds"]), 16)
    s_dma.inc(nc.sync.dma_start(v["ccol"][:, 52:56], Wd["L0_ffn_qk_cols"]), 16)
    s_dma.inc(nc.sync.dma_start(v["bcol"][:, 0:32], Wd["L0_ffn_lngb_cols"]), 16)
    nc.vector.memset(v["ones_c128"], 1.0)
    s_dma.wait(nc.vector)
    s_dma.wait(nc.tensor)
    s_dma.wait(nc.scalar)

    for lay in range(2):
        wq = Wd["L0_ffn_wqkv"].rearrange("l (g p) f -> l p g f", p=128)[lay]
        nc.sync.wait_ge(s_dma.h, s_dma.n)
        mv = s_dma.inc(nc.sync.dma_start(
            MSK.rearrange("p j c -> p (j c)"), Wd["c_locmask"]), 16)
        nc.vector.wait_ge(s_dma.h, mv)

        emit_build_xf(P, X, XF, 1, f"f{lay}")

        def sh(gg, c0, cn):
            off = 1 if gg < 4 else 0
            return XF[:, gg, off + c0: off + c0 + cn]

        def qk_src(m):
            return wq[:, :, m * 128:(m + 1) * 128]

        def qk_rhs(g, tb):
            return sh(g, tb * 512, 512)

        def qk_evac(m, tb, pt):
            return nc.vector.tensor_copy(QK[:, m, tb * 512:(tb + 1) * 512], pt[:])

        emit_proj_fm(P, stg, "fqk", qk_src, 8, qk_rhs, DG, qk_evac,
                     nc.vector)
        P.barrier()

        def v_src(k):
            return wq[:, k, 1024:1536]

        def v_lhs(k, i):
            return sh(k, i * 128, 128)

        def v_evac(i, vb, pt):
            dstv = VA[:, i, :, 0:64]
            return nc.vector.tensor_copy(
                dstv, pt.rearrange("p (h e) -> p h e", h=8))

        emit_proj_tm(P, stg, "fv", v_lhs, v_src, DG, v_evac, nc.vector,
                     n_ob=1)
        nc.vector.memset(VA[:, :, :, 64:65].rearrange("p i h e -> p (i h e)"), 1.0)
        P.barrier()

        s_n = P.sem("fn")
        s_ns = P.sem("fns")
        sqsc = [FB[:, 0:2048], FB[:, 2048:4096]]
        tscr = [FB[:, 4096:4608], FB[:, 4608:5120]]
        t2s = FB[:, 5120:5632]
        t3s = FB[:, 5632:6144]
        rstdbA = [v["STG"][0:1, 1024:1280].bitcast(BF16),
                  v["STG"][0:1, 1536:1792].bitcast(BF16)]
        rstdbB = [v["STG"][0:1, 1280:1536].bitcast(BF16),
                  v["STG"][0:1, 1792:2048].bitcast(BF16)]
        rAs = [v["WBf32"][0:1, 0:512], v["WBf32"][0:1, 512:1024]]
        rBs = [v["WBf32"][32:33, 0:512], v["WBf32"][32:33, 512:1024]]
        epsrow = v["STG"][0:1, 0:256].bitcast(BF16)
        sq_v = {}
        pn_v = {}
        sx_v = {}
        rc_v = {}
        tm_v = {}
        pm_v = {}
        bc_v = {}
        t3_v = {}
        fn_v = {}
        ev24 = s_n.inc(nc.vector.memset(epsrow, 1e-24), 1)
        nc.tensor.wait_ge(s_n.h, ev24)
        with (
            nc.psum_tensor(f"ps_fna0{lay}", [1, 512]) as pnA0,
            nc.psum_tensor(f"ps_fnb0{lay}", [1, 512]) as pnB0,
            nc.psum_tensor(f"ps_fna1{lay}", [1, 512]) as pnA1,
            nc.psum_tensor(f"ps_fnb1{lay}", [1, 512]) as pnB1,
            nc.psum_tensor(f"ps_fra{lay}", [128, 512]) as prnA,
            nc.psum_tensor(f"ps_frb{lay}", [128, 512]) as prnB,
            nc.psum_tensor(f"ps_fp0{lay}", [128, 512]) as prot0,
            nc.psum_tensor(f"ps_fp1{lay}", [128, 512]) as prot1,
        ):
            pnA = [pnA0, pnA1]
            pnB = [pnB0, pnB1]
            prot = [prot0, prot1]
            sq_v[0] = s_ns.inc(nc.scalar.activation(sqsc[0], QK[:, 0, :],
                                                    AF.Square), 1)
            for tile in range(8):
                isq = tile < 4
                scol = v["ccol"][:, 52 + 2 * lay + (0 if isq else 1):
                                 53 + 2 * lay + (0 if isq else 1)]
                for tb in range(4):
                    it = tile * 4 + tb
                    lo = tb * 512
                    qsl = QK[:, tile, lo:lo + 512]
                    ts = tscr[it % 2]
                    cosT = ROT[:, (0 if isq else 4096) + lo:
                               (0 if isq else 4096) + lo + 512]
                    sinT = ROT[:, (2048 if isq else 6144) + lo:
                               (2048 if isq else 6144) + lo + 512]
                    if tb == 1 and tile + 1 < 8:
                        if tile >= 1:
                            nc.scalar.wait_ge(s_pe.h, pn_v[(tile - 1) * 4 + 3])
                        sq_v[tile + 1] = s_ns.inc(nc.scalar.activation(
                            sqsc[(tile + 1) % 2], QK[:, tile + 1, :],
                            AF.Square), 1)
                    if tb == 0:
                        nc.tensor.wait_ge(s_ns.h, sq_v[tile])
                    if it >= 2:
                        nc.tensor.wait_ge(s_ns.h, sx_v[it - 2])
                    nc.tensor.matmul(pnA[it % 2][:], v["ones_c128"][0:64, :],
                                     sqsc[tile % 2][0:64, lo:lo + 512],
                                     start=True, stop=False)
                    nc.tensor.matmul(pnA[it % 2][:], v["ones_rowb"][:, 0:1],
                                     epsrow, start=False, stop=True)
                    nc.tensor.matmul(pnB[it % 2][:],
                                     v["ones_c128"][64:128, :],
                                     sqsc[tile % 2][64:128, lo:lo + 512],
                                     start=True, stop=False,
                                     tile_position=(64, 0))
                    ins = nc.tensor.matmul(pnB[it % 2][:],
                                           v["ones_rowb"][:, 0:1],
                                           epsrow, start=False, stop=True)
                    pn_v[it] = s_pe.inc(ins, 1)
                    nc.scalar.wait_ge(s_pe.h, pn_v[it])
                    if it >= 2:
                        nc.scalar.wait_ge(s_n.h, rc_v[it - 2])
                    s_ns.inc(nc.scalar.activation(rAs[it % 2],
                                                  pnA[it % 2][:], AF.Sqrt), 1)
                    sx_v[it] = s_ns.inc(nc.scalar.activation(
                        rBs[it % 2], pnB[it % 2][:], AF.Sqrt), 1)
                    if it >= 2:
                        nc.vector.wait_ge(s_pe.h, pm_v[it - 2])
                    tm_v[it] = s_n.inc(nc.vector.tensor_scalar_mul(
                        ts, qsl, scol), 1)
                    nc.vector.wait_ge(s_ns.h, sx_v[it])
                    with nc.allow_low_precision(reason="bf16 store of rstd"):
                        nc.vector.reciprocal(rstdbA[it % 2], rAs[it % 2])
                        rc_v[it] = s_n.inc(nc.vector.reciprocal(
                            rstdbB[it % 2], rBs[it % 2]), 1)
                    nc.tensor.wait_ge(s_n.h, tm_v[it])
                    if it >= 2:
                        nc.tensor.wait_ge(s_n.h, t3_v[it - 2])
                    pm_v[it] = s_pe.inc(nc.tensor.matmul(
                        prot[it % 2][:], v["perm"], ts,
                        start=True, stop=True), 1)
                    nc.tensor.wait_ge(s_n.h, rc_v[it])
                    if it >= 1:
                        nc.tensor.wait_ge(s_n.h, fn_v[it - 1])
                    nc.tensor.matmul(prnA[0:64, :], v["ones_rowb"][:, 0:64],
                                     rstdbA[it % 2], start=True, stop=True)
                    bc_v[it] = s_pe.inc(nc.tensor.matmul(
                        prnB[0:64, :], v["ones_rowb"][:, 0:64],
                        rstdbB[it % 2], start=True, stop=True), 1)
                    nc.vector.wait_ge(s_pe.h, pm_v[it])
                    nc.vector.tensor_mul(t2s, ts, cosT)
                    nc.vector.tensor_mul(t3s[0:64, :], prot[it % 2][0:64, :],
                                         sinT[0:64, :])
                    t3_v[it] = s_n.inc(nc.vector.tensor_mul(
                        t3s[64:128, :], prot[it % 2][64:128, :],
                        sinT[64:128, :]), 1)
                    vfence(P)
                    nc.vector.tensor_add(t2s[0:64, :], t2s[0:64, :],
                                         t3s[0:64, :])
                    nc.vector.tensor_add(t2s[64:128, :], t2s[64:128, :],
                                         t3s[64:128, :])
                    nc.vector.wait_ge(s_pe.h, bc_v[it])
                    vfence(P)
                    nc.vector.tensor_tensor(qsl[0:64, :], t2s[0:64, :],
                                            prnA[0:64, :], ALU.mult)
                    fn_v[it] = s_n.inc(nc.vector.tensor_tensor(
                        qsl[64:128, :], t2s[64:128, :], prnB[0:64, :],
                        ALU.mult), 1)
        P.barrier()

        s_red = P.sem("f0red")
        s_mmb = P.sem("f0mmb")
        s_dx = P.sem("f0dx")
        mrt = v["STG"][0:16, 0:128].bitcast(BF16)
        dmax_written = 0
        red_val = {}
        f = 0
        with (
            nc.psum_tensor(f"ps_f1_{lay}0", [128, 1024]) as q0,
            nc.psum_tensor(f"ps_f1_{lay}1", [128, 1024]) as q1,
            nc.psum_tensor(f"ps_f1_{lay}2", [128, 1024]) as q2,
            nc.psum_tensor(f"ps_f1t{lay}", [16, 256], BF16) as ptr,
        ):
            qs = [q0, q1, q2]
            for hp in range(4):
                for w in range(4):
                    klo = (w - 1) * 512 if w > 0 else 0
                    kw = 1024 if w > 0 else 512
                    for sub in range(4):
                        col = w * 4 + sub
                        for par in range(2):
                            pt = qs[f % 3]
                            if f >= 3:
                                nc.tensor.wait_ge(s_red.h, red_val[f - 3])
                            lh = QK[64 * par:64 * par + 64, hp,
                                    w * 512 + sub * 128: w * 512 + sub * 128 + 128]
                            ins = None
                            for tkb in range(kw // 512):
                                rh = QK[64 * par:64 * par + 64, 4 + hp,
                                        klo + tkb * 512: klo + tkb * 512 + 512]
                                ins = nc.tensor.matmul(
                                    pt[:, tkb * 512:(tkb + 1) * 512], lh, rh,
                                    start=True, stop=True,
                                    tile_position=(64 * par, 0))
                            mv = s_mmb.inc(ins, 1)
                            nc.vector.wait_ge(s_mmb.h, mv)
                            tgt = v["mcols"] if par == 0 else v["colsA"]
                            red_val[f] = s_red.inc(nc.vector.reduce_max(
                                tgt[:, col:col + 1], pt[:, 0:kw], axis=AX), 1)
                            f += 1
                vfence(P)
                mcb = v["STG"][:, 256:272].bitcast(BF16)
                cv = s_red.inc(nc.vector.tensor_copy(mcb, v["GF"][:, 2048:2080]), 1)
                nc.tensor.wait_ge(s_red.h, cv)
                t2 = None
                for q in range(2):
                    t2 = nc.tensor.transpose(ptr[:, q * 128:q * 128 + 128],
                                             mcb[:, q * 16:(q + 1) * 16],
                                             v["ident"])
                mv = s_mmb.inc(t2, 1)
                nc.vector.wait_ge(s_mmb.h, mv)
                if hp > 0:
                    nc.vector.wait_ge(s_dx.h, dmax_written)
                ev = s_red.inc(nc.vector.tensor_copy(mrt, ptr[:]), 1)
                nc.sync.wait_ge(s_red.h, ev)
                s_dx.inc(nc.sync.dma_start(
                    dmax[2 * hp, :].rearrange("(i p) -> i p", p=128),
                    mrt[:, 0:128]), 16)
                s_dx.inc(nc.sync.dma_start(
                    dmax[2 * hp + 1, :].rearrange("(i p) -> i p", p=128),
                    mrt[:, 128:256]), 16)
                dmax_written = s_dx.n
        P.barrier()

        s_row = P.sem("f0row")
        s_exp = P.sem("f0exp")
        s_avp = P.sem("f0avp")
        s_ave = P.sem("f0ave")
        s_mmc = P.sem("f0mmc")
        s_msk = P.sem("f0msk")
        s_nrm = P.sem("f0nrm")
        s_mr = [P.sem("f0mr0"), P.sem("f0mr1")]
        mrow = [v["STG"][0:1, 1024:1280].bitcast(BF16),
                v["STG"][0:1, 1280:1536].bitcast(BF16)]
        rs_row = v["STG"][0:1, 0:512]
        rs_rowb = v["STG"][0:1, 512:768].bitcast(BF16)
        av_sb = v["STG"][64:128, 0:256].bitcast(BF16)
        s_dx.wait(nc.sync)
        row0, n0 = s_row.n, s_nrm.n
        av_cnt = {}
        exp_cnt = {}
        msk_cnt = {}
        mmc_k = {}
        last_av = {}
        jr = 0
        with (
            nc.psum_tensor(f"ps_f2a{lay}", [128, 1024]) as p2a,
            nc.psum_tensor(f"ps_f2b{lay}", [128, 1024]) as p2b,
            nc.psum_tensor(f"ps_f2c{lay}", [128, 1024]) as p2c,
            nc.psum_tensor(f"ps_fav{lay}", [128, 512]) as pav,
            nc.psum_tensor(f"ps_frr{lay}", [128, 512]) as prr,
        ):
            p2 = [p2a, p2b, p2c]
            pend = None
            tt_prev = 0

            def f_av(hw2, grp, t0, w2, h2):
                nc.tensor.wait_ge(s_msk.h, msk_cnt[(hw2, grp)])
                if grp * 2 == t0 and hw2 >= 1:
                    nc.tensor.wait_ge(s_nrm.h, n0 + hw2)
                ins = None
                for k2 in range(2):
                    tkt = grp * 2 + k2
                    jt = (w2 - 1) * 4 + tkt
                    ins = nc.tensor.matmul(pav[0:65, :], VA[:, jt, h2, :],
                                           PT[:, tkt, :],
                                           start=(tkt == t0), stop=(tkt == 7))
                av_cnt[(hw2, grp)] = s_avp.inc(ins, 1)
                last_av[grp] = av_cnt[(hw2, grp)]

            def f_norm2(pn):
                nonlocal tt_prev
                g2, rb2, w2, nv, ttp = pn
                if ttp:
                    nc.tensor.wait_ge(s_ave.h, ttp)
                nc.tensor.wait_ge(s_nrm.h, nv)
                pins = nc.tensor.matmul(prr[0:64, :], v["ones_rowb"][:, 0:64],
                                        rs_rowb, start=True, stop=True)
                pv = s_mmc.inc(pins, 1)
                nc.vector.wait_ge(s_mmc.h, pv)
                dsta = AVN[rb2:rb2 + 64, g2, w2 * 512:(w2 + 1) * 512]
                tt_prev = s_ave.inc(nc.vector.tensor_tensor(
                    dsta, av_sb[:, 0:512], prr[0:64, :], ALU.mult), 1)

            rowv = [0, 0]
            rowv[0] = s_mr[0].inc(nc.sync.dma_start(
                mrow[0], dmax[0, 0:512][None, :]), 16)
            for h in range(8):
                g, rb = h // 2, (h % 2) * 64
                for w in range(4):
                    hw = h * 4 + w
                    sl = hw % 2
                    t0 = 4 if w == 0 else 0
                    grps = list(range(t0 // 2, 4))
                    if hw + 1 < 32:
                        nsl = (hw + 1) % 2
                        h2, w2 = (hw + 1) // 4, (hw + 1) % 4
                        if hw >= 1:
                            nc.sync.wait_ge(s_row.h, row0 + hw)
                        rowv[nsl] = s_mr[nsl].inc(nc.sync.dma_start(
                            mrow[nsl], dmax[h2, w2 * 512:(w2 + 1) * 512][None, :]),
                            16)
                    if hw >= 2:
                        nc.vector.wait_ge(s_mmc.h, mmc_k[hw - 2])
                    if hw < 2:
                        nc.vector.memset(katk[sl][64:65, 0:1024], 1.0)
                    klo = (w - 1) * 512 if w > 0 else 0
                    kw = 1024 if w > 0 else 512
                    koff = 0 if w > 0 else 512
                    nc.vector.tensor_copy(katk[sl][0:64, koff:koff + kw],
                                          QK[rb:rb + 64, 4 + g, klo:klo + kw])
                    nc.vector.tensor_copy(qaug[sl][0:64, 0:512],
                                          QK[rb:rb + 64, g,
                                             w * 512:(w + 1) * 512])
                    nc.vector.wait_ge(s_mr[sl].h, rowv[sl])
                    rv = s_row.inc(nc.vector.tensor_scalar_mul(
                        qaug[sl][64:65, 0:512], mrow[sl], -1.0), 1)
                    rhs = qaug[sl][0:65, 0:512]
                    for gi, grp in enumerate(grps):
                        pt = p2[jr % 3]
                        if gi == 0:
                            nc.tensor.wait_ge(s_row.h, rv)
                        if jr >= 3:
                            nc.tensor.wait_ge(s_exp.h, exp_cnt[jr - 3])
                        ins = None
                        for k2 in range(2):
                            tkt = grp * 2 + k2
                            ins = nc.tensor.matmul(
                                pt[:, k2 * 512:(k2 + 1) * 512],
                                katk[sl][0:65, tkt * 128:(tkt + 1) * 128],
                                rhs, start=True, stop=True)
                        mv = s_mmc.inc(ins, 1)
                        if gi == len(grps) - 1:
                            mmc_k[hw] = mv
                        nc.scalar.wait_ge(s_mmc.h, mv)
                        if grp in last_av:
                            nc.scalar.wait_ge(s_avp.h, last_av[grp])
                        dstp = PT[:, grp * 2:grp * 2 + 2, :] \
                            .rearrange("p j c -> p (j c)")
                        exp_cnt[jr] = s_exp.inc(
                            nc.scalar.activation(dstp, pt[:], AF.Exp,
                                                 scale=8.0), 1)
                        nc.vector.wait_ge(s_exp.h, exp_cnt[jr])
                        for k2 in range(2):
                            tkt = grp * 2 + k2
                            mm = nc.vector.tensor_mul(PT[:, tkt, :],
                                                      PT[:, tkt, :],
                                                      MSK[:, tkt, :])
                        msk_cnt[(hw, grp)] = s_msk.inc(mm, 1)
                        jr += 1
                        if gi >= 1:
                            f_av(hw, grps[gi - 1], t0, w, h)
                        if gi == 1 and pend is not None:
                            f_norm2(pend)
                            pend = None
                    f_av(hw, grps[-1], t0, w, h)
                    nc.vector.wait_ge(s_avp.h, av_cnt[(hw, grps[-1])])
                    nc.vector.reciprocal(rs_row, pav[64:65, :])
                    vfence(P)
                    nc.vector.tensor_copy(rs_rowb, rs_row)
                    nv = s_nrm.inc(nc.vector.tensor_copy(av_sb[:, 0:512],
                                                         pav[0:64, :]), 1)
                    pend = (g, rb, w, nv, tt_prev)
            f_norm2(pend)
        P.barrier()

        wo = Wd["L0_ffn_wo"].rearrange("l (k p) f -> l p k f", p=128)[lay]

        def o_src(k):
            return wo[:, k, :]

        def o_lhs(k, i):
            return AVN[:, k, i * 128:(i + 1) * 128]

        def o_evac(i, ob, pt):
            Xsl = X[:, i, ob * 512:(ob + 1) * 512]
            return nc.vector.tensor_tensor(Xsl, Xsl, pt, ALU.add)

        s_sp.wait(nc.vector)
        emit_proj_tm(P, stg, "fo", o_lhs, o_src, 4, o_evac, nc.vector,
                     n_ob=2)
        P.barrier()

        emit_build_xf(P, X, XF, 1, f"g{lay}")
        hn = FB[:, 0:4096].rearrange("p (g c) -> p g c", g=DG)
        H1 = FB[:, 4096:15360].rearrange("p (m c) -> p m c", m=22)
        scr1 = FB[:, 15360:15872]
        scr2 = FB[:, 15872:16384]
        w1 = Wd["L0_ffn_w1p"].rearrange("l (g p) f -> l p g f", p=128)[lay]
        w2 = Wd["L0_ffn_w2p"].rearrange("l (k p) f -> l p k f", p=128)[lay]
        load_row_bf16(P, Wd["L0_ffn_b2"][lay, 0:512], v["brow_bf"][:, 0:512], s_rows)
        load_row_bf16(P, Wd["L0_ffn_b2"][lay, 512:1024], v["brow_bf"][:, 512:1024],
                      s_rows)
        s_rows.wait(nc.tensor)
        bv = s_dma.inc(nc.sync.dma_start(v["ccol"][:, 8:52],
                                         Wd["L0_ffn_b1_cols"][lay]), 16)
        nc.vector.wait_ge(s_dma.h, bv)
        nc.scalar.wait_ge(s_dma.h, bv)
        s_f = P.sem("ff")
        s_fs = P.sem("ffs")
        for tb in range(4):
            with (
                nc.psum_tensor(f"ps_fmu{lay}{tb}", [1, 512]) as pmu,
                nc.psum_tensor(f"ps_fm2{lay}{tb}", [1, 512]) as pm2,
                nc.psum_tensor(f"ps_fb1{lay}{tb}", [128, 512]) as pb1,
                nc.psum_tensor(f"ps_fb2{lay}{tb}", [128, 512]) as pb2,
            ):
                insmu = None
                for g in range(DG):
                    insmu = nc.tensor.matmul(pmu[:], v["ones_c128"],
                                             sh(g, tb * 512, 512),
                                             start=(g == 0), stop=(g == DG - 1))
                m2pe = {}
                for g in range(DG):
                    sq = scr1 if g % 2 == 0 else scr2
                    if g >= 2:
                        nc.scalar.wait_ge(s_pe.h, m2pe[g - 2])
                    qv = s_fs.inc(nc.scalar.activation(sq, sh(g, tb * 512, 512),
                                                       AF.Square), 1)
                    nc.tensor.wait_ge(s_fs.h, qv)
                    insmu = nc.tensor.matmul(pm2[:], v["ones_c128"], sq,
                                             start=(g == 0), stop=(g == DG - 1))
                    m2pe[g] = s_pe.inc(insmu, 1)
                nc.vector.wait_ge(s_pe.h, m2pe[DG - 1])
                r0 = v["STG"][0:1, 0:512]
                r1 = v["STG"][0:1, 512:1024]
                nc.vector.tensor_scalar_mul(r0, pmu[:], -1.0 / 1024.0)
                vfence(P)
                nc.vector.tensor_mul(r1, r0, r0)
                vfence(P)
                nc.vector.scalar_tensor_tensor(r1, pm2[:], 1.0 / 1024.0, r1,
                                               ALU.mult, ALU.subtract)
                vfence(P)
                nc.vector.tensor_single_scalar(r1, r1, 1e-5, ALU.add)
                fv = s_f.inc(nc.vector.engine_nop(), 1)
                s_f.wait(nc.scalar, fv)
                fv = s_fs.inc(nc.scalar.activation(r1, r1, AF.Sqrt), 1)
                s_fs.wait(nc.vector, fv)
                nc.vector.reciprocal(r1, r1)
                vfence(P)
                nmb = v["STG"][0:1, 1024:1280].bitcast(BF16)
                rsb = v["STG"][0:1, 1280:1536].bitcast(BF16)
                nc.vector.tensor_copy(nmb, r0)
                vfence(P)
                fv = s_f.inc(nc.vector.tensor_copy(rsb, r1), 1)
                nc.tensor.wait_ge(s_f.h, fv)
                i1 = nc.tensor.matmul(pb1[:], v["ones_rowb"], nmb,
                                      start=True, stop=True)
                i2 = nc.tensor.matmul(pb2[:], v["ones_rowb"], rsb,
                                      start=True, stop=True)
                pe_bcast = s_pe.inc(i2, 1)
                nc.vector.wait_ge(s_pe.h, pe_bcast)
                for g in range(DG):
                    nc.vector.tensor_add(scr1, sh(g, tb * 512, 512), pb1[:])
                    vfence(P)
                    nc.vector.tensor_mul(scr2, scr1, pb2[:])
                    vfence(P)
                    s_f.inc(nc.vector.tensor_scalar(
                        hn[:, g, :], scr2,
                        v["bcol"][:, lay * 8 + g: lay * 8 + g + 1],
                        v["bcol"][:, 16 + lay * 8 + g: 17 + lay * 8 + g],
                        ALU.mult, ALU.add), 1)
                s_f.wait(nc.tensor)
                nc.sync.wait_ge(s_pe.h, pe_bcast)
            s_w1 = P.sem("fw1")
            w10 = s_w1.n
            with (
                nc.psum_tensor(f"ps_w1a{lay}{tb}", [128, 512]) as pw1a,
                nc.psum_tensor(f"ps_w1b{lay}{tb}", [128, 512]) as pw1b,
            ):
                pws = [pw1a, pw1b]
                for m in range(44):
                    dst = v["WB"][:, (m % 2) * 1024:(m % 2) * 1024 + 1024]
                    stg.stage(w1[:, :, m * 128:(m + 1) * 128], dst, 1024,
                              consumer_sem=s_w1 if m >= 2 else None,
                              consumer_val=w10 + m - 1)
                    stg.wait(nc.tensor)
                    pt = pws[m % 2]
                    if m >= 2:
                        nc.tensor.wait_ge(s_w1.h, s_w1.n - 1)
                    ins = None
                    for g in range(DG):
                        ins = nc.tensor.matmul(pt[:], dst[:, g * 128:(g + 1) * 128],
                                               hn[:, g, :], start=(g == 0),
                                               stop=(g == DG - 1))
                    mv = s_pe.inc(ins, 1)
                    if m % 2 == 0:
                        nc.vector.wait_ge(s_pe.h, mv)
                        nc.vector.wait_ge(s_w1.h, s_w1.n)
                        s_w1.inc(nc.vector.tensor_scalar_add(
                            scr1, pt[:], v["ccol"][:, 8 + m:9 + m]), 1)
                    else:
                        nc.scalar.wait_ge(s_pe.h, mv)
                        gv = s_fs.inc(nc.scalar.activation(
                            scr2, pt[:], AF.Gelu,
                            bias=v["ccol"][:, 8 + m:9 + m]), 1)
                        nc.vector.wait_ge(s_fs.h, gv)
                        nc.vector.wait_ge(s_w1.h, s_w1.n)
                        s_w1.inc(nc.vector.tensor_mul(H1[:, m // 2, :],
                                                      scr1, scr2), 1)
            P.barrier()

            def w2_src(k):
                return w2[:, k, :]

            def w2_lhs(k, i):
                return H1[:, k, (i % 4) * 128:(i % 4) * 128 + 128]

            def w2_evac(i, ob, pt):
                Xsl = X[:, i, ob * 512:(ob + 1) * 512]
                return nc.vector.tensor_tensor(Xsl, Xsl, pt, ALU.add)

            emit_proj_tm(P, stg, "fw2", w2_lhs, w2_src, 22, w2_evac,
                         nc.vector, n_ob=2,
                         bias_row=v["brow_bf"][:, 0:1024], tb_list=(tb,))
            P.barrier()

    s_sp.wait(nc.sync)
    SQ = FD[:, 0:1024]
    x0d = P.dram["x0"].rearrange("(i p) d -> p i d", p=128)
    x0s = [v["STG"][:, 0:1024], v["STG"][:, 1024:2048]]
    s_x0 = [P.sem("f0x0a"), P.sem("f0x0b")]
    for i in range(TI):
        sl = i % 2
        if i >= 2:
            nc.sync.wait_ge(s_ln.h, s_ln.n)
        dv = s_x0[sl].inc(nc.sync.dma_start(x0s[sl], x0d[:, i, :]), 16)
        nc.vector.wait_ge(s_x0[sl].h, dv)
        nc.vector.tensor_add(X[:, i, :], X[:, i, :], x0s[sl])
        vfence(P)
        emit_ln(P, X[:, i, :], SQ, 1024, s_ln)
    P.barrier()


def emit_l1_attn(P, stg, Wd):
    nc = P.nc
    v = P.v
    P.gate_w("L1_attn_W_out")
    X = v["X"]
    FA, FB, FC, FD = v["FA"], v["FB"], v["FC"], v["FD"]
    BT = FA[0:16, 0:2048]
    CT = FA[64:80, 0:2048]
    Bblk = FA[:, 2048:6144].rearrange("p (i q) -> p i q", i=TI)
    Cblk = FA[:, 6144:10240].rearrange("p (i q) -> p i q", i=TI)
    Hall = FA[:, 10240:14352]
    BTt = FA[:, 14352:14608].rearrange("p (i s) -> p i s", i=TI)
    Wqb = FA[:, 14608:15120]
    Evq = FA[:, 15120:15632]
    WTs = FA[:, 15632:15760]
    HTs = FA[:, 15760:16016]
    hspre = FA[:, 16016:16144]
    Eyq = FA[:, 16016:16392]
    xsT = FB[:, 0:16384].rearrange("p (g c) -> p g c", g=DG)
    dtT = FC[:, 0:16384].rearrange("p (g c) -> p g c", g=DG)
    Gbc = FD[:, 0:16384].rearrange("p (j c) -> p j c", j=8)
    Xr = v["X"].rearrange("p i d -> p (i d)")
    va = Xr[:, 0:2048]
    Dc = Xr[:, 2048:2304]
    Sall = Xr[:, 2560:6656]
    yac = Xr[:, 6656:8704]
    zsl = Xr[:, 8704:9728].bitcast(BF16)
    esc = Xr[:, 9728:11776]
    Eb = Xr[:, 11776:12800].bitcast(BF16)
    Pb = Xr[:, 12800:13824].bitcast(BF16)
    PGb = Xr[:, 13824:14848].bitcast(BF16)

    s_dma = P.sem("m_dma")
    s_rows = P.sem("m_rows")
    s_ln = P.sem("m_ln")
    s_sp = P.sem("m_spill")
    s_pe = P.sem("m_pe")

    s_dma.inc(nc.sync.dma_start(v["bcol"][:, 0:8], Wd["L1_attn_logA_cols"]), 16)
    s_dma.inc(nc.sync.dma_start(v["bcol"][:, 8:16], Wd["L1_attn_dtb_cols"]), 16)
    s_dma.inc(nc.sync.dma_start(v["bcol"][:, 16:24], Wd["L1_attn_D_cols"]), 16)
    s_dma.inc(nc.sync.dma_start(v["bcol"][:, 24:32], Wd["L1_attn_convb_cols"]), 16)
    s_dma.inc(nc.sync.dma_start(v["ccol"][:, 8:40], Wd["L1_attn_convw_cols"]), 16)
    Acol = v["ccol"][:, 40:48]
    s_dma.wait(nc.scalar)
    s_ln.inc(nc.scalar.activation(Acol, v["bcol"][:, 0:8], AF.Exp), 1)
    bmask = FA[:, 14608:14864]
    cmask = FA[:, 14864:15120]
    tid16 = FA[64:80, 15120:15248]

    xn_tok = FB[:, 0:16384].rearrange("p (i d) -> p i d", i=TI)
    gb_bc = FD[:, 0:2048]
    with nc.psum_tensor("ps_mgb", [128, 1024]) as pgb:
        for which, wname in ((0, "L1_attn_ln_g"), (1, "L1_attn_ln_b")):
            load_row_bf16(P, Wd[wname][0:512], v["brow_bf"][:, 0:512], s_rows)
            load_row_bf16(P, Wd[wname][512:1024], v["brow_bf"][:, 512:1024], s_rows)
            s_rows.wait(nc.tensor)
            if which == 1:
                s_ln.wait(nc.tensor)
            ins = None
            for q in range(2):
                ins = nc.tensor.matmul(pgb[:, q * 512:(q + 1) * 512],
                                       v["ones_rowb"][:, 0:128],
                                       v["brow_bf"][:, q * 512:(q + 1) * 512],
                                       start=True, stop=True)
            pv = s_pe.inc(ins, 1)
            nc.vector.wait_ge(s_pe.h, pv)
            s_ln.inc(nc.vector.tensor_copy(
                gb_bc[:, which * 1024:(which + 1) * 1024], pgb[:]), 1)
    lc = v["lncol"]
    t1f = v["STG"][:, 0:1024]
    gbcf = gb_bc[:, 0:1024]
    bbcf = gb_bc[:, 1024:2048]
    for i in range(TI):
        Xv = X[:, i, :]
        nc.vector.wait_ge(s_ln.h, s_ln.n)
        nc.vector.reduce_sum(lc[:, 0:1], Xv, axis=AX)
        vfence(P)
        nc.vector.tensor_scalar_mul(lc[:, 1:2], lc[:, 0:1], -1.0 / 1024.0)
        vv = s_ln.inc(nc.vector.engine_nop(), 1)
        s_ln.wait(nc.scalar, vv)
        vv = s_ln.inc(nc.scalar.activation(FD[:, 2560:3584], Xv, AF.Square,
                                           bias=lc[:, 1:2], scale=1.0,
                                           accum_out=lc[:, 2:3]), 1)
        s_ln.wait(nc.vector, vv)
        nc.vector.tensor_scalar(lc[:, 3:4], lc[:, 2:3], 1.0 / 1024.0, 1e-5,
                                ALU.mult, ALU.add)
        vv = s_ln.inc(nc.vector.engine_nop(), 1)
        s_ln.wait(nc.scalar, vv)
        vv = s_ln.inc(nc.scalar.activation(lc[:, 4:5], lc[:, 3:4], AF.Sqrt), 1)
        s_ln.wait(nc.vector, vv)
        nc.vector.reciprocal(lc[:, 5:6], lc[:, 4:5])
        vfence(P)
        nc.vector.tensor_scalar(t1f, Xv, lc[:, 1:2], lc[:, 5:6],
                                ALU.add, ALU.mult)
        vfence(P)
        nc.vector.tensor_mul(t1f, t1f, gbcf)
        vfence(P)
        s_ln.inc(nc.vector.tensor_tensor(xn_tok[:, i, :], t1f, bbcf, ALU.add), 1)
    P.barrier()

    if M_PHASES < 2:
        return
    s_sp.inc(nc.sync.dma_start(
        P.dram["x0"].rearrange("(i p) d -> p i d", p=128), X), 16)
    s_sp.wait(nc.sync)
    xnF = FA[:, 0:16384].rearrange("p (g c) -> p g c", g=DG)
    emit_build_xf(P, xn_tok, xnF, 0, "mx", src_bf16=True)

    win = Wd["L1_attn_W_in"].rearrange("(g p) f -> p g f", p=128)
    zscr = FD[:, 2048:2560]
    s_z = P.sem("m_z")

    def win_src(m):
        return win[:, :, m * 128:(m + 1) * 128]

    def win_rhs(g, tb):
        return xnF[:, g, tb * 512:(tb + 1) * 512]

    def win_evac(m, tb, pt):
        if m < 8:
            return nc.vector.tensor_copy(dtT[:, m, tb * 512:(tb + 1) * 512], pt[:])
        zv = s_z.inc(nc.vector.tensor_copy(zscr, pt[:]), 1)
        nc.scalar.wait_ge(s_z.h, zv)
        dv = s_z.inc(nc.scalar.dma_start(
            P.dram["z"][(m - 8) * 128:(m - 7) * 128, tb * 512:(tb + 1) * 512],
            zscr), 16)
        nc.vector.wait_ge(s_z.h, dv)
        return nc.vector.engine_nop()

    emit_proj_fm(P, stg, "mwin", win_src, 16, win_rhs, DG, win_evac, nc.vector)
    P.barrier()
    if M_PHASES < 3:
        return

    cw = v["ccol"][:, 8:40].rearrange("p (g k) -> p g k", g=DG)
    s_cv = P.sem("m_cv")
    for g in range(DG):
        xp = dtT[:, g, :]
        xs = xsT[:, g, :]
        acc = t1f.bitcast(BF16)
        nc.vector.wait_ge(s_cv.h, s_cv.n)
        nc.vector.tensor_scalar_mul(acc, xp, cw[:, g, 3:4])
        vfence(P)
        for k in range(3):
            sh = 3 - k
            nc.vector.scalar_tensor_tensor(acc[:, sh:2048], xp[:, 0:2048 - sh],
                                           cw[:, g, k:k + 1], acc[:, sh:2048],
                                           ALU.mult, ALU.add)
            vfence(P)
        cvv = s_cv.inc(nc.vector.engine_nop(), 1)
        s_cv.wait(nc.scalar, cvv)
        s_cv.inc(nc.scalar.activation(xs, acc, AF.Silu,
                                      bias=v["bcol"][:, 24 + g:25 + g]), 1)
        s_cv.wait(nc.vector)
    P.barrier()
    if M_PHASES < 4:
        return

    dtw = Wd["L1_attn_dt_w"].rearrange("(g p) f -> p g f", p=128)
    s_dt = P.sem("m_dt")
    spsc = esc[:, 0:512]

    def dt_src(m):
        return dtw[:, :, m * 128:(m + 1) * 128]

    def dt_rhs(g, tb):
        return xsT[:, g, tb * 512:(tb + 1) * 512]

    def dt_evac(m, tb, pt):
        sev = P.sem("pf_ev")
        nc.scalar.wait_ge(sev.h, sev.n)
        dv = s_dt.inc(nc.scalar.activation(spsc, pt[:], AF.Exp,
                                           bias=v["bcol"][:, 8 + m:9 + m]), 1)
        nc.scalar.wait_ge(s_dt.h, dv)
        return nc.scalar.activation(dtT[:, m, tb * 512:(tb + 1) * 512],
                                    spsc, AF.Ln, bias=1.0)

    emit_proj_fm(P, stg, "mdt", dt_src, 8, dt_rhs, DG, dt_evac, nc.scalar)
    P.barrier()
    if M_PHASES < 5:
        return

    s_bc = P.sem("m_bc")
    bcw = v["WB"][:, 0:128].rearrange("p (g s) -> p g s", g=DG)
    ccw = v["WB"][:, 128:256].rearrange("p (g s) -> p g s", g=DG)
    stg.stage(Wd["L1_attn_B_w"].rearrange("(g p) s -> p g s", p=128),
              bcw.rearrange("p g s -> p (g s)"), 128)
    stg.wait(nc.tensor)
    stg.stage(Wd["L1_attn_C_w"].rearrange("(g p) s -> p g s", p=128),
              ccw.rearrange("p g s -> p (g s)"), 128)
    stg.wait(nc.tensor)
    with (
        nc.psum_tensor("ps_mb", [16, 512]) as pb,
        nc.psum_tensor("ps_mc", [16, 512]) as pc,
    ):
        for tb in range(4):
            if tb > 0:
                s_bc.wait(nc.tensor)
            insb = insc = None
            for g in range(DG):
                insb = nc.tensor.matmul(pb[:], bcw[:, g, :],
                                        xsT[:, g, tb * 512:(tb + 1) * 512],
                                        start=(g == 0), stop=(g == DG - 1))
            for g in range(DG):
                insc = nc.tensor.matmul(pc[:], ccw[:, g, :],
                                        xsT[:, g, tb * 512:(tb + 1) * 512],
                                        start=(g == 0), stop=(g == DG - 1))
            pv = s_pe.inc(insc, 1)
            nc.vector.wait_ge(s_pe.h, pv)
            s_bc.inc(nc.vector.tensor_copy(BT[:, tb * 512:(tb + 1) * 512], pb[:]), 1)
            s_bc.inc(nc.vector.tensor_copy(CT[:, tb * 512:(tb + 1) * 512], pc[:]), 1)
    P.barrier()
    prodsc = FA[0:16, 2048:2560]
    ctsc = FA[0:16, 2560:3072]
    onesc16 = v["CS"][0:16, 300:301]
    nc.vector.memset(onesc16, 1.0)
    s_g = P.sem("m_g")
    with (
        nc.psum_tensor("ps_mgr", [1, 512]) as pgr,
        nc.psum_tensor("ps_mgb2", [128, 512]) as pgb2,
    ):
        for j in range(8):
            for tb in range(4):
                lo = tb * 512
                nc.vector.wait_ge(s_g.h, s_g.n)
                nc.vector.tensor_copy(ctsc, CT[:, lo:lo + 512])
                vfence(P)
                if j == 0:
                    gv = s_g.inc(nc.vector.tensor_mul(
                        prodsc, ctsc, BT[:, lo:lo + 512]), 1)
                elif lo == 0:
                    nc.vector.memset(prodsc[:, 0:j], 0.0)
                    gv = s_g.inc(nc.vector.tensor_mul(
                        prodsc[:, j:512], ctsc[:, j:512], BT[:, 0:512 - j]), 1)
                else:
                    gv = s_g.inc(nc.vector.tensor_mul(
                        prodsc, ctsc, BT[:, lo - j:lo + 512 - j]), 1)
                nc.tensor.wait_ge(s_g.h, gv)
                i1 = nc.tensor.matmul(pgr[:], onesc16, prodsc,
                                      start=True, stop=True)
                pv = s_pe.inc(i1, 1)
                nc.vector.wait_ge(s_pe.h, pv)
                gv = s_g.inc(nc.vector.tensor_copy(v["brow_bf"][:, 0:512], pgr[:]), 1)
                nc.tensor.wait_ge(s_g.h, gv)
                i2 = nc.tensor.matmul(pgb2[:], v["ones_rowb"],
                                      v["brow_bf"][:, 0:512], start=True, stop=True)
                pv = s_pe.inc(i2, 1)
                nc.vector.wait_ge(s_pe.h, pv)
                s_g.inc(nc.vector.tensor_copy(Gbc[:, j, lo:lo + 512], pgb2[:]), 1)
    P.barrier()
    s_bb = P.sem("m_bb")
    s_dma.inc(nc.sync.dma_start(bmask, Wd["c_bmask8"]), 16)
    s_dma.inc(nc.sync.dma_start(cmask, Wd["c_cmask"]), 16)
    s_dma.inc(nc.sync.dma_start(tid16, Wd["c_tid16"]), 16)
    s_dma.wait(nc.vector)
    s_dma.wait(nc.tensor)
    with (
        nc.psum_tensor("ps_mt", [128, 512], BF16) as pt4,
        nc.psum_tensor("ps_mtc", [128, 512]) as ptc,
    ):
        for q in range(4):
            if q > 0:
                nc.tensor.wait_ge(s_bb.h, s_bb.n)
            ins = None
            for k in range(4):
                i = q * 4 + k
                ins = nc.tensor.transpose(pt4[:, k * 16:(k + 1) * 16],
                                          BT[:, i * 128:(i + 1) * 128],
                                          v["ident"][0:16, 0:16])
            pv = s_pe.inc(ins, 1)
            nc.vector.wait_ge(s_pe.h, pv)
            s_bb.inc(nc.vector.tensor_copy(
                BTt[:, 4 * q:4 * q + 4, :].rearrange("p i s -> p (i s)"),
                pt4[:, 0:64]), 1)
        nc.vector.wait_ge(s_bb.h, s_bb.n)
        for i in range(TI):
            a1 = bmask.rearrange("p (c s) -> p c s", c=16)
            a2 = BTt[:, i, :].rearrange("p (o s) -> p o s", o=1)
            b1, b2 = bass.broadcast_tensor_aps(a1, a2)
            s_bb.inc(nc.vector.tensor_tensor(
                Bblk[:, i, :].rearrange("p (c s) -> p c s", c=16),
                b1, b2, ALU.mult), 1)
        for i in range(TI):
            for h2 in range(2):
                jj = i * 2 + h2
                pcc = ptc[:, (jj % 4) * 128:(jj % 4) * 128 + 128]
                if jj >= 4:
                    nc.tensor.wait_ge(s_bb.h, s_bb.n)
                i3 = nc.tensor.matmul(pcc, tid16,
                                      CT[:, i * 128:(i + 1) * 128],
                                      start=True, stop=True)
                pv = s_pe.inc(i3, 1)
                nc.vector.wait_ge(s_pe.h, pv)
                s_bb.inc(nc.vector.tensor_tensor(
                    Cblk[:, i, h2 * 128:(h2 + 1) * 128], pcc,
                    cmask[:, h2 * 128:(h2 + 1) * 128], ALU.mult), 1)
    P.barrier()
    if M_PHASES < 6:
        return

    s_sc = P.sem("m_sc")
    s_scs = P.sem("m_scs")
    s_zd = P.sem("m_zd")
    va3 = va.rearrange("p (c r) -> p c r", r=8)
    for g in M_GORDER:
        nc.scalar.wait_ge(s_scs.h, s_scs.n)
        zdv = s_zd.inc(nc.scalar.dma_start(
            zsl, P.dram["z"][g * 128:(g + 1) * 128, :]), 16)
        nc.vector.wait_ge(s_sc.h, s_sc.n)
        nc.vector.wait_ge(s_scs.h, s_scs.n)
        nc.vector.tensor_scalar(va, dtT[:, g, :], Acol[:, g:g + 1], -1.0,
                                ALU.mult, ALU.mult)
        vfence(P)
        nc.vector.tensor_mul(dtT[:, g, :], dtT[:, g, :], xsT[:, g, :])
        ug = dtT[:, g, :]
        ug3 = ug.rearrange("p (c r) -> p c r", r=8)
        vfence(P)
        for pstep in range(1, 8):
            nc.vector.tensor_add(va3[:, :, pstep:pstep + 1],
                                 va3[:, :, pstep:pstep + 1],
                                 va3[:, :, pstep - 1:pstep])
            vfence(P)
        scv = s_sc.inc(nc.vector.engine_nop(), 1)
        s_sc.wait(nc.scalar, scv)
        dcv = s_scs.inc(nc.scalar.activation(Dc, va3[:, :, 7:8], AF.Exp), 1)
        nc.vector.tensor_mul(yac, ug, Gbc[:, 0, :])
        vfence(P)
        esc3 = esc.rearrange("p (c r) -> p c r", r=8)
        Eb3 = Eb.rearrange("p (c r) -> p c r", r=8)
        Pb3 = Pb.rearrange("p (c r) -> p c r", r=8)
        PG3 = PGb.rearrange("p (c r) -> p c r", r=8)
        yac3 = yac.rearrange("p (c r) -> p c r", r=8)
        for j in range(1, 8):
            w = 8 - j
            G3 = Gbc[:, j, :].rearrange("p (c r) -> p c r", r=8)
            nc.vector.tensor_sub(esc3[:, :, j:8], va3[:, :, j:8], va3[:, :, 0:w])
            ev = s_sc.inc(nc.vector.engine_nop(), 1)
            s_sc.wait(nc.scalar, ev)
            ev = s_scs.inc(nc.scalar.activation(Eb3[:, :, j:8], esc3[:, :, j:8],
                                                AF.Exp), 1)
            s_scs.wait(nc.vector, ev)
            nc.vector.tensor_mul(Pb3[:, :, j:8], Eb3[:, :, j:8], ug3[:, :, 0:w])
            vfence(P)
            nc.vector.tensor_mul(PG3[:, :, j:8], Pb3[:, :, j:8], G3[:, :, j:8])
            vfence(P)
            nc.vector.tensor_add(yac3[:, :, j:8], yac3[:, :, j:8], PG3[:, :, j:8])
            vfence(P)
        with (
            nc.psum_tensor(f"ps_ms{g}", [128, 512]) as pS,
            nc.psum_tensor(f"ps_mw{g}", [128, 512], BF16) as pW,
            nc.psum_tensor(f"ps_mh{g}", [128, 256], BF16) as pH,
            nc.psum_tensor(f"ps_my{g}", [128, 512]) as pY,
        ):
            for q in range(4):
                lo = q * 512
                vq3 = va3[:, 64 * q:64 * (q + 1), :]
                b1, b2 = bass.broadcast_tensor_aps(vq3[:, :, 7:8], vq3)
                nc.vector.wait_ge(s_sc.h, s_sc.n)
                nc.vector.wait_ge(s_scs.h, s_scs.n)
                nc.vector.tensor_tensor(
                    esc3[:, 0:64, :], b1, b2, ALU.subtract)
                ev = s_sc.inc(nc.vector.engine_nop(), 1)
                s_sc.wait(nc.scalar, ev)
                ev = s_scs.inc(nc.scalar.activation(Wqb, esc[:, 0:512], AF.Exp), 1)
                s_scs.wait(nc.vector, ev)
                nc.vector.tensor_mul(Wqb, Wqb, ug[:, lo:lo + 512])
                wv = s_sc.inc(nc.vector.engine_nop(), 1)
                nc.tensor.wait_ge(s_sc.h, wv)
                for k4 in range(4):
                    i = q * 4 + k4
                    it = nc.tensor.transpose(pW[:, 0:128],
                                             Wqb[:, k4 * 128:(k4 + 1) * 128],
                                             v["ident"])
                    pv = s_pe.inc(it, 1)
                    nc.vector.wait_ge(s_pe.h, pv)
                    wtv = s_sc.inc(nc.vector.tensor_copy(WTs, pW[:, 0:128]), 1)
                    nc.tensor.wait_ge(s_sc.h, wtv)
                    iS = nc.tensor.matmul(pS[:, 0:256], WTs, Bblk[:, i, :],
                                          start=True, stop=True)
                    pv = s_pe.inc(iS, 1)
                    nc.vector.wait_ge(s_pe.h, pv)
                    s_sc.inc(nc.vector.tensor_copy(
                        Sall[:, i * 256:(i + 1) * 256], pS[:, 0:256]), 1)
            s_sc.wait(nc.vector)
            s_scs.wait(nc.vector)
            nc.vector.memset(
                Hall.rearrange("p (s c) -> p s c", s=16)[:, :, 0:1], 0.0)
            vfence(P)
            Hall3 = Hall.rearrange("p (s c) -> p s c", s=16)
            Sall3 = Sall.rearrange("p (a s) -> p a s", s=16)
            for s16 in range(16):
                nc.vector.tensor_tensor_scan(
                    Hall3[:, s16, 1:257], Dc, Sall3[:, :, s16],
                    0.0, ALU.mult, ALU.add)
            hv = s_sc.inc(nc.vector.engine_nop(), 1)
            nc.tensor.wait_ge(s_sc.h, hv)
            nc.vector.wait_ge(s_sc.h, hv)
            for q in range(4):
                lo = q * 512
                nc.scalar.wait_ge(s_sc.h, s_sc.n)
                eqv = s_scs.inc(nc.scalar.activation(Evq, va[:, lo:lo + 512],
                                                     AF.Exp), 1)
                for k4 in range(4):
                    i = q * 4 + k4
                    for h2 in range(2):
                        hsrc = Hall3[:, :, i * 16 + h2 * 8:
                                     i * 16 + h2 * 8 + 8] \
                            .rearrange("p s c -> p c s")
                        nc.vector.wait_ge(s_pe.h, s_pe.n)
                        hcv = s_sc.inc(nc.vector.tensor_copy(
                            hspre.rearrange("p (c s) -> p c s", c=8), hsrc), 1)
                        nc.tensor.wait_ge(s_sc.h, hcv)
                        it = nc.tensor.transpose(pH[:, h2 * 128:(h2 + 1) * 128],
                                                 hspre, v["ident"])
                        pv = s_pe.inc(it, 1)
                        nc.vector.wait_ge(s_pe.h, pv)
                        htv = s_sc.inc(nc.vector.tensor_copy(
                            HTs[:, h2 * 128:(h2 + 1) * 128],
                            pH[:, h2 * 128:(h2 + 1) * 128]), 1)
                    nc.tensor.wait_ge(s_sc.h, s_sc.n)
                    ins = None
                    for h2 in range(2):
                        ins = nc.tensor.matmul(
                            pY[:, k4 * 128:(k4 + 1) * 128],
                            HTs[:, h2 * 128:(h2 + 1) * 128],
                            Cblk[:, i, h2 * 128:(h2 + 1) * 128],
                            start=(h2 == 0), stop=(h2 == 1))
                    pv = s_pe.inc(ins, 1)
                nc.vector.wait_ge(s_pe.h, s_pe.n)
                nc.vector.wait_ge(s_scs.h, eqv)
                nc.vector.tensor_mul(Pb[:, 0:512], Evq, pY[:])
                vfence(P)
                cyv = s_sc.inc(nc.vector.tensor_tensor(
                    yac[:, lo:lo + 512], yac[:, lo:lo + 512],
                    Pb[:, 0:512], ALU.add), 1)
                nc.tensor.wait_ge(s_sc.h, cyv)
        nc.vector.wait_ge(s_zd.h, zdv)
        nc.vector.wait_ge(s_sc.h, s_sc.n)
        nc.vector.wait_ge(s_scs.h, s_scs.n)
        nc.vector.tensor_scalar_mul(Eb[:, 0:2048], xsT[:, g, :],
                                    v["bcol"][:, 16 + g:17 + g])
        vfence(P)
        nc.vector.tensor_add(esc, yac, Eb[:, 0:2048])
        zv = s_sc.inc(nc.vector.engine_nop(), 1)
        s_sc.wait(nc.scalar, zv)
        zv = s_scs.inc(nc.scalar.activation(Pb[:, 0:2048], zsl, AF.Silu), 1)
        s_scs.wait(nc.vector, zv)
        s_sc.inc(nc.vector.tensor_mul(xsT[:, g, :], esc, Pb[:, 0:2048]), 1)
    P.barrier()
    if M_PHASES < 7:
        return

    s_sp.inc(nc.sync.dma_start(
        X, P.dram["x0"].rearrange("(i p) d -> p i d", p=128)), 16)
    s_sp.wait(nc.vector)
    s_sp.wait(nc.tensor)
    wout = Wd["L1_attn_W_out"].rearrange("(k p) f -> p k f", p=128)

    def wo_src(k):
        return wout[:, k, :]

    def wo_lhs(k, i):
        return xsT[:, k, i * 128:(i + 1) * 128]

    def wo_evac(i, ob, pt):
        Xsl = X[:, i, ob * 512:(ob + 1) * 512]
        return nc.vector.scalar_tensor_tensor(Xsl, Xsl, 2.0, pt,
                                              ALU.mult, ALU.add)

    emit_proj_tm(P, stg, "mout", wo_lhs, wo_src, DG, wo_evac, nc.vector, n_ob=2)
    P.barrier()
    SQ = FD[:, 0:1024]
    for i in range(TI):
        emit_ln(P, X[:, i, :], SQ, 1024, s_ln)
    P.barrier()


def emit_l1_ffn(P, stg, Wd):
    nc = P.nc
    v = P.v
    P.gate_w("L1_ffn_lora_Bc")
    X = v["X"]
    XF = v["FA"][:, 0:16384].rearrange("p (g c) -> p g c", g=DG)
    INTER = v["FB"][:, 0:16384].rearrange("p (m c) -> p m c", m=32)
    OH = v["FC"][:, 6656:6784].rearrange("p (i e) -> p i e", i=TI)
    RM = v["FC"][:, 6784:7296].rearrange("p (i r) -> p i r", i=TI)
    rmT = v["FC"][0:32, 0:2048]
    loraB = v["FC"][0:32, 2048:6144]

    emit_build_xf(P, X, XF, 0, "m")

    s_rows = P.sem("f_rows")
    load_row_bf16(P, Wd["L1_ffn_gate_b"], v["brow_bf"][:, 0:8], s_rows)
    s_rows.wait(nc.tensor)

    s_g = P.sem("f_g")
    s_goh = P.sem("f_goh")
    ohf = v["ccol"][:, 8:16]
    cum = v["ccol"][:, 16:24]
    t1c = v["ccol"][:, 24:32]
    mcol = v["ccol"][:, 32:33]
    zer8 = v["ccol"][:, 40:48]
    nc.vector.memset(zer8, 0.0)
    gwt = Wd["L1_ffn_gate_Wt"].rearrange("(g p) e -> p g e", p=128)
    gdst = v["WB"][:, 0:1024]
    stg.stage(gwt, gdst[:, 0:64].rearrange("p (g e) -> p g e", g=DG), 64)
    stg.wait(nc.tensor)
    gd3 = gdst[:, 0:64].rearrange("p (g e) -> p g e", g=DG)
    with (
        nc.psum_tensor("ps_fg0", [128, 512]) as pg0,
        nc.psum_tensor("ps_fg1", [128, 512]) as pg1,
    ):
        pg = [pg0[:, 0:8], pg1[:, 0:8]]
        for i in range(TI):
            pt = pg[i % 2]
            if i >= 2:
                nc.tensor.wait_ge(s_goh.h, i - 1)
            ins = None
            for g in range(DG):
                ins = nc.tensor.matmul(pt, XF[:, g, i * 128:(i + 1) * 128],
                                       gd3[:, g, :], start=(g == 0), stop=False)
            ins = nc.tensor.matmul(pt, P.v["ones_rowb"],
                                   v["brow_bf"][:, 0:8], start=False, stop=True)
            mv = s_g.inc(ins, 1)
            nc.vector.wait_ge(s_g.h, mv)
            nc.vector.wait_ge(s_goh.h, s_goh.n)
            nc.vector.reduce_max(mcol, pt, axis=AX)
            vfence(P)
            nc.vector.tensor_scalar(ohf, pt, mcol, 1.0, ALU.is_ge, ALU.mult)
            vfence(P)
            nc.vector.tensor_tensor_scan(cum, ohf, zer8, 0.0, ALU.add, ALU.add)
            vfence(P)
            nc.vector.tensor_scalar(t1c, cum, 1.0, 1.0, ALU.is_equal, ALU.mult)
            vfence(P)
            s_goh.inc(nc.vector.tensor_tensor(OH[:, i, :], t1c, ohf, ALU.mult), 1)
    P.barrier()

    lat = Wd["L1_ffn_lora_At"].rearrange("(g p) r -> p g r", p=128)
    s_r = P.sem("f_r")
    s_rm = P.sem("f_rm")
    stg.stage(lat, gdst[:, 64:320].rearrange("p (g r) -> p g r", g=DG), 256)
    la3 = gdst[:, 64:320].rearrange("p (g r) -> p g r", g=DG)
    stg.wait(nc.tensor)
    with (
        nc.psum_tensor("ps_fr0", [128, 512]) as pr0,
        nc.psum_tensor("ps_fr1", [128, 512]) as pr1,
    ):
        pr = [pr0[:, 0:32], pr1[:, 0:32]]
        for i in range(TI):
            pt = pr[i % 2]
            if i >= 2:
                nc.tensor.wait_ge(s_rm.h, i - 1)
            ins = None
            for g in range(DG):
                ins = nc.tensor.matmul(pt, XF[:, g, i * 128:(i + 1) * 128],
                                       la3[:, g, :], start=(g == 0),
                                       stop=(g == DG - 1))
            mv = s_r.inc(ins, 1)
            nc.vector.wait_ge(s_r.h, mv)
            a1 = pt.rearrange("p (e r) -> p e r", e=8)
            a2 = OH[:, i, :].rearrange("p (e o) -> p e o", o=1)
            b1, b2 = bass.broadcast_tensor_aps(a1, a2)
            s_rm.inc(nc.vector.tensor_tensor(
                RM[:, i, :].rearrange("p (e r) -> p e r", e=8), b1, b2,
                ALU.mult), 1)
    s_t = P.sem("f_t")
    s_te = P.sem("f_te")
    with nc.psum_tensor("ps_ft", [32, 512], BF16) as ptr:
        s_rm.wait(nc.tensor)
        for q in range(4):
            if q > 0:
                nc.tensor.wait_ge(s_te.h, q)
            ins = None
            for k in range(4):
                i = q * 4 + k
                ins = nc.tensor.transpose(ptr[:, k * 128:(k + 1) * 128],
                                          RM[:, i, :], v["ident"])
            tv = s_t.inc(ins, 1)
            nc.vector.wait_ge(s_t.h, tv)
            s_te.inc(nc.vector.tensor_copy(
                rmT[:, q * 512:(q + 1) * 512], ptr[:]), 1)
    lb = Wd["L1_ffn_lora_Bc"]
    for q in range(4):
        stg.stage(lb[:, q * 1024:(q + 1) * 1024],
                  loraB[:, q * 1024:(q + 1) * 1024], 1024)
        stg.wait(nc.tensor)
    s_dma = P.sem("f_dma")
    bv = s_dma.inc(nc.sync.dma_start(v["bcol"][:, 0:32],
                                     Wd["L1_ffn_wib_cols"]), 16)
    nc.scalar.wait_ge(s_dma.h, bv)
    load_row_bf16(P, Wd["L1_ffn_wo_b"][0:512], v["brow_bf"][:, 0:512], s_rows)
    load_row_bf16(P, Wd["L1_ffn_wo_b"][512:1024], v["brow_bf"][:, 512:1024], s_rows)
    s_rows.wait(nc.tensor)
    P.barrier()

    wi = Wd["L1_ffn_wi_Wt"].rearrange("(g p) f -> p g f", p=128)
    wo = Wd["L1_ffn_wo_Wt"].rearrange("(k p) f -> p k f", p=128)
    s_ln = P.sem("f_ln")
    SQ = v["FD"][:, 0:1024]
    for tb in range(4):
        def wi_src(m):
            return wi[:, :, m * 128:(m + 1) * 128]

        def wi_rhs(g, _tb):
            return XF[:, g, tb * 512:(tb + 1) * 512]

        def wi_extra(m, _tb, pt):
            return nc.tensor.matmul(pt[:], loraB[:, m * 128:(m + 1) * 128],
                                    rmT[:, tb * 512:(tb + 1) * 512],
                                    start=False, stop=True)

        def wi_evac(m, _tb, pt):
            return nc.scalar.activation(INTER[:, m, :], pt[:], AF.Relu,
                                        bias=v["bcol"][:, m:m + 1], scale=1.0)

        emit_proj_fm(P, stg, "fwi", wi_src, 32, wi_rhs, DG, wi_evac,
                     nc.scalar, n_tb=1, extra_mm=wi_extra)
        P.barrier()

        def wo_src(k):
            return wo[:, k, :]

        def wo_lhs(k, i):
            return INTER[:, k, (i % 4) * 128:(i % 4) * 128 + 128]

        def wo_evac(i, ob, pt):
            Xsl = X[:, i, ob * 512:(ob + 1) * 512]
            return nc.vector.tensor_tensor(Xsl, Xsl, pt, ALU.add)

        emit_proj_tm(P, stg, "fwo", wo_lhs, wo_src, 32, wo_evac, nc.vector,
                     n_ob=2, bias_row=v["brow_bf"][:, 0:1024], tb_list=(tb,))
        P.barrier()
    for i in range(TI):
        emit_ln(P, X[:, i, :], SQ, 1024, s_ln)
    P.barrier()


# version 11
# speedup vs baseline: 1.0722x; 1.0722x over previous
import numpy as np
import ml_dtypes
from contextlib import ExitStack
import concourse.bass as bass
import concourse.mybir as mybir
from concourse.bass_utils import run_bass_kernel_spmd

F32 = mybir.dt.float32
BF16 = mybir.dt.bfloat16
AF = mybir.ActivationFunctionType
ALU = mybir.AluOpType
AX = mybir.AxisListType.X
BF = ml_dtypes.bfloat16

T, TI, D, DG, NCORES = 2048, 16, 1024, 8, 8

BLOCKS = ["l0_attn", "l0_ffn", "l1_attn", "l1_ffn"]
A_PHASES = 9
M_PHASES = 9
M_GORDER = list(range(DG))



class Sem:
    def __init__(self, handle):
        self.h = handle
        self.n = 0

    def inc(self, instr, k):
        instr.then_inc(self.h, k)
        self.n += k
        return self.n

    def wait(self, eng, val=None):
        eng.wait_ge(self.h, self.n if val is None else val)


class Prog:
    def __init__(self, nc, stack):
        self.nc = nc
        self.stack = stack
        self.sems = {}
        self.v = {}
        self.dram = {}

    def uniq(self):
        self._u = getattr(self, "_u", 0) + 1
        return self._u

    def uniq2(self):
        return self.uniq()

    def sem(self, name):
        if name not in self.sems:
            h = self.stack.enter_context(self.nc.semaphore(f"s_{name}"))
            self.sems[name] = Sem(h)
        return self.sems[name]

    def gate_w(self, name, eng=None):
        if hasattr(self, "ccname") and name in self.ccname:
            for sem, val in self.ccname[name]:
                (eng or self.nc.sync).wait_ge(sem.h, val)

    def barrier(self):
        nc = self.nc
        s = self.sem("barrier")
        c = self.v["ccol"]
        s.inc(nc.vector.memset(c[:, 0:1], 0.0), 1)
        s.inc(nc.scalar.memzero(c[:, 1:2]), 1)
        s.inc(nc.tensor.nop(), 1)
        s.inc(nc.sync.nop(), 1)
        for e in (nc.tensor, nc.vector, nc.scalar, nc.sync):
            s.wait(e)


ARENA_COLS = 53188


def carve(P, mega):
    v = P.v
    m = mega[:]
    v["X"] = m[:, 0:16384].rearrange("p (i d) -> p i d", i=TI)
    v["FA"] = m[:, 16384:24580].bitcast(BF16)
    v["FB"] = m[:, 24580:32772].bitcast(BF16)
    v["FC"] = m[:, 32772:40964].bitcast(BF16)
    v["FD"] = m[:, 40964:49284].bitcast(BF16)
    v["GF"] = m[:, 49284:51972]
    v["WB"] = m[:, 51972:52996].bitcast(BF16)
    v["WBf32"] = m[:, 51972:52996]
    v["CS"] = m[:, 52996:53188].bitcast(BF16)

    v["FAf32"] = m[:, 16384:24580]
    v["FDf32"] = m[:, 40964:49284]
    v["STG"] = v["GF"][:, 0:2048]
    v["mcols"] = v["GF"][:, 2048:2064]
    v["colsA"] = v["GF"][:, 2064:2080]
    v["lncol"] = v["GF"][:, 2080:2088]
    v["bcol"] = v["GF"][:, 2088:2120]
    v["ccol"] = v["GF"][:, 2120:2176]
    brow = v["GF"][:, 2176:2688]
    v["brow_bf"] = brow[0:1, :].bitcast(BF16)
    v["frow"] = brow[32:33, :]
    v["ident"] = v["CS"][:, 0:128]
    v["perm"] = v["CS"][:, 128:256]
    v["ones_rowb"] = v["CS"][0:1, 256:384]
    v["ones_c128"] = v["CS"][:, 383:384]




def vfence(P):
    s = P.sem("vf")
    v = s.inc(P.nc.vector.engine_nop(), 1)
    P.nc.vector.wait_ge(s.h, v)


class WStager:

    def __init__(self, P):
        self.P = P
        self.ring = P.v["STG"]
        self.s_ld = [P.sem("wst_ld0"), P.sem("wst_ld1")]
        self.s_cast = P.sem("wst_cast")
        self.slot = 0

    def stage(self, src_ap, dst_bf16, cols, consumer_sem=None, consumer_val=0):
        P = self.P
        assert cols <= 1024
        sl = self.slot
        self.slot ^= 1
        if src_ap.dtype == BF16:
            dst = dst_bf16
            if len(src_ap.shape) == 3 and len(dst.shape) == 2:
                dst = dst.rearrange("p (a b) -> p a b", a=src_ap.shape[1])
            P.nc.sync.wait_ge(self.s_ld[sl].h, self.s_ld[sl].n)
            if consumer_sem is not None and consumer_val > 0:
                P.nc.sync.wait_ge(consumer_sem.h, consumer_val)
            dv = self.s_ld[sl].inc(P.nc.sync.dma_start(dst, src_ap), 16)
            self.last = (self.s_ld[sl], dv)
            return None
        rows = src_ap.shape[0]
        st = self.ring[0:rows, sl * 1024: sl * 1024 + cols]
        std = st
        if len(src_ap.shape) == 3:
            std = st.rearrange("p (a b) -> p a b", a=src_ap.shape[1])
        if self.s_cast.n >= 2:
            P.nc.sync.wait_ge(self.s_cast.h, self.s_cast.n - 1)
        ldv = self.s_ld[sl].inc(P.nc.sync.dma_start(std, src_ap), 16)
        P.nc.vector.wait_ge(self.s_ld[sl].h, ldv)
        if consumer_sem is not None and consumer_val > 0:
            P.nc.vector.wait_ge(consumer_sem.h, consumer_val)
        cv = self.s_cast.inc(P.nc.vector.tensor_copy(dst_bf16, st), 1)
        self.last = (self.s_cast, cv)
        return cv

    def wait(self, eng):
        s, val = self.last
        eng.wait_ge(s.h, val)


def load_row_bf16(P, src_dram_1d, dst_bf_row, s):
    nc = P.nc
    n = src_dram_1d.shape[0]
    sd = P.sem("rowdma")
    nc.sync.wait_ge(s.h, s.n)
    dv = sd.inc(nc.sync.dma_start(P.v["frow"][:, 0:n], src_dram_1d[None, :]), 16)
    nc.vector.wait_ge(sd.h, dv)
    return s.inc(nc.vector.tensor_copy(dst_bf_row[:, 0:n], P.v["frow"][:, 0:n]), 1)




def emit_ln(P, Xv, sq_out, cols, s):
    nc = P.nc
    lc = P.v["lncol"]
    inv = 1.0 / float(cols)
    nc.vector.wait_ge(s.h, s.n)
    nc.vector.reduce_sum(lc[:, 0:1], Xv, axis=AX)
    vfence(P)
    nc.vector.tensor_scalar_mul(lc[:, 1:2], lc[:, 0:1], -inv)
    v = s.inc(nc.vector.engine_nop(), 1)
    s.wait(nc.scalar, v)
    v = s.inc(nc.scalar.activation(sq_out, Xv, AF.Square, bias=lc[:, 1:2],
                                   scale=1.0, accum_out=lc[:, 2:3]), 1)
    s.wait(nc.vector, v)
    nc.vector.tensor_scalar(lc[:, 3:4], lc[:, 2:3], inv, 1e-5, ALU.mult, ALU.add)
    v = s.inc(nc.vector.engine_nop(), 1)
    s.wait(nc.scalar, v)
    v = s.inc(nc.scalar.activation(lc[:, 4:5], lc[:, 3:4], AF.Sqrt), 1)
    s.wait(nc.vector, v)
    nc.vector.reciprocal(lc[:, 5:6], lc[:, 4:5])
    vfence(P)
    s.inc(nc.vector.tensor_scalar(Xv, Xv, lc[:, 1:2], lc[:, 5:6],
                                  ALU.add, ALU.mult), 1)




def emit_build_xf(P, Xv, XFv, off, tag, src_bf16=False):
    nc = P.nc
    s_c = P.sem("bxf_c")
    s_t = P.sem("bxf_t")
    s_e = P.sem("bxf_e")
    t0, e0 = s_t.n, s_e.n
    wb = P.v["WB"]
    if off:
        nc.vector.memset(XFv[:, :, 0:1].rearrange("p g c -> p (g c)"), 0.0)
    with (
        nc.psum_tensor(f"ps_bx0_{tag}", [128, 512], BF16) as p0,
        nc.psum_tensor(f"ps_bx1_{tag}", [128, 512], BF16) as p1,
    ):
        ps = [p0, p1]
        for i in range(TI):
            if src_bf16:
                buf = Xv[:, i, :]
            else:
                buf = wb[:, (i % 2) * 1024:(i % 2) * 1024 + 1024]
                if i >= 2:
                    nc.vector.wait_ge(s_t.h, t0 + 2 * (i - 1))
                cv = s_c.inc(nc.vector.tensor_copy(buf, Xv[:, i, :]), 1)
                nc.tensor.wait_ge(s_c.h, cv)
            for half in range(2):
                pset = i * 2 + half
                pt = ps[pset % 2]
                if pset >= 2:
                    nc.tensor.wait_ge(s_e.h, e0 + pset - 1)
                ins = None
                for k in range(4):
                    g = half * 4 + k
                    ins = nc.tensor.transpose(pt[:, k * 128:(k + 1) * 128],
                                              buf[:, g * 128:g * 128 + 128],
                                              P.v["ident"])
                tv = s_t.inc(ins, 1)
                nc.vector.wait_ge(s_t.h, tv)
                dst = XFv[:, 4 * half:4 * half + 4,
                          off + i * 128: off + i * 128 + 128]
                s_e.inc(nc.vector.tensor_copy(
                    dst, pt[:].rearrange("p (g c) -> p g c", g=4)), 1)
    P.barrier()




def emit_proj_fm(P, stg, tag, w_src_fn, n_m, rhs_fn, KG, evac, eng, n_tb=4,
                 extra_mm=None):
    nc = P.nc
    s_mm = P.sem("pf_mm")
    s_ev = P.sem("pf_ev")
    ev0 = s_ev.n
    wb = P.v["WB"]
    with (
        nc.psum_tensor(f"ps_{tag}{P.uniq()}0", [128, 512]) as p0,
        nc.psum_tensor(f"ps_{tag}{P.uniq2()}1", [128, 512]) as p1,
        nc.psum_tensor(f"ps_{tag}{P.uniq2()}2", [128, 512]) as p2,
        nc.psum_tensor(f"ps_{tag}{P.uniq2()}3", [128, 512]) as p3,
    ):
        ps = [p0, p1, p2, p3]
        for m in range(n_m):
            dst = wb[:, (m % 2) * 1024:(m % 2) * 1024 + KG * 128]
            cval = ev0 + (m - 1) * n_tb
            stg.stage(w_src_fn(m), dst, KG * 128,
                      consumer_sem=s_ev if m >= 2 else None,
                      consumer_val=cval)
            stg.wait(nc.tensor)
            for tb in range(n_tb):
                j = m * n_tb + tb
                pt = ps[j % 4]
                if j >= 4:
                    nc.tensor.wait_ge(s_ev.h, ev0 + j - 3)
                ins = None
                for g in range(KG):
                    ins = nc.tensor.matmul(pt[:], dst[:, g * 128:(g + 1) * 128],
                                           rhs_fn(g, tb), start=(g == 0),
                                           stop=(extra_mm is None and g == KG - 1))
                if extra_mm is not None:
                    ins = extra_mm(m, tb, pt)
                mv = s_mm.inc(ins, 1)
                eng.wait_ge(s_mm.h, mv)
                s_ev.inc(evac(m, tb, pt), 1)




def emit_proj_tm(P, stg, tag, lhsT_fn, w_src_fn, n_k, evac, eng, n_ob=2,
                 bias_row=None, tb_list=(0, 1, 2, 3)):
    nc = P.nc
    s_mm = P.sem("pt_mm")
    s_ev = P.sem("pt_ev")
    s_kd = P.sem("pt_kd")
    ev0, kd0 = s_ev.n, s_kd.n
    wb = P.v["WB"]
    with nc.psum_tensor(f"ps_{tag}{P.uniq()}", [128, 4096]) as pall:
        for tbi, tb in enumerate(tb_list):
            for k in range(n_k):
                dst = wb[:, (k % 2) * 1024:(k % 2) * 1024 + n_ob * 512]
                rel = tbi * n_k + k - 1
                stg.stage(w_src_fn(k), dst, n_ob * 512,
                          consumer_sem=s_kd if rel > 0 else None,
                          consumer_val=kd0 + rel)
                stg.wait(nc.tensor)
                if k == 0 and tbi > 0:
                    nc.tensor.wait_ge(s_ev.h, ev0 + tbi * 4 * n_ob)
                ins = None
                for ii in range(4):
                    lh = lhsT_fn(k, tb * 4 + ii)
                    for ob in range(n_ob):
                        pt = pall[:, (ii * n_ob + ob) * 512:
                                  (ii * n_ob + ob) * 512 + 512]
                        ins = nc.tensor.matmul(
                            pt, lh, dst[:, ob * 512:(ob + 1) * 512],
                            start=(k == 0),
                            stop=(bias_row is None and k == n_k - 1))
                s_kd.inc(ins, 1)
            if bias_row is not None:
                ins = None
                for ii in range(4):
                    for ob in range(n_ob):
                        pt = pall[:, (ii * n_ob + ob) * 512:
                                  (ii * n_ob + ob) * 512 + 512]
                        ins = nc.tensor.matmul(
                            pt, P.v["ones_rowb"],
                            bias_row[:, ob * 512:(ob + 1) * 512],
                            start=False, stop=True)
                mv = s_mm.inc(ins, 1)
                eng.wait_ge(s_mm.h, mv)
            else:
                eng.wait_ge(s_kd.h, s_kd.n)
            for ii in range(4):
                for ob in range(n_ob):
                    pt = pall[:, (ii * n_ob + ob) * 512:(ii * n_ob + ob) * 512 + 512]
                    s_ev.inc(evac(tb * 4 + ii, ob, pt), 1)




def emit_l0_attn(P, stg, Wd):
    nc = P.nc
    v = P.v
    X = v["X"]
    XF = v["FA"][:, 0:16384].rearrange("p (g c) -> p g c", g=DG)
    QT = v["FB"][:, 0:16384].rearrange("p (g c) -> p g c", g=DG)
    KT = v["FC"][:, 0:16384].rearrange("p (g c) -> p g c", g=DG)
    VA = v["FD"][:, 0:16640].rearrange("p (i h e) -> p i h e", i=TI, h=16)
    AVN = QT

    emit_build_xf(P, X, XF, 0, "a")
    if A_PHASES < 1:
        return

    s_dma = P.sem("a_dma")
    s_rows = P.sem("a_rows")
    bq = s_dma.inc(nc.sync.dma_start(v["bcol"][:, 0:16],
                                     Wd["L0_attn_bq_cols"]), 16)
    load_row_bf16(P, Wd["L0_attn_b_qkv"][2048:2560], v["brow_bf"][:, 0:512], s_rows)
    load_row_bf16(P, Wd["L0_attn_b_qkv"][2560:3072], v["brow_bf"][:, 512:1024], s_rows)
    s_rows.wait(nc.tensor)

    wq = Wd["L0_attn_W_qkv"].rearrange("(g p) f -> p g f", p=128)

    def qk_src(m):
        return wq[:, :, m * 128:(m + 1) * 128]

    def qk_rhs(g, tb):
        return XF[:, g, tb * 512:(tb + 1) * 512]

    def qk_evac(m, tb, pt):
        tgt, g = (QT, m) if m < 8 else (KT, m - 8)
        return nc.vector.tensor_scalar_add(tgt[:, g, tb * 512:(tb + 1) * 512],
                                           pt[:], v["bcol"][:, m:m + 1])

    P.gate_w("L0_attn_W_qkv")
    nc.vector.wait_ge(s_dma.h, bq)
    emit_proj_fm(P, stg, "aqk", qk_src, 16, qk_rhs, DG, qk_evac, nc.vector)
    P.barrier()

    def v_src(k):
        return wq[:, k, 2048:3072]

    def v_lhs(k, i):
        return XF[:, k, i * 128:(i + 1) * 128]

    def v_evac(i, vb, pt):
        dstv = VA[:, i, 8 * vb:8 * vb + 8, 0:64]
        return nc.vector.tensor_copy(dstv, pt.rearrange("p (h e) -> p h e", h=8))

    emit_proj_tm(P, stg, "av", v_lhs, v_src, DG, v_evac, nc.vector,
                 n_ob=2, bias_row=v["brow_bf"][:, 0:1024], tb_list=(2, 3))
    if hasattr(P, "convdone"):
        cs, cn_, cos, covs = P.convdone
        nc.vector.wait_ge(cs.h, cn_)
    sev = P.sem("pt_ev")
    nc.tensor.wait_ge(sev.h, sev.n)
    nc.sync.wait_ge(sev.h, sev.n)
    emit_proj_tm(P, stg, "av", v_lhs, v_src, DG, v_evac, nc.vector,
                 n_ob=2, bias_row=v["brow_bf"][:, 0:1024], tb_list=(0, 1))
    nc.vector.memset(VA[:, :, :, 64:65].rearrange("p i h e -> p (i h e)"), 1.0)
    P.barrier()
    if A_PHASES < 2:
        return

    dmax = P.dram["mrow"]
    s_red = P.sem("a_red")
    s_mmb = P.sem("a_mmb")
    mrt = v["STG"][0:16, 0:128].bitcast(BF16)
    hmax = v["STG"][:, 288:352]
    if hasattr(P, "convdone"):
        _, _, cos, covs = P.convdone
        nc.vector.wait_ge(cos[0].h, covs[0])
        nc.vector.wait_ge(cos[1].h, covs[1])
    dmax_written = 0
    red0 = s_red.n
    with (
        nc.psum_tensor("ps_s1_0", [128, 1024]) as q0,
        nc.psum_tensor("ps_s1_1", [128, 1024]) as q1,
        nc.psum_tensor("ps_s1_2", [128, 1024]) as q2,
        nc.psum_tensor("ps_s1t", [16, 256], BF16) as ptr,
    ):
        qs = [q0, q1, q2]
        f = 0
        red_val = {}
        for hp in range(8):
            for par in range(2):
                for i in range(TI):
                    lh = QT[64 * par:64 * par + 64, hp, i * 128:(i + 1) * 128]
                    for half in range(2):
                        pt = qs[f % 3]
                        if f >= 3:
                            nc.tensor.wait_ge(s_red.h, red_val[f - 3])
                        ins = None
                        for kk in range(2):
                            tkb = half * 2 + kk
                            rh = KT[64 * par:64 * par + 64, hp,
                                    tkb * 512:(tkb + 1) * 512]
                            ins = nc.tensor.matmul(pt[:, kk * 512:(kk + 1) * 512],
                                                   lh, rh, start=True, stop=True,
                                                   tile_position=(64 * par, 0))
                        mv = s_mmb.inc(ins, 1)
                        nc.vector.wait_ge(s_mmb.h, mv)
                        red_val[f] = s_red.inc(nc.vector.reduce_max(
                            hmax[:, par * 32 + i * 2 + half:
                                 par * 32 + i * 2 + half + 1], pt[:], axis=AX), 1)
                        f += 1
            vfence(P)
            hm = hmax.rearrange("p (i two) -> p i two", two=2)
            cm = nc.vector.tensor_tensor(
                v["GF"][:, 2048:2080].rearrange("p (i o) -> p i o", o=1),
                hm[:, :, 0:1], hm[:, :, 1:2], ALU.max)
            s_red.inc(cm, 1)
            vfence(P)
            mcb = v["STG"][:, 256:272].bitcast(BF16)
            cv = s_red.inc(nc.vector.tensor_copy(mcb, v["GF"][:, 2048:2080]), 1)
            nc.tensor.wait_ge(s_red.h, cv)
            t2 = None
            for q in range(2):
                t2 = nc.tensor.transpose(ptr[:, q * 128:q * 128 + 128],
                                         mcb[:, q * 16:(q + 1) * 16], v["ident"])
            mv = s_mmb.inc(t2, 1)
            nc.vector.wait_ge(s_mmb.h, mv)
            if hp > 0:
                nc.vector.wait_ge(s_dma.h, dmax_written)
            ev = s_red.inc(nc.vector.tensor_copy(mrt, ptr[:]), 1)
            nc.sync.wait_ge(s_red.h, ev)
            s_dma.inc(nc.sync.dma_start(
                dmax[2 * hp, :].rearrange("(i p) -> i p", p=128),
                mrt[:, 0:128]), 16)
            s_dma.inc(nc.sync.dma_start(
                dmax[2 * hp + 1, :].rearrange("(i p) -> i p", p=128),
                mrt[:, 128:256]), 16)
            dmax_written = s_dma.n
    P.barrier()
    if A_PHASES < 3:
        return

    PT = v["FA"][:, 0:8192].rearrange("p (j c) -> p j c", j=16)
    qaug = [v["FA"][:, 8192:10240], v["FA"][:, 12288:14336]]
    kaug = [v["FA"][:, 10240:12288], v["FA"][:, 14336:16384]]
    mrow = [v["STG"][0:1, 1024:2048].bitcast(BF16),
            v["WB"][0:1, 0:2048]]
    rs_row = v["STG"][0:1, 0:512]
    rs_rowb = v["STG"][0:1, 512:768].bitcast(BF16)
    av_sb = v["STG"][64:128, 0:256].bitcast(BF16)
    s_row = P.sem("a_row")
    s_exp = P.sem("a_exp")
    s_avp = P.sem("a_avp")
    s_ave = P.sem("a_ave")
    s_mmc = P.sem("a_mmc")
    s_nrm = P.sem("a_nrm")
    s_mr = [P.sem("a_mr0"), P.sem("a_mr1")]
    s_dma.wait(nc.sync)
    row0, e0, n0 = s_row.n, s_exp.n, s_nrm.n
    av_cnt = {}
    exp_cnt = {}
    mmc_s2 = {}
    with (
        nc.psum_tensor("ps_s2a", [128, 1024]) as p2a,
        nc.psum_tensor("ps_s2b", [128, 1024]) as p2b,
        nc.psum_tensor("ps_s2c", [128, 1024]) as p2c,
        nc.psum_tensor("ps_avo", [128, 512]) as pav,
        nc.psum_tensor("ps_rr", [128, 512]) as prr,
    ):
        p2 = [p2a, p2b, p2c]
        pend = None
        tt_prev = 0

        def emit_av(bg, grp):
            hh = bg // 4
            nc.tensor.wait_ge(s_exp.h, exp_cnt[bg * 8 + grp])
            if grp == 0 and bg >= 1:
                nc.tensor.wait_ge(s_nrm.h, n0 + bg)
            ins = None
            for k2 in range(2):
                jtk = grp * 2 + k2
                ins = nc.tensor.matmul(pav[0:65, :], VA[:, jtk, hh, :],
                                       PT[:, jtk, :], start=(jtk == 0),
                                       stop=(jtk == TI - 1))
            av_cnt[(bg, grp)] = s_avp.inc(ins, 1)

        def emit_norm2(pn):
            nonlocal tt_prev
            g2, rb2, b2, nv, ttp = pn
            if ttp:
                nc.tensor.wait_ge(s_ave.h, ttp)
            nc.tensor.wait_ge(s_nrm.h, nv)
            pins = nc.tensor.matmul(prr[0:64, :], v["ones_rowb"][:, 0:64],
                                    rs_rowb, start=True, stop=True)
            pv = s_mmc.inc(pins, 1)
            nc.vector.wait_ge(s_mmc.h, pv)
            dsta = AVN[rb2:rb2 + 64, g2, b2 * 512:(b2 + 1) * 512]
            tt_prev = s_ave.inc(nc.vector.tensor_tensor(
                dsta, av_sb[:, 0:512], prr[0:64, :], ALU.mult), 1)

        rowv = [0, 0]
        rowv[0] = s_mr[0].inc(nc.sync.dma_start(mrow[0], dmax[0, :][None, :]), 16)
        for h in range(16):
            sl = h % 2
            if h + 1 < 16:
                nsl = (h + 1) % 2
                if h >= 1:
                    nc.sync.wait_ge(s_row.h, row0 + h)
                rowv[nsl] = s_mr[nsl].inc(
                    nc.sync.dma_start(mrow[nsl], dmax[h + 1, :][None, :]), 16)
            g, rb = h // 2, (h % 2) * 64
            if h >= 2:
                nc.vector.wait_ge(s_mmc.h, mmc_s2[h - 2])
            if h < 2:
                nc.vector.memset(kaug[sl][64:65, :], 1.0)
            nc.vector.tensor_copy(kaug[sl][0:64, :], KT[rb:rb + 64, g, :])
            nc.vector.tensor_copy(qaug[sl][0:64, :], QT[rb:rb + 64, g, :])
            nc.vector.wait_ge(s_mr[sl].h, rowv[sl])
            rv = s_row.inc(nc.vector.tensor_scalar_mul(
                qaug[sl][64:65, :], mrow[sl], -8.0), 1)
            first_of_h = True
            for b in range(4):
                bg = h * 4 + b
                for grp in range(8):
                    j = bg * 8 + grp
                    pt2 = p2[j % 3]
                    if first_of_h and grp == 0:
                        nc.tensor.wait_ge(s_row.h, rv)
                        first_of_h = False
                    if j >= 3:
                        nc.tensor.wait_ge(s_exp.h, e0 + j - 2)
                    ins = None
                    for k2 in range(2):
                        jtk = grp * 2 + k2
                        ins = nc.tensor.matmul(
                            pt2[:, k2 * 512:(k2 + 1) * 512],
                            kaug[sl][0:65, jtk * 128:(jtk + 1) * 128],
                            qaug[sl][0:65, b * 512:(b + 1) * 512],
                            start=True, stop=True)
                    mv = s_mmc.inc(ins, 1)
                    if bg == h * 4 + 3 and grp == 7:
                        mmc_s2[h] = mv
                    nc.scalar.wait_ge(s_mmc.h, mv)
                    if bg >= 1:
                        nc.scalar.wait_ge(s_avp.h, av_cnt[(bg - 1, grp)])
                    dstp = PT[:, grp * 2:grp * 2 + 2, :] \
                        .rearrange("p j c -> p (j c)")
                    exp_cnt[j] = s_exp.inc(
                        nc.scalar.activation(dstp, pt2[:], AF.Exp,
                                             scale=0.125), 1)
                    if grp >= 1:
                        emit_av(bg, grp - 1)
                    if grp == 2 and pend is not None:
                        emit_norm2(pend)
                        pend = None
                emit_av(bg, 7)
                nc.vector.wait_ge(s_avp.h, av_cnt[(bg, 7)])
                nc.vector.reciprocal(rs_row, pav[64:65, :])
                vfence(P)
                nc.vector.tensor_copy(rs_rowb, rs_row)
                nv = s_nrm.inc(nc.vector.tensor_copy(av_sb[:, 0:512],
                                                     pav[0:64, :]), 1)
                pend = (g, rb, b, nv, tt_prev)
        emit_norm2(pend)
    P.barrier()
    if A_PHASES < 4:
        return

    P.gate_w("L0_attn_W_o")
    load_row_bf16(P, Wd["L0_attn_b_o"][0:512], v["brow_bf"][:, 0:512], s_rows)
    load_row_bf16(P, Wd["L0_attn_b_o"][512:1024], v["brow_bf"][:, 512:1024], s_rows)
    s_rows.wait(nc.tensor)
    wo = Wd["L0_attn_W_o"].rearrange("(k p) f -> p k f", p=128)
    AV3 = AVN

    def o_src(k):
        return wo[:, k, :]

    def o_lhs(k, i):
        return AV3[:, k, i * 128:(i + 1) * 128]

    def o_evac(i, ob, pt):
        Xsl = X[:, i, ob * 512:(ob + 1) * 512]
        return nc.vector.tensor_tensor(Xsl, Xsl, pt, ALU.add)

    emit_proj_tm(P, stg, "aout", o_lhs, o_src, DG, o_evac, nc.vector,
                 n_ob=2, bias_row=v["brow_bf"][:, 0:1024])
    P.barrier()
    SQ = v["FD"][:, 0:1024]
    s_ln = P.sem("a_ln")
    for i in range(TI):
        emit_ln(P, X[:, i, :], SQ, 1024, s_ln)
    P.barrier()



CONVERT_WEIGHTS = []


def emit_convert_weights(P, Wd, Wbf):
    nc = P.nc
    if not hasattr(P, "ccname"):
        P.ccname = {}
    FDf = P.v["FDf32"]
    fin = [FDf[:, 0:2048], FDf[:, 2048:4096]]
    fout = [P.v["STG"][:, 0:1024].bitcast(BF16),
            P.v["STG"][:, 1024:2048].bitcast(BF16)]
    s_ci = [P.sem("cw_i0"), P.sem("cw_i1")]
    s_cc = P.sem("cw_c")
    s_co = [P.sem("cw_o0"), P.sem("cw_o1")]
    chunks = []
    for nm in CONVERT_WEIGHTS:
        w = Wd[nm]
        L = 1
        for d in w.shape:
            L *= d
        cols = L // 128
        spec = " ".join(chr(97 + i) for i in range(len(w.shape)))
        wf = w.rearrange(f"{spec} -> ({spec})")
        bf = Wbf[nm].rearrange(f"{spec} -> ({spec})")
        off = 0
        while off < cols:
            cn = min(2048, cols - off)
            chunks.append((nm, wf, bf, off, cn))
            off += cn

    def in_dma(ci):
        nm, wf, bf, off, cn = chunks[ci]
        sl = ci % 2
        srcap = wf[off * 128:(off + cn) * 128].rearrange("(p c) -> p c", c=cn)
        return s_ci[sl].inc(nc.gpsimd.dma_start(fin[sl][:, 0:cn], srcap), 16)

    n_ch = len(chunks)
    iv = {}
    for k in range(min(2, n_ch)):
        iv[k] = in_dma(k)
    co_v = {}
    cc_v = {}
    for ci, (nm, wf, bf, off, cn) in enumerate(chunks):
        sl4, sl2 = ci % 2, ci % 2
        dstap = bf[off * 128:(off + cn) * 128].rearrange("(p c) -> p c", c=cn)
        nc.gpsimd.wait_ge(s_ci[sl4].h, iv[ci])
        if ci >= 2:
            nc.gpsimd.wait_ge(s_co[sl2].h, co_v[ci - 2])
        cv = s_cc.inc(nc.gpsimd.tensor_copy(fout[sl2][:, 0:cn],
                                            fin[sl4][:, 0:cn]), 1)
        cc_v[ci] = cv
        nc.gpsimd.wait_ge(s_cc.h, cv)
        co_v[ci] = s_co[sl2].inc(nc.gpsimd.dma_start(dstap,
                                                     fout[sl2][:, 0:cn]), 16)
        P.ccname[nm] = [(s_co[0], s_co[0].n), (s_co[1], s_co[1].n)]
        if ci + 2 < n_ch:
            iv[ci + 2] = in_dma(ci + 2)
    P.ccsem = s_cc
    P.convdone = (s_cc, s_cc.n, s_co, [s_co[0].n, s_co[1].n])




def build_program(blocks):
    nc = bass.Bass("TRN2", target_bir_lowering=False)
    stack = ExitStack()
    P = Prog(nc, stack)

    Wd = {}
    names = [
        ("L0_attn_x", [T, D]),
        ("L0_attn_W_qkv", [D, 3072]), ("L0_attn_b_qkv", [3072]),
        ("L0_attn_W_o", [D, D]), ("L0_attn_b_o", [D]),
        ("L0_ffn_wqkv", [2, D, 1536]), ("L0_ffn_q_scale", [2, 64]),
        ("L0_ffn_k_scale", [2, 64]), ("L0_ffn_wo", [2, 512, D]),
        ("L0_ffn_ln_g", [2, D]), ("L0_ffn_ln_b", [2, D]),
        ("L0_ffn_w1p", [2, D, 5632]), ("L0_ffn_b1p", [2, 5632]),
        ("L0_ffn_w2p", [2, 2816, D]), ("L0_ffn_b2", [2, D]),
        ("L1_attn_ln_g", [D]), ("L1_attn_ln_b", [D]),
        ("L1_attn_W_in", [D, 2048]), ("L1_attn_conv_w", [D, 1, 4]),
        ("L1_attn_conv_b", [D]), ("L1_attn_dt_w", [D, D]),
        ("L1_attn_dt_b", [D]), ("L1_attn_B_w", [D, 16]),
        ("L1_attn_C_w", [D, 16]), ("L1_attn_log_A", [D]),
        ("L1_attn_D_param", [D]), ("L1_attn_W_out", [D, D]),
        ("L1_ffn_gate_Wt", [D, 8]), ("L1_ffn_gate_b", [8]),
        ("L1_ffn_wi_Wt", [D, 4096]), ("L1_ffn_wi_b", [4096]),
        ("L1_ffn_wo_Wt", [4096, D]), ("L1_ffn_wo_b", [D]),
        ("L1_ffn_lora_At", [D, 32]), ("L1_ffn_lora_Bc", [32, 4096]),
        ("L0_attn_bq_cols", [128, 16]),
        ("L1_ffn_wib_cols", [128, 32]),
        ("L1_attn_logA_cols", [128, 8]), ("L1_attn_dtb_cols", [128, 8]),
        ("L1_attn_D_cols", [128, 8]), ("L1_attn_convb_cols", [128, 8]),
        ("L1_attn_convw_cols", [128, 32]),
        ("c_bmask8", [128, 256]), ("c_cmask", [128, 256]), ("c_tid16", [16, 128]),
        ("c_rot_cs", [128, 2048]), ("c_rot_ss", [128, 2048]),
        ("c_rot_cds", [128, 2048]), ("c_rot_sds", [128, 2048]),
        ("c_locmask", [128, 4096]), ("c_perm", [128, 128]),
        ("L0_ffn_qk_cols", [128, 4]), ("L0_ffn_lngb_cols", [128, 32]),
        ("L0_ffn_b1_cols", [2, 128, 44]),
        ("c_ident", [128, 128]),
    ]
    for nm, shp in names:
        dt = BF16 if nm in ("c_ident", "c_bmask8", "c_cmask", "c_tid16",
                            "c_rot_cs", "c_rot_ss", "c_rot_cds", "c_rot_sds",
                            "c_locmask", "c_perm") else F32
        Wd[nm] = nc.dram_tensor(nm, shp, dt, kind="ExternalInput").ap()
    xout = nc.dram_tensor("x_out", [T, D], F32, kind="ExternalOutput").ap()

    P.dram["mrow"] = nc.dram_tensor("scr_mrow", [16, 2048], BF16,
                                    kind="Internal").ap()
    Wbf = {}
    for nm in CONVERT_WEIGHTS:
        shp = dict(names)[nm]
        Wbf[nm] = nc.dram_tensor(f"bf_{nm}", shp, BF16, kind="Internal").ap()
    P.dram["x0"] = nc.dram_tensor("scr_x0", [T, D], F32, kind="Internal").ap()
    P.dram["z"] = nc.dram_tensor("scr_z", [D, T], BF16, kind="Internal").ap()

    with stack:
        mega = stack.enter_context(nc.sbuf_tensor("mega", [128, ARENA_COLS], F32))
        with nc.Block() as _block:
            carve(P, mega)
            stg = WStager(P)
            s_io = P.sem("io")
            s_io.inc(nc.sync.dma_start(
                P.v["X"],
                Wd["L0_attn_x"].rearrange("(i p) d -> p i d", p=128)), 16)
            s_io.inc(nc.sync.dma_start(P.v["ident"], Wd["c_ident"]), 16)
            s_io.inc(nc.sync.dma_start(P.v["perm"], Wd["c_perm"]), 16)
            nc.vector.memset(P.v["ones_rowb"][:], 1.0)
            for e in (nc.tensor, nc.vector, nc.scalar, nc.sync):
                s_io.wait(e)
            if CONVERT_WEIGHTS:
                emit_convert_weights(P, Wd, Wbf)
                for nm in CONVERT_WEIGHTS:
                    Wd[nm] = Wbf[nm]

            for b in blocks:
                if b == "l0_attn":
                    emit_l0_attn(P, stg, Wd)
                elif b == "l0_ffn":
                    emit_l0_ffn(P, stg, Wd)
                elif b == "l1_attn":
                    emit_l1_attn(P, stg, Wd)
                elif b == "l1_ffn":
                    emit_l1_ffn(P, stg, Wd)

            s_out = P.sem("out")
            s_out.inc(nc.sync.dma_start(
                xout.rearrange("(i p) d -> p i d", p=128), P.v["X"]), 16)
            s_out.wait(nc.sync)
    return nc




def prep_inputs(inputs):
    f32 = np.float32
    g = {k: np.asarray(v, dtype=f32) for k, v in inputs.items()}

    HID, HIDP = 2730, 2816
    com = {}
    com["L0_attn_W_qkv"] = g["L0_attn_W_qkv"]
    com["L0_attn_b_qkv"] = g["L0_attn_b_qkv"]
    com["L0_attn_W_o"] = g["L0_attn_W_o"]
    com["L0_attn_b_o"] = g["L0_attn_b_o"]
    com["L0_ffn_wqkv"] = g["L0_ffn_wqkv"]
    com["L0_ffn_q_scale"] = g["L0_ffn_q_scale"]
    com["L0_ffn_k_scale"] = g["L0_ffn_k_scale"]
    com["L0_ffn_wo"] = g["L0_ffn_wo"]
    com["L0_ffn_ln_g"] = g["L0_ffn_ln_g"]
    com["L0_ffn_ln_b"] = g["L0_ffn_ln_b"]
    w1 = g["L0_ffn_w1"]
    b1 = g["L0_ffn_b1"]
    w1p = np.zeros((2, D, 2 * HIDP), f32)
    b1p = np.zeros((2, 2 * HIDP), f32)
    perm = np.zeros(2 * HIDP, np.int64)
    keep = np.zeros(2 * HIDP, bool)
    for blk in range(22):
        lo, hi = blk * 128, min((blk + 1) * 128, HID)
        n = hi - lo
        pa = 2 * blk * 128
        pg = (2 * blk + 1) * 128
        perm[pa:pa + n] = np.arange(lo, hi)
        keep[pa:pa + n] = True
        perm[pg:pg + n] = HID + np.arange(lo, hi)
        keep[pg:pg + n] = True
    w1p[:, :, keep[: 2 * HIDP]] = w1[:, :, perm[keep]]
    b1p[:, keep] = b1[:, perm[keep]]
    com["L0_ffn_w1p"] = w1p
    com["L0_ffn_b1p"] = b1p
    w2 = g["L0_ffn_w2"]
    w2p = np.zeros((2, HIDP, D), f32)
    w2p[:, :HID, :] = w2
    com["L0_ffn_w2p"] = w2p
    com["L0_ffn_b2"] = g["L0_ffn_b2"]
    for k in ["L1_attn_ln_g", "L1_attn_ln_b", "L1_attn_W_in",
              "L1_attn_conv_b", "L1_attn_dt_w", "L1_attn_dt_b",
              "L1_attn_B_w", "L1_attn_C_w", "L1_attn_log_A",
              "L1_attn_D_param", "L1_attn_W_out"]:
        com[k] = g[k]
    com["L1_attn_conv_w"] = g["L1_attn_conv_w"]
    com["L1_ffn_gate_Wt"] = np.ascontiguousarray(g["L1_ffn_gate_W"].T)
    com["L1_ffn_gate_b"] = g["L1_ffn_gate_b"]
    com["L1_ffn_wi_Wt"] = np.ascontiguousarray(g["L1_ffn_wi_W"].T)
    com["L1_ffn_wi_b"] = g["L1_ffn_wi_b"]
    com["L1_ffn_wo_Wt"] = np.ascontiguousarray(g["L1_ffn_wo_W"].T)
    com["L1_ffn_wo_b"] = g["L1_ffn_wo_b"]
    la = g["L1_ffn_lora_A"]
    com["L1_ffn_lora_At"] = np.ascontiguousarray(
        la.reshape(32, D).T)
    lb = g["L1_ffn_lora_B"]
    com["L1_ffn_lora_Bc"] = np.ascontiguousarray(
        lb.transpose(0, 2, 1).reshape(32, 4096))
    com["L0_attn_bq_cols"] = np.ascontiguousarray(
        g["L0_attn_b_qkv"][:2048].reshape(16, 128).T)
    com["L1_ffn_wib_cols"] = np.ascontiguousarray(
        g["L1_ffn_wi_b"].reshape(32, 128).T)
    com["L1_attn_logA_cols"] = np.ascontiguousarray(
        g["L1_attn_log_A"].reshape(8, 128).T)
    com["L1_attn_dtb_cols"] = np.ascontiguousarray(
        g["L1_attn_dt_b"].reshape(8, 128).T)
    com["L1_attn_D_cols"] = np.ascontiguousarray(
        g["L1_attn_D_param"].reshape(8, 128).T)
    com["L1_attn_convb_cols"] = np.ascontiguousarray(
        g["L1_attn_conv_b"].reshape(8, 128).T)
    com["L1_attn_convw_cols"] = np.ascontiguousarray(
        g["L1_attn_conv_w"][:, 0, :].reshape(8, 128, 4).transpose(1, 0, 2)
        .reshape(128, 32))
    bm = np.zeros((128, 16, 16), BF)
    for p in range(128):
        bm[p, p // 8, :] = 1
    com["c_bmask8"] = bm.reshape(128, 256)
    cm = np.zeros((128, 2, 128), BF)
    for q in range(128):
        c_l_in_half = q // 16
        for h2 in range(2):
            c_l = h2 * 8 + c_l_in_half
            for p in range(128):
                if c_l == p // 8:
                    cm[q, h2, p] = 1
    com["c_cmask"] = cm.reshape(128, 256)
    tid = np.zeros((16, 128), BF)
    for qq in range(128):
        tid[qq % 16, qq] = 1
    com["c_tid16"] = tid
    DH, WIN, NN = 64, 512, 2048
    half_idx = np.arange(0, DH, 2, dtype=np.float64)
    inv_freq = 1.0 / 10000.0 ** (half_idx / DH)
    tt = np.arange(NN, dtype=np.float64)
    freqs = tt[:, None] * inv_freq[None, :]
    freqs = np.concatenate([freqs, freqs], axis=-1)
    basev = (half_idx + 0.4 * DH) / (1.4 * DH)
    power = (tt - NN // 2) / float(WIN // 2)
    scalev = basev[None, :] ** power[:, None]
    scalev = np.concatenate([scalev, scalev], axis=-1)
    cosv, sinv = np.cos(freqs), np.sin(freqs)
    def rep2(m):
        r = np.empty((128, NN))
        r[0:64, :] = m.T
        r[64:128, :] = m.T
        return r.astype(BF)
    com["c_rot_cs"] = rep2(cosv * scalev)
    com["c_rot_ss"] = rep2(sinv * scalev)
    com["c_rot_cds"] = rep2(cosv / scalev)
    com["c_rot_sds"] = rep2(sinv / scalev)
    lm = np.zeros((128, 8, 512), BF)
    for tkt in range(8):
        for p in range(128):
            tk = tkt * 128 + p
            c0, c1 = max(0, tk - 512), min(511, tk)
            if c0 <= c1:
                lm[p, tkt, c0:c1 + 1] = 1
    com["c_locmask"] = lm.reshape(128, 4096)
    pm = np.zeros((128, 128), BF)
    for mcol in range(128):
        base0 = 64 * (mcol // 64)
        ml = mcol % 64
        if ml < 64 // 2:
            pm[base0 + ml + 32, mcol] = -1
        else:
            pm[base0 + ml - 32, mcol] = 1
    com["c_perm"] = pm
    com["L0_ffn_qk_cols"] = np.stack(
        [g["L0_ffn_q_scale"][0], g["L0_ffn_k_scale"][0],
         g["L0_ffn_q_scale"][1], g["L0_ffn_k_scale"][1]],
        axis=1)[np.tile(np.arange(64), 2)].astype(f32)
    lngb = np.zeros((128, 32), f32)
    for l in range(2):
        for gg in range(8):
            lngb[:, l * 8 + gg] = g["L0_ffn_ln_g"][l][gg * 128: (gg + 1) * 128]
            lngb[:, 16 + l * 8 + gg] = g["L0_ffn_ln_b"][l][gg * 128: (gg + 1) * 128]
    com["L0_ffn_lngb_cols"] = lngb
    com["L0_ffn_b1_cols"] = np.ascontiguousarray(
        b1p.reshape(2, 44, 128).transpose(0, 2, 1))
    com["c_ident"] = np.eye(128, dtype=BF)

    x = g["L0_attn_x"]
    in_maps = []
    for c in range(NCORES):
        m = dict(com)
        m["L0_attn_x"] = np.ascontiguousarray(x[c])
        in_maps.append(m)
    return in_maps


def kernel(**inputs):
    in_maps = prep_inputs(inputs)
    nc = build_program(BLOCKS)
    res = run_bass_kernel_spmd(nc, in_maps, core_ids=list(range(NCORES)))
    out = np.stack([np.asarray(r["x_out"], dtype=np.float32)
                    for r in res.results], axis=0)
    return out


def emit_l0_ffn(P, stg, Wd):
    nc = P.nc
    v = P.v
    X = v["X"]
    P.gate_w("L0_ffn_w2p")
    FA, FB, FC, FD = v["FA"], v["FB"], v["FC"], v["FD"]
    XF = FA[:, 0:16392].rearrange("p (g c) -> p g c", g=DG)
    QK = FC[:, 0:16384].rearrange("p (g c) -> p g c", g=DG)
    VA = FD[:, 0:8320].rearrange("p (i h e) -> p i h e", i=TI, h=8)
    ROT = FD[:, 8320:16512]
    AVN = FB[:, 0:8192].rearrange("p (g c) -> p g c", g=4)
    PT = FB[:, 8192:12288].rearrange("p (j c) -> p j c", j=8)
    MSK = FB[:, 12288:16384].rearrange("p (j c) -> p j c", j=8)
    katk = [FA[:, 0:2048], FA[:, 2560:4608]]
    qaug = [FA[:, 2048:2560], FA[:, 4608:5120]]
    dmax = P.dram["mrow"]
    s_dma = P.sem("f0_dma")
    s_rows = P.sem("f0_rows")
    s_pe = P.sem("f0_pe")
    s_ln = P.sem("f0_ln")

    s_sp = P.sem("f0_spill")
    s_sp.inc(nc.sync.dma_start(
        P.dram["x0"].rearrange("(i p) d -> p i d", p=128), X), 16)
    s_dma.inc(nc.sync.dma_start(ROT[:, 0:2048], Wd["c_rot_cs"]), 16)
    s_dma.inc(nc.sync.dma_start(ROT[:, 2048:4096], Wd["c_rot_ss"]), 16)
    s_dma.inc(nc.sync.dma_start(ROT[:, 4096:6144], Wd["c_rot_cds"]), 16)
    s_dma.inc(nc.sync.dma_start(ROT[:, 6144:8192], Wd["c_rot_sds"]), 16)
    s_dma.inc(nc.sync.dma_start(v["ccol"][:, 52:56], Wd["L0_ffn_qk_cols"]), 16)
    s_dma.inc(nc.sync.dma_start(v["bcol"][:, 0:32], Wd["L0_ffn_lngb_cols"]), 16)
    nc.vector.memset(v["ones_c128"], 1.0)
    s_dma.wait(nc.vector)
    s_dma.wait(nc.tensor)
    s_dma.wait(nc.scalar)

    for lay in range(2):
        wq = Wd["L0_ffn_wqkv"].rearrange("l (g p) f -> l p g f", p=128)[lay]
        nc.sync.wait_ge(s_dma.h, s_dma.n)
        mv = s_dma.inc(nc.sync.dma_start(
            MSK.rearrange("p j c -> p (j c)"), Wd["c_locmask"]), 16)
        nc.vector.wait_ge(s_dma.h, mv)

        emit_build_xf(P, X, XF, 1, f"f{lay}")

        def sh(gg, c0, cn):
            off = 1 if gg < 4 else 0
            return XF[:, gg, off + c0: off + c0 + cn]

        def qk_src(m):
            return wq[:, :, m * 128:(m + 1) * 128]

        def qk_rhs(g, tb):
            return sh(g, tb * 512, 512)

        def qk_evac(m, tb, pt):
            return nc.vector.tensor_copy(QK[:, m, tb * 512:(tb + 1) * 512], pt[:])

        emit_proj_fm(P, stg, "fqk", qk_src, 8, qk_rhs, DG, qk_evac,
                     nc.vector)
        P.barrier()

        def v_src(k):
            return wq[:, k, 1024:1536]

        def v_lhs(k, i):
            return sh(k, i * 128, 128)

        def v_evac(i, vb, pt):
            dstv = VA[:, i, :, 0:64]
            return nc.vector.tensor_copy(
                dstv, pt.rearrange("p (h e) -> p h e", h=8))

        emit_proj_tm(P, stg, "fv", v_lhs, v_src, DG, v_evac, nc.vector,
                     n_ob=1)
        nc.vector.memset(VA[:, :, :, 64:65].rearrange("p i h e -> p (i h e)"), 1.0)
        P.barrier()

        s_n = P.sem("fn")
        s_ns = P.sem("fns")
        sqsc = [FB[:, 0:2048], FB[:, 2048:4096]]
        tscr = [FB[:, 4096:4608], FB[:, 4608:5120]]
        t2s = FB[:, 5120:5632]
        t3s = FB[:, 5632:6144]
        rstdbA = [v["STG"][0:1, 1024:1280].bitcast(BF16),
                  v["STG"][0:1, 1536:1792].bitcast(BF16)]
        rstdbB = [v["STG"][0:1, 1280:1536].bitcast(BF16),
                  v["STG"][0:1, 1792:2048].bitcast(BF16)]
        rAs = [v["WBf32"][0:1, 0:512], v["WBf32"][0:1, 512:1024]]
        rBs = [v["WBf32"][32:33, 0:512], v["WBf32"][32:33, 512:1024]]
        epsrow = v["STG"][0:1, 0:256].bitcast(BF16)
        sq_v = {}
        pn_v = {}
        sx_v = {}
        rc_v = {}
        tm_v = {}
        pm_v = {}
        bc_v = {}
        t3_v = {}
        fn_v = {}
        ev24 = s_n.inc(nc.vector.memset(epsrow, 1e-24), 1)
        nc.tensor.wait_ge(s_n.h, ev24)
        with (
            nc.psum_tensor(f"ps_fna0{lay}", [1, 512]) as pnA0,
            nc.psum_tensor(f"ps_fnb0{lay}", [1, 512]) as pnB0,
            nc.psum_tensor(f"ps_fna1{lay}", [1, 512]) as pnA1,
            nc.psum_tensor(f"ps_fnb1{lay}", [1, 512]) as pnB1,
            nc.psum_tensor(f"ps_fra{lay}", [128, 512]) as prnA,
            nc.psum_tensor(f"ps_frb{lay}", [128, 512]) as prnB,
            nc.psum_tensor(f"ps_fp0{lay}", [128, 512]) as prot0,
            nc.psum_tensor(f"ps_fp1{lay}", [128, 512]) as prot1,
        ):
            pnA = [pnA0, pnA1]
            pnB = [pnB0, pnB1]
            prot = [prot0, prot1]
            sq_v[0] = s_ns.inc(nc.scalar.activation(sqsc[0], QK[:, 0, :],
                                                    AF.Square), 1)
            for tile in range(8):
                isq = tile < 4
                scol = v["ccol"][:, 52 + 2 * lay + (0 if isq else 1):
                                 53 + 2 * lay + (0 if isq else 1)]
                for tb in range(4):
                    it = tile * 4 + tb
                    lo = tb * 512
                    qsl = QK[:, tile, lo:lo + 512]
                    ts = tscr[it % 2]
                    cosT = ROT[:, (0 if isq else 4096) + lo:
                               (0 if isq else 4096) + lo + 512]
                    sinT = ROT[:, (2048 if isq else 6144) + lo:
                               (2048 if isq else 6144) + lo + 512]
                    if tb == 1 and tile + 1 < 8:
                        if tile >= 1:
                            nc.scalar.wait_ge(s_pe.h, pn_v[(tile - 1) * 4 + 3])
                        sq_v[tile + 1] = s_ns.inc(nc.scalar.activation(
                            sqsc[(tile + 1) % 2], QK[:, tile + 1, :],
                            AF.Square), 1)
                    if tb == 0:
                        nc.tensor.wait_ge(s_ns.h, sq_v[tile])
                    if it >= 2:
                        nc.tensor.wait_ge(s_ns.h, sx_v[it - 2])
                    nc.tensor.matmul(pnA[it % 2][:], v["ones_c128"][0:64, :],
                                     sqsc[tile % 2][0:64, lo:lo + 512],
                                     start=True, stop=False)
                    nc.tensor.matmul(pnA[it % 2][:], v["ones_rowb"][:, 0:1],
                                     epsrow, start=False, stop=True)
                    nc.tensor.matmul(pnB[it % 2][:],
                                     v["ones_c128"][64:128, :],
                                     sqsc[tile % 2][64:128, lo:lo + 512],
                                     start=True, stop=False,
                                     tile_position=(64, 0))
                    ins = nc.tensor.matmul(pnB[it % 2][:],
                                           v["ones_rowb"][:, 0:1],
                                           epsrow, start=False, stop=True)
                    pn_v[it] = s_pe.inc(ins, 1)
                    nc.scalar.wait_ge(s_pe.h, pn_v[it])
                    if it >= 2:
                        nc.scalar.wait_ge(s_n.h, rc_v[it - 2])
                    s_ns.inc(nc.scalar.activation(rAs[it % 2],
                                                  pnA[it % 2][:], AF.Sqrt), 1)
                    sx_v[it] = s_ns.inc(nc.scalar.activation(
                        rBs[it % 2], pnB[it % 2][:], AF.Sqrt), 1)
                    if it >= 2:
                        nc.vector.wait_ge(s_pe.h, pm_v[it - 2])
                    tm_v[it] = s_n.inc(nc.vector.tensor_scalar_mul(
                        ts, qsl, scol), 1)
                    nc.vector.wait_ge(s_ns.h, sx_v[it])
                    with nc.allow_low_precision(reason="bf16 store of rstd"):
                        nc.vector.reciprocal(rstdbA[it % 2], rAs[it % 2])
                        rc_v[it] = s_n.inc(nc.vector.reciprocal(
                            rstdbB[it % 2], rBs[it % 2]), 1)
                    nc.tensor.wait_ge(s_n.h, tm_v[it])
                    if it >= 2:
                        nc.tensor.wait_ge(s_n.h, t3_v[it - 2])
                    pm_v[it] = s_pe.inc(nc.tensor.matmul(
                        prot[it % 2][:], v["perm"], ts,
                        start=True, stop=True), 1)
                    nc.tensor.wait_ge(s_n.h, rc_v[it])
                    if it >= 1:
                        nc.tensor.wait_ge(s_n.h, fn_v[it - 1])
                    nc.tensor.matmul(prnA[0:64, :], v["ones_rowb"][:, 0:64],
                                     rstdbA[it % 2], start=True, stop=True)
                    bc_v[it] = s_pe.inc(nc.tensor.matmul(
                        prnB[0:64, :], v["ones_rowb"][:, 0:64],
                        rstdbB[it % 2], start=True, stop=True), 1)
                    nc.vector.wait_ge(s_pe.h, pm_v[it])
                    nc.vector.tensor_mul(t2s, ts, cosT)
                    nc.vector.tensor_mul(t3s[0:64, :], prot[it % 2][0:64, :],
                                         sinT[0:64, :])
                    t3_v[it] = s_n.inc(nc.vector.tensor_mul(
                        t3s[64:128, :], prot[it % 2][64:128, :],
                        sinT[64:128, :]), 1)
                    vfence(P)
                    nc.vector.tensor_add(t2s[0:64, :], t2s[0:64, :],
                                         t3s[0:64, :])
                    nc.vector.tensor_add(t2s[64:128, :], t2s[64:128, :],
                                         t3s[64:128, :])
                    nc.vector.wait_ge(s_pe.h, bc_v[it])
                    vfence(P)
                    nc.vector.tensor_tensor(qsl[0:64, :], t2s[0:64, :],
                                            prnA[0:64, :], ALU.mult)
                    fn_v[it] = s_n.inc(nc.vector.tensor_tensor(
                        qsl[64:128, :], t2s[64:128, :], prnB[0:64, :],
                        ALU.mult), 1)
        P.barrier()

        s_red = P.sem("f0red")
        s_mmb = P.sem("f0mmb")
        s_dx = P.sem("f0dx")
        mrt = v["STG"][0:16, 0:128].bitcast(BF16)
        dmax_written = 0
        red_val = {}
        f = 0
        with (
            nc.psum_tensor(f"ps_f1_{lay}0", [128, 1024]) as q0,
            nc.psum_tensor(f"ps_f1_{lay}1", [128, 1024]) as q1,
            nc.psum_tensor(f"ps_f1_{lay}2", [128, 1024]) as q2,
            nc.psum_tensor(f"ps_f1t{lay}", [16, 256], BF16) as ptr,
        ):
            qs = [q0, q1, q2]
            for hp in range(4):
                for w in range(4):
                    klo = (w - 1) * 512 if w > 0 else 0
                    kw = 1024 if w > 0 else 512
                    for sub in range(4):
                        col = w * 4 + sub
                        for par in range(2):
                            pt = qs[f % 3]
                            if f >= 3:
                                nc.tensor.wait_ge(s_red.h, red_val[f - 3])
                            lh = QK[64 * par:64 * par + 64, hp,
                                    w * 512 + sub * 128: w * 512 + sub * 128 + 128]
                            ins = None
                            for tkb in range(kw // 512):
                                rh = QK[64 * par:64 * par + 64, 4 + hp,
                                        klo + tkb * 512: klo + tkb * 512 + 512]
                                ins = nc.tensor.matmul(
                                    pt[:, tkb * 512:(tkb + 1) * 512], lh, rh,
                                    start=True, stop=True,
                                    tile_position=(64 * par, 0))
                            mv = s_mmb.inc(ins, 1)
                            nc.vector.wait_ge(s_mmb.h, mv)
                            tgt = v["mcols"] if par == 0 else v["colsA"]
                            red_val[f] = s_red.inc(nc.vector.reduce_max(
                                tgt[:, col:col + 1], pt[:, 0:kw], axis=AX), 1)
                            f += 1
                vfence(P)
                mcb = v["STG"][:, 256:272].bitcast(BF16)
                cv = s_red.inc(nc.vector.tensor_copy(mcb, v["GF"][:, 2048:2080]), 1)
                nc.tensor.wait_ge(s_red.h, cv)
                t2 = None
                for q in range(2):
                    t2 = nc.tensor.transpose(ptr[:, q * 128:q * 128 + 128],
                                             mcb[:, q * 16:(q + 1) * 16],
                                             v["ident"])
                mv = s_mmb.inc(t2, 1)
                nc.vector.wait_ge(s_mmb.h, mv)
                if hp > 0:
                    nc.vector.wait_ge(s_dx.h, dmax_written)
                ev = s_red.inc(nc.vector.tensor_copy(mrt, ptr[:]), 1)
                nc.sync.wait_ge(s_red.h, ev)
                s_dx.inc(nc.sync.dma_start(
                    dmax[2 * hp, :].rearrange("(i p) -> i p", p=128),
                    mrt[:, 0:128]), 16)
                s_dx.inc(nc.sync.dma_start(
                    dmax[2 * hp + 1, :].rearrange("(i p) -> i p", p=128),
                    mrt[:, 128:256]), 16)
                dmax_written = s_dx.n
        P.barrier()

        s_row = P.sem("f0row")
        s_exp = P.sem("f0exp")
        s_avp = P.sem("f0avp")
        s_ave = P.sem("f0ave")
        s_mmc = P.sem("f0mmc")
        s_msk = P.sem("f0msk")
        s_nrm = P.sem("f0nrm")
        s_mr = [P.sem("f0mr0"), P.sem("f0mr1")]
        mrow = [v["STG"][0:1, 1024:1280].bitcast(BF16),
                v["STG"][0:1, 1280:1536].bitcast(BF16)]
        rs_row = v["STG"][0:1, 0:512]
        rs_rowb = v["STG"][0:1, 512:768].bitcast(BF16)
        av_sb = v["STG"][64:128, 0:256].bitcast(BF16)
        s_dx.wait(nc.sync)
        row0, n0 = s_row.n, s_nrm.n
        av_cnt = {}
        exp_cnt = {}
        msk_cnt = {}
        mmc_k = {}
        last_av = {}
        jr = 0
        with (
            nc.psum_tensor(f"ps_f2a{lay}", [128, 1024]) as p2a,
            nc.psum_tensor(f"ps_f2b{lay}", [128, 1024]) as p2b,
            nc.psum_tensor(f"ps_f2c{lay}", [128, 1024]) as p2c,
            nc.psum_tensor(f"ps_fav{lay}", [128, 512]) as pav,
            nc.psum_tensor(f"ps_frr{lay}", [128, 512]) as prr,
        ):
            p2 = [p2a, p2b, p2c]
            pend = None
            tt_prev = 0

            def f_av(hw2, grp, t0, w2, h2):
                nc.tensor.wait_ge(s_msk.h, msk_cnt[(hw2, grp)])
                if grp * 2 == t0 and hw2 >= 1:
                    nc.tensor.wait_ge(s_nrm.h, n0 + hw2)
                ins = None
                for k2 in range(2):
                    tkt = grp * 2 + k2
                    jt = (w2 - 1) * 4 + tkt
                    ins = nc.tensor.matmul(pav[0:65, :], VA[:, jt, h2, :],
                                           PT[:, tkt, :],
                                           start=(tkt == t0), stop=(tkt == 7))
                av_cnt[(hw2, grp)] = s_avp.inc(ins, 1)
                last_av[grp] = av_cnt[(hw2, grp)]

            def f_norm2(pn):
                nonlocal tt_prev
                g2, rb2, w2, nv, ttp = pn
                if ttp:
                    nc.tensor.wait_ge(s_ave.h, ttp)
                nc.tensor.wait_ge(s_nrm.h, nv)
                pins = nc.tensor.matmul(prr[0:64, :], v["ones_rowb"][:, 0:64],
                                        rs_rowb, start=True, stop=True)
                pv = s_mmc.inc(pins, 1)
                nc.vector.wait_ge(s_mmc.h, pv)
                dsta = AVN[rb2:rb2 + 64, g2, w2 * 512:(w2 + 1) * 512]
                tt_prev = s_ave.inc(nc.vector.tensor_tensor(
                    dsta, av_sb[:, 0:512], prr[0:64, :], ALU.mult), 1)

            rowv = [0, 0]
            rowv[0] = s_mr[0].inc(nc.sync.dma_start(
                mrow[0], dmax[0, 0:512][None, :]), 16)
            for h in range(8):
                g, rb = h // 2, (h % 2) * 64
                for w in range(4):
                    hw = h * 4 + w
                    sl = hw % 2
                    t0 = 4 if w == 0 else 0
                    grps = list(range(t0 // 2, 4))
                    if hw + 1 < 32:
                        nsl = (hw + 1) % 2
                        h2, w2 = (hw + 1) // 4, (hw + 1) % 4
                        if hw >= 1:
                            nc.sync.wait_ge(s_row.h, row0 + hw)
                        rowv[nsl] = s_mr[nsl].inc(nc.sync.dma_start(
                            mrow[nsl], dmax[h2, w2 * 512:(w2 + 1) * 512][None, :]),
                            16)
                    if hw >= 2:
                        nc.vector.wait_ge(s_mmc.h, mmc_k[hw - 2])
                    if hw < 2:
                        nc.vector.memset(katk[sl][64:65, 0:1024], 1.0)
                    klo = (w - 1) * 512 if w > 0 else 0
                    kw = 1024 if w > 0 else 512
                    koff = 0 if w > 0 else 512
                    nc.vector.tensor_copy(katk[sl][0:64, koff:koff + kw],
                                          QK[rb:rb + 64, 4 + g, klo:klo + kw])
                    nc.vector.tensor_copy(qaug[sl][0:64, 0:512],
                                          QK[rb:rb + 64, g,
                                             w * 512:(w + 1) * 512])
                    nc.vector.wait_ge(s_mr[sl].h, rowv[sl])
                    rv = s_row.inc(nc.vector.tensor_scalar_mul(
                        qaug[sl][64:65, 0:512], mrow[sl], -1.0), 1)
                    rhs = qaug[sl][0:65, 0:512]
                    for gi, grp in enumerate(grps):
                        pt = p2[jr % 3]
                        if gi == 0:
                            nc.tensor.wait_ge(s_row.h, rv)
                        if jr >= 3:
                            nc.tensor.wait_ge(s_exp.h, exp_cnt[jr - 3])
                        ins = None
                        for k2 in range(2):
                            tkt = grp * 2 + k2
                            ins = nc.tensor.matmul(
                                pt[:, k2 * 512:(k2 + 1) * 512],
                                katk[sl][0:65, tkt * 128:(tkt + 1) * 128],
                                rhs, start=True, stop=True)
                        mv = s_mmc.inc(ins, 1)
                        if gi == len(grps) - 1:
                            mmc_k[hw] = mv
                        nc.scalar.wait_ge(s_mmc.h, mv)
                        if grp in last_av:
                            nc.scalar.wait_ge(s_avp.h, last_av[grp])
                        dstp = PT[:, grp * 2:grp * 2 + 2, :] \
                            .rearrange("p j c -> p (j c)")
                        exp_cnt[jr] = s_exp.inc(
                            nc.scalar.activation(dstp, pt[:], AF.Exp,
                                                 scale=8.0), 1)
                        nc.vector.wait_ge(s_exp.h, exp_cnt[jr])
                        for k2 in range(2):
                            tkt = grp * 2 + k2
                            mm = nc.vector.tensor_mul(PT[:, tkt, :],
                                                      PT[:, tkt, :],
                                                      MSK[:, tkt, :])
                        msk_cnt[(hw, grp)] = s_msk.inc(mm, 1)
                        jr += 1
                        if gi >= 1:
                            f_av(hw, grps[gi - 1], t0, w, h)
                        if gi == 1 and pend is not None:
                            f_norm2(pend)
                            pend = None
                    f_av(hw, grps[-1], t0, w, h)
                    nc.vector.wait_ge(s_avp.h, av_cnt[(hw, grps[-1])])
                    nc.vector.reciprocal(rs_row, pav[64:65, :])
                    vfence(P)
                    nc.vector.tensor_copy(rs_rowb, rs_row)
                    nv = s_nrm.inc(nc.vector.tensor_copy(av_sb[:, 0:512],
                                                         pav[0:64, :]), 1)
                    pend = (g, rb, w, nv, tt_prev)
            f_norm2(pend)
        P.barrier()

        wo = Wd["L0_ffn_wo"].rearrange("l (k p) f -> l p k f", p=128)[lay]

        def o_src(k):
            return wo[:, k, :]

        def o_lhs(k, i):
            return AVN[:, k, i * 128:(i + 1) * 128]

        def o_evac(i, ob, pt):
            Xsl = X[:, i, ob * 512:(ob + 1) * 512]
            return nc.vector.tensor_tensor(Xsl, Xsl, pt, ALU.add)

        s_sp.wait(nc.vector)
        emit_proj_tm(P, stg, "fo", o_lhs, o_src, 4, o_evac, nc.vector,
                     n_ob=2)
        P.barrier()

        emit_build_xf(P, X, XF, 1, f"g{lay}")
        hn = FB[:, 0:4096].rearrange("p (g c) -> p g c", g=DG)
        H1 = FB[:, 4096:15360].rearrange("p (m c) -> p m c", m=22)
        scr1 = FB[:, 15360:15872]
        scr2 = FB[:, 15872:16384]
        w1 = Wd["L0_ffn_w1p"].rearrange("l (g p) f -> l p g f", p=128)[lay]
        w2 = Wd["L0_ffn_w2p"].rearrange("l (k p) f -> l p k f", p=128)[lay]
        load_row_bf16(P, Wd["L0_ffn_b2"][lay, 0:512], v["brow_bf"][:, 0:512], s_rows)
        load_row_bf16(P, Wd["L0_ffn_b2"][lay, 512:1024], v["brow_bf"][:, 512:1024],
                      s_rows)
        s_rows.wait(nc.tensor)
        bv = s_dma.inc(nc.sync.dma_start(v["ccol"][:, 8:52],
                                         Wd["L0_ffn_b1_cols"][lay]), 16)
        nc.vector.wait_ge(s_dma.h, bv)
        nc.scalar.wait_ge(s_dma.h, bv)
        s_f = P.sem("ff")
        s_fs = P.sem("ffs")
        for tb in range(4):
            with (
                nc.psum_tensor(f"ps_fmu{lay}{tb}", [1, 512]) as pmu,
                nc.psum_tensor(f"ps_fm2{lay}{tb}", [1, 512]) as pm2,
                nc.psum_tensor(f"ps_fb1{lay}{tb}", [128, 512]) as pb1,
                nc.psum_tensor(f"ps_fb2{lay}{tb}", [128, 512]) as pb2,
            ):
                insmu = None
                for g in range(DG):
                    insmu = nc.tensor.matmul(pmu[:], v["ones_c128"],
                                             sh(g, tb * 512, 512),
                                             start=(g == 0), stop=(g == DG - 1))
                m2pe = {}
                for g in range(DG):
                    sq = scr1 if g % 2 == 0 else scr2
                    if g >= 2:
                        nc.scalar.wait_ge(s_pe.h, m2pe[g - 2])
                    qv = s_fs.inc(nc.scalar.activation(sq, sh(g, tb * 512, 512),
                                                       AF.Square), 1)
                    nc.tensor.wait_ge(s_fs.h, qv)
                    insmu = nc.tensor.matmul(pm2[:], v["ones_c128"], sq,
                                             start=(g == 0), stop=(g == DG - 1))
                    m2pe[g] = s_pe.inc(insmu, 1)
                nc.vector.wait_ge(s_pe.h, m2pe[DG - 1])
                r0 = v["STG"][0:1, 0:512]
                r1 = v["STG"][0:1, 512:1024]
                nc.vector.tensor_scalar_mul(r0, pmu[:], -1.0 / 1024.0)
                vfence(P)
                nc.vector.tensor_mul(r1, r0, r0)
                vfence(P)
                nc.vector.scalar_tensor_tensor(r1, pm2[:], 1.0 / 1024.0, r1,
                                               ALU.mult, ALU.subtract)
                vfence(P)
                nc.vector.tensor_single_scalar(r1, r1, 1e-5, ALU.add)
                fv = s_f.inc(nc.vector.engine_nop(), 1)
                s_f.wait(nc.scalar, fv)
                fv = s_fs.inc(nc.scalar.activation(r1, r1, AF.Sqrt), 1)
                s_fs.wait(nc.vector, fv)
                nc.vector.reciprocal(r1, r1)
                vfence(P)
                nmb = v["STG"][0:1, 1024:1280].bitcast(BF16)
                rsb = v["STG"][0:1, 1280:1536].bitcast(BF16)
                nc.vector.tensor_copy(nmb, r0)
                vfence(P)
                fv = s_f.inc(nc.vector.tensor_copy(rsb, r1), 1)
                nc.tensor.wait_ge(s_f.h, fv)
                i1 = nc.tensor.matmul(pb1[:], v["ones_rowb"], nmb,
                                      start=True, stop=True)
                i2 = nc.tensor.matmul(pb2[:], v["ones_rowb"], rsb,
                                      start=True, stop=True)
                pe_bcast = s_pe.inc(i2, 1)
                nc.vector.wait_ge(s_pe.h, pe_bcast)
                for g in range(DG):
                    nc.vector.tensor_add(scr1, sh(g, tb * 512, 512), pb1[:])
                    vfence(P)
                    nc.vector.tensor_mul(scr2, scr1, pb2[:])
                    vfence(P)
                    s_f.inc(nc.vector.tensor_scalar(
                        hn[:, g, :], scr2,
                        v["bcol"][:, lay * 8 + g: lay * 8 + g + 1],
                        v["bcol"][:, 16 + lay * 8 + g: 17 + lay * 8 + g],
                        ALU.mult, ALU.add), 1)
                s_f.wait(nc.tensor)
                nc.sync.wait_ge(s_pe.h, pe_bcast)
            s_w1 = P.sem("fw1")
            w10 = s_w1.n
            with (
                nc.psum_tensor(f"ps_w1a{lay}{tb}", [128, 512]) as pw1a,
                nc.psum_tensor(f"ps_w1b{lay}{tb}", [128, 512]) as pw1b,
            ):
                pws = [pw1a, pw1b]
                for m in range(44):
                    dst = v["WB"][:, (m % 2) * 1024:(m % 2) * 1024 + 1024]
                    stg.stage(w1[:, :, m * 128:(m + 1) * 128], dst, 1024,
                              consumer_sem=s_w1 if m >= 2 else None,
                              consumer_val=w10 + m - 1)
                    stg.wait(nc.tensor)
                    pt = pws[m % 2]
                    if m >= 2:
                        nc.tensor.wait_ge(s_w1.h, s_w1.n - 1)
                    ins = None
                    for g in range(DG):
                        ins = nc.tensor.matmul(pt[:], dst[:, g * 128:(g + 1) * 128],
                                               hn[:, g, :], start=(g == 0),
                                               stop=(g == DG - 1))
                    mv = s_pe.inc(ins, 1)
                    if m % 2 == 0:
                        nc.vector.wait_ge(s_pe.h, mv)
                        nc.vector.wait_ge(s_w1.h, s_w1.n)
                        s_w1.inc(nc.vector.tensor_scalar_add(
                            scr1, pt[:], v["ccol"][:, 8 + m:9 + m]), 1)
                    else:
                        nc.scalar.wait_ge(s_pe.h, mv)
                        gv = s_fs.inc(nc.scalar.activation(
                            scr2, pt[:], AF.Gelu,
                            bias=v["ccol"][:, 8 + m:9 + m]), 1)
                        nc.vector.wait_ge(s_fs.h, gv)
                        nc.vector.wait_ge(s_w1.h, s_w1.n)
                        s_w1.inc(nc.vector.tensor_mul(H1[:, m // 2, :],
                                                      scr1, scr2), 1)
            P.barrier()

            def w2_src(k):
                return w2[:, k, :]

            def w2_lhs(k, i):
                return H1[:, k, (i % 4) * 128:(i % 4) * 128 + 128]

            def w2_evac(i, ob, pt):
                Xsl = X[:, i, ob * 512:(ob + 1) * 512]
                return nc.vector.tensor_tensor(Xsl, Xsl, pt, ALU.add)

            emit_proj_tm(P, stg, "fw2", w2_lhs, w2_src, 22, w2_evac,
                         nc.vector, n_ob=2,
                         bias_row=v["brow_bf"][:, 0:1024], tb_list=(tb,))
            P.barrier()

    s_sp.wait(nc.sync)
    SQ = FD[:, 0:1024]
    x0d = P.dram["x0"].rearrange("(i p) d -> p i d", p=128)
    x0s = [v["STG"][:, 0:1024], v["STG"][:, 1024:2048]]
    s_x0 = [P.sem("f0x0a"), P.sem("f0x0b")]
    for i in range(TI):
        sl = i % 2
        if i >= 2:
            nc.sync.wait_ge(s_ln.h, s_ln.n)
        dv = s_x0[sl].inc(nc.sync.dma_start(x0s[sl], x0d[:, i, :]), 16)
        nc.vector.wait_ge(s_x0[sl].h, dv)
        nc.vector.tensor_add(X[:, i, :], X[:, i, :], x0s[sl])
        vfence(P)
        emit_ln(P, X[:, i, :], SQ, 1024, s_ln)
    P.barrier()


def emit_l1_attn(P, stg, Wd):
    nc = P.nc
    v = P.v
    P.gate_w("L1_attn_W_out")
    X = v["X"]
    FA, FB, FC, FD = v["FA"], v["FB"], v["FC"], v["FD"]
    BT = FA[0:16, 0:2048]
    CT = FA[64:80, 0:2048]
    Bblk = FA[:, 2048:6144].rearrange("p (i q) -> p i q", i=TI)
    Cblk = FA[:, 6144:10240].rearrange("p (i q) -> p i q", i=TI)
    Hall = FA[:, 10240:14352]
    BTt = FA[:, 14352:14608].rearrange("p (i s) -> p i s", i=TI)
    Wqb = FA[:, 14608:15120]
    Evq = FA[:, 15120:15632]
    WTs = FA[:, 15632:15760]
    HTs = FA[:, 15760:16016]
    hspre = FA[:, 16016:16144]
    Eyq = FA[:, 16016:16392]
    xsT = FB[:, 0:16384].rearrange("p (g c) -> p g c", g=DG)
    dtT = FC[:, 0:16384].rearrange("p (g c) -> p g c", g=DG)
    Gbc = FD[:, 0:16384].rearrange("p (j c) -> p j c", j=8)
    Xr = v["X"].rearrange("p i d -> p (i d)")
    va = Xr[:, 0:2048]
    Dc = Xr[:, 2048:2304]
    Sall = Xr[:, 2560:6656]
    yac = Xr[:, 6656:8704]
    zsl = Xr[:, 8704:9728].bitcast(BF16)
    esc = Xr[:, 9728:11776]
    Eb = Xr[:, 11776:12800].bitcast(BF16)
    Pb = Xr[:, 12800:13824].bitcast(BF16)
    PGb = Xr[:, 13824:14848].bitcast(BF16)

    s_dma = P.sem("m_dma")
    s_rows = P.sem("m_rows")
    s_ln = P.sem("m_ln")
    s_sp = P.sem("m_spill")
    s_pe = P.sem("m_pe")

    s_dma.inc(nc.sync.dma_start(v["bcol"][:, 0:8], Wd["L1_attn_logA_cols"]), 16)
    s_dma.inc(nc.sync.dma_start(v["bcol"][:, 8:16], Wd["L1_attn_dtb_cols"]), 16)
    s_dma.inc(nc.sync.dma_start(v["bcol"][:, 16:24], Wd["L1_attn_D_cols"]), 16)
    s_dma.inc(nc.sync.dma_start(v["bcol"][:, 24:32], Wd["L1_attn_convb_cols"]), 16)
    s_dma.inc(nc.sync.dma_start(v["ccol"][:, 8:40], Wd["L1_attn_convw_cols"]), 16)
    Acol = v["ccol"][:, 40:48]
    s_dma.wait(nc.scalar)
    s_ln.inc(nc.scalar.activation(Acol, v["bcol"][:, 0:8], AF.Exp), 1)
    bmask = FA[:, 14608:14864]
    cmask = FA[:, 14864:15120]
    tid16 = FA[64:80, 15120:15248]

    xn_tok = FB[:, 0:16384].rearrange("p (i d) -> p i d", i=TI)
    gb_bc = FD[:, 0:2048]
    with nc.psum_tensor("ps_mgb", [128, 1024]) as pgb:
        for which, wname in ((0, "L1_attn_ln_g"), (1, "L1_attn_ln_b")):
            load_row_bf16(P, Wd[wname][0:512], v["brow_bf"][:, 0:512], s_rows)
            load_row_bf16(P, Wd[wname][512:1024], v["brow_bf"][:, 512:1024], s_rows)
            s_rows.wait(nc.tensor)
            if which == 1:
                s_ln.wait(nc.tensor)
            ins = None
            for q in range(2):
                ins = nc.tensor.matmul(pgb[:, q * 512:(q + 1) * 512],
                                       v["ones_rowb"][:, 0:128],
                                       v["brow_bf"][:, q * 512:(q + 1) * 512],
                                       start=True, stop=True)
            pv = s_pe.inc(ins, 1)
            nc.vector.wait_ge(s_pe.h, pv)
            s_ln.inc(nc.vector.tensor_copy(
                gb_bc[:, which * 1024:(which + 1) * 1024], pgb[:]), 1)
    lc = v["lncol"]
    t1f = v["STG"][:, 0:1024]
    gbcf = gb_bc[:, 0:1024]
    bbcf = gb_bc[:, 1024:2048]
    for i in range(TI):
        Xv = X[:, i, :]
        nc.vector.wait_ge(s_ln.h, s_ln.n)
        nc.vector.reduce_sum(lc[:, 0:1], Xv, axis=AX)
        vfence(P)
        nc.vector.tensor_scalar_mul(lc[:, 1:2], lc[:, 0:1], -1.0 / 1024.0)
        vv = s_ln.inc(nc.vector.engine_nop(), 1)
        s_ln.wait(nc.scalar, vv)
        vv = s_ln.inc(nc.scalar.activation(FD[:, 2560:3584], Xv, AF.Square,
                                           bias=lc[:, 1:2], scale=1.0,
                                           accum_out=lc[:, 2:3]), 1)
        s_ln.wait(nc.vector, vv)
        nc.vector.tensor_scalar(lc[:, 3:4], lc[:, 2:3], 1.0 / 1024.0, 1e-5,
                                ALU.mult, ALU.add)
        vv = s_ln.inc(nc.vector.engine_nop(), 1)
        s_ln.wait(nc.scalar, vv)
        vv = s_ln.inc(nc.scalar.activation(lc[:, 4:5], lc[:, 3:4], AF.Sqrt), 1)
        s_ln.wait(nc.vector, vv)
        nc.vector.reciprocal(lc[:, 5:6], lc[:, 4:5])
        vfence(P)
        nc.vector.tensor_scalar(t1f, Xv, lc[:, 1:2], lc[:, 5:6],
                                ALU.add, ALU.mult)
        vfence(P)
        nc.vector.tensor_mul(t1f, t1f, gbcf)
        vfence(P)
        s_ln.inc(nc.vector.tensor_tensor(xn_tok[:, i, :], t1f, bbcf, ALU.add), 1)
    P.barrier()

    if M_PHASES < 2:
        return
    s_sp.inc(nc.sync.dma_start(
        P.dram["x0"].rearrange("(i p) d -> p i d", p=128), X), 16)
    s_sp.wait(nc.sync)
    xnF = FA[:, 0:16384].rearrange("p (g c) -> p g c", g=DG)
    emit_build_xf(P, xn_tok, xnF, 0, "mx", src_bf16=True)

    win = Wd["L1_attn_W_in"].rearrange("(g p) f -> p g f", p=128)
    zscr = FD[:, 2048:2560]
    s_z = P.sem("m_z")

    def win_src(m):
        return win[:, :, m * 128:(m + 1) * 128]

    def win_rhs(g, tb):
        return xnF[:, g, tb * 512:(tb + 1) * 512]

    def win_evac(m, tb, pt):
        if m < 8:
            return nc.vector.tensor_copy(dtT[:, m, tb * 512:(tb + 1) * 512], pt[:])
        zv = s_z.inc(nc.vector.tensor_copy(zscr, pt[:]), 1)
        nc.scalar.wait_ge(s_z.h, zv)
        dv = s_z.inc(nc.scalar.dma_start(
            P.dram["z"][(m - 8) * 128:(m - 7) * 128, tb * 512:(tb + 1) * 512],
            zscr), 16)
        nc.vector.wait_ge(s_z.h, dv)
        return nc.vector.engine_nop()

    emit_proj_fm(P, stg, "mwin", win_src, 16, win_rhs, DG, win_evac, nc.vector)
    P.barrier()
    if M_PHASES < 3:
        return

    cw = v["ccol"][:, 8:40].rearrange("p (g k) -> p g k", g=DG)
    s_cv = P.sem("m_cv")
    for g in range(DG):
        xp = dtT[:, g, :]
        xs = xsT[:, g, :]
        acc = t1f.bitcast(BF16)
        nc.vector.wait_ge(s_cv.h, s_cv.n)
        nc.vector.tensor_scalar_mul(acc, xp, cw[:, g, 3:4])
        vfence(P)
        for k in range(3):
            sh = 3 - k
            nc.vector.scalar_tensor_tensor(acc[:, sh:2048], xp[:, 0:2048 - sh],
                                           cw[:, g, k:k + 1], acc[:, sh:2048],
                                           ALU.mult, ALU.add)
            vfence(P)
        cvv = s_cv.inc(nc.vector.engine_nop(), 1)
        s_cv.wait(nc.scalar, cvv)
        s_cv.inc(nc.scalar.activation(xs, acc, AF.Silu,
                                      bias=v["bcol"][:, 24 + g:25 + g]), 1)
        s_cv.wait(nc.vector)
    P.barrier()
    if M_PHASES < 4:
        return

    dtw = Wd["L1_attn_dt_w"].rearrange("(g p) f -> p g f", p=128)
    s_dt = P.sem("m_dt")
    spsc = esc[:, 0:512]

    def dt_src(m):
        return dtw[:, :, m * 128:(m + 1) * 128]

    def dt_rhs(g, tb):
        return xsT[:, g, tb * 512:(tb + 1) * 512]

    def dt_evac(m, tb, pt):
        sev = P.sem("pf_ev")
        nc.scalar.wait_ge(sev.h, sev.n)
        dv = s_dt.inc(nc.scalar.activation(spsc, pt[:], AF.Exp,
                                           bias=v["bcol"][:, 8 + m:9 + m]), 1)
        nc.scalar.wait_ge(s_dt.h, dv)
        return nc.scalar.activation(dtT[:, m, tb * 512:(tb + 1) * 512],
                                    spsc, AF.Ln, bias=1.0)

    emit_proj_fm(P, stg, "mdt", dt_src, 8, dt_rhs, DG, dt_evac, nc.scalar)
    P.barrier()
    if M_PHASES < 5:
        return

    s_bc = P.sem("m_bc")
    bcw = v["WB"][:, 0:128].rearrange("p (g s) -> p g s", g=DG)
    ccw = v["WB"][:, 128:256].rearrange("p (g s) -> p g s", g=DG)
    stg.stage(Wd["L1_attn_B_w"].rearrange("(g p) s -> p g s", p=128),
              bcw.rearrange("p g s -> p (g s)"), 128)
    stg.wait(nc.tensor)
    stg.stage(Wd["L1_attn_C_w"].rearrange("(g p) s -> p g s", p=128),
              ccw.rearrange("p g s -> p (g s)"), 128)
    stg.wait(nc.tensor)
    with (
        nc.psum_tensor("ps_mb", [16, 512]) as pb,
        nc.psum_tensor("ps_mc", [16, 512]) as pc,
    ):
        for tb in range(4):
            if tb > 0:
                s_bc.wait(nc.tensor)
            insb = insc = None
            for g in range(DG):
                insb = nc.tensor.matmul(pb[:], bcw[:, g, :],
                                        xsT[:, g, tb * 512:(tb + 1) * 512],
                                        start=(g == 0), stop=(g == DG - 1))
            for g in range(DG):
                insc = nc.tensor.matmul(pc[:], ccw[:, g, :],
                                        xsT[:, g, tb * 512:(tb + 1) * 512],
                                        start=(g == 0), stop=(g == DG - 1))
            pv = s_pe.inc(insc, 1)
            nc.vector.wait_ge(s_pe.h, pv)
            s_bc.inc(nc.vector.tensor_copy(BT[:, tb * 512:(tb + 1) * 512], pb[:]), 1)
            s_bc.inc(nc.vector.tensor_copy(CT[:, tb * 512:(tb + 1) * 512], pc[:]), 1)
    P.barrier()
    prodsc = FA[0:16, 2048:2560]
    ctsc = FA[0:16, 2560:3072]
    onesc16 = v["CS"][0:16, 300:301]
    nc.vector.memset(onesc16, 1.0)
    s_g = P.sem("m_g")
    with (
        nc.psum_tensor("ps_mgr", [1, 512]) as pgr,
        nc.psum_tensor("ps_mgb2", [128, 512]) as pgb2,
    ):
        for j in range(8):
            for tb in range(4):
                lo = tb * 512
                nc.vector.wait_ge(s_g.h, s_g.n)
                nc.vector.tensor_copy(ctsc, CT[:, lo:lo + 512])
                vfence(P)
                if j == 0:
                    gv = s_g.inc(nc.vector.tensor_mul(
                        prodsc, ctsc, BT[:, lo:lo + 512]), 1)
                elif lo == 0:
                    nc.vector.memset(prodsc[:, 0:j], 0.0)
                    gv = s_g.inc(nc.vector.tensor_mul(
                        prodsc[:, j:512], ctsc[:, j:512], BT[:, 0:512 - j]), 1)
                else:
                    gv = s_g.inc(nc.vector.tensor_mul(
                        prodsc, ctsc, BT[:, lo - j:lo + 512 - j]), 1)
                nc.tensor.wait_ge(s_g.h, gv)
                i1 = nc.tensor.matmul(pgr[:], onesc16, prodsc,
                                      start=True, stop=True)
                pv = s_pe.inc(i1, 1)
                nc.vector.wait_ge(s_pe.h, pv)
                gv = s_g.inc(nc.vector.tensor_copy(v["brow_bf"][:, 0:512], pgr[:]), 1)
                nc.tensor.wait_ge(s_g.h, gv)
                i2 = nc.tensor.matmul(pgb2[:], v["ones_rowb"],
                                      v["brow_bf"][:, 0:512], start=True, stop=True)
                pv = s_pe.inc(i2, 1)
                nc.vector.wait_ge(s_pe.h, pv)
                s_g.inc(nc.vector.tensor_copy(Gbc[:, j, lo:lo + 512], pgb2[:]), 1)
    P.barrier()
    s_bb = P.sem("m_bb")
    s_dma.inc(nc.sync.dma_start(bmask, Wd["c_bmask8"]), 16)
    s_dma.inc(nc.sync.dma_start(cmask, Wd["c_cmask"]), 16)
    s_dma.inc(nc.sync.dma_start(tid16, Wd["c_tid16"]), 16)
    s_dma.wait(nc.vector)
    s_dma.wait(nc.tensor)
    with (
        nc.psum_tensor("ps_mt", [128, 512], BF16) as pt4,
        nc.psum_tensor("ps_mtc", [128, 512]) as ptc,
    ):
        for q in range(4):
            if q > 0:
                nc.tensor.wait_ge(s_bb.h, s_bb.n)
            ins = None
            for k in range(4):
                i = q * 4 + k
                ins = nc.tensor.transpose(pt4[:, k * 16:(k + 1) * 16],
                                          BT[:, i * 128:(i + 1) * 128],
                                          v["ident"][0:16, 0:16])
            pv = s_pe.inc(ins, 1)
            nc.vector.wait_ge(s_pe.h, pv)
            s_bb.inc(nc.vector.tensor_copy(
                BTt[:, 4 * q:4 * q + 4, :].rearrange("p i s -> p (i s)"),
                pt4[:, 0:64]), 1)
        nc.vector.wait_ge(s_bb.h, s_bb.n)
        for i in range(TI):
            a1 = bmask.rearrange("p (c s) -> p c s", c=16)
            a2 = BTt[:, i, :].rearrange("p (o s) -> p o s", o=1)
            b1, b2 = bass.broadcast_tensor_aps(a1, a2)
            s_bb.inc(nc.vector.tensor_tensor(
                Bblk[:, i, :].rearrange("p (c s) -> p c s", c=16),
                b1, b2, ALU.mult), 1)
        for i in range(TI):
            for h2 in range(2):
                jj = i * 2 + h2
                pcc = ptc[:, (jj % 4) * 128:(jj % 4) * 128 + 128]
                if jj >= 4:
                    nc.tensor.wait_ge(s_bb.h, s_bb.n)
                i3 = nc.tensor.matmul(pcc, tid16,
                                      CT[:, i * 128:(i + 1) * 128],
                                      start=True, stop=True)
                pv = s_pe.inc(i3, 1)
                nc.vector.wait_ge(s_pe.h, pv)
                s_bb.inc(nc.vector.tensor_tensor(
                    Cblk[:, i, h2 * 128:(h2 + 1) * 128], pcc,
                    cmask[:, h2 * 128:(h2 + 1) * 128], ALU.mult), 1)
    P.barrier()
    if M_PHASES < 6:
        return

    s_sc = P.sem("m_sc")
    s_scs = P.sem("m_scs")
    s_zd = P.sem("m_zd")
    va3 = va.rearrange("p (c r) -> p c r", r=8)
    for g in M_GORDER:
        nc.scalar.wait_ge(s_scs.h, s_scs.n)
        zdv = s_zd.inc(nc.scalar.dma_start(
            zsl, P.dram["z"][g * 128:(g + 1) * 128, :]), 16)
        nc.vector.wait_ge(s_sc.h, s_sc.n)
        nc.vector.wait_ge(s_scs.h, s_scs.n)
        nc.vector.tensor_scalar(va, dtT[:, g, :], Acol[:, g:g + 1], -1.0,
                                ALU.mult, ALU.mult)
        vfence(P)
        nc.vector.tensor_mul(dtT[:, g, :], dtT[:, g, :], xsT[:, g, :])
        ug = dtT[:, g, :]
        ug3 = ug.rearrange("p (c r) -> p c r", r=8)
        vfence(P)
        for pstep in range(1, 8):
            nc.vector.tensor_add(va3[:, :, pstep:pstep + 1],
                                 va3[:, :, pstep:pstep + 1],
                                 va3[:, :, pstep - 1:pstep])
            vfence(P)
        scv = s_sc.inc(nc.vector.engine_nop(), 1)
        s_sc.wait(nc.scalar, scv)
        dcv = s_scs.inc(nc.scalar.activation(Dc, va3[:, :, 7:8], AF.Exp), 1)
        nc.vector.tensor_mul(yac, ug, Gbc[:, 0, :])
        vfence(P)
        esc3 = esc.rearrange("p (c r) -> p c r", r=8)
        Eb3 = Eb.rearrange("p (c r) -> p c r", r=8)
        Pb3 = Pb.rearrange("p (c r) -> p c r", r=8)
        PG3 = PGb.rearrange("p (c r) -> p c r", r=8)
        yac3 = yac.rearrange("p (c r) -> p c r", r=8)
        for j in range(1, 8):
            w = 8 - j
            G3 = Gbc[:, j, :].rearrange("p (c r) -> p c r", r=8)
            nc.vector.tensor_sub(esc3[:, :, j:8], va3[:, :, j:8], va3[:, :, 0:w])
            ev = s_sc.inc(nc.vector.engine_nop(), 1)
            s_sc.wait(nc.scalar, ev)
            ev = s_scs.inc(nc.scalar.activation(Eb3[:, :, j:8], esc3[:, :, j:8],
                                                AF.Exp), 1)
            s_scs.wait(nc.vector, ev)
            nc.vector.tensor_mul(Pb3[:, :, j:8], Eb3[:, :, j:8], ug3[:, :, 0:w])
            vfence(P)
            nc.vector.tensor_mul(PG3[:, :, j:8], Pb3[:, :, j:8], G3[:, :, j:8])
            vfence(P)
            nc.vector.tensor_add(yac3[:, :, j:8], yac3[:, :, j:8], PG3[:, :, j:8])
            vfence(P)
        with (
            nc.psum_tensor(f"ps_ms{g}", [128, 512]) as pS,
            nc.psum_tensor(f"ps_mw{g}", [128, 512], BF16) as pW,
            nc.psum_tensor(f"ps_mh{g}", [128, 256], BF16) as pH,
            nc.psum_tensor(f"ps_my{g}", [128, 512]) as pY,
        ):
            for q in range(4):
                lo = q * 512
                vq3 = va3[:, 64 * q:64 * (q + 1), :]
                b1, b2 = bass.broadcast_tensor_aps(vq3[:, :, 7:8], vq3)
                nc.vector.wait_ge(s_sc.h, s_sc.n)
                nc.vector.wait_ge(s_scs.h, s_scs.n)
                nc.vector.tensor_tensor(
                    esc3[:, 0:64, :], b1, b2, ALU.subtract)
                ev = s_sc.inc(nc.vector.engine_nop(), 1)
                s_sc.wait(nc.scalar, ev)
                ev = s_scs.inc(nc.scalar.activation(Wqb, esc[:, 0:512], AF.Exp), 1)
                s_scs.wait(nc.vector, ev)
                nc.vector.tensor_mul(Wqb, Wqb, ug[:, lo:lo + 512])
                wv = s_sc.inc(nc.vector.engine_nop(), 1)
                nc.tensor.wait_ge(s_sc.h, wv)
                for k4 in range(4):
                    i = q * 4 + k4
                    it = nc.tensor.transpose(pW[:, 0:128],
                                             Wqb[:, k4 * 128:(k4 + 1) * 128],
                                             v["ident"])
                    pv = s_pe.inc(it, 1)
                    nc.vector.wait_ge(s_pe.h, pv)
                    wtv = s_sc.inc(nc.vector.tensor_copy(WTs, pW[:, 0:128]), 1)
                    nc.tensor.wait_ge(s_sc.h, wtv)
                    iS = nc.tensor.matmul(pS[:, 0:256], WTs, Bblk[:, i, :],
                                          start=True, stop=True)
                    pv = s_pe.inc(iS, 1)
                    nc.vector.wait_ge(s_pe.h, pv)
                    s_sc.inc(nc.vector.tensor_copy(
                        Sall[:, i * 256:(i + 1) * 256], pS[:, 0:256]), 1)
            s_sc.wait(nc.vector)
            s_scs.wait(nc.vector)
            nc.vector.memset(
                Hall.rearrange("p (s c) -> p s c", s=16)[:, :, 0:1], 0.0)
            vfence(P)
            Hall3 = Hall.rearrange("p (s c) -> p s c", s=16)
            Sall3 = Sall.rearrange("p (a s) -> p a s", s=16)
            for s16 in range(16):
                nc.vector.tensor_tensor_scan(
                    Hall3[:, s16, 1:257], Dc, Sall3[:, :, s16],
                    0.0, ALU.mult, ALU.add)
            hv = s_sc.inc(nc.vector.engine_nop(), 1)
            nc.tensor.wait_ge(s_sc.h, hv)
            nc.vector.wait_ge(s_sc.h, hv)
            for q in range(4):
                lo = q * 512
                nc.scalar.wait_ge(s_sc.h, s_sc.n)
                eqv = s_scs.inc(nc.scalar.activation(Evq, va[:, lo:lo + 512],
                                                     AF.Exp), 1)
                for k4 in range(4):
                    i = q * 4 + k4
                    for h2 in range(2):
                        hsrc = Hall3[:, :, i * 16 + h2 * 8:
                                     i * 16 + h2 * 8 + 8] \
                            .rearrange("p s c -> p c s")
                        nc.vector.wait_ge(s_pe.h, s_pe.n)
                        hcv = s_sc.inc(nc.vector.tensor_copy(
                            hspre.rearrange("p (c s) -> p c s", c=8), hsrc), 1)
                        nc.tensor.wait_ge(s_sc.h, hcv)
                        it = nc.tensor.transpose(pH[:, h2 * 128:(h2 + 1) * 128],
                                                 hspre, v["ident"])
                        pv = s_pe.inc(it, 1)
                        nc.vector.wait_ge(s_pe.h, pv)
                        htv = s_sc.inc(nc.vector.tensor_copy(
                            HTs[:, h2 * 128:(h2 + 1) * 128],
                            pH[:, h2 * 128:(h2 + 1) * 128]), 1)
                    nc.tensor.wait_ge(s_sc.h, s_sc.n)
                    ins = None
                    for h2 in range(2):
                        ins = nc.tensor.matmul(
                            pY[:, k4 * 128:(k4 + 1) * 128],
                            HTs[:, h2 * 128:(h2 + 1) * 128],
                            Cblk[:, i, h2 * 128:(h2 + 1) * 128],
                            start=(h2 == 0), stop=(h2 == 1))
                    pv = s_pe.inc(ins, 1)
                nc.vector.wait_ge(s_pe.h, s_pe.n)
                nc.vector.wait_ge(s_scs.h, eqv)
                nc.vector.tensor_mul(Pb[:, 0:512], Evq, pY[:])
                vfence(P)
                cyv = s_sc.inc(nc.vector.tensor_tensor(
                    yac[:, lo:lo + 512], yac[:, lo:lo + 512],
                    Pb[:, 0:512], ALU.add), 1)
                nc.tensor.wait_ge(s_sc.h, cyv)
        nc.vector.wait_ge(s_zd.h, zdv)
        nc.vector.wait_ge(s_sc.h, s_sc.n)
        nc.vector.wait_ge(s_scs.h, s_scs.n)
        nc.vector.tensor_scalar_mul(Eb[:, 0:2048], xsT[:, g, :],
                                    v["bcol"][:, 16 + g:17 + g])
        vfence(P)
        nc.vector.tensor_add(esc, yac, Eb[:, 0:2048])
        zv = s_sc.inc(nc.vector.engine_nop(), 1)
        s_sc.wait(nc.scalar, zv)
        zv = s_scs.inc(nc.scalar.activation(Pb[:, 0:2048], zsl, AF.Silu), 1)
        s_scs.wait(nc.vector, zv)
        s_sc.inc(nc.vector.tensor_mul(xsT[:, g, :], esc, Pb[:, 0:2048]), 1)
    P.barrier()
    if M_PHASES < 7:
        return

    s_sp.inc(nc.sync.dma_start(
        X, P.dram["x0"].rearrange("(i p) d -> p i d", p=128)), 16)
    s_sp.wait(nc.vector)
    s_sp.wait(nc.tensor)
    wout = Wd["L1_attn_W_out"].rearrange("(k p) f -> p k f", p=128)

    def wo_src(k):
        return wout[:, k, :]

    def wo_lhs(k, i):
        return xsT[:, k, i * 128:(i + 1) * 128]

    def wo_evac(i, ob, pt):
        Xsl = X[:, i, ob * 512:(ob + 1) * 512]
        return nc.vector.scalar_tensor_tensor(Xsl, Xsl, 2.0, pt,
                                              ALU.mult, ALU.add)

    emit_proj_tm(P, stg, "mout", wo_lhs, wo_src, DG, wo_evac, nc.vector, n_ob=2)
    P.barrier()
    SQ = FD[:, 0:1024]
    for i in range(TI):
        emit_ln(P, X[:, i, :], SQ, 1024, s_ln)
    P.barrier()


def emit_l1_ffn(P, stg, Wd):
    nc = P.nc
    v = P.v
    P.gate_w("L1_ffn_lora_Bc")
    X = v["X"]
    XF = v["FA"][:, 0:16384].rearrange("p (g c) -> p g c", g=DG)
    INTER = v["FB"][:, 0:16384].rearrange("p (m c) -> p m c", m=32)
    OH = v["FC"][:, 6656:6784].rearrange("p (i e) -> p i e", i=TI)
    RM = v["FC"][:, 6784:7296].rearrange("p (i r) -> p i r", i=TI)
    rmT = v["FC"][0:32, 0:2048]
    loraB = v["FC"][0:32, 2048:6144]

    emit_build_xf(P, X, XF, 0, "m")

    s_rows = P.sem("f_rows")
    load_row_bf16(P, Wd["L1_ffn_gate_b"], v["brow_bf"][:, 0:8], s_rows)
    s_rows.wait(nc.tensor)

    s_g = P.sem("f_g")
    s_goh = P.sem("f_goh")
    ohf = v["ccol"][:, 8:16]
    cum = v["ccol"][:, 16:24]
    t1c = v["ccol"][:, 24:32]
    mcol = v["ccol"][:, 32:33]
    zer8 = v["ccol"][:, 40:48]
    nc.vector.memset(zer8, 0.0)
    gwt = Wd["L1_ffn_gate_Wt"].rearrange("(g p) e -> p g e", p=128)
    gdst = v["WB"][:, 0:1024]
    stg.stage(gwt, gdst[:, 0:64].rearrange("p (g e) -> p g e", g=DG), 64)
    stg.wait(nc.tensor)
    gd3 = gdst[:, 0:64].rearrange("p (g e) -> p g e", g=DG)
    with (
        nc.psum_tensor("ps_fg0", [128, 512]) as pg0,
        nc.psum_tensor("ps_fg1", [128, 512]) as pg1,
    ):
        pg = [pg0[:, 0:8], pg1[:, 0:8]]
        for i in range(TI):
            pt = pg[i % 2]
            if i >= 2:
                nc.tensor.wait_ge(s_goh.h, i - 1)
            ins = None
            for g in range(DG):
                ins = nc.tensor.matmul(pt, XF[:, g, i * 128:(i + 1) * 128],
                                       gd3[:, g, :], start=(g == 0), stop=False)
            ins = nc.tensor.matmul(pt, P.v["ones_rowb"],
                                   v["brow_bf"][:, 0:8], start=False, stop=True)
            mv = s_g.inc(ins, 1)
            nc.vector.wait_ge(s_g.h, mv)
            nc.vector.wait_ge(s_goh.h, s_goh.n)
            nc.vector.reduce_max(mcol, pt, axis=AX)
            vfence(P)
            nc.vector.tensor_scalar(ohf, pt, mcol, 1.0, ALU.is_ge, ALU.mult)
            vfence(P)
            nc.vector.tensor_tensor_scan(cum, ohf, zer8, 0.0, ALU.add, ALU.add)
            vfence(P)
            nc.vector.tensor_scalar(t1c, cum, 1.0, 1.0, ALU.is_equal, ALU.mult)
            vfence(P)
            s_goh.inc(nc.vector.tensor_tensor(OH[:, i, :], t1c, ohf, ALU.mult), 1)
    P.barrier()

    lat = Wd["L1_ffn_lora_At"].rearrange("(g p) r -> p g r", p=128)
    s_r = P.sem("f_r")
    s_rm = P.sem("f_rm")
    stg.stage(lat, gdst[:, 64:320].rearrange("p (g r) -> p g r", g=DG), 256)
    la3 = gdst[:, 64:320].rearrange("p (g r) -> p g r", g=DG)
    stg.wait(nc.tensor)
    with (
        nc.psum_tensor("ps_fr0", [128, 512]) as pr0,
        nc.psum_tensor("ps_fr1", [128, 512]) as pr1,
    ):
        pr = [pr0[:, 0:32], pr1[:, 0:32]]
        for i in range(TI):
            pt = pr[i % 2]
            if i >= 2:
                nc.tensor.wait_ge(s_rm.h, i - 1)
            ins = None
            for g in range(DG):
                ins = nc.tensor.matmul(pt, XF[:, g, i * 128:(i + 1) * 128],
                                       la3[:, g, :], start=(g == 0),
                                       stop=(g == DG - 1))
            mv = s_r.inc(ins, 1)
            nc.vector.wait_ge(s_r.h, mv)
            a1 = pt.rearrange("p (e r) -> p e r", e=8)
            a2 = OH[:, i, :].rearrange("p (e o) -> p e o", o=1)
            b1, b2 = bass.broadcast_tensor_aps(a1, a2)
            s_rm.inc(nc.vector.tensor_tensor(
                RM[:, i, :].rearrange("p (e r) -> p e r", e=8), b1, b2,
                ALU.mult), 1)
    s_t = P.sem("f_t")
    s_te = P.sem("f_te")
    with nc.psum_tensor("ps_ft", [32, 512], BF16) as ptr:
        s_rm.wait(nc.tensor)
        for q in range(4):
            if q > 0:
                nc.tensor.wait_ge(s_te.h, q)
            ins = None
            for k in range(4):
                i = q * 4 + k
                ins = nc.tensor.transpose(ptr[:, k * 128:(k + 1) * 128],
                                          RM[:, i, :], v["ident"])
            tv = s_t.inc(ins, 1)
            nc.vector.wait_ge(s_t.h, tv)
            s_te.inc(nc.vector.tensor_copy(
                rmT[:, q * 512:(q + 1) * 512], ptr[:]), 1)
    lb = Wd["L1_ffn_lora_Bc"]
    for q in range(4):
        stg.stage(lb[:, q * 1024:(q + 1) * 1024],
                  loraB[:, q * 1024:(q + 1) * 1024], 1024)
        stg.wait(nc.tensor)
    s_dma = P.sem("f_dma")
    bv = s_dma.inc(nc.sync.dma_start(v["bcol"][:, 0:32],
                                     Wd["L1_ffn_wib_cols"]), 16)
    nc.scalar.wait_ge(s_dma.h, bv)
    load_row_bf16(P, Wd["L1_ffn_wo_b"][0:512], v["brow_bf"][:, 0:512], s_rows)
    load_row_bf16(P, Wd["L1_ffn_wo_b"][512:1024], v["brow_bf"][:, 512:1024], s_rows)
    s_rows.wait(nc.tensor)
    P.barrier()

    wi = Wd["L1_ffn_wi_Wt"].rearrange("(g p) f -> p g f", p=128)
    wo = Wd["L1_ffn_wo_Wt"].rearrange("(k p) f -> p k f", p=128)
    s_ln = P.sem("f_ln")
    SQ = v["FD"][:, 0:1024]
    for tb in range(4):
        def wi_src(m):
            return wi[:, :, m * 128:(m + 1) * 128]

        def wi_rhs(g, _tb):
            return XF[:, g, tb * 512:(tb + 1) * 512]

        def wi_extra(m, _tb, pt):
            return nc.tensor.matmul(pt[:], loraB[:, m * 128:(m + 1) * 128],
                                    rmT[:, tb * 512:(tb + 1) * 512],
                                    start=False, stop=True)

        def wi_evac(m, _tb, pt):
            return nc.scalar.activation(INTER[:, m, :], pt[:], AF.Relu,
                                        bias=v["bcol"][:, m:m + 1], scale=1.0)

        emit_proj_fm(P, stg, "fwi", wi_src, 32, wi_rhs, DG, wi_evac,
                     nc.scalar, n_tb=1, extra_mm=wi_extra)
        P.barrier()

        def wo_src(k):
            return wo[:, k, :]

        def wo_lhs(k, i):
            return INTER[:, k, (i % 4) * 128:(i % 4) * 128 + 128]

        def wo_evac(i, ob, pt):
            Xsl = X[:, i, ob * 512:(ob + 1) * 512]
            return nc.vector.tensor_tensor(Xsl, Xsl, pt, ALU.add)

        emit_proj_tm(P, stg, "fwo", wo_lhs, wo_src, 32, wo_evac, nc.vector,
                     n_ob=2, bias_row=v["brow_bf"][:, 0:1024], tb_list=(tb,))
        P.barrier()
    for i in range(TI):
        emit_ln(P, X[:, i, :], SQ, 1024, s_ln)
    P.barrier()


# version 12
# speedup vs baseline: 1.0785x; 1.0059x over previous
import numpy as np
import ml_dtypes
from contextlib import ExitStack
import concourse.bass as bass
import concourse.mybir as mybir
from concourse.bass_utils import run_bass_kernel_spmd

F32 = mybir.dt.float32
BF16 = mybir.dt.bfloat16
AF = mybir.ActivationFunctionType
ALU = mybir.AluOpType
AX = mybir.AxisListType.X
BF = ml_dtypes.bfloat16

T, TI, D, DG, NCORES = 2048, 16, 1024, 8, 8

BLOCKS = ["l0_attn", "l0_ffn", "l1_attn", "l1_ffn"]
A_PHASES = 9
M_PHASES = 9
M_GORDER = list(range(DG))



class Sem:
    def __init__(self, handle):
        self.h = handle
        self.n = 0

    def inc(self, instr, k):
        instr.then_inc(self.h, k)
        self.n += k
        return self.n

    def wait(self, eng, val=None):
        eng.wait_ge(self.h, self.n if val is None else val)


class Prog:
    def __init__(self, nc, stack):
        self.nc = nc
        self.stack = stack
        self.sems = {}
        self.v = {}
        self.dram = {}

    def uniq(self):
        self._u = getattr(self, "_u", 0) + 1
        return self._u

    def uniq2(self):
        return self.uniq()

    def sem(self, name):
        if name not in self.sems:
            h = self.stack.enter_context(self.nc.semaphore(f"s_{name}"))
            self.sems[name] = Sem(h)
        return self.sems[name]

    def gate_w(self, name, eng=None):
        if hasattr(self, "ccname") and name in self.ccname:
            for sem, val in self.ccname[name]:
                (eng or self.nc.sync).wait_ge(sem.h, val)

    def barrier(self):
        nc = self.nc
        s = self.sem("barrier")
        c = self.v["ccol"]
        s.inc(nc.vector.memset(c[:, 0:1], 0.0), 1)
        s.inc(nc.scalar.memzero(c[:, 1:2]), 1)
        s.inc(nc.tensor.nop(), 1)
        s.inc(nc.sync.nop(), 1)
        for e in (nc.tensor, nc.vector, nc.scalar, nc.sync):
            s.wait(e)


ARENA_COLS = 53188


def carve(P, mega):
    v = P.v
    m = mega[:]
    v["X"] = m[:, 0:16384].rearrange("p (i d) -> p i d", i=TI)
    v["FA"] = m[:, 16384:24580].bitcast(BF16)
    v["FB"] = m[:, 24580:32772].bitcast(BF16)
    v["FC"] = m[:, 32772:40964].bitcast(BF16)
    v["FD"] = m[:, 40964:49284].bitcast(BF16)
    v["GF"] = m[:, 49284:51972]
    v["WB"] = m[:, 51972:52996].bitcast(BF16)
    v["WBf32"] = m[:, 51972:52996]
    v["CS"] = m[:, 52996:53188].bitcast(BF16)

    v["FAf32"] = m[:, 16384:24580]
    v["FDf32"] = m[:, 40964:49284]
    v["STG"] = v["GF"][:, 0:2048]
    v["mcols"] = v["GF"][:, 2048:2064]
    v["colsA"] = v["GF"][:, 2064:2080]
    v["lncol"] = v["GF"][:, 2080:2088]
    v["bcol"] = v["GF"][:, 2088:2120]
    v["ccol"] = v["GF"][:, 2120:2176]
    brow = v["GF"][:, 2176:2688]
    v["brow_bf"] = brow[0:1, :].bitcast(BF16)
    v["frow"] = brow[32:33, :]
    v["ident"] = v["CS"][:, 0:128]
    v["perm"] = v["CS"][:, 128:256]
    v["ones_rowb"] = v["CS"][0:1, 256:384]
    v["ones_c128"] = v["CS"][:, 383:384]




def vfence(P):
    s = P.sem("vf")
    v = s.inc(P.nc.vector.engine_nop(), 1)
    P.nc.vector.wait_ge(s.h, v)


class WStager:

    def __init__(self, P):
        self.P = P
        self.ring = P.v["STG"]
        self.s_ld = [P.sem("wst_ld0"), P.sem("wst_ld1")]
        self.s_cast = P.sem("wst_cast")
        self.slot = 0

    def stage(self, src_ap, dst_bf16, cols, consumer_sem=None, consumer_val=0):
        P = self.P
        assert cols <= 1024
        sl = self.slot
        self.slot ^= 1
        if src_ap.dtype == BF16:
            dst = dst_bf16
            if len(src_ap.shape) == 3 and len(dst.shape) == 2:
                dst = dst.rearrange("p (a b) -> p a b", a=src_ap.shape[1])
            P.nc.sync.wait_ge(self.s_ld[sl].h, self.s_ld[sl].n)
            if consumer_sem is not None and consumer_val > 0:
                P.nc.sync.wait_ge(consumer_sem.h, consumer_val)
            dv = self.s_ld[sl].inc(P.nc.sync.dma_start(dst, src_ap), 16)
            self.last = (self.s_ld[sl], dv)
            return None
        rows = src_ap.shape[0]
        st = self.ring[0:rows, sl * 1024: sl * 1024 + cols]
        std = st
        if len(src_ap.shape) == 3:
            std = st.rearrange("p (a b) -> p a b", a=src_ap.shape[1])
        if self.s_cast.n >= 2:
            P.nc.sync.wait_ge(self.s_cast.h, self.s_cast.n - 1)
        ldv = self.s_ld[sl].inc(P.nc.sync.dma_start(std, src_ap), 16)
        P.nc.vector.wait_ge(self.s_ld[sl].h, ldv)
        if consumer_sem is not None and consumer_val > 0:
            P.nc.vector.wait_ge(consumer_sem.h, consumer_val)
        cv = self.s_cast.inc(P.nc.vector.tensor_copy(dst_bf16, st), 1)
        self.last = (self.s_cast, cv)
        return cv

    def wait(self, eng):
        s, val = self.last
        eng.wait_ge(s.h, val)


def load_row_bf16(P, src_dram_1d, dst_bf_row, s):
    nc = P.nc
    n = src_dram_1d.shape[0]
    sd = P.sem("rowdma")
    nc.sync.wait_ge(s.h, s.n)
    dv = sd.inc(nc.sync.dma_start(P.v["frow"][:, 0:n], src_dram_1d[None, :]), 16)
    nc.vector.wait_ge(sd.h, dv)
    return s.inc(nc.vector.tensor_copy(dst_bf_row[:, 0:n], P.v["frow"][:, 0:n]), 1)




def emit_ln(P, Xv, sq_out, cols, s):
    nc = P.nc
    lc = P.v["lncol"]
    inv = 1.0 / float(cols)
    nc.vector.wait_ge(s.h, s.n)
    nc.vector.reduce_sum(lc[:, 0:1], Xv, axis=AX)
    vfence(P)
    nc.vector.tensor_scalar_mul(lc[:, 1:2], lc[:, 0:1], -inv)
    v = s.inc(nc.vector.engine_nop(), 1)
    s.wait(nc.scalar, v)
    v = s.inc(nc.scalar.activation(sq_out, Xv, AF.Square, bias=lc[:, 1:2],
                                   scale=1.0, accum_out=lc[:, 2:3]), 1)
    s.wait(nc.vector, v)
    nc.vector.tensor_scalar(lc[:, 3:4], lc[:, 2:3], inv, 1e-5, ALU.mult, ALU.add)
    v = s.inc(nc.vector.engine_nop(), 1)
    s.wait(nc.scalar, v)
    v = s.inc(nc.scalar.activation(lc[:, 4:5], lc[:, 3:4], AF.Sqrt), 1)
    s.wait(nc.vector, v)
    nc.vector.reciprocal(lc[:, 5:6], lc[:, 4:5])
    vfence(P)
    s.inc(nc.vector.tensor_scalar(Xv, Xv, lc[:, 1:2], lc[:, 5:6],
                                  ALU.add, ALU.mult), 1)


def emit_ln_all(P, X3, sqs, s):
    nc = P.nc
    inv = 1.0 / 1024.0
    lcg = P.v["STG"][:, 0:96]
    A, B = lcg[:, 0:16], lcg[:, 16:32]
    C, Dg = lcg[:, 32:48], lcg[:, 48:64]
    E, F = lcg[:, 64:80], lcg[:, 80:96]
    for i in range(TI):
        nc.vector.reduce_sum(A[:, i:i + 1], X3[:, i, :], axis=AX)
    vfence(P)
    vB = s.inc(nc.vector.tensor_scalar_mul(B, A, -inv), 1)
    s.wait(nc.scalar, vB)
    sq_v = 0
    sqv = {}
    for i in range(TI):
        sq = sqs[:, (i % 2) * 1024:(i % 2) * 1024 + 1024]
        if i >= 2:
            nc.scalar.wait_ge(s.h, sqv[i - 2])
        sq_v = s.inc(nc.scalar.activation(sq, X3[:, i, :], AF.Square,
                                          bias=B[:, i:i + 1], scale=1.0,
                                          accum_out=C[:, i:i + 1]), 1)
        sqv[i] = sq_v
    s.wait(nc.vector, sq_v)
    vD = s.inc(nc.vector.tensor_scalar(Dg, C, inv, 1e-5,
                                       ALU.mult, ALU.add), 1)
    s.wait(nc.scalar, vD)
    vE = s.inc(nc.scalar.activation(E, Dg, AF.Sqrt), 1)
    s.wait(nc.vector, vE)
    nc.vector.reciprocal(F, E)
    vfence(P)
    for i in range(TI):
        s.inc(nc.vector.tensor_scalar(X3[:, i, :], X3[:, i, :],
                                      B[:, i:i + 1], F[:, i:i + 1],
                                      ALU.add, ALU.mult), 1)




def emit_build_xf(P, Xv, XFv, off, tag, src_bf16=False):
    nc = P.nc
    s_c = P.sem("bxf_c")
    s_t = P.sem("bxf_t")
    s_e = P.sem("bxf_e")
    t0, e0 = s_t.n, s_e.n
    wb = P.v["WB"]
    if off:
        nc.vector.memset(XFv[:, :, 0:1].rearrange("p g c -> p (g c)"), 0.0)
    with (
        nc.psum_tensor(f"ps_bx0_{tag}", [128, 512], BF16) as p0,
        nc.psum_tensor(f"ps_bx1_{tag}", [128, 512], BF16) as p1,
    ):
        ps = [p0, p1]
        for i in range(TI):
            if src_bf16:
                buf = Xv[:, i, :]
            else:
                buf = wb[:, (i % 2) * 1024:(i % 2) * 1024 + 1024]
                if i >= 2:
                    nc.vector.wait_ge(s_t.h, t0 + 2 * (i - 1))
                cv = s_c.inc(nc.vector.tensor_copy(buf, Xv[:, i, :]), 1)
                nc.tensor.wait_ge(s_c.h, cv)
            for half in range(2):
                pset = i * 2 + half
                pt = ps[pset % 2]
                if pset >= 2:
                    nc.tensor.wait_ge(s_e.h, e0 + pset - 1)
                ins = None
                for k in range(4):
                    g = half * 4 + k
                    ins = nc.tensor.transpose(pt[:, k * 128:(k + 1) * 128],
                                              buf[:, g * 128:g * 128 + 128],
                                              P.v["ident"])
                tv = s_t.inc(ins, 1)
                nc.vector.wait_ge(s_t.h, tv)
                dst = XFv[:, 4 * half:4 * half + 4,
                          off + i * 128: off + i * 128 + 128]
                s_e.inc(nc.vector.tensor_copy(
                    dst, pt[:].rearrange("p (g c) -> p g c", g=4)), 1)
    P.barrier()




def emit_proj_fm(P, stg, tag, w_src_fn, n_m, rhs_fn, KG, evac, eng, n_tb=4,
                 extra_mm=None):
    nc = P.nc
    s_mm = P.sem("pf_mm")
    s_ev = P.sem("pf_ev")
    ev0 = s_ev.n
    wb = P.v["WB"]
    with (
        nc.psum_tensor(f"ps_{tag}{P.uniq()}0", [128, 512]) as p0,
        nc.psum_tensor(f"ps_{tag}{P.uniq2()}1", [128, 512]) as p1,
        nc.psum_tensor(f"ps_{tag}{P.uniq2()}2", [128, 512]) as p2,
        nc.psum_tensor(f"ps_{tag}{P.uniq2()}3", [128, 512]) as p3,
    ):
        ps = [p0, p1, p2, p3]
        for m in range(n_m):
            dst = wb[:, (m % 2) * 1024:(m % 2) * 1024 + KG * 128]
            cval = ev0 + (m - 1) * n_tb
            stg.stage(w_src_fn(m), dst, KG * 128,
                      consumer_sem=s_ev if m >= 2 else None,
                      consumer_val=cval)
            stg.wait(nc.tensor)
            for tb in range(n_tb):
                j = m * n_tb + tb
                pt = ps[j % 4]
                if j >= 4:
                    nc.tensor.wait_ge(s_ev.h, ev0 + j - 3)
                ins = None
                for g in range(KG):
                    ins = nc.tensor.matmul(pt[:], dst[:, g * 128:(g + 1) * 128],
                                           rhs_fn(g, tb), start=(g == 0),
                                           stop=(extra_mm is None and g == KG - 1))
                if extra_mm is not None:
                    ins = extra_mm(m, tb, pt)
                mv = s_mm.inc(ins, 1)
                eng.wait_ge(s_mm.h, mv)
                s_ev.inc(evac(m, tb, pt), 1)




def emit_proj_tm(P, stg, tag, lhsT_fn, w_src_fn, n_k, evac, eng, n_ob=2,
                 bias_row=None, tb_list=(0, 1, 2, 3)):
    nc = P.nc
    s_mm = P.sem("pt_mm")
    s_ev = P.sem("pt_ev")
    s_kd = P.sem("pt_kd")
    ev0, kd0 = s_ev.n, s_kd.n
    wb = P.v["WB"]
    with nc.psum_tensor(f"ps_{tag}{P.uniq()}", [128, 4096]) as pall:
        for tbi, tb in enumerate(tb_list):
            for k in range(n_k):
                dst = wb[:, (k % 2) * 1024:(k % 2) * 1024 + n_ob * 512]
                rel = tbi * n_k + k - 1
                stg.stage(w_src_fn(k), dst, n_ob * 512,
                          consumer_sem=s_kd if rel > 0 else None,
                          consumer_val=kd0 + rel)
                stg.wait(nc.tensor)
                if k == 0 and tbi > 0:
                    nc.tensor.wait_ge(s_ev.h, ev0 + tbi * 4 * n_ob)
                ins = None
                for ii in range(4):
                    lh = lhsT_fn(k, tb * 4 + ii)
                    for ob in range(n_ob):
                        pt = pall[:, (ii * n_ob + ob) * 512:
                                  (ii * n_ob + ob) * 512 + 512]
                        ins = nc.tensor.matmul(
                            pt, lh, dst[:, ob * 512:(ob + 1) * 512],
                            start=(k == 0),
                            stop=(bias_row is None and k == n_k - 1))
                s_kd.inc(ins, 1)
            if bias_row is not None:
                ins = None
                for ii in range(4):
                    for ob in range(n_ob):
                        pt = pall[:, (ii * n_ob + ob) * 512:
                                  (ii * n_ob + ob) * 512 + 512]
                        ins = nc.tensor.matmul(
                            pt, P.v["ones_rowb"],
                            bias_row[:, ob * 512:(ob + 1) * 512],
                            start=False, stop=True)
                mv = s_mm.inc(ins, 1)
                eng.wait_ge(s_mm.h, mv)
            else:
                eng.wait_ge(s_kd.h, s_kd.n)
            for ii in range(4):
                for ob in range(n_ob):
                    pt = pall[:, (ii * n_ob + ob) * 512:(ii * n_ob + ob) * 512 + 512]
                    s_ev.inc(evac(tb * 4 + ii, ob, pt), 1)




def emit_l0_attn(P, stg, Wd):
    nc = P.nc
    v = P.v
    X = v["X"]
    XF = v["FA"][:, 0:16384].rearrange("p (g c) -> p g c", g=DG)
    QT = v["FB"][:, 0:16384].rearrange("p (g c) -> p g c", g=DG)
    KT = v["FC"][:, 0:16384].rearrange("p (g c) -> p g c", g=DG)
    VA = v["FD"][:, 0:16640].rearrange("p (i h e) -> p i h e", i=TI, h=16)
    AVN = QT

    emit_build_xf(P, X, XF, 0, "a")
    if A_PHASES < 1:
        return

    s_dma = P.sem("a_dma")
    s_rows = P.sem("a_rows")
    bq = s_dma.inc(nc.sync.dma_start(v["bcol"][:, 0:16],
                                     Wd["L0_attn_bq_cols"]), 16)
    load_row_bf16(P, Wd["L0_attn_b_qkv"][2048:2560], v["brow_bf"][:, 0:512], s_rows)
    load_row_bf16(P, Wd["L0_attn_b_qkv"][2560:3072], v["brow_bf"][:, 512:1024], s_rows)
    s_rows.wait(nc.tensor)

    wq = Wd["L0_attn_W_qkv"].rearrange("(g p) f -> p g f", p=128)

    def qk_src(m):
        return wq[:, :, m * 128:(m + 1) * 128]

    def qk_rhs(g, tb):
        return XF[:, g, tb * 512:(tb + 1) * 512]

    def qk_evac(m, tb, pt):
        tgt, g = (QT, m) if m < 8 else (KT, m - 8)
        return nc.vector.tensor_scalar_add(tgt[:, g, tb * 512:(tb + 1) * 512],
                                           pt[:], v["bcol"][:, m:m + 1])

    P.gate_w("L0_attn_W_qkv")
    nc.vector.wait_ge(s_dma.h, bq)
    emit_proj_fm(P, stg, "aqk", qk_src, 16, qk_rhs, DG, qk_evac, nc.vector)
    P.barrier()

    def v_src(k):
        return wq[:, k, 2048:3072]

    def v_lhs(k, i):
        return XF[:, k, i * 128:(i + 1) * 128]

    def v_evac(i, vb, pt):
        dstv = VA[:, i, 8 * vb:8 * vb + 8, 0:64]
        return nc.vector.tensor_copy(dstv, pt.rearrange("p (h e) -> p h e", h=8))

    emit_proj_tm(P, stg, "av", v_lhs, v_src, DG, v_evac, nc.vector,
                 n_ob=2, bias_row=v["brow_bf"][:, 0:1024], tb_list=(2, 3))
    if hasattr(P, "convdone"):
        cs, cn_, cos, covs = P.convdone
        nc.vector.wait_ge(cs.h, cn_)
    sev = P.sem("pt_ev")
    nc.tensor.wait_ge(sev.h, sev.n)
    nc.sync.wait_ge(sev.h, sev.n)
    emit_proj_tm(P, stg, "av", v_lhs, v_src, DG, v_evac, nc.vector,
                 n_ob=2, bias_row=v["brow_bf"][:, 0:1024], tb_list=(0, 1))
    nc.vector.memset(VA[:, :, :, 64:65].rearrange("p i h e -> p (i h e)"), 1.0)
    P.barrier()
    if A_PHASES < 2:
        return

    dmax = P.dram["mrow"]
    s_red = P.sem("a_red")
    s_mmb = P.sem("a_mmb")
    mrt = v["STG"][0:16, 0:128].bitcast(BF16)
    hmax = v["STG"][:, 288:352]
    if hasattr(P, "convdone"):
        _, _, cos, covs = P.convdone
        nc.vector.wait_ge(cos[0].h, covs[0])
        nc.vector.wait_ge(cos[1].h, covs[1])
    dmax_written = 0
    red0 = s_red.n
    with (
        nc.psum_tensor("ps_s1_0", [128, 1024]) as q0,
        nc.psum_tensor("ps_s1_1", [128, 1024]) as q1,
        nc.psum_tensor("ps_s1_2", [128, 1024]) as q2,
        nc.psum_tensor("ps_s1t", [16, 256], BF16) as ptr,
    ):
        qs = [q0, q1, q2]
        f = 0
        red_val = {}
        for hp in range(8):
            for par in range(2):
                for i in range(TI):
                    lh = QT[64 * par:64 * par + 64, hp, i * 128:(i + 1) * 128]
                    for half in range(2):
                        pt = qs[f % 3]
                        if f >= 3:
                            nc.tensor.wait_ge(s_red.h, red_val[f - 3])
                        ins = None
                        for kk in range(2):
                            tkb = half * 2 + kk
                            rh = KT[64 * par:64 * par + 64, hp,
                                    tkb * 512:(tkb + 1) * 512]
                            ins = nc.tensor.matmul(pt[:, kk * 512:(kk + 1) * 512],
                                                   lh, rh, start=True, stop=True,
                                                   tile_position=(64 * par, 0))
                        mv = s_mmb.inc(ins, 1)
                        nc.vector.wait_ge(s_mmb.h, mv)
                        red_val[f] = s_red.inc(nc.vector.reduce_max(
                            hmax[:, par * 32 + i * 2 + half:
                                 par * 32 + i * 2 + half + 1], pt[:], axis=AX), 1)
                        f += 1
            vfence(P)
            hm = hmax.rearrange("p (i two) -> p i two", two=2)
            cm = nc.vector.tensor_tensor(
                v["GF"][:, 2048:2080].rearrange("p (i o) -> p i o", o=1),
                hm[:, :, 0:1], hm[:, :, 1:2], ALU.max)
            s_red.inc(cm, 1)
            vfence(P)
            mcb = v["STG"][:, 256:272].bitcast(BF16)
            cv = s_red.inc(nc.vector.tensor_copy(mcb, v["GF"][:, 2048:2080]), 1)
            nc.tensor.wait_ge(s_red.h, cv)
            t2 = None
            for q in range(2):
                t2 = nc.tensor.transpose(ptr[:, q * 128:q * 128 + 128],
                                         mcb[:, q * 16:(q + 1) * 16], v["ident"])
            mv = s_mmb.inc(t2, 1)
            nc.vector.wait_ge(s_mmb.h, mv)
            if hp > 0:
                nc.vector.wait_ge(s_dma.h, dmax_written)
            ev = s_red.inc(nc.vector.tensor_copy(mrt, ptr[:]), 1)
            nc.sync.wait_ge(s_red.h, ev)
            s_dma.inc(nc.sync.dma_start(
                dmax[2 * hp, :].rearrange("(i p) -> i p", p=128),
                mrt[:, 0:128]), 16)
            s_dma.inc(nc.sync.dma_start(
                dmax[2 * hp + 1, :].rearrange("(i p) -> i p", p=128),
                mrt[:, 128:256]), 16)
            dmax_written = s_dma.n
    P.barrier()
    if A_PHASES < 3:
        return

    PT = v["FA"][:, 0:8192].rearrange("p (j c) -> p j c", j=16)
    qaug = [v["FA"][:, 8192:10240], v["FA"][:, 12288:14336]]
    kaug = [v["FA"][:, 10240:12288], v["FA"][:, 14336:16384]]
    mrow = [v["STG"][0:1, 1024:2048].bitcast(BF16),
            v["WB"][0:1, 0:2048]]
    rs_row = v["STG"][0:1, 0:512]
    rs_rowb = v["STG"][0:1, 512:768].bitcast(BF16)
    av_sb = v["STG"][64:128, 0:256].bitcast(BF16)
    s_row = P.sem("a_row")
    s_exp = P.sem("a_exp")
    s_avp = P.sem("a_avp")
    s_ave = P.sem("a_ave")
    s_mmc = P.sem("a_mmc")
    s_nrm = P.sem("a_nrm")
    s_mr = [P.sem("a_mr0"), P.sem("a_mr1")]
    s_dma.wait(nc.sync)
    row0, e0, n0 = s_row.n, s_exp.n, s_nrm.n
    av_cnt = {}
    exp_cnt = {}
    mmc_s2 = {}
    with (
        nc.psum_tensor("ps_s2a", [128, 1024]) as p2a,
        nc.psum_tensor("ps_s2b", [128, 1024]) as p2b,
        nc.psum_tensor("ps_s2c", [128, 1024]) as p2c,
        nc.psum_tensor("ps_avo", [128, 512]) as pav,
        nc.psum_tensor("ps_rr", [128, 512]) as prr,
    ):
        p2 = [p2a, p2b, p2c]
        pend = None
        tt_prev = 0

        def emit_av(bg, grp):
            hh = bg // 4
            nc.tensor.wait_ge(s_exp.h, exp_cnt[bg * 8 + grp])
            if grp == 0 and bg >= 1:
                nc.tensor.wait_ge(s_nrm.h, n0 + bg)
            ins = None
            for k2 in range(2):
                jtk = grp * 2 + k2
                ins = nc.tensor.matmul(pav[0:65, :], VA[:, jtk, hh, :],
                                       PT[:, jtk, :], start=(jtk == 0),
                                       stop=(jtk == TI - 1))
            av_cnt[(bg, grp)] = s_avp.inc(ins, 1)

        def emit_norm2(pn):
            nonlocal tt_prev
            g2, rb2, b2, nv, ttp = pn
            if ttp:
                nc.tensor.wait_ge(s_ave.h, ttp)
            nc.tensor.wait_ge(s_nrm.h, nv)
            pins = nc.tensor.matmul(prr[0:64, :], v["ones_rowb"][:, 0:64],
                                    rs_rowb, start=True, stop=True)
            pv = s_mmc.inc(pins, 1)
            nc.vector.wait_ge(s_mmc.h, pv)
            dsta = AVN[rb2:rb2 + 64, g2, b2 * 512:(b2 + 1) * 512]
            tt_prev = s_ave.inc(nc.vector.tensor_tensor(
                dsta, av_sb[:, 0:512], prr[0:64, :], ALU.mult), 1)

        rowv = [0, 0]
        rowv[0] = s_mr[0].inc(nc.sync.dma_start(mrow[0], dmax[0, :][None, :]), 16)
        for h in range(16):
            sl = h % 2
            if h + 1 < 16:
                nsl = (h + 1) % 2
                if h >= 1:
                    nc.sync.wait_ge(s_row.h, row0 + h)
                rowv[nsl] = s_mr[nsl].inc(
                    nc.sync.dma_start(mrow[nsl], dmax[h + 1, :][None, :]), 16)
            g, rb = h // 2, (h % 2) * 64
            if h >= 2:
                nc.vector.wait_ge(s_mmc.h, mmc_s2[h - 2])
            if h < 2:
                nc.vector.memset(kaug[sl][64:65, :], 1.0)
            nc.vector.tensor_copy(kaug[sl][0:64, :], KT[rb:rb + 64, g, :])
            nc.vector.tensor_copy(qaug[sl][0:64, :], QT[rb:rb + 64, g, :])
            nc.vector.wait_ge(s_mr[sl].h, rowv[sl])
            rv = s_row.inc(nc.vector.tensor_scalar_mul(
                qaug[sl][64:65, :], mrow[sl], -8.0), 1)
            first_of_h = True
            for b in range(4):
                bg = h * 4 + b
                for grp in range(8):
                    j = bg * 8 + grp
                    pt2 = p2[j % 3]
                    if first_of_h and grp == 0:
                        nc.tensor.wait_ge(s_row.h, rv)
                        first_of_h = False
                    if j >= 3:
                        nc.tensor.wait_ge(s_exp.h, e0 + j - 2)
                    ins = None
                    for k2 in range(2):
                        jtk = grp * 2 + k2
                        ins = nc.tensor.matmul(
                            pt2[:, k2 * 512:(k2 + 1) * 512],
                            kaug[sl][0:65, jtk * 128:(jtk + 1) * 128],
                            qaug[sl][0:65, b * 512:(b + 1) * 512],
                            start=True, stop=True)
                    mv = s_mmc.inc(ins, 1)
                    if bg == h * 4 + 3 and grp == 7:
                        mmc_s2[h] = mv
                    nc.scalar.wait_ge(s_mmc.h, mv)
                    if bg >= 1:
                        nc.scalar.wait_ge(s_avp.h, av_cnt[(bg - 1, grp)])
                    dstp = PT[:, grp * 2:grp * 2 + 2, :] \
                        .rearrange("p j c -> p (j c)")
                    exp_cnt[j] = s_exp.inc(
                        nc.scalar.activation(dstp, pt2[:], AF.Exp,
                                             scale=0.125), 1)
                    if grp >= 1:
                        emit_av(bg, grp - 1)
                    if grp == 2 and pend is not None:
                        emit_norm2(pend)
                        pend = None
                emit_av(bg, 7)
                nc.vector.wait_ge(s_avp.h, av_cnt[(bg, 7)])
                nc.vector.reciprocal(rs_row, pav[64:65, :])
                vfence(P)
                nc.vector.tensor_copy(rs_rowb, rs_row)
                nv = s_nrm.inc(nc.vector.tensor_copy(av_sb[:, 0:512],
                                                     pav[0:64, :]), 1)
                pend = (g, rb, b, nv, tt_prev)
        emit_norm2(pend)
    P.barrier()
    if A_PHASES < 4:
        return

    P.gate_w("L0_attn_W_o")
    load_row_bf16(P, Wd["L0_attn_b_o"][0:512], v["brow_bf"][:, 0:512], s_rows)
    load_row_bf16(P, Wd["L0_attn_b_o"][512:1024], v["brow_bf"][:, 512:1024], s_rows)
    s_rows.wait(nc.tensor)
    wo = Wd["L0_attn_W_o"].rearrange("(k p) f -> p k f", p=128)
    AV3 = AVN

    def o_src(k):
        return wo[:, k, :]

    def o_lhs(k, i):
        return AV3[:, k, i * 128:(i + 1) * 128]

    def o_evac(i, ob, pt):
        Xsl = X[:, i, ob * 512:(ob + 1) * 512]
        return nc.vector.tensor_tensor(Xsl, Xsl, pt, ALU.add)

    emit_proj_tm(P, stg, "aout", o_lhs, o_src, DG, o_evac, nc.vector,
                 n_ob=2, bias_row=v["brow_bf"][:, 0:1024])
    P.barrier()
    s_ln = P.sem("a_ln")
    emit_ln_all(P, X, v["FD"][:, 0:2048], s_ln)
    P.barrier()



CONVERT_WEIGHTS = []


def emit_convert_weights(P, Wd, Wbf):
    nc = P.nc
    if not hasattr(P, "ccname"):
        P.ccname = {}
    FDf = P.v["FDf32"]
    fin = [FDf[:, 0:2048], FDf[:, 2048:4096]]
    fout = [P.v["STG"][:, 0:1024].bitcast(BF16),
            P.v["STG"][:, 1024:2048].bitcast(BF16)]
    s_ci = [P.sem("cw_i0"), P.sem("cw_i1")]
    s_cc = P.sem("cw_c")
    s_co = [P.sem("cw_o0"), P.sem("cw_o1")]
    chunks = []
    for nm in CONVERT_WEIGHTS:
        w = Wd[nm]
        L = 1
        for d in w.shape:
            L *= d
        cols = L // 128
        spec = " ".join(chr(97 + i) for i in range(len(w.shape)))
        wf = w.rearrange(f"{spec} -> ({spec})")
        bf = Wbf[nm].rearrange(f"{spec} -> ({spec})")
        off = 0
        while off < cols:
            cn = min(2048, cols - off)
            chunks.append((nm, wf, bf, off, cn))
            off += cn

    def in_dma(ci):
        nm, wf, bf, off, cn = chunks[ci]
        sl = ci % 2
        srcap = wf[off * 128:(off + cn) * 128].rearrange("(p c) -> p c", c=cn)
        return s_ci[sl].inc(nc.gpsimd.dma_start(fin[sl][:, 0:cn], srcap), 16)

    n_ch = len(chunks)
    iv = {}
    for k in range(min(2, n_ch)):
        iv[k] = in_dma(k)
    co_v = {}
    cc_v = {}
    for ci, (nm, wf, bf, off, cn) in enumerate(chunks):
        sl4, sl2 = ci % 2, ci % 2
        dstap = bf[off * 128:(off + cn) * 128].rearrange("(p c) -> p c", c=cn)
        nc.gpsimd.wait_ge(s_ci[sl4].h, iv[ci])
        if ci >= 2:
            nc.gpsimd.wait_ge(s_co[sl2].h, co_v[ci - 2])
        cv = s_cc.inc(nc.gpsimd.tensor_copy(fout[sl2][:, 0:cn],
                                            fin[sl4][:, 0:cn]), 1)
        cc_v[ci] = cv
        nc.gpsimd.wait_ge(s_cc.h, cv)
        co_v[ci] = s_co[sl2].inc(nc.gpsimd.dma_start(dstap,
                                                     fout[sl2][:, 0:cn]), 16)
        P.ccname[nm] = [(s_co[0], s_co[0].n), (s_co[1], s_co[1].n)]
        if ci + 2 < n_ch:
            iv[ci + 2] = in_dma(ci + 2)
    P.ccsem = s_cc
    P.convdone = (s_cc, s_cc.n, s_co, [s_co[0].n, s_co[1].n])




def build_program(blocks):
    nc = bass.Bass("TRN2", target_bir_lowering=False)
    stack = ExitStack()
    P = Prog(nc, stack)

    Wd = {}
    names = [
        ("L0_attn_x", [T, D]),
        ("L0_attn_W_qkv", [D, 3072]), ("L0_attn_b_qkv", [3072]),
        ("L0_attn_W_o", [D, D]), ("L0_attn_b_o", [D]),
        ("L0_ffn_wqkv", [2, D, 1536]), ("L0_ffn_q_scale", [2, 64]),
        ("L0_ffn_k_scale", [2, 64]), ("L0_ffn_wo", [2, 512, D]),
        ("L0_ffn_ln_g", [2, D]), ("L0_ffn_ln_b", [2, D]),
        ("L0_ffn_w1p", [2, D, 5632]), ("L0_ffn_b1p", [2, 5632]),
        ("L0_ffn_w2p", [2, 2816, D]), ("L0_ffn_b2", [2, D]),
        ("L1_attn_ln_g", [D]), ("L1_attn_ln_b", [D]),
        ("L1_attn_W_in", [D, 2048]), ("L1_attn_conv_w", [D, 1, 4]),
        ("L1_attn_conv_b", [D]), ("L1_attn_dt_w", [D, D]),
        ("L1_attn_dt_b", [D]), ("L1_attn_B_w", [D, 16]),
        ("L1_attn_C_w", [D, 16]), ("L1_attn_log_A", [D]),
        ("L1_attn_D_param", [D]), ("L1_attn_W_out", [D, D]),
        ("L1_ffn_gate_Wt", [D, 8]), ("L1_ffn_gate_b", [8]),
        ("L1_ffn_wi_Wt", [D, 4096]), ("L1_ffn_wi_b", [4096]),
        ("L1_ffn_wo_Wt", [4096, D]), ("L1_ffn_wo_b", [D]),
        ("L1_ffn_lora_At", [D, 32]), ("L1_ffn_lora_Bc", [32, 4096]),
        ("L0_attn_bq_cols", [128, 16]),
        ("L1_ffn_wib_cols", [128, 32]),
        ("L1_attn_logA_cols", [128, 8]), ("L1_attn_dtb_cols", [128, 8]),
        ("L1_attn_D_cols", [128, 8]), ("L1_attn_convb_cols", [128, 8]),
        ("L1_attn_convw_cols", [128, 32]),
        ("c_bmask8", [128, 256]), ("c_cmask", [128, 256]), ("c_tid16", [16, 128]),
        ("c_rot_cs", [128, 2048]), ("c_rot_ss", [128, 2048]),
        ("c_rot_cds", [128, 2048]), ("c_rot_sds", [128, 2048]),
        ("c_locmask", [128, 4096]), ("c_perm", [128, 128]),
        ("L0_ffn_qk_cols", [128, 4]), ("L0_ffn_lngb_cols", [128, 32]),
        ("L0_ffn_b1_cols", [2, 128, 44]),
        ("c_ident", [128, 128]),
    ]
    for nm, shp in names:
        dt = BF16 if nm in ("c_ident", "c_bmask8", "c_cmask", "c_tid16",
                            "c_rot_cs", "c_rot_ss", "c_rot_cds", "c_rot_sds",
                            "c_locmask", "c_perm") else F32
        Wd[nm] = nc.dram_tensor(nm, shp, dt, kind="ExternalInput").ap()
    xout = nc.dram_tensor("x_out", [T, D], F32, kind="ExternalOutput").ap()

    P.dram["mrow"] = nc.dram_tensor("scr_mrow", [16, 2048], BF16,
                                    kind="Internal").ap()
    Wbf = {}
    for nm in CONVERT_WEIGHTS:
        shp = dict(names)[nm]
        Wbf[nm] = nc.dram_tensor(f"bf_{nm}", shp, BF16, kind="Internal").ap()
    P.dram["x0"] = nc.dram_tensor("scr_x0", [T, D], F32, kind="Internal").ap()
    P.dram["z"] = nc.dram_tensor("scr_z", [D, T], BF16, kind="Internal").ap()

    with stack:
        mega = stack.enter_context(nc.sbuf_tensor("mega", [128, ARENA_COLS], F32))
        with nc.Block() as _block:
            carve(P, mega)
            stg = WStager(P)
            s_io = P.sem("io")
            s_io.inc(nc.sync.dma_start(
                P.v["X"],
                Wd["L0_attn_x"].rearrange("(i p) d -> p i d", p=128)), 16)
            s_io.inc(nc.sync.dma_start(P.v["ident"], Wd["c_ident"]), 16)
            s_io.inc(nc.sync.dma_start(P.v["perm"], Wd["c_perm"]), 16)
            nc.vector.memset(P.v["ones_rowb"][:], 1.0)
            for e in (nc.tensor, nc.vector, nc.scalar, nc.sync):
                s_io.wait(e)
            if CONVERT_WEIGHTS:
                emit_convert_weights(P, Wd, Wbf)
                for nm in CONVERT_WEIGHTS:
                    Wd[nm] = Wbf[nm]

            for b in blocks:
                if b == "l0_attn":
                    emit_l0_attn(P, stg, Wd)
                elif b == "l0_ffn":
                    emit_l0_ffn(P, stg, Wd)
                elif b == "l1_attn":
                    emit_l1_attn(P, stg, Wd)
                elif b == "l1_ffn":
                    emit_l1_ffn(P, stg, Wd)

            s_out = P.sem("out")
            s_out.inc(nc.sync.dma_start(
                xout.rearrange("(i p) d -> p i d", p=128), P.v["X"]), 16)
            s_out.wait(nc.sync)
    return nc




def prep_inputs(inputs):
    f32 = np.float32
    g = {k: np.asarray(v, dtype=f32) for k, v in inputs.items()}

    HID, HIDP = 2730, 2816
    com = {}
    com["L0_attn_W_qkv"] = g["L0_attn_W_qkv"]
    com["L0_attn_b_qkv"] = g["L0_attn_b_qkv"]
    com["L0_attn_W_o"] = g["L0_attn_W_o"]
    com["L0_attn_b_o"] = g["L0_attn_b_o"]
    com["L0_ffn_wqkv"] = g["L0_ffn_wqkv"]
    com["L0_ffn_q_scale"] = g["L0_ffn_q_scale"]
    com["L0_ffn_k_scale"] = g["L0_ffn_k_scale"]
    com["L0_ffn_wo"] = g["L0_ffn_wo"]
    com["L0_ffn_ln_g"] = g["L0_ffn_ln_g"]
    com["L0_ffn_ln_b"] = g["L0_ffn_ln_b"]
    w1 = g["L0_ffn_w1"]
    b1 = g["L0_ffn_b1"]
    w1p = np.zeros((2, D, 2 * HIDP), f32)
    b1p = np.zeros((2, 2 * HIDP), f32)
    perm = np.zeros(2 * HIDP, np.int64)
    keep = np.zeros(2 * HIDP, bool)
    for blk in range(22):
        lo, hi = blk * 128, min((blk + 1) * 128, HID)
        n = hi - lo
        pa = 2 * blk * 128
        pg = (2 * blk + 1) * 128
        perm[pa:pa + n] = np.arange(lo, hi)
        keep[pa:pa + n] = True
        perm[pg:pg + n] = HID + np.arange(lo, hi)
        keep[pg:pg + n] = True
    w1p[:, :, keep[: 2 * HIDP]] = w1[:, :, perm[keep]]
    b1p[:, keep] = b1[:, perm[keep]]
    com["L0_ffn_w1p"] = w1p
    com["L0_ffn_b1p"] = b1p
    w2 = g["L0_ffn_w2"]
    w2p = np.zeros((2, HIDP, D), f32)
    w2p[:, :HID, :] = w2
    com["L0_ffn_w2p"] = w2p
    com["L0_ffn_b2"] = g["L0_ffn_b2"]
    for k in ["L1_attn_ln_g", "L1_attn_ln_b", "L1_attn_W_in",
              "L1_attn_conv_b", "L1_attn_dt_w", "L1_attn_dt_b",
              "L1_attn_B_w", "L1_attn_C_w", "L1_attn_log_A",
              "L1_attn_D_param", "L1_attn_W_out"]:
        com[k] = g[k]
    com["L1_attn_conv_w"] = g["L1_attn_conv_w"]
    com["L1_ffn_gate_Wt"] = np.ascontiguousarray(g["L1_ffn_gate_W"].T)
    com["L1_ffn_gate_b"] = g["L1_ffn_gate_b"]
    com["L1_ffn_wi_Wt"] = np.ascontiguousarray(g["L1_ffn_wi_W"].T)
    com["L1_ffn_wi_b"] = g["L1_ffn_wi_b"]
    com["L1_ffn_wo_Wt"] = np.ascontiguousarray(g["L1_ffn_wo_W"].T)
    com["L1_ffn_wo_b"] = g["L1_ffn_wo_b"]
    la = g["L1_ffn_lora_A"]
    com["L1_ffn_lora_At"] = np.ascontiguousarray(
        la.reshape(32, D).T)
    lb = g["L1_ffn_lora_B"]
    com["L1_ffn_lora_Bc"] = np.ascontiguousarray(
        lb.transpose(0, 2, 1).reshape(32, 4096))
    com["L0_attn_bq_cols"] = np.ascontiguousarray(
        g["L0_attn_b_qkv"][:2048].reshape(16, 128).T)
    com["L1_ffn_wib_cols"] = np.ascontiguousarray(
        g["L1_ffn_wi_b"].reshape(32, 128).T)
    com["L1_attn_logA_cols"] = np.ascontiguousarray(
        g["L1_attn_log_A"].reshape(8, 128).T)
    com["L1_attn_dtb_cols"] = np.ascontiguousarray(
        g["L1_attn_dt_b"].reshape(8, 128).T)
    com["L1_attn_D_cols"] = np.ascontiguousarray(
        g["L1_attn_D_param"].reshape(8, 128).T)
    com["L1_attn_convb_cols"] = np.ascontiguousarray(
        g["L1_attn_conv_b"].reshape(8, 128).T)
    com["L1_attn_convw_cols"] = np.ascontiguousarray(
        g["L1_attn_conv_w"][:, 0, :].reshape(8, 128, 4).transpose(1, 0, 2)
        .reshape(128, 32))
    bm = np.zeros((128, 16, 16), BF)
    for p in range(128):
        bm[p, p // 8, :] = 1
    com["c_bmask8"] = bm.reshape(128, 256)
    cm = np.zeros((128, 2, 128), BF)
    for q in range(128):
        c_l_in_half = q // 16
        for h2 in range(2):
            c_l = h2 * 8 + c_l_in_half
            for p in range(128):
                if c_l == p // 8:
                    cm[q, h2, p] = 1
    com["c_cmask"] = cm.reshape(128, 256)
    tid = np.zeros((16, 128), BF)
    for qq in range(128):
        tid[qq % 16, qq] = 1
    com["c_tid16"] = tid
    DH, WIN, NN = 64, 512, 2048
    half_idx = np.arange(0, DH, 2, dtype=np.float64)
    inv_freq = 1.0 / 10000.0 ** (half_idx / DH)
    tt = np.arange(NN, dtype=np.float64)
    freqs = tt[:, None] * inv_freq[None, :]
    freqs = np.concatenate([freqs, freqs], axis=-1)
    basev = (half_idx + 0.4 * DH) / (1.4 * DH)
    power = (tt - NN // 2) / float(WIN // 2)
    scalev = basev[None, :] ** power[:, None]
    scalev = np.concatenate([scalev, scalev], axis=-1)
    cosv, sinv = np.cos(freqs), np.sin(freqs)
    def rep2(m):
        r = np.empty((128, NN))
        r[0:64, :] = m.T
        r[64:128, :] = m.T
        return r.astype(BF)
    com["c_rot_cs"] = rep2(cosv * scalev)
    com["c_rot_ss"] = rep2(sinv * scalev)
    com["c_rot_cds"] = rep2(cosv / scalev)
    com["c_rot_sds"] = rep2(sinv / scalev)
    lm = np.zeros((128, 8, 512), BF)
    for tkt in range(8):
        for p in range(128):
            tk = tkt * 128 + p
            c0, c1 = max(0, tk - 512), min(511, tk)
            if c0 <= c1:
                lm[p, tkt, c0:c1 + 1] = 1
    com["c_locmask"] = lm.reshape(128, 4096)
    pm = np.zeros((128, 128), BF)
    for mcol in range(128):
        base0 = 64 * (mcol // 64)
        ml = mcol % 64
        if ml < 64 // 2:
            pm[base0 + ml + 32, mcol] = -1
        else:
            pm[base0 + ml - 32, mcol] = 1
    com["c_perm"] = pm
    com["L0_ffn_qk_cols"] = np.stack(
        [g["L0_ffn_q_scale"][0], g["L0_ffn_k_scale"][0],
         g["L0_ffn_q_scale"][1], g["L0_ffn_k_scale"][1]],
        axis=1)[np.tile(np.arange(64), 2)].astype(f32)
    lngb = np.zeros((128, 32), f32)
    for l in range(2):
        for gg in range(8):
            lngb[:, l * 8 + gg] = g["L0_ffn_ln_g"][l][gg * 128: (gg + 1) * 128]
            lngb[:, 16 + l * 8 + gg] = g["L0_ffn_ln_b"][l][gg * 128: (gg + 1) * 128]
    com["L0_ffn_lngb_cols"] = lngb
    com["L0_ffn_b1_cols"] = np.ascontiguousarray(
        b1p.reshape(2, 44, 128).transpose(0, 2, 1))
    com["c_ident"] = np.eye(128, dtype=BF)

    x = g["L0_attn_x"]
    in_maps = []
    for c in range(NCORES):
        m = dict(com)
        m["L0_attn_x"] = np.ascontiguousarray(x[c])
        in_maps.append(m)
    return in_maps


def kernel(**inputs):
    in_maps = prep_inputs(inputs)
    nc = build_program(BLOCKS)
    res = run_bass_kernel_spmd(nc, in_maps, core_ids=list(range(NCORES)))
    out = np.stack([np.asarray(r["x_out"], dtype=np.float32)
                    for r in res.results], axis=0)
    return out


def emit_l0_ffn(P, stg, Wd):
    nc = P.nc
    v = P.v
    X = v["X"]
    P.gate_w("L0_ffn_w2p")
    FA, FB, FC, FD = v["FA"], v["FB"], v["FC"], v["FD"]
    XF = FA[:, 0:16392].rearrange("p (g c) -> p g c", g=DG)
    QK = FC[:, 0:16384].rearrange("p (g c) -> p g c", g=DG)
    VA = FD[:, 0:8320].rearrange("p (i h e) -> p i h e", i=TI, h=8)
    ROT = FD[:, 8320:16512]
    AVN = FB[:, 0:8192].rearrange("p (g c) -> p g c", g=4)
    PT = FB[:, 8192:12288].rearrange("p (j c) -> p j c", j=8)
    MSK = FB[:, 12288:16384].rearrange("p (j c) -> p j c", j=8)
    katk = [FA[:, 0:2048], FA[:, 2560:4608]]
    qaug = [FA[:, 2048:2560], FA[:, 4608:5120]]
    dmax = P.dram["mrow"]
    s_dma = P.sem("f0_dma")
    s_rows = P.sem("f0_rows")
    s_pe = P.sem("f0_pe")
    s_ln = P.sem("f0_ln")

    s_sp = P.sem("f0_spill")
    s_sp.inc(nc.sync.dma_start(
        P.dram["x0"].rearrange("(i p) d -> p i d", p=128), X), 16)
    s_dma.inc(nc.sync.dma_start(ROT[:, 0:2048], Wd["c_rot_cs"]), 16)
    s_dma.inc(nc.sync.dma_start(ROT[:, 2048:4096], Wd["c_rot_ss"]), 16)
    s_dma.inc(nc.sync.dma_start(ROT[:, 4096:6144], Wd["c_rot_cds"]), 16)
    s_dma.inc(nc.sync.dma_start(ROT[:, 6144:8192], Wd["c_rot_sds"]), 16)
    s_dma.inc(nc.sync.dma_start(v["ccol"][:, 52:56], Wd["L0_ffn_qk_cols"]), 16)
    s_dma.inc(nc.sync.dma_start(v["bcol"][:, 0:32], Wd["L0_ffn_lngb_cols"]), 16)
    nc.vector.memset(v["ones_c128"], 1.0)
    s_dma.wait(nc.vector)
    s_dma.wait(nc.tensor)
    s_dma.wait(nc.scalar)

    for lay in range(2):
        wq = Wd["L0_ffn_wqkv"].rearrange("l (g p) f -> l p g f", p=128)[lay]
        nc.sync.wait_ge(s_dma.h, s_dma.n)
        mv = s_dma.inc(nc.sync.dma_start(
            MSK.rearrange("p j c -> p (j c)"), Wd["c_locmask"]), 16)
        nc.vector.wait_ge(s_dma.h, mv)

        emit_build_xf(P, X, XF, 1, f"f{lay}")

        def sh(gg, c0, cn):
            off = 1 if gg < 4 else 0
            return XF[:, gg, off + c0: off + c0 + cn]

        def qk_src(m):
            return wq[:, :, m * 128:(m + 1) * 128]

        def qk_rhs(g, tb):
            return sh(g, tb * 512, 512)

        def qk_evac(m, tb, pt):
            return nc.vector.tensor_copy(QK[:, m, tb * 512:(tb + 1) * 512], pt[:])

        emit_proj_fm(P, stg, "fqk", qk_src, 8, qk_rhs, DG, qk_evac,
                     nc.vector)
        P.barrier()

        def v_src(k):
            return wq[:, k, 1024:1536]

        def v_lhs(k, i):
            return sh(k, i * 128, 128)

        def v_evac(i, vb, pt):
            dstv = VA[:, i, :, 0:64]
            return nc.vector.tensor_copy(
                dstv, pt.rearrange("p (h e) -> p h e", h=8))

        emit_proj_tm(P, stg, "fv", v_lhs, v_src, DG, v_evac, nc.vector,
                     n_ob=1)
        nc.vector.memset(VA[:, :, :, 64:65].rearrange("p i h e -> p (i h e)"), 1.0)
        P.barrier()

        s_n = P.sem("fn")
        s_ns = P.sem("fns")
        sqsc = [FB[:, 0:2048], FB[:, 2048:4096]]
        tscr = [FB[:, 4096:4608], FB[:, 4608:5120]]
        t2s = FB[:, 5120:5632]
        t3s = FB[:, 5632:6144]
        rstdbA = [v["STG"][0:1, 1024:1280].bitcast(BF16),
                  v["STG"][0:1, 1536:1792].bitcast(BF16)]
        rstdbB = [v["STG"][0:1, 1280:1536].bitcast(BF16),
                  v["STG"][0:1, 1792:2048].bitcast(BF16)]
        rAs = [v["WBf32"][0:1, 0:512], v["WBf32"][0:1, 512:1024]]
        rBs = [v["WBf32"][32:33, 0:512], v["WBf32"][32:33, 512:1024]]
        epsrow = v["STG"][0:1, 0:256].bitcast(BF16)
        sq_v = {}
        pn_v = {}
        sx_v = {}
        rc_v = {}
        tm_v = {}
        pm_v = {}
        bc_v = {}
        t3_v = {}
        fn_v = {}
        ev24 = s_n.inc(nc.vector.memset(epsrow, 1e-24), 1)
        nc.tensor.wait_ge(s_n.h, ev24)
        with (
            nc.psum_tensor(f"ps_fna0{lay}", [1, 512]) as pnA0,
            nc.psum_tensor(f"ps_fnb0{lay}", [1, 512]) as pnB0,
            nc.psum_tensor(f"ps_fna1{lay}", [1, 512]) as pnA1,
            nc.psum_tensor(f"ps_fnb1{lay}", [1, 512]) as pnB1,
            nc.psum_tensor(f"ps_fra{lay}", [128, 512]) as prnA,
            nc.psum_tensor(f"ps_frb{lay}", [128, 512]) as prnB,
            nc.psum_tensor(f"ps_fp0{lay}", [128, 512]) as prot0,
            nc.psum_tensor(f"ps_fp1{lay}", [128, 512]) as prot1,
        ):
            pnA = [pnA0, pnA1]
            pnB = [pnB0, pnB1]
            prot = [prot0, prot1]
            sq_v[0] = s_ns.inc(nc.scalar.activation(sqsc[0], QK[:, 0, :],
                                                    AF.Square), 1)
            for tile in range(8):
                isq = tile < 4
                scol = v["ccol"][:, 52 + 2 * lay + (0 if isq else 1):
                                 53 + 2 * lay + (0 if isq else 1)]
                for tb in range(4):
                    it = tile * 4 + tb
                    lo = tb * 512
                    qsl = QK[:, tile, lo:lo + 512]
                    ts = tscr[it % 2]
                    cosT = ROT[:, (0 if isq else 4096) + lo:
                               (0 if isq else 4096) + lo + 512]
                    sinT = ROT[:, (2048 if isq else 6144) + lo:
                               (2048 if isq else 6144) + lo + 512]
                    if tb == 1 and tile + 1 < 8:
                        if tile >= 1:
                            nc.scalar.wait_ge(s_pe.h, pn_v[(tile - 1) * 4 + 3])
                        sq_v[tile + 1] = s_ns.inc(nc.scalar.activation(
                            sqsc[(tile + 1) % 2], QK[:, tile + 1, :],
                            AF.Square), 1)
                    if tb == 0:
                        nc.tensor.wait_ge(s_ns.h, sq_v[tile])
                    if it >= 2:
                        nc.tensor.wait_ge(s_ns.h, sx_v[it - 2])
                    nc.tensor.matmul(pnA[it % 2][:], v["ones_c128"][0:64, :],
                                     sqsc[tile % 2][0:64, lo:lo + 512],
                                     start=True, stop=False)
                    nc.tensor.matmul(pnA[it % 2][:], v["ones_rowb"][:, 0:1],
                                     epsrow, start=False, stop=True)
                    nc.tensor.matmul(pnB[it % 2][:],
                                     v["ones_c128"][64:128, :],
                                     sqsc[tile % 2][64:128, lo:lo + 512],
                                     start=True, stop=False,
                                     tile_position=(64, 0))
                    ins = nc.tensor.matmul(pnB[it % 2][:],
                                           v["ones_rowb"][:, 0:1],
                                           epsrow, start=False, stop=True)
                    pn_v[it] = s_pe.inc(ins, 1)
                    nc.scalar.wait_ge(s_pe.h, pn_v[it])
                    if it >= 2:
                        nc.scalar.wait_ge(s_n.h, rc_v[it - 2])
                    s_ns.inc(nc.scalar.activation(rAs[it % 2],
                                                  pnA[it % 2][:], AF.Sqrt), 1)
                    sx_v[it] = s_ns.inc(nc.scalar.activation(
                        rBs[it % 2], pnB[it % 2][:], AF.Sqrt), 1)
                    if it >= 2:
                        nc.vector.wait_ge(s_pe.h, pm_v[it - 2])
                    tm_v[it] = s_n.inc(nc.vector.tensor_scalar_mul(
                        ts, qsl, scol), 1)
                    nc.vector.wait_ge(s_ns.h, sx_v[it])
                    with nc.allow_low_precision(reason="bf16 store of rstd"):
                        nc.vector.reciprocal(rstdbA[it % 2], rAs[it % 2])
                        rc_v[it] = s_n.inc(nc.vector.reciprocal(
                            rstdbB[it % 2], rBs[it % 2]), 1)
                    nc.tensor.wait_ge(s_n.h, tm_v[it])
                    if it >= 2:
                        nc.tensor.wait_ge(s_n.h, t3_v[it - 2])
                    pm_v[it] = s_pe.inc(nc.tensor.matmul(
                        prot[it % 2][:], v["perm"], ts,
                        start=True, stop=True), 1)
                    nc.tensor.wait_ge(s_n.h, rc_v[it])
                    if it >= 1:
                        nc.tensor.wait_ge(s_n.h, fn_v[it - 1])
                    nc.tensor.matmul(prnA[0:64, :], v["ones_rowb"][:, 0:64],
                                     rstdbA[it % 2], start=True, stop=True)
                    bc_v[it] = s_pe.inc(nc.tensor.matmul(
                        prnB[0:64, :], v["ones_rowb"][:, 0:64],
                        rstdbB[it % 2], start=True, stop=True), 1)
                    nc.vector.wait_ge(s_pe.h, pm_v[it])
                    nc.vector.tensor_mul(t2s, ts, cosT)
                    nc.vector.tensor_mul(t3s[0:64, :], prot[it % 2][0:64, :],
                                         sinT[0:64, :])
                    t3_v[it] = s_n.inc(nc.vector.tensor_mul(
                        t3s[64:128, :], prot[it % 2][64:128, :],
                        sinT[64:128, :]), 1)
                    vfence(P)
                    nc.vector.tensor_add(t2s[0:64, :], t2s[0:64, :],
                                         t3s[0:64, :])
                    nc.vector.tensor_add(t2s[64:128, :], t2s[64:128, :],
                                         t3s[64:128, :])
                    nc.vector.wait_ge(s_pe.h, bc_v[it])
                    vfence(P)
                    nc.vector.tensor_tensor(qsl[0:64, :], t2s[0:64, :],
                                            prnA[0:64, :], ALU.mult)
                    fn_v[it] = s_n.inc(nc.vector.tensor_tensor(
                        qsl[64:128, :], t2s[64:128, :], prnB[0:64, :],
                        ALU.mult), 1)
        P.barrier()

        s_red = P.sem("f0red")
        s_mmb = P.sem("f0mmb")
        s_dx = P.sem("f0dx")
        mrt = v["STG"][0:16, 0:128].bitcast(BF16)
        dmax_written = 0
        red_val = {}
        f = 0
        with (
            nc.psum_tensor(f"ps_f1_{lay}0", [128, 1024]) as q0,
            nc.psum_tensor(f"ps_f1_{lay}1", [128, 1024]) as q1,
            nc.psum_tensor(f"ps_f1_{lay}2", [128, 1024]) as q2,
            nc.psum_tensor(f"ps_f1t{lay}", [16, 256], BF16) as ptr,
        ):
            qs = [q0, q1, q2]
            for hp in range(4):
                for w in range(4):
                    klo = (w - 1) * 512 if w > 0 else 0
                    kw = 1024 if w > 0 else 512
                    for sub in range(4):
                        col = w * 4 + sub
                        for par in range(2):
                            pt = qs[f % 3]
                            if f >= 3:
                                nc.tensor.wait_ge(s_red.h, red_val[f - 3])
                            lh = QK[64 * par:64 * par + 64, hp,
                                    w * 512 + sub * 128: w * 512 + sub * 128 + 128]
                            ins = None
                            for tkb in range(kw // 512):
                                rh = QK[64 * par:64 * par + 64, 4 + hp,
                                        klo + tkb * 512: klo + tkb * 512 + 512]
                                ins = nc.tensor.matmul(
                                    pt[:, tkb * 512:(tkb + 1) * 512], lh, rh,
                                    start=True, stop=True,
                                    tile_position=(64 * par, 0))
                            mv = s_mmb.inc(ins, 1)
                            nc.vector.wait_ge(s_mmb.h, mv)
                            tgt = v["mcols"] if par == 0 else v["colsA"]
                            red_val[f] = s_red.inc(nc.vector.reduce_max(
                                tgt[:, col:col + 1], pt[:, 0:kw], axis=AX), 1)
                            f += 1
                vfence(P)
                mcb = v["STG"][:, 256:272].bitcast(BF16)
                cv = s_red.inc(nc.vector.tensor_copy(mcb, v["GF"][:, 2048:2080]), 1)
                nc.tensor.wait_ge(s_red.h, cv)
                t2 = None
                for q in range(2):
                    t2 = nc.tensor.transpose(ptr[:, q * 128:q * 128 + 128],
                                             mcb[:, q * 16:(q + 1) * 16],
                                             v["ident"])
                mv = s_mmb.inc(t2, 1)
                nc.vector.wait_ge(s_mmb.h, mv)
                if hp > 0:
                    nc.vector.wait_ge(s_dx.h, dmax_written)
                ev = s_red.inc(nc.vector.tensor_copy(mrt, ptr[:]), 1)
                nc.sync.wait_ge(s_red.h, ev)
                s_dx.inc(nc.sync.dma_start(
                    dmax[2 * hp, :].rearrange("(i p) -> i p", p=128),
                    mrt[:, 0:128]), 16)
                s_dx.inc(nc.sync.dma_start(
                    dmax[2 * hp + 1, :].rearrange("(i p) -> i p", p=128),
                    mrt[:, 128:256]), 16)
                dmax_written = s_dx.n
        P.barrier()

        s_row = P.sem("f0row")
        s_exp = P.sem("f0exp")
        s_avp = P.sem("f0avp")
        s_ave = P.sem("f0ave")
        s_mmc = P.sem("f0mmc")
        s_msk = P.sem("f0msk")
        s_nrm = P.sem("f0nrm")
        s_mr = [P.sem("f0mr0"), P.sem("f0mr1")]
        mrow = [v["STG"][0:1, 1024:1280].bitcast(BF16),
                v["STG"][0:1, 1280:1536].bitcast(BF16)]
        rs_row = v["STG"][0:1, 0:512]
        rs_rowb = v["STG"][0:1, 512:768].bitcast(BF16)
        av_sb = v["STG"][64:128, 0:256].bitcast(BF16)
        s_dx.wait(nc.sync)
        row0, n0 = s_row.n, s_nrm.n
        av_cnt = {}
        exp_cnt = {}
        msk_cnt = {}
        mmc_k = {}
        last_av = {}
        jr = 0
        with (
            nc.psum_tensor(f"ps_f2a{lay}", [128, 1024]) as p2a,
            nc.psum_tensor(f"ps_f2b{lay}", [128, 1024]) as p2b,
            nc.psum_tensor(f"ps_f2c{lay}", [128, 1024]) as p2c,
            nc.psum_tensor(f"ps_fav{lay}", [128, 512]) as pav,
            nc.psum_tensor(f"ps_frr{lay}", [128, 512]) as prr,
        ):
            p2 = [p2a, p2b, p2c]
            pend = None
            tt_prev = 0

            def f_av(hw2, grp, t0, w2, h2):
                nc.tensor.wait_ge(s_msk.h, msk_cnt[(hw2, grp)])
                if grp * 2 == t0 and hw2 >= 1:
                    nc.tensor.wait_ge(s_nrm.h, n0 + hw2)
                ins = None
                for k2 in range(2):
                    tkt = grp * 2 + k2
                    jt = (w2 - 1) * 4 + tkt
                    ins = nc.tensor.matmul(pav[0:65, :], VA[:, jt, h2, :],
                                           PT[:, tkt, :],
                                           start=(tkt == t0), stop=(tkt == 7))
                av_cnt[(hw2, grp)] = s_avp.inc(ins, 1)
                last_av[grp] = av_cnt[(hw2, grp)]

            def f_norm2(pn):
                nonlocal tt_prev
                g2, rb2, w2, nv, ttp = pn
                if ttp:
                    nc.tensor.wait_ge(s_ave.h, ttp)
                nc.tensor.wait_ge(s_nrm.h, nv)
                pins = nc.tensor.matmul(prr[0:64, :], v["ones_rowb"][:, 0:64],
                                        rs_rowb, start=True, stop=True)
                pv = s_mmc.inc(pins, 1)
                nc.vector.wait_ge(s_mmc.h, pv)
                dsta = AVN[rb2:rb2 + 64, g2, w2 * 512:(w2 + 1) * 512]
                tt_prev = s_ave.inc(nc.vector.tensor_tensor(
                    dsta, av_sb[:, 0:512], prr[0:64, :], ALU.mult), 1)

            rowv = [0, 0]
            rowv[0] = s_mr[0].inc(nc.sync.dma_start(
                mrow[0], dmax[0, 0:512][None, :]), 16)
            for h in range(8):
                g, rb = h // 2, (h % 2) * 64
                for w in range(4):
                    hw = h * 4 + w
                    sl = hw % 2
                    t0 = 4 if w == 0 else 0
                    grps = list(range(t0 // 2, 4))
                    if hw + 1 < 32:
                        nsl = (hw + 1) % 2
                        h2, w2 = (hw + 1) // 4, (hw + 1) % 4
                        if hw >= 1:
                            nc.sync.wait_ge(s_row.h, row0 + hw)
                        rowv[nsl] = s_mr[nsl].inc(nc.sync.dma_start(
                            mrow[nsl], dmax[h2, w2 * 512:(w2 + 1) * 512][None, :]),
                            16)
                    if hw >= 2:
                        nc.vector.wait_ge(s_mmc.h, mmc_k[hw - 2])
                    if hw < 2:
                        nc.vector.memset(katk[sl][64:65, 0:1024], 1.0)
                    klo = (w - 1) * 512 if w > 0 else 0
                    kw = 1024 if w > 0 else 512
                    koff = 0 if w > 0 else 512
                    nc.vector.tensor_copy(katk[sl][0:64, koff:koff + kw],
                                          QK[rb:rb + 64, 4 + g, klo:klo + kw])
                    nc.vector.tensor_copy(qaug[sl][0:64, 0:512],
                                          QK[rb:rb + 64, g,
                                             w * 512:(w + 1) * 512])
                    nc.vector.wait_ge(s_mr[sl].h, rowv[sl])
                    rv = s_row.inc(nc.vector.tensor_scalar_mul(
                        qaug[sl][64:65, 0:512], mrow[sl], -1.0), 1)
                    rhs = qaug[sl][0:65, 0:512]
                    for gi, grp in enumerate(grps):
                        pt = p2[jr % 3]
                        if gi == 0:
                            nc.tensor.wait_ge(s_row.h, rv)
                        if jr >= 3:
                            nc.tensor.wait_ge(s_exp.h, exp_cnt[jr - 3])
                        ins = None
                        for k2 in range(2):
                            tkt = grp * 2 + k2
                            ins = nc.tensor.matmul(
                                pt[:, k2 * 512:(k2 + 1) * 512],
                                katk[sl][0:65, tkt * 128:(tkt + 1) * 128],
                                rhs, start=True, stop=True)
                        mv = s_mmc.inc(ins, 1)
                        if gi == len(grps) - 1:
                            mmc_k[hw] = mv
                        nc.scalar.wait_ge(s_mmc.h, mv)
                        if grp in last_av:
                            nc.scalar.wait_ge(s_avp.h, last_av[grp])
                        dstp = PT[:, grp * 2:grp * 2 + 2, :] \
                            .rearrange("p j c -> p (j c)")
                        exp_cnt[jr] = s_exp.inc(
                            nc.scalar.activation(dstp, pt[:], AF.Exp,
                                                 scale=8.0), 1)
                        nc.vector.wait_ge(s_exp.h, exp_cnt[jr])
                        for k2 in range(2):
                            tkt = grp * 2 + k2
                            mm = nc.vector.tensor_mul(PT[:, tkt, :],
                                                      PT[:, tkt, :],
                                                      MSK[:, tkt, :])
                        msk_cnt[(hw, grp)] = s_msk.inc(mm, 1)
                        jr += 1
                        if gi >= 1:
                            f_av(hw, grps[gi - 1], t0, w, h)
                        if gi == 1 and pend is not None:
                            f_norm2(pend)
                            pend = None
                    f_av(hw, grps[-1], t0, w, h)
                    nc.vector.wait_ge(s_avp.h, av_cnt[(hw, grps[-1])])
                    nc.vector.reciprocal(rs_row, pav[64:65, :])
                    vfence(P)
                    nc.vector.tensor_copy(rs_rowb, rs_row)
                    nv = s_nrm.inc(nc.vector.tensor_copy(av_sb[:, 0:512],
                                                         pav[0:64, :]), 1)
                    pend = (g, rb, w, nv, tt_prev)
            f_norm2(pend)
        P.barrier()

        wo = Wd["L0_ffn_wo"].rearrange("l (k p) f -> l p k f", p=128)[lay]

        def o_src(k):
            return wo[:, k, :]

        def o_lhs(k, i):
            return AVN[:, k, i * 128:(i + 1) * 128]

        def o_evac(i, ob, pt):
            Xsl = X[:, i, ob * 512:(ob + 1) * 512]
            return nc.vector.tensor_tensor(Xsl, Xsl, pt, ALU.add)

        s_sp.wait(nc.vector)
        emit_proj_tm(P, stg, "fo", o_lhs, o_src, 4, o_evac, nc.vector,
                     n_ob=2)
        P.barrier()

        emit_build_xf(P, X, XF, 1, f"g{lay}")
        hn = FB[:, 0:4096].rearrange("p (g c) -> p g c", g=DG)
        H1 = FB[:, 4096:15360].rearrange("p (m c) -> p m c", m=22)
        scr1 = FB[:, 15360:15872]
        scr2 = FB[:, 15872:16384]
        w1 = Wd["L0_ffn_w1p"].rearrange("l (g p) f -> l p g f", p=128)[lay]
        w2 = Wd["L0_ffn_w2p"].rearrange("l (k p) f -> l p k f", p=128)[lay]
        load_row_bf16(P, Wd["L0_ffn_b2"][lay, 0:512], v["brow_bf"][:, 0:512], s_rows)
        load_row_bf16(P, Wd["L0_ffn_b2"][lay, 512:1024], v["brow_bf"][:, 512:1024],
                      s_rows)
        s_rows.wait(nc.tensor)
        bv = s_dma.inc(nc.sync.dma_start(v["ccol"][:, 8:52],
                                         Wd["L0_ffn_b1_cols"][lay]), 16)
        nc.vector.wait_ge(s_dma.h, bv)
        nc.scalar.wait_ge(s_dma.h, bv)
        s_f = P.sem("ff")
        s_fs = P.sem("ffs")
        for tb in range(4):
            with (
                nc.psum_tensor(f"ps_fmu{lay}{tb}", [1, 512]) as pmu,
                nc.psum_tensor(f"ps_fm2{lay}{tb}", [1, 512]) as pm2,
                nc.psum_tensor(f"ps_fb1{lay}{tb}", [128, 512]) as pb1,
                nc.psum_tensor(f"ps_fb2{lay}{tb}", [128, 512]) as pb2,
            ):
                insmu = None
                for g in range(DG):
                    insmu = nc.tensor.matmul(pmu[:], v["ones_c128"],
                                             sh(g, tb * 512, 512),
                                             start=(g == 0), stop=(g == DG - 1))
                m2pe = {}
                for g in range(DG):
                    sq = scr1 if g % 2 == 0 else scr2
                    if g >= 2:
                        nc.scalar.wait_ge(s_pe.h, m2pe[g - 2])
                    qv = s_fs.inc(nc.scalar.activation(sq, sh(g, tb * 512, 512),
                                                       AF.Square), 1)
                    nc.tensor.wait_ge(s_fs.h, qv)
                    insmu = nc.tensor.matmul(pm2[:], v["ones_c128"], sq,
                                             start=(g == 0), stop=(g == DG - 1))
                    m2pe[g] = s_pe.inc(insmu, 1)
                nc.vector.wait_ge(s_pe.h, m2pe[DG - 1])
                r0 = v["STG"][0:1, 0:512]
                r1 = v["STG"][0:1, 512:1024]
                nc.vector.tensor_scalar_mul(r0, pmu[:], -1.0 / 1024.0)
                vfence(P)
                nc.vector.tensor_mul(r1, r0, r0)
                vfence(P)
                nc.vector.scalar_tensor_tensor(r1, pm2[:], 1.0 / 1024.0, r1,
                                               ALU.mult, ALU.subtract)
                vfence(P)
                nc.vector.tensor_single_scalar(r1, r1, 1e-5, ALU.add)
                fv = s_f.inc(nc.vector.engine_nop(), 1)
                s_f.wait(nc.scalar, fv)
                fv = s_fs.inc(nc.scalar.activation(r1, r1, AF.Sqrt), 1)
                s_fs.wait(nc.vector, fv)
                nc.vector.reciprocal(r1, r1)
                vfence(P)
                nmb = v["STG"][0:1, 1024:1280].bitcast(BF16)
                rsb = v["STG"][0:1, 1280:1536].bitcast(BF16)
                nc.vector.tensor_copy(nmb, r0)
                vfence(P)
                fv = s_f.inc(nc.vector.tensor_copy(rsb, r1), 1)
                nc.tensor.wait_ge(s_f.h, fv)
                i1 = nc.tensor.matmul(pb1[:], v["ones_rowb"], nmb,
                                      start=True, stop=True)
                i2 = nc.tensor.matmul(pb2[:], v["ones_rowb"], rsb,
                                      start=True, stop=True)
                pe_bcast = s_pe.inc(i2, 1)
                nc.vector.wait_ge(s_pe.h, pe_bcast)
                for g in range(DG):
                    nc.vector.tensor_add(scr1, sh(g, tb * 512, 512), pb1[:])
                    vfence(P)
                    nc.vector.tensor_mul(scr2, scr1, pb2[:])
                    vfence(P)
                    s_f.inc(nc.vector.tensor_scalar(
                        hn[:, g, :], scr2,
                        v["bcol"][:, lay * 8 + g: lay * 8 + g + 1],
                        v["bcol"][:, 16 + lay * 8 + g: 17 + lay * 8 + g],
                        ALU.mult, ALU.add), 1)
                s_f.wait(nc.tensor)
                nc.sync.wait_ge(s_pe.h, pe_bcast)
            s_w1 = P.sem("fw1")
            w10 = s_w1.n
            with (
                nc.psum_tensor(f"ps_w1a{lay}{tb}", [128, 512]) as pw1a,
                nc.psum_tensor(f"ps_w1b{lay}{tb}", [128, 512]) as pw1b,
            ):
                pws = [pw1a, pw1b]
                for m in range(44):
                    dst = v["WB"][:, (m % 2) * 1024:(m % 2) * 1024 + 1024]
                    stg.stage(w1[:, :, m * 128:(m + 1) * 128], dst, 1024,
                              consumer_sem=s_w1 if m >= 2 else None,
                              consumer_val=w10 + m - 1)
                    stg.wait(nc.tensor)
                    pt = pws[m % 2]
                    if m >= 2:
                        nc.tensor.wait_ge(s_w1.h, s_w1.n - 1)
                    ins = None
                    for g in range(DG):
                        ins = nc.tensor.matmul(pt[:], dst[:, g * 128:(g + 1) * 128],
                                               hn[:, g, :], start=(g == 0),
                                               stop=(g == DG - 1))
                    mv = s_pe.inc(ins, 1)
                    if m % 2 == 0:
                        nc.vector.wait_ge(s_pe.h, mv)
                        nc.vector.wait_ge(s_w1.h, s_w1.n)
                        s_w1.inc(nc.vector.tensor_scalar_add(
                            scr1, pt[:], v["ccol"][:, 8 + m:9 + m]), 1)
                    else:
                        nc.scalar.wait_ge(s_pe.h, mv)
                        gv = s_fs.inc(nc.scalar.activation(
                            scr2, pt[:], AF.Gelu,
                            bias=v["ccol"][:, 8 + m:9 + m]), 1)
                        nc.vector.wait_ge(s_fs.h, gv)
                        nc.vector.wait_ge(s_w1.h, s_w1.n)
                        s_w1.inc(nc.vector.tensor_mul(H1[:, m // 2, :],
                                                      scr1, scr2), 1)
            P.barrier()

            def w2_src(k):
                return w2[:, k, :]

            def w2_lhs(k, i):
                return H1[:, k, (i % 4) * 128:(i % 4) * 128 + 128]

            def w2_evac(i, ob, pt):
                Xsl = X[:, i, ob * 512:(ob + 1) * 512]
                return nc.vector.tensor_tensor(Xsl, Xsl, pt, ALU.add)

            emit_proj_tm(P, stg, "fw2", w2_lhs, w2_src, 22, w2_evac,
                         nc.vector, n_ob=2,
                         bias_row=v["brow_bf"][:, 0:1024], tb_list=(tb,))
            P.barrier()

    s_sp.wait(nc.sync)
    x0d = P.dram["x0"].rearrange("(i p) d -> p i d", p=128)
    x0s = [v["STG"][:, 0:1024], v["STG"][:, 1024:2048]]
    s_x0 = [P.sem("f0x0a"), P.sem("f0x0b")]
    addv = {}
    for i in range(TI):
        sl = i % 2
        if i >= 2:
            nc.sync.wait_ge(s_ln.h, addv[i - 2])
        dv = s_x0[sl].inc(nc.sync.dma_start(x0s[sl], x0d[:, i, :]), 16)
        nc.vector.wait_ge(s_x0[sl].h, dv)
        addv[i] = s_ln.inc(nc.vector.tensor_add(X[:, i, :], X[:, i, :],
                                                x0s[sl]), 1)
    vfence(P)
    emit_ln_all(P, X, FD[:, 0:2048], s_ln)
    P.barrier()


def emit_l1_attn(P, stg, Wd):
    nc = P.nc
    v = P.v
    P.gate_w("L1_attn_W_out")
    X = v["X"]
    FA, FB, FC, FD = v["FA"], v["FB"], v["FC"], v["FD"]
    BT = FA[0:16, 0:2048]
    CT = FA[64:80, 0:2048]
    Bblk = FA[:, 2048:6144].rearrange("p (i q) -> p i q", i=TI)
    Cblk = FA[:, 6144:10240].rearrange("p (i q) -> p i q", i=TI)
    Hall = FA[:, 10240:14352]
    BTt = FA[:, 14352:14608].rearrange("p (i s) -> p i s", i=TI)
    Wqb = FA[:, 14608:15120]
    Evq = FA[:, 15120:15632]
    WTs = FA[:, 15632:15760]
    HTs = FA[:, 15760:16016]
    hspre = FA[:, 16016:16144]
    Eyq = FA[:, 16016:16392]
    xsT = FB[:, 0:16384].rearrange("p (g c) -> p g c", g=DG)
    dtT = FC[:, 0:16384].rearrange("p (g c) -> p g c", g=DG)
    Gbc = FD[:, 0:16384].rearrange("p (j c) -> p j c", j=8)
    Xr = v["X"].rearrange("p i d -> p (i d)")
    va = Xr[:, 0:2048]
    Dc = Xr[:, 2048:2304]
    Sall = Xr[:, 2560:6656]
    yac = Xr[:, 6656:8704]
    zsl = Xr[:, 8704:9728].bitcast(BF16)
    esc = Xr[:, 9728:11776]
    Eb = Xr[:, 11776:12800].bitcast(BF16)
    Pb = Xr[:, 12800:13824].bitcast(BF16)
    PGb = Xr[:, 13824:14848].bitcast(BF16)

    s_dma = P.sem("m_dma")
    s_rows = P.sem("m_rows")
    s_ln = P.sem("m_ln")
    s_sp = P.sem("m_spill")
    s_pe = P.sem("m_pe")

    s_dma.inc(nc.sync.dma_start(v["bcol"][:, 0:8], Wd["L1_attn_logA_cols"]), 16)
    s_dma.inc(nc.sync.dma_start(v["bcol"][:, 8:16], Wd["L1_attn_dtb_cols"]), 16)
    s_dma.inc(nc.sync.dma_start(v["bcol"][:, 16:24], Wd["L1_attn_D_cols"]), 16)
    s_dma.inc(nc.sync.dma_start(v["bcol"][:, 24:32], Wd["L1_attn_convb_cols"]), 16)
    s_dma.inc(nc.sync.dma_start(v["ccol"][:, 8:40], Wd["L1_attn_convw_cols"]), 16)
    Acol = v["ccol"][:, 40:48]
    s_dma.wait(nc.scalar)
    s_ln.inc(nc.scalar.activation(Acol, v["bcol"][:, 0:8], AF.Exp), 1)
    bmask = FA[:, 14608:14864]
    cmask = FA[:, 14864:15120]
    tid16 = FA[64:80, 15120:15248]

    xn_tok = FB[:, 0:16384].rearrange("p (i d) -> p i d", i=TI)
    gb_bc = FD[:, 0:2048]
    with nc.psum_tensor("ps_mgb", [128, 1024]) as pgb:
        for which, wname in ((0, "L1_attn_ln_g"), (1, "L1_attn_ln_b")):
            load_row_bf16(P, Wd[wname][0:512], v["brow_bf"][:, 0:512], s_rows)
            load_row_bf16(P, Wd[wname][512:1024], v["brow_bf"][:, 512:1024], s_rows)
            s_rows.wait(nc.tensor)
            if which == 1:
                s_ln.wait(nc.tensor)
            ins = None
            for q in range(2):
                ins = nc.tensor.matmul(pgb[:, q * 512:(q + 1) * 512],
                                       v["ones_rowb"][:, 0:128],
                                       v["brow_bf"][:, q * 512:(q + 1) * 512],
                                       start=True, stop=True)
            pv = s_pe.inc(ins, 1)
            nc.vector.wait_ge(s_pe.h, pv)
            s_ln.inc(nc.vector.tensor_copy(
                gb_bc[:, which * 1024:(which + 1) * 1024], pgb[:]), 1)
    lc = v["lncol"]
    t1f = v["STG"][:, 0:1024]
    gbcf = gb_bc[:, 0:1024]
    bbcf = gb_bc[:, 1024:2048]
    for i in range(TI):
        Xv = X[:, i, :]
        nc.vector.wait_ge(s_ln.h, s_ln.n)
        nc.vector.reduce_sum(lc[:, 0:1], Xv, axis=AX)
        vfence(P)
        nc.vector.tensor_scalar_mul(lc[:, 1:2], lc[:, 0:1], -1.0 / 1024.0)
        vv = s_ln.inc(nc.vector.engine_nop(), 1)
        s_ln.wait(nc.scalar, vv)
        vv = s_ln.inc(nc.scalar.activation(FD[:, 2560:3584], Xv, AF.Square,
                                           bias=lc[:, 1:2], scale=1.0,
                                           accum_out=lc[:, 2:3]), 1)
        s_ln.wait(nc.vector, vv)
        nc.vector.tensor_scalar(lc[:, 3:4], lc[:, 2:3], 1.0 / 1024.0, 1e-5,
                                ALU.mult, ALU.add)
        vv = s_ln.inc(nc.vector.engine_nop(), 1)
        s_ln.wait(nc.scalar, vv)
        vv = s_ln.inc(nc.scalar.activation(lc[:, 4:5], lc[:, 3:4], AF.Sqrt), 1)
        s_ln.wait(nc.vector, vv)
        nc.vector.reciprocal(lc[:, 5:6], lc[:, 4:5])
        vfence(P)
        nc.vector.tensor_scalar(t1f, Xv, lc[:, 1:2], lc[:, 5:6],
                                ALU.add, ALU.mult)
        vfence(P)
        nc.vector.tensor_mul(t1f, t1f, gbcf)
        vfence(P)
        s_ln.inc(nc.vector.tensor_tensor(xn_tok[:, i, :], t1f, bbcf, ALU.add), 1)
    P.barrier()

    if M_PHASES < 2:
        return
    s_sp.inc(nc.sync.dma_start(
        P.dram["x0"].rearrange("(i p) d -> p i d", p=128), X), 16)
    s_sp.wait(nc.sync)
    xnF = FA[:, 0:16384].rearrange("p (g c) -> p g c", g=DG)
    emit_build_xf(P, xn_tok, xnF, 0, "mx", src_bf16=True)

    win = Wd["L1_attn_W_in"].rearrange("(g p) f -> p g f", p=128)
    zscr = FD[:, 2048:2560]
    s_z = P.sem("m_z")

    def win_src(m):
        return win[:, :, m * 128:(m + 1) * 128]

    def win_rhs(g, tb):
        return xnF[:, g, tb * 512:(tb + 1) * 512]

    def win_evac(m, tb, pt):
        if m < 8:
            return nc.vector.tensor_copy(dtT[:, m, tb * 512:(tb + 1) * 512], pt[:])
        zv = s_z.inc(nc.vector.tensor_copy(zscr, pt[:]), 1)
        nc.scalar.wait_ge(s_z.h, zv)
        dv = s_z.inc(nc.scalar.dma_start(
            P.dram["z"][(m - 8) * 128:(m - 7) * 128, tb * 512:(tb + 1) * 512],
            zscr), 16)
        nc.vector.wait_ge(s_z.h, dv)
        return nc.vector.engine_nop()

    emit_proj_fm(P, stg, "mwin", win_src, 16, win_rhs, DG, win_evac, nc.vector)
    P.barrier()
    if M_PHASES < 3:
        return

    cw = v["ccol"][:, 8:40].rearrange("p (g k) -> p g k", g=DG)
    s_cv = P.sem("m_cv")
    for g in range(DG):
        xp = dtT[:, g, :]
        xs = xsT[:, g, :]
        acc = t1f.bitcast(BF16)
        nc.vector.wait_ge(s_cv.h, s_cv.n)
        nc.vector.tensor_scalar_mul(acc, xp, cw[:, g, 3:4])
        vfence(P)
        for k in range(3):
            sh = 3 - k
            nc.vector.scalar_tensor_tensor(acc[:, sh:2048], xp[:, 0:2048 - sh],
                                           cw[:, g, k:k + 1], acc[:, sh:2048],
                                           ALU.mult, ALU.add)
            vfence(P)
        cvv = s_cv.inc(nc.vector.engine_nop(), 1)
        s_cv.wait(nc.scalar, cvv)
        s_cv.inc(nc.scalar.activation(xs, acc, AF.Silu,
                                      bias=v["bcol"][:, 24 + g:25 + g]), 1)
        s_cv.wait(nc.vector)
    P.barrier()
    if M_PHASES < 4:
        return

    dtw = Wd["L1_attn_dt_w"].rearrange("(g p) f -> p g f", p=128)
    s_dt = P.sem("m_dt")
    spsc = esc[:, 0:512]

    def dt_src(m):
        return dtw[:, :, m * 128:(m + 1) * 128]

    def dt_rhs(g, tb):
        return xsT[:, g, tb * 512:(tb + 1) * 512]

    def dt_evac(m, tb, pt):
        sev = P.sem("pf_ev")
        nc.scalar.wait_ge(sev.h, sev.n)
        dv = s_dt.inc(nc.scalar.activation(spsc, pt[:], AF.Exp,
                                           bias=v["bcol"][:, 8 + m:9 + m]), 1)
        nc.scalar.wait_ge(s_dt.h, dv)
        return nc.scalar.activation(dtT[:, m, tb * 512:(tb + 1) * 512],
                                    spsc, AF.Ln, bias=1.0)

    emit_proj_fm(P, stg, "mdt", dt_src, 8, dt_rhs, DG, dt_evac, nc.scalar)
    P.barrier()
    if M_PHASES < 5:
        return

    s_bc = P.sem("m_bc")
    bcw = v["WB"][:, 0:128].rearrange("p (g s) -> p g s", g=DG)
    ccw = v["WB"][:, 128:256].rearrange("p (g s) -> p g s", g=DG)
    stg.stage(Wd["L1_attn_B_w"].rearrange("(g p) s -> p g s", p=128),
              bcw.rearrange("p g s -> p (g s)"), 128)
    stg.wait(nc.tensor)
    stg.stage(Wd["L1_attn_C_w"].rearrange("(g p) s -> p g s", p=128),
              ccw.rearrange("p g s -> p (g s)"), 128)
    stg.wait(nc.tensor)
    with (
        nc.psum_tensor("ps_mb", [16, 512]) as pb,
        nc.psum_tensor("ps_mc", [16, 512]) as pc,
    ):
        for tb in range(4):
            if tb > 0:
                s_bc.wait(nc.tensor)
            insb = insc = None
            for g in range(DG):
                insb = nc.tensor.matmul(pb[:], bcw[:, g, :],
                                        xsT[:, g, tb * 512:(tb + 1) * 512],
                                        start=(g == 0), stop=(g == DG - 1))
            for g in range(DG):
                insc = nc.tensor.matmul(pc[:], ccw[:, g, :],
                                        xsT[:, g, tb * 512:(tb + 1) * 512],
                                        start=(g == 0), stop=(g == DG - 1))
            pv = s_pe.inc(insc, 1)
            nc.vector.wait_ge(s_pe.h, pv)
            s_bc.inc(nc.vector.tensor_copy(BT[:, tb * 512:(tb + 1) * 512], pb[:]), 1)
            s_bc.inc(nc.vector.tensor_copy(CT[:, tb * 512:(tb + 1) * 512], pc[:]), 1)
    P.barrier()
    prodsc = FA[0:16, 2048:2560]
    ctsc = FA[0:16, 2560:3072]
    onesc16 = v["CS"][0:16, 300:301]
    nc.vector.memset(onesc16, 1.0)
    s_g = P.sem("m_g")
    with (
        nc.psum_tensor("ps_mgr", [1, 512]) as pgr,
        nc.psum_tensor("ps_mgb2", [128, 512]) as pgb2,
    ):
        for j in range(8):
            for tb in range(4):
                lo = tb * 512
                nc.vector.wait_ge(s_g.h, s_g.n)
                nc.vector.tensor_copy(ctsc, CT[:, lo:lo + 512])
                vfence(P)
                if j == 0:
                    gv = s_g.inc(nc.vector.tensor_mul(
                        prodsc, ctsc, BT[:, lo:lo + 512]), 1)
                elif lo == 0:
                    nc.vector.memset(prodsc[:, 0:j], 0.0)
                    gv = s_g.inc(nc.vector.tensor_mul(
                        prodsc[:, j:512], ctsc[:, j:512], BT[:, 0:512 - j]), 1)
                else:
                    gv = s_g.inc(nc.vector.tensor_mul(
                        prodsc, ctsc, BT[:, lo - j:lo + 512 - j]), 1)
                nc.tensor.wait_ge(s_g.h, gv)
                i1 = nc.tensor.matmul(pgr[:], onesc16, prodsc,
                                      start=True, stop=True)
                pv = s_pe.inc(i1, 1)
                nc.vector.wait_ge(s_pe.h, pv)
                gv = s_g.inc(nc.vector.tensor_copy(v["brow_bf"][:, 0:512], pgr[:]), 1)
                nc.tensor.wait_ge(s_g.h, gv)
                i2 = nc.tensor.matmul(pgb2[:], v["ones_rowb"],
                                      v["brow_bf"][:, 0:512], start=True, stop=True)
                pv = s_pe.inc(i2, 1)
                nc.vector.wait_ge(s_pe.h, pv)
                s_g.inc(nc.vector.tensor_copy(Gbc[:, j, lo:lo + 512], pgb2[:]), 1)
    P.barrier()
    s_bb = P.sem("m_bb")
    s_dma.inc(nc.sync.dma_start(bmask, Wd["c_bmask8"]), 16)
    s_dma.inc(nc.sync.dma_start(cmask, Wd["c_cmask"]), 16)
    s_dma.inc(nc.sync.dma_start(tid16, Wd["c_tid16"]), 16)
    s_dma.wait(nc.vector)
    s_dma.wait(nc.tensor)
    with (
        nc.psum_tensor("ps_mt", [128, 512], BF16) as pt4,
        nc.psum_tensor("ps_mtc", [128, 512]) as ptc,
    ):
        for q in range(4):
            if q > 0:
                nc.tensor.wait_ge(s_bb.h, s_bb.n)
            ins = None
            for k in range(4):
                i = q * 4 + k
                ins = nc.tensor.transpose(pt4[:, k * 16:(k + 1) * 16],
                                          BT[:, i * 128:(i + 1) * 128],
                                          v["ident"][0:16, 0:16])
            pv = s_pe.inc(ins, 1)
            nc.vector.wait_ge(s_pe.h, pv)
            s_bb.inc(nc.vector.tensor_copy(
                BTt[:, 4 * q:4 * q + 4, :].rearrange("p i s -> p (i s)"),
                pt4[:, 0:64]), 1)
        nc.vector.wait_ge(s_bb.h, s_bb.n)
        for i in range(TI):
            a1 = bmask.rearrange("p (c s) -> p c s", c=16)
            a2 = BTt[:, i, :].rearrange("p (o s) -> p o s", o=1)
            b1, b2 = bass.broadcast_tensor_aps(a1, a2)
            s_bb.inc(nc.vector.tensor_tensor(
                Bblk[:, i, :].rearrange("p (c s) -> p c s", c=16),
                b1, b2, ALU.mult), 1)
        for i in range(TI):
            for h2 in range(2):
                jj = i * 2 + h2
                pcc = ptc[:, (jj % 4) * 128:(jj % 4) * 128 + 128]
                if jj >= 4:
                    nc.tensor.wait_ge(s_bb.h, s_bb.n)
                i3 = nc.tensor.matmul(pcc, tid16,
                                      CT[:, i * 128:(i + 1) * 128],
                                      start=True, stop=True)
                pv = s_pe.inc(i3, 1)
                nc.vector.wait_ge(s_pe.h, pv)
                s_bb.inc(nc.vector.tensor_tensor(
                    Cblk[:, i, h2 * 128:(h2 + 1) * 128], pcc,
                    cmask[:, h2 * 128:(h2 + 1) * 128], ALU.mult), 1)
    P.barrier()
    if M_PHASES < 6:
        return

    s_sc = P.sem("m_sc")
    s_scs = P.sem("m_scs")
    s_zd = P.sem("m_zd")
    va3 = va.rearrange("p (c r) -> p c r", r=8)
    for g in M_GORDER:
        nc.scalar.wait_ge(s_scs.h, s_scs.n)
        zdv = s_zd.inc(nc.scalar.dma_start(
            zsl, P.dram["z"][g * 128:(g + 1) * 128, :]), 16)
        nc.vector.wait_ge(s_sc.h, s_sc.n)
        nc.vector.wait_ge(s_scs.h, s_scs.n)
        nc.vector.tensor_scalar(va, dtT[:, g, :], Acol[:, g:g + 1], -1.0,
                                ALU.mult, ALU.mult)
        vfence(P)
        nc.vector.tensor_mul(dtT[:, g, :], dtT[:, g, :], xsT[:, g, :])
        ug = dtT[:, g, :]
        ug3 = ug.rearrange("p (c r) -> p c r", r=8)
        vfence(P)
        for pstep in range(1, 8):
            nc.vector.tensor_add(va3[:, :, pstep:pstep + 1],
                                 va3[:, :, pstep:pstep + 1],
                                 va3[:, :, pstep - 1:pstep])
            vfence(P)
        scv = s_sc.inc(nc.vector.engine_nop(), 1)
        s_sc.wait(nc.scalar, scv)
        dcv = s_scs.inc(nc.scalar.activation(Dc, va3[:, :, 7:8], AF.Exp), 1)
        nc.vector.tensor_mul(yac, ug, Gbc[:, 0, :])
        vfence(P)
        esc3 = esc.rearrange("p (c r) -> p c r", r=8)
        Eb3 = Eb.rearrange("p (c r) -> p c r", r=8)
        Pb3 = Pb.rearrange("p (c r) -> p c r", r=8)
        PG3 = PGb.rearrange("p (c r) -> p c r", r=8)
        yac3 = yac.rearrange("p (c r) -> p c r", r=8)
        for j in range(1, 8):
            w = 8 - j
            G3 = Gbc[:, j, :].rearrange("p (c r) -> p c r", r=8)
            nc.vector.tensor_sub(esc3[:, :, j:8], va3[:, :, j:8], va3[:, :, 0:w])
            ev = s_sc.inc(nc.vector.engine_nop(), 1)
            s_sc.wait(nc.scalar, ev)
            ev = s_scs.inc(nc.scalar.activation(Eb3[:, :, j:8], esc3[:, :, j:8],
                                                AF.Exp), 1)
            s_scs.wait(nc.vector, ev)
            nc.vector.tensor_mul(Pb3[:, :, j:8], Eb3[:, :, j:8], ug3[:, :, 0:w])
            vfence(P)
            nc.vector.tensor_mul(PG3[:, :, j:8], Pb3[:, :, j:8], G3[:, :, j:8])
            vfence(P)
            nc.vector.tensor_add(yac3[:, :, j:8], yac3[:, :, j:8], PG3[:, :, j:8])
            vfence(P)
        with (
            nc.psum_tensor(f"ps_ms{g}", [128, 512]) as pS,
            nc.psum_tensor(f"ps_mw{g}", [128, 512], BF16) as pW,
            nc.psum_tensor(f"ps_mh{g}", [128, 256], BF16) as pH,
            nc.psum_tensor(f"ps_my{g}", [128, 512]) as pY,
        ):
            for q in range(4):
                lo = q * 512
                vq3 = va3[:, 64 * q:64 * (q + 1), :]
                b1, b2 = bass.broadcast_tensor_aps(vq3[:, :, 7:8], vq3)
                nc.vector.wait_ge(s_sc.h, s_sc.n)
                nc.vector.wait_ge(s_scs.h, s_scs.n)
                nc.vector.tensor_tensor(
                    esc3[:, 0:64, :], b1, b2, ALU.subtract)
                ev = s_sc.inc(nc.vector.engine_nop(), 1)
                s_sc.wait(nc.scalar, ev)
                ev = s_scs.inc(nc.scalar.activation(Wqb, esc[:, 0:512], AF.Exp), 1)
                s_scs.wait(nc.vector, ev)
                nc.vector.tensor_mul(Wqb, Wqb, ug[:, lo:lo + 512])
                wv = s_sc.inc(nc.vector.engine_nop(), 1)
                nc.tensor.wait_ge(s_sc.h, wv)
                for k4 in range(4):
                    i = q * 4 + k4
                    it = nc.tensor.transpose(pW[:, 0:128],
                                             Wqb[:, k4 * 128:(k4 + 1) * 128],
                                             v["ident"])
                    pv = s_pe.inc(it, 1)
                    nc.vector.wait_ge(s_pe.h, pv)
                    wtv = s_sc.inc(nc.vector.tensor_copy(WTs, pW[:, 0:128]), 1)
                    nc.tensor.wait_ge(s_sc.h, wtv)
                    iS = nc.tensor.matmul(pS[:, 0:256], WTs, Bblk[:, i, :],
                                          start=True, stop=True)
                    pv = s_pe.inc(iS, 1)
                    nc.vector.wait_ge(s_pe.h, pv)
                    s_sc.inc(nc.vector.tensor_copy(
                        Sall[:, i * 256:(i + 1) * 256], pS[:, 0:256]), 1)
            s_sc.wait(nc.vector)
            s_scs.wait(nc.vector)
            nc.vector.memset(
                Hall.rearrange("p (s c) -> p s c", s=16)[:, :, 0:1], 0.0)
            vfence(P)
            Hall3 = Hall.rearrange("p (s c) -> p s c", s=16)
            Sall3 = Sall.rearrange("p (a s) -> p a s", s=16)
            for s16 in range(16):
                nc.vector.tensor_tensor_scan(
                    Hall3[:, s16, 1:257], Dc, Sall3[:, :, s16],
                    0.0, ALU.mult, ALU.add)
            hv = s_sc.inc(nc.vector.engine_nop(), 1)
            nc.tensor.wait_ge(s_sc.h, hv)
            nc.vector.wait_ge(s_sc.h, hv)
            for q in range(4):
                lo = q * 512
                nc.scalar.wait_ge(s_sc.h, s_sc.n)
                eqv = s_scs.inc(nc.scalar.activation(Evq, va[:, lo:lo + 512],
                                                     AF.Exp), 1)
                for k4 in range(4):
                    i = q * 4 + k4
                    for h2 in range(2):
                        hsrc = Hall3[:, :, i * 16 + h2 * 8:
                                     i * 16 + h2 * 8 + 8] \
                            .rearrange("p s c -> p c s")
                        nc.vector.wait_ge(s_pe.h, s_pe.n)
                        hcv = s_sc.inc(nc.vector.tensor_copy(
                            hspre.rearrange("p (c s) -> p c s", c=8), hsrc), 1)
                        nc.tensor.wait_ge(s_sc.h, hcv)
                        it = nc.tensor.transpose(pH[:, h2 * 128:(h2 + 1) * 128],
                                                 hspre, v["ident"])
                        pv = s_pe.inc(it, 1)
                        nc.vector.wait_ge(s_pe.h, pv)
                        htv = s_sc.inc(nc.vector.tensor_copy(
                            HTs[:, h2 * 128:(h2 + 1) * 128],
                            pH[:, h2 * 128:(h2 + 1) * 128]), 1)
                    nc.tensor.wait_ge(s_sc.h, s_sc.n)
                    ins = None
                    for h2 in range(2):
                        ins = nc.tensor.matmul(
                            pY[:, k4 * 128:(k4 + 1) * 128],
                            HTs[:, h2 * 128:(h2 + 1) * 128],
                            Cblk[:, i, h2 * 128:(h2 + 1) * 128],
                            start=(h2 == 0), stop=(h2 == 1))
                    pv = s_pe.inc(ins, 1)
                nc.vector.wait_ge(s_pe.h, s_pe.n)
                nc.vector.wait_ge(s_scs.h, eqv)
                nc.vector.tensor_mul(Pb[:, 0:512], Evq, pY[:])
                vfence(P)
                cyv = s_sc.inc(nc.vector.tensor_tensor(
                    yac[:, lo:lo + 512], yac[:, lo:lo + 512],
                    Pb[:, 0:512], ALU.add), 1)
                nc.tensor.wait_ge(s_sc.h, cyv)
        nc.vector.wait_ge(s_zd.h, zdv)
        nc.vector.wait_ge(s_sc.h, s_sc.n)
        nc.vector.wait_ge(s_scs.h, s_scs.n)
        nc.vector.tensor_scalar_mul(Eb[:, 0:2048], xsT[:, g, :],
                                    v["bcol"][:, 16 + g:17 + g])
        vfence(P)
        nc.vector.tensor_add(esc, yac, Eb[:, 0:2048])
        zv = s_sc.inc(nc.vector.engine_nop(), 1)
        s_sc.wait(nc.scalar, zv)
        zv = s_scs.inc(nc.scalar.activation(Pb[:, 0:2048], zsl, AF.Silu), 1)
        s_scs.wait(nc.vector, zv)
        s_sc.inc(nc.vector.tensor_mul(xsT[:, g, :], esc, Pb[:, 0:2048]), 1)
    P.barrier()
    if M_PHASES < 7:
        return

    s_sp.inc(nc.sync.dma_start(
        X, P.dram["x0"].rearrange("(i p) d -> p i d", p=128)), 16)
    s_sp.wait(nc.vector)
    s_sp.wait(nc.tensor)
    wout = Wd["L1_attn_W_out"].rearrange("(k p) f -> p k f", p=128)

    def wo_src(k):
        return wout[:, k, :]

    def wo_lhs(k, i):
        return xsT[:, k, i * 128:(i + 1) * 128]

    def wo_evac(i, ob, pt):
        Xsl = X[:, i, ob * 512:(ob + 1) * 512]
        return nc.vector.scalar_tensor_tensor(Xsl, Xsl, 2.0, pt,
                                              ALU.mult, ALU.add)

    emit_proj_tm(P, stg, "mout", wo_lhs, wo_src, DG, wo_evac, nc.vector, n_ob=2)
    P.barrier()
    emit_ln_all(P, X, FD[:, 0:2048], s_ln)
    P.barrier()


def emit_l1_ffn(P, stg, Wd):
    nc = P.nc
    v = P.v
    P.gate_w("L1_ffn_lora_Bc")
    X = v["X"]
    XF = v["FA"][:, 0:16384].rearrange("p (g c) -> p g c", g=DG)
    INTER = v["FB"][:, 0:16384].rearrange("p (m c) -> p m c", m=32)
    OH = v["FC"][:, 6656:6784].rearrange("p (i e) -> p i e", i=TI)
    RM = v["FC"][:, 6784:7296].rearrange("p (i r) -> p i r", i=TI)
    rmT = v["FC"][0:32, 0:2048]
    loraB = v["FC"][0:32, 2048:6144]

    emit_build_xf(P, X, XF, 0, "m")

    s_rows = P.sem("f_rows")
    load_row_bf16(P, Wd["L1_ffn_gate_b"], v["brow_bf"][:, 0:8], s_rows)
    s_rows.wait(nc.tensor)

    s_g = P.sem("f_g")
    s_goh = P.sem("f_goh")
    ohf = v["ccol"][:, 8:16]
    cum = v["ccol"][:, 16:24]
    t1c = v["ccol"][:, 24:32]
    mcol = v["ccol"][:, 32:33]
    zer8 = v["ccol"][:, 40:48]
    nc.vector.memset(zer8, 0.0)
    gwt = Wd["L1_ffn_gate_Wt"].rearrange("(g p) e -> p g e", p=128)
    gdst = v["WB"][:, 0:1024]
    stg.stage(gwt, gdst[:, 0:64].rearrange("p (g e) -> p g e", g=DG), 64)
    stg.wait(nc.tensor)
    gd3 = gdst[:, 0:64].rearrange("p (g e) -> p g e", g=DG)
    with (
        nc.psum_tensor("ps_fg0", [128, 512]) as pg0,
        nc.psum_tensor("ps_fg1", [128, 512]) as pg1,
    ):
        pg = [pg0[:, 0:8], pg1[:, 0:8]]
        for i in range(TI):
            pt = pg[i % 2]
            if i >= 2:
                nc.tensor.wait_ge(s_goh.h, i - 1)
            ins = None
            for g in range(DG):
                ins = nc.tensor.matmul(pt, XF[:, g, i * 128:(i + 1) * 128],
                                       gd3[:, g, :], start=(g == 0), stop=False)
            ins = nc.tensor.matmul(pt, P.v["ones_rowb"],
                                   v["brow_bf"][:, 0:8], start=False, stop=True)
            mv = s_g.inc(ins, 1)
            nc.vector.wait_ge(s_g.h, mv)
            nc.vector.wait_ge(s_goh.h, s_goh.n)
            nc.vector.reduce_max(mcol, pt, axis=AX)
            vfence(P)
            nc.vector.tensor_scalar(ohf, pt, mcol, 1.0, ALU.is_ge, ALU.mult)
            vfence(P)
            nc.vector.tensor_tensor_scan(cum, ohf, zer8, 0.0, ALU.add, ALU.add)
            vfence(P)
            nc.vector.tensor_scalar(t1c, cum, 1.0, 1.0, ALU.is_equal, ALU.mult)
            vfence(P)
            s_goh.inc(nc.vector.tensor_tensor(OH[:, i, :], t1c, ohf, ALU.mult), 1)
    P.barrier()

    lat = Wd["L1_ffn_lora_At"].rearrange("(g p) r -> p g r", p=128)
    s_r = P.sem("f_r")
    s_rm = P.sem("f_rm")
    stg.stage(lat, gdst[:, 64:320].rearrange("p (g r) -> p g r", g=DG), 256)
    la3 = gdst[:, 64:320].rearrange("p (g r) -> p g r", g=DG)
    stg.wait(nc.tensor)
    with (
        nc.psum_tensor("ps_fr0", [128, 512]) as pr0,
        nc.psum_tensor("ps_fr1", [128, 512]) as pr1,
    ):
        pr = [pr0[:, 0:32], pr1[:, 0:32]]
        for i in range(TI):
            pt = pr[i % 2]
            if i >= 2:
                nc.tensor.wait_ge(s_rm.h, i - 1)
            ins = None
            for g in range(DG):
                ins = nc.tensor.matmul(pt, XF[:, g, i * 128:(i + 1) * 128],
                                       la3[:, g, :], start=(g == 0),
                                       stop=(g == DG - 1))
            mv = s_r.inc(ins, 1)
            nc.vector.wait_ge(s_r.h, mv)
            a1 = pt.rearrange("p (e r) -> p e r", e=8)
            a2 = OH[:, i, :].rearrange("p (e o) -> p e o", o=1)
            b1, b2 = bass.broadcast_tensor_aps(a1, a2)
            s_rm.inc(nc.vector.tensor_tensor(
                RM[:, i, :].rearrange("p (e r) -> p e r", e=8), b1, b2,
                ALU.mult), 1)
    s_t = P.sem("f_t")
    s_te = P.sem("f_te")
    with nc.psum_tensor("ps_ft", [32, 512], BF16) as ptr:
        s_rm.wait(nc.tensor)
        for q in range(4):
            if q > 0:
                nc.tensor.wait_ge(s_te.h, q)
            ins = None
            for k in range(4):
                i = q * 4 + k
                ins = nc.tensor.transpose(ptr[:, k * 128:(k + 1) * 128],
                                          RM[:, i, :], v["ident"])
            tv = s_t.inc(ins, 1)
            nc.vector.wait_ge(s_t.h, tv)
            s_te.inc(nc.vector.tensor_copy(
                rmT[:, q * 512:(q + 1) * 512], ptr[:]), 1)
    lb = Wd["L1_ffn_lora_Bc"]
    for q in range(4):
        stg.stage(lb[:, q * 1024:(q + 1) * 1024],
                  loraB[:, q * 1024:(q + 1) * 1024], 1024)
        stg.wait(nc.tensor)
    s_dma = P.sem("f_dma")
    bv = s_dma.inc(nc.sync.dma_start(v["bcol"][:, 0:32],
                                     Wd["L1_ffn_wib_cols"]), 16)
    nc.scalar.wait_ge(s_dma.h, bv)
    load_row_bf16(P, Wd["L1_ffn_wo_b"][0:512], v["brow_bf"][:, 0:512], s_rows)
    load_row_bf16(P, Wd["L1_ffn_wo_b"][512:1024], v["brow_bf"][:, 512:1024], s_rows)
    s_rows.wait(nc.tensor)
    P.barrier()

    wi = Wd["L1_ffn_wi_Wt"].rearrange("(g p) f -> p g f", p=128)
    wo = Wd["L1_ffn_wo_Wt"].rearrange("(k p) f -> p k f", p=128)
    s_ln = P.sem("f_ln")
    SQ = v["FD"][:, 0:1024]
    for tb in range(4):
        def wi_src(m):
            return wi[:, :, m * 128:(m + 1) * 128]

        def wi_rhs(g, _tb):
            return XF[:, g, tb * 512:(tb + 1) * 512]

        def wi_extra(m, _tb, pt):
            return nc.tensor.matmul(pt[:], loraB[:, m * 128:(m + 1) * 128],
                                    rmT[:, tb * 512:(tb + 1) * 512],
                                    start=False, stop=True)

        def wi_evac(m, _tb, pt):
            return nc.scalar.activation(INTER[:, m, :], pt[:], AF.Relu,
                                        bias=v["bcol"][:, m:m + 1], scale=1.0)

        emit_proj_fm(P, stg, "fwi", wi_src, 32, wi_rhs, DG, wi_evac,
                     nc.scalar, n_tb=1, extra_mm=wi_extra)
        P.barrier()

        def wo_src(k):
            return wo[:, k, :]

        def wo_lhs(k, i):
            return INTER[:, k, (i % 4) * 128:(i % 4) * 128 + 128]

        def wo_evac(i, ob, pt):
            Xsl = X[:, i, ob * 512:(ob + 1) * 512]
            return nc.vector.tensor_tensor(Xsl, Xsl, pt, ALU.add)

        emit_proj_tm(P, stg, "fwo", wo_lhs, wo_src, 32, wo_evac, nc.vector,
                     n_ob=2, bias_row=v["brow_bf"][:, 0:1024], tb_list=(tb,))
        P.barrier()
    emit_ln_all(P, X, v["FD"][:, 0:2048], s_ln)
    P.barrier()


# version 13
# speedup vs baseline: 1.0891x; 1.0099x over previous
import numpy as np
import ml_dtypes
from contextlib import ExitStack
import concourse.bass as bass
import concourse.mybir as mybir
from concourse.bass_utils import run_bass_kernel_spmd

F32 = mybir.dt.float32
BF16 = mybir.dt.bfloat16
AF = mybir.ActivationFunctionType
ALU = mybir.AluOpType
AX = mybir.AxisListType.X
BF = ml_dtypes.bfloat16

T, TI, D, DG, NCORES = 2048, 16, 1024, 8, 8

BLOCKS = ["l0_attn", "l0_ffn", "l1_attn", "l1_ffn"]
A_PHASES = 9
M_PHASES = 9
M_GORDER = list(range(DG))



class Sem:
    def __init__(self, handle):
        self.h = handle
        self.n = 0

    def inc(self, instr, k):
        instr.then_inc(self.h, k)
        self.n += k
        return self.n

    def wait(self, eng, val=None):
        eng.wait_ge(self.h, self.n if val is None else val)


class Prog:
    def __init__(self, nc, stack):
        self.nc = nc
        self.stack = stack
        self.sems = {}
        self.v = {}
        self.dram = {}

    def uniq(self):
        self._u = getattr(self, "_u", 0) + 1
        return self._u

    def uniq2(self):
        return self.uniq()

    def sem(self, name):
        if name not in self.sems:
            h = self.stack.enter_context(self.nc.semaphore(f"s_{name}"))
            self.sems[name] = Sem(h)
        return self.sems[name]

    def gate_w(self, name, eng=None):
        if hasattr(self, "ccname") and name in self.ccname:
            for sem, val in self.ccname[name]:
                (eng or self.nc.sync).wait_ge(sem.h, val)

    def barrier(self):
        nc = self.nc
        s = self.sem("barrier")
        c = self.v["ccol"]
        s.inc(nc.vector.memset(c[:, 0:1], 0.0), 1)
        s.inc(nc.scalar.memzero(c[:, 1:2]), 1)
        s.inc(nc.tensor.nop(), 1)
        s.inc(nc.sync.nop(), 1)
        for e in (nc.tensor, nc.vector, nc.scalar, nc.sync):
            s.wait(e)


ARENA_COLS = 53188


def carve(P, mega):
    v = P.v
    m = mega[:]
    v["X"] = m[:, 0:16384].rearrange("p (i d) -> p i d", i=TI)
    v["FA"] = m[:, 16384:24580].bitcast(BF16)
    v["FB"] = m[:, 24580:32772].bitcast(BF16)
    v["FC"] = m[:, 32772:40964].bitcast(BF16)
    v["FD"] = m[:, 40964:49284].bitcast(BF16)
    v["GF"] = m[:, 49284:51972]
    v["WB"] = m[:, 51972:52996].bitcast(BF16)
    v["WBf32"] = m[:, 51972:52996]
    v["CS"] = m[:, 52996:53188].bitcast(BF16)

    v["FAf32"] = m[:, 16384:24580]
    v["FDf32"] = m[:, 40964:49284]
    v["STG"] = v["GF"][:, 0:2048]
    v["mcols"] = v["GF"][:, 2048:2064]
    v["colsA"] = v["GF"][:, 2064:2080]
    v["lncol"] = v["GF"][:, 2080:2088]
    v["bcol"] = v["GF"][:, 2088:2120]
    v["ccol"] = v["GF"][:, 2120:2176]
    brow = v["GF"][:, 2176:2688]
    v["brow_bf"] = brow[0:1, :].bitcast(BF16)
    v["frow"] = brow[32:33, :]
    v["ident"] = v["CS"][:, 0:128]
    v["perm"] = v["CS"][:, 128:256]
    v["ones_rowb"] = v["CS"][0:1, 256:384]
    v["ones_c128"] = v["CS"][:, 383:384]




def vfence(P):
    s = P.sem("vf")
    v = s.inc(P.nc.vector.engine_nop(), 1)
    P.nc.vector.wait_ge(s.h, v)


class WStager:

    def __init__(self, P):
        self.P = P
        self.ring = P.v["STG"]
        self.s_ld = [P.sem("wst_ld0"), P.sem("wst_ld1")]
        self.s_cast = P.sem("wst_cast")
        self.slot = 0

    def stage(self, src_ap, dst_bf16, cols, consumer_sem=None, consumer_val=0):
        P = self.P
        assert cols <= 1024
        sl = self.slot
        self.slot ^= 1
        if src_ap.dtype == BF16:
            dst = dst_bf16
            if len(src_ap.shape) == 3 and len(dst.shape) == 2:
                dst = dst.rearrange("p (a b) -> p a b", a=src_ap.shape[1])
            P.nc.sync.wait_ge(self.s_ld[sl].h, self.s_ld[sl].n)
            if consumer_sem is not None and consumer_val > 0:
                P.nc.sync.wait_ge(consumer_sem.h, consumer_val)
            dv = self.s_ld[sl].inc(P.nc.sync.dma_start(dst, src_ap), 16)
            self.last = (self.s_ld[sl], dv)
            return None
        rows = src_ap.shape[0]
        st = self.ring[0:rows, sl * 1024: sl * 1024 + cols]
        std = st
        if len(src_ap.shape) == 3:
            std = st.rearrange("p (a b) -> p a b", a=src_ap.shape[1])
        if self.s_cast.n >= 2:
            P.nc.sync.wait_ge(self.s_cast.h, self.s_cast.n - 1)
        ldv = self.s_ld[sl].inc(P.nc.sync.dma_start(std, src_ap), 16)
        P.nc.vector.wait_ge(self.s_ld[sl].h, ldv)
        if consumer_sem is not None and consumer_val > 0:
            P.nc.vector.wait_ge(consumer_sem.h, consumer_val)
        cv = self.s_cast.inc(P.nc.vector.tensor_copy(dst_bf16, st), 1)
        self.last = (self.s_cast, cv)
        return cv

    def wait(self, eng):
        s, val = self.last
        eng.wait_ge(s.h, val)


def load_row_bf16(P, src_dram_1d, dst_bf_row, s):
    nc = P.nc
    n = src_dram_1d.shape[0]
    sd = P.sem("rowdma")
    nc.sync.wait_ge(s.h, s.n)
    dv = sd.inc(nc.sync.dma_start(P.v["frow"][:, 0:n], src_dram_1d[None, :]), 16)
    nc.vector.wait_ge(sd.h, dv)
    return s.inc(nc.vector.tensor_copy(dst_bf_row[:, 0:n], P.v["frow"][:, 0:n]), 1)




def emit_ln(P, Xv, sq_out, cols, s):
    nc = P.nc
    lc = P.v["lncol"]
    inv = 1.0 / float(cols)
    nc.vector.wait_ge(s.h, s.n)
    nc.vector.reduce_sum(lc[:, 0:1], Xv, axis=AX)
    vfence(P)
    nc.vector.tensor_scalar_mul(lc[:, 1:2], lc[:, 0:1], -inv)
    v = s.inc(nc.vector.engine_nop(), 1)
    s.wait(nc.scalar, v)
    v = s.inc(nc.scalar.activation(sq_out, Xv, AF.Square, bias=lc[:, 1:2],
                                   scale=1.0, accum_out=lc[:, 2:3]), 1)
    s.wait(nc.vector, v)
    nc.vector.tensor_scalar(lc[:, 3:4], lc[:, 2:3], inv, 1e-5, ALU.mult, ALU.add)
    v = s.inc(nc.vector.engine_nop(), 1)
    s.wait(nc.scalar, v)
    v = s.inc(nc.scalar.activation(lc[:, 4:5], lc[:, 3:4], AF.Sqrt), 1)
    s.wait(nc.vector, v)
    nc.vector.reciprocal(lc[:, 5:6], lc[:, 4:5])
    vfence(P)
    s.inc(nc.vector.tensor_scalar(Xv, Xv, lc[:, 1:2], lc[:, 5:6],
                                  ALU.add, ALU.mult), 1)


def emit_ln_all(P, X3, sqs, s):
    nc = P.nc
    inv = 1.0 / 1024.0
    lcg = P.v["STG"][:, 0:96]
    A, B = lcg[:, 0:16], lcg[:, 16:32]
    C, Dg = lcg[:, 32:48], lcg[:, 48:64]
    E, F = lcg[:, 64:80], lcg[:, 80:96]
    for i in range(TI):
        nc.vector.reduce_sum(A[:, i:i + 1], X3[:, i, :], axis=AX)
    vfence(P)
    vB = s.inc(nc.vector.tensor_scalar_mul(B, A, -inv), 1)
    s.wait(nc.scalar, vB)
    sq_v = 0
    sqv = {}
    for i in range(TI):
        sq = sqs[:, (i % 2) * 1024:(i % 2) * 1024 + 1024]
        if i >= 2:
            nc.scalar.wait_ge(s.h, sqv[i - 2])
        sq_v = s.inc(nc.scalar.activation(sq, X3[:, i, :], AF.Square,
                                          bias=B[:, i:i + 1], scale=1.0,
                                          accum_out=C[:, i:i + 1]), 1)
        sqv[i] = sq_v
    s.wait(nc.vector, sq_v)
    vD = s.inc(nc.vector.tensor_scalar(Dg, C, inv, 1e-5,
                                       ALU.mult, ALU.add), 1)
    s.wait(nc.scalar, vD)
    vE = s.inc(nc.scalar.activation(E, Dg, AF.Sqrt), 1)
    s.wait(nc.vector, vE)
    nc.vector.reciprocal(F, E)
    vfence(P)
    for i in range(TI):
        s.inc(nc.vector.tensor_scalar(X3[:, i, :], X3[:, i, :],
                                      B[:, i:i + 1], F[:, i:i + 1],
                                      ALU.add, ALU.mult), 1)




def emit_build_xf(P, Xv, XFv, off, tag, src_bf16=False):
    nc = P.nc
    s_c = P.sem("bxf_c")
    s_t = P.sem("bxf_t")
    s_e = P.sem("bxf_e")
    t0, e0 = s_t.n, s_e.n
    wb = P.v["WB"]
    if off:
        nc.vector.memset(XFv[:, :, 0:1].rearrange("p g c -> p (g c)"), 0.0)
    with (
        nc.psum_tensor(f"ps_bx0_{tag}", [128, 512], BF16) as p0,
        nc.psum_tensor(f"ps_bx1_{tag}", [128, 512], BF16) as p1,
    ):
        ps = [p0, p1]
        for i in range(TI):
            if src_bf16:
                buf = Xv[:, i, :]
            else:
                buf = wb[:, (i % 2) * 1024:(i % 2) * 1024 + 1024]
                if i >= 2:
                    nc.vector.wait_ge(s_t.h, t0 + 2 * (i - 1))
                cv = s_c.inc(nc.vector.tensor_copy(buf, Xv[:, i, :]), 1)
                nc.tensor.wait_ge(s_c.h, cv)
            for half in range(2):
                pset = i * 2 + half
                pt = ps[pset % 2]
                if pset >= 2:
                    nc.tensor.wait_ge(s_e.h, e0 + pset - 1)
                ins = None
                for k in range(4):
                    g = half * 4 + k
                    ins = nc.tensor.transpose(pt[:, k * 128:(k + 1) * 128],
                                              buf[:, g * 128:g * 128 + 128],
                                              P.v["ident"])
                tv = s_t.inc(ins, 1)
                nc.vector.wait_ge(s_t.h, tv)
                dst = XFv[:, 4 * half:4 * half + 4,
                          off + i * 128: off + i * 128 + 128]
                s_e.inc(nc.vector.tensor_copy(
                    dst, pt[:].rearrange("p (g c) -> p g c", g=4)), 1)
    P.barrier()




def emit_proj_fm(P, stg, tag, w_src_fn, n_m, rhs_fn, KG, evac, eng, n_tb=4,
                 extra_mm=None):
    nc = P.nc
    s_mm = P.sem("pf_mm")
    s_ev = P.sem("pf_ev")
    ev0 = s_ev.n
    wb = P.v["WB"]
    with (
        nc.psum_tensor(f"ps_{tag}{P.uniq()}0", [128, 512]) as p0,
        nc.psum_tensor(f"ps_{tag}{P.uniq2()}1", [128, 512]) as p1,
        nc.psum_tensor(f"ps_{tag}{P.uniq2()}2", [128, 512]) as p2,
        nc.psum_tensor(f"ps_{tag}{P.uniq2()}3", [128, 512]) as p3,
    ):
        ps = [p0, p1, p2, p3]
        for m in range(n_m):
            dst = wb[:, (m % 2) * 1024:(m % 2) * 1024 + KG * 128]
            cval = ev0 + (m - 1) * n_tb
            stg.stage(w_src_fn(m), dst, KG * 128,
                      consumer_sem=s_ev if m >= 2 else None,
                      consumer_val=cval)
            stg.wait(nc.tensor)
            for tb in range(n_tb):
                j = m * n_tb + tb
                pt = ps[j % 4]
                if j >= 4:
                    nc.tensor.wait_ge(s_ev.h, ev0 + j - 3)
                ins = None
                for g in range(KG):
                    ins = nc.tensor.matmul(pt[:], dst[:, g * 128:(g + 1) * 128],
                                           rhs_fn(g, tb), start=(g == 0),
                                           stop=(extra_mm is None and g == KG - 1))
                if extra_mm is not None:
                    ins = extra_mm(m, tb, pt)
                mv = s_mm.inc(ins, 1)
                eng.wait_ge(s_mm.h, mv)
                s_ev.inc(evac(m, tb, pt), 1)




def emit_proj_tm(P, stg, tag, lhsT_fn, w_src_fn, n_k, evac, eng, n_ob=2,
                 bias_row=None, tb_list=(0, 1, 2, 3)):
    nc = P.nc
    s_mm = P.sem("pt_mm")
    s_ev = P.sem("pt_ev")
    s_kd = P.sem("pt_kd")
    ev0, kd0 = s_ev.n, s_kd.n
    wb = P.v["WB"]
    with nc.psum_tensor(f"ps_{tag}{P.uniq()}", [128, 4096]) as pall:
        for tbi, tb in enumerate(tb_list):
            for k in range(n_k):
                dst = wb[:, (k % 2) * 1024:(k % 2) * 1024 + n_ob * 512]
                rel = tbi * n_k + k - 1
                stg.stage(w_src_fn(k), dst, n_ob * 512,
                          consumer_sem=s_kd if rel > 0 else None,
                          consumer_val=kd0 + rel)
                stg.wait(nc.tensor)
                if k == 0 and tbi > 0:
                    nc.tensor.wait_ge(s_ev.h, ev0 + tbi * 4 * n_ob)
                ins = None
                for ii in range(4):
                    lh = lhsT_fn(k, tb * 4 + ii)
                    for ob in range(n_ob):
                        pt = pall[:, (ii * n_ob + ob) * 512:
                                  (ii * n_ob + ob) * 512 + 512]
                        ins = nc.tensor.matmul(
                            pt, lh, dst[:, ob * 512:(ob + 1) * 512],
                            start=(k == 0),
                            stop=(bias_row is None and k == n_k - 1))
                s_kd.inc(ins, 1)
            if bias_row is not None:
                ins = None
                for ii in range(4):
                    for ob in range(n_ob):
                        pt = pall[:, (ii * n_ob + ob) * 512:
                                  (ii * n_ob + ob) * 512 + 512]
                        ins = nc.tensor.matmul(
                            pt, P.v["ones_rowb"],
                            bias_row[:, ob * 512:(ob + 1) * 512],
                            start=False, stop=True)
                mv = s_mm.inc(ins, 1)
                eng.wait_ge(s_mm.h, mv)
            else:
                eng.wait_ge(s_kd.h, s_kd.n)
            for ii in range(4):
                for ob in range(n_ob):
                    pt = pall[:, (ii * n_ob + ob) * 512:(ii * n_ob + ob) * 512 + 512]
                    s_ev.inc(evac(tb * 4 + ii, ob, pt), 1)




def emit_l0_attn(P, stg, Wd):
    nc = P.nc
    v = P.v
    X = v["X"]
    XF = v["FA"][:, 0:16384].rearrange("p (g c) -> p g c", g=DG)
    QT = v["FB"][:, 0:16384].rearrange("p (g c) -> p g c", g=DG)
    KT = v["FC"][:, 0:16384].rearrange("p (g c) -> p g c", g=DG)
    VA = v["FD"][:, 0:16640].rearrange("p (i h e) -> p i h e", i=TI, h=16)
    AVN = QT

    emit_build_xf(P, X, XF, 0, "a")
    if A_PHASES < 1:
        return

    s_dma = P.sem("a_dma")
    s_rows = P.sem("a_rows")
    bq = s_dma.inc(nc.sync.dma_start(v["bcol"][:, 0:16],
                                     Wd["L0_attn_bq_cols"]), 16)
    load_row_bf16(P, Wd["L0_attn_b_qkv"][2048:2560], v["brow_bf"][:, 0:512], s_rows)
    load_row_bf16(P, Wd["L0_attn_b_qkv"][2560:3072], v["brow_bf"][:, 512:1024], s_rows)
    s_rows.wait(nc.tensor)

    wq = Wd["L0_attn_W_qkv"].rearrange("(g p) f -> p g f", p=128)

    def qk_src(m):
        return wq[:, :, m * 128:(m + 1) * 128]

    def qk_rhs(g, tb):
        return XF[:, g, tb * 512:(tb + 1) * 512]

    def qk_evac(m, tb, pt):
        tgt, g = (QT, m) if m < 8 else (KT, m - 8)
        return nc.vector.tensor_scalar_add(tgt[:, g, tb * 512:(tb + 1) * 512],
                                           pt[:], v["bcol"][:, m:m + 1])

    P.gate_w("L0_attn_W_qkv")
    nc.vector.wait_ge(s_dma.h, bq)
    emit_proj_fm(P, stg, "aqk", qk_src, 16, qk_rhs, DG, qk_evac, nc.vector)
    P.barrier()

    def v_src(k):
        return wq[:, k, 2048:3072]

    def v_lhs(k, i):
        return XF[:, k, i * 128:(i + 1) * 128]

    def v_evac(i, vb, pt):
        dstv = VA[:, i, 8 * vb:8 * vb + 8, 0:64]
        return nc.vector.tensor_copy(dstv, pt.rearrange("p (h e) -> p h e", h=8))

    emit_proj_tm(P, stg, "av", v_lhs, v_src, DG, v_evac, nc.vector,
                 n_ob=2, bias_row=v["brow_bf"][:, 0:1024], tb_list=(2, 3))
    if hasattr(P, "convdone"):
        cs, cn_, cos, covs = P.convdone
        nc.vector.wait_ge(cs.h, cn_)
    sev = P.sem("pt_ev")
    nc.tensor.wait_ge(sev.h, sev.n)
    nc.sync.wait_ge(sev.h, sev.n)
    emit_proj_tm(P, stg, "av", v_lhs, v_src, DG, v_evac, nc.vector,
                 n_ob=2, bias_row=v["brow_bf"][:, 0:1024], tb_list=(0, 1))
    nc.vector.memset(VA[:, :, :, 64:65].rearrange("p i h e -> p (i h e)"), 1.0)
    P.barrier()
    if A_PHASES < 2:
        return

    dmax = P.dram["mrow"]
    s_red = P.sem("a_red")
    s_mmb = P.sem("a_mmb")
    mrt = v["STG"][0:16, 0:128].bitcast(BF16)
    hmax = v["STG"][:, 288:352]
    if hasattr(P, "convdone"):
        _, _, cos, covs = P.convdone
        nc.vector.wait_ge(cos[0].h, covs[0])
        nc.vector.wait_ge(cos[1].h, covs[1])
    dmax_written = 0
    red0 = s_red.n
    with (
        nc.psum_tensor("ps_s1_0", [128, 1024]) as q0,
        nc.psum_tensor("ps_s1_1", [128, 1024]) as q1,
        nc.psum_tensor("ps_s1_2", [128, 1024]) as q2,
        nc.psum_tensor("ps_s1t", [16, 256], BF16) as ptr,
    ):
        qs = [q0, q1, q2]
        f = 0
        red_val = {}
        for hp in range(8):
            for par in range(2):
                for i in range(TI):
                    lh = QT[64 * par:64 * par + 64, hp, i * 128:(i + 1) * 128]
                    for half in range(2):
                        pt = qs[f % 3]
                        if f >= 3:
                            nc.tensor.wait_ge(s_red.h, red_val[f - 3])
                        ins = None
                        for kk in range(2):
                            tkb = half * 2 + kk
                            rh = KT[64 * par:64 * par + 64, hp,
                                    tkb * 512:(tkb + 1) * 512]
                            ins = nc.tensor.matmul(pt[:, kk * 512:(kk + 1) * 512],
                                                   lh, rh, start=True, stop=True,
                                                   tile_position=(64 * par, 0))
                        mv = s_mmb.inc(ins, 1)
                        nc.vector.wait_ge(s_mmb.h, mv)
                        red_val[f] = s_red.inc(nc.vector.reduce_max(
                            hmax[:, par * 32 + i * 2 + half:
                                 par * 32 + i * 2 + half + 1], pt[:], axis=AX), 1)
                        f += 1
            vfence(P)
            hm = hmax.rearrange("p (i two) -> p i two", two=2)
            cm = nc.vector.tensor_tensor(
                v["GF"][:, 2048:2080].rearrange("p (i o) -> p i o", o=1),
                hm[:, :, 0:1], hm[:, :, 1:2], ALU.max)
            s_red.inc(cm, 1)
            vfence(P)
            mcb = v["STG"][:, 256:272].bitcast(BF16)
            cv = s_red.inc(nc.vector.tensor_copy(mcb, v["GF"][:, 2048:2080]), 1)
            nc.tensor.wait_ge(s_red.h, cv)
            t2 = None
            for q in range(2):
                t2 = nc.tensor.transpose(ptr[:, q * 128:q * 128 + 128],
                                         mcb[:, q * 16:(q + 1) * 16], v["ident"])
            mv = s_mmb.inc(t2, 1)
            nc.vector.wait_ge(s_mmb.h, mv)
            if hp > 0:
                nc.vector.wait_ge(s_dma.h, dmax_written)
            ev = s_red.inc(nc.vector.tensor_copy(mrt, ptr[:]), 1)
            nc.sync.wait_ge(s_red.h, ev)
            s_dma.inc(nc.sync.dma_start(
                dmax[2 * hp, :].rearrange("(i p) -> i p", p=128),
                mrt[:, 0:128]), 16)
            s_dma.inc(nc.sync.dma_start(
                dmax[2 * hp + 1, :].rearrange("(i p) -> i p", p=128),
                mrt[:, 128:256]), 16)
            dmax_written = s_dma.n
    P.barrier()
    if A_PHASES < 3:
        return

    PT = v["FA"][:, 0:8192].rearrange("p (j c) -> p j c", j=16)
    qaug = [v["FA"][:, 8192:10240], v["FA"][:, 12288:14336]]
    kaug = [v["FA"][:, 10240:12288], v["FA"][:, 14336:16384]]
    mrow = [v["STG"][0:1, 1024:2048].bitcast(BF16),
            v["WB"][0:1, 0:2048]]
    rs_row = v["STG"][0:1, 0:512]
    rs_rowb = v["STG"][0:1, 512:768].bitcast(BF16)
    av_sb = v["STG"][64:128, 0:256].bitcast(BF16)
    s_row = P.sem("a_row")
    s_exp = P.sem("a_exp")
    s_avp = P.sem("a_avp")
    s_ave = P.sem("a_ave")
    s_mmc = P.sem("a_mmc")
    s_nrm = P.sem("a_nrm")
    s_mr = [P.sem("a_mr0"), P.sem("a_mr1")]
    s_dma.wait(nc.sync)
    row0, e0, n0 = s_row.n, s_exp.n, s_nrm.n
    av_cnt = {}
    exp_cnt = {}
    mmc_s2 = {}
    with (
        nc.psum_tensor("ps_s2a", [128, 1024]) as p2a,
        nc.psum_tensor("ps_s2b", [128, 1024]) as p2b,
        nc.psum_tensor("ps_s2c", [128, 1024]) as p2c,
        nc.psum_tensor("ps_avo", [128, 512]) as pav,
        nc.psum_tensor("ps_rr", [128, 512]) as prr,
    ):
        p2 = [p2a, p2b, p2c]
        pend = None
        tt_prev = 0

        def emit_av(bg, grp):
            hh = bg // 4
            nc.tensor.wait_ge(s_exp.h, exp_cnt[bg * 8 + grp])
            if grp == 0 and bg >= 1:
                nc.tensor.wait_ge(s_nrm.h, n0 + bg)
            ins = None
            for k2 in range(2):
                jtk = grp * 2 + k2
                ins = nc.tensor.matmul(pav[0:65, :], VA[:, jtk, hh, :],
                                       PT[:, jtk, :], start=(jtk == 0),
                                       stop=(jtk == TI - 1))
            av_cnt[(bg, grp)] = s_avp.inc(ins, 1)

        def emit_norm2(pn):
            nonlocal tt_prev
            g2, rb2, b2, nv, ttp = pn
            if ttp:
                nc.tensor.wait_ge(s_ave.h, ttp)
            nc.tensor.wait_ge(s_nrm.h, nv)
            pins = nc.tensor.matmul(prr[0:64, :], v["ones_rowb"][:, 0:64],
                                    rs_rowb, start=True, stop=True)
            pv = s_mmc.inc(pins, 1)
            nc.vector.wait_ge(s_mmc.h, pv)
            dsta = AVN[rb2:rb2 + 64, g2, b2 * 512:(b2 + 1) * 512]
            tt_prev = s_ave.inc(nc.vector.tensor_tensor(
                dsta, av_sb[:, 0:512], prr[0:64, :], ALU.mult), 1)

        rowv = [0, 0]
        rowv[0] = s_mr[0].inc(nc.sync.dma_start(mrow[0], dmax[0, :][None, :]), 16)
        for h in range(16):
            sl = h % 2
            if h + 1 < 16:
                nsl = (h + 1) % 2
                if h >= 1:
                    nc.sync.wait_ge(s_row.h, row0 + h)
                rowv[nsl] = s_mr[nsl].inc(
                    nc.sync.dma_start(mrow[nsl], dmax[h + 1, :][None, :]), 16)
            g, rb = h // 2, (h % 2) * 64
            if h >= 2:
                nc.vector.wait_ge(s_mmc.h, mmc_s2[h - 2])
            if h < 2:
                nc.vector.memset(kaug[sl][64:65, :], 1.0)
            nc.vector.tensor_copy(kaug[sl][0:64, :], KT[rb:rb + 64, g, :])
            nc.vector.tensor_copy(qaug[sl][0:64, :], QT[rb:rb + 64, g, :])
            nc.vector.wait_ge(s_mr[sl].h, rowv[sl])
            rv = s_row.inc(nc.vector.tensor_scalar_mul(
                qaug[sl][64:65, :], mrow[sl], -8.0), 1)
            first_of_h = True
            for b in range(4):
                bg = h * 4 + b
                for grp in range(8):
                    j = bg * 8 + grp
                    pt2 = p2[j % 3]
                    if first_of_h and grp == 0:
                        nc.tensor.wait_ge(s_row.h, rv)
                        first_of_h = False
                    if j >= 3:
                        nc.tensor.wait_ge(s_exp.h, e0 + j - 2)
                    ins = None
                    for k2 in range(2):
                        jtk = grp * 2 + k2
                        ins = nc.tensor.matmul(
                            pt2[:, k2 * 512:(k2 + 1) * 512],
                            kaug[sl][0:65, jtk * 128:(jtk + 1) * 128],
                            qaug[sl][0:65, b * 512:(b + 1) * 512],
                            start=True, stop=True)
                    mv = s_mmc.inc(ins, 1)
                    if bg == h * 4 + 3 and grp == 7:
                        mmc_s2[h] = mv
                    nc.scalar.wait_ge(s_mmc.h, mv)
                    if bg >= 1:
                        nc.scalar.wait_ge(s_avp.h, av_cnt[(bg - 1, grp)])
                    dstp = PT[:, grp * 2:grp * 2 + 2, :] \
                        .rearrange("p j c -> p (j c)")
                    exp_cnt[j] = s_exp.inc(
                        nc.scalar.activation(dstp, pt2[:], AF.Exp,
                                             scale=0.125), 1)
                    if grp >= 1:
                        emit_av(bg, grp - 1)
                    if grp == 2 and pend is not None:
                        emit_norm2(pend)
                        pend = None
                emit_av(bg, 7)
                nc.vector.wait_ge(s_avp.h, av_cnt[(bg, 7)])
                nc.vector.reciprocal(rs_row, pav[64:65, :])
                vfence(P)
                nc.vector.tensor_copy(rs_rowb, rs_row)
                nv = s_nrm.inc(nc.vector.tensor_copy(av_sb[:, 0:512],
                                                     pav[0:64, :]), 1)
                pend = (g, rb, b, nv, tt_prev)
        emit_norm2(pend)
    P.barrier()
    if A_PHASES < 4:
        return

    P.gate_w("L0_attn_W_o")
    load_row_bf16(P, Wd["L0_attn_b_o"][0:512], v["brow_bf"][:, 0:512], s_rows)
    load_row_bf16(P, Wd["L0_attn_b_o"][512:1024], v["brow_bf"][:, 512:1024], s_rows)
    s_rows.wait(nc.tensor)
    wo = Wd["L0_attn_W_o"].rearrange("(k p) f -> p k f", p=128)
    AV3 = AVN

    def o_src(k):
        return wo[:, k, :]

    def o_lhs(k, i):
        return AV3[:, k, i * 128:(i + 1) * 128]

    def o_evac(i, ob, pt):
        Xsl = X[:, i, ob * 512:(ob + 1) * 512]
        return nc.vector.tensor_tensor(Xsl, Xsl, pt, ALU.add)

    emit_proj_tm(P, stg, "aout", o_lhs, o_src, DG, o_evac, nc.vector,
                 n_ob=2, bias_row=v["brow_bf"][:, 0:1024])
    P.barrier()
    s_ln = P.sem("a_ln")
    emit_ln_all(P, X, v["FD"][:, 0:2048], s_ln)
    P.barrier()



CONVERT_WEIGHTS = []


def emit_convert_weights(P, Wd, Wbf):
    nc = P.nc
    if not hasattr(P, "ccname"):
        P.ccname = {}
    FDf = P.v["FDf32"]
    fin = [FDf[:, 0:2048], FDf[:, 2048:4096]]
    fout = [P.v["STG"][:, 0:1024].bitcast(BF16),
            P.v["STG"][:, 1024:2048].bitcast(BF16)]
    s_ci = [P.sem("cw_i0"), P.sem("cw_i1")]
    s_cc = P.sem("cw_c")
    s_co = [P.sem("cw_o0"), P.sem("cw_o1")]
    chunks = []
    for nm in CONVERT_WEIGHTS:
        w = Wd[nm]
        L = 1
        for d in w.shape:
            L *= d
        cols = L // 128
        spec = " ".join(chr(97 + i) for i in range(len(w.shape)))
        wf = w.rearrange(f"{spec} -> ({spec})")
        bf = Wbf[nm].rearrange(f"{spec} -> ({spec})")
        off = 0
        while off < cols:
            cn = min(2048, cols - off)
            chunks.append((nm, wf, bf, off, cn))
            off += cn

    def in_dma(ci):
        nm, wf, bf, off, cn = chunks[ci]
        sl = ci % 2
        srcap = wf[off * 128:(off + cn) * 128].rearrange("(p c) -> p c", c=cn)
        return s_ci[sl].inc(nc.gpsimd.dma_start(fin[sl][:, 0:cn], srcap), 16)

    n_ch = len(chunks)
    iv = {}
    for k in range(min(2, n_ch)):
        iv[k] = in_dma(k)
    co_v = {}
    cc_v = {}
    for ci, (nm, wf, bf, off, cn) in enumerate(chunks):
        sl4, sl2 = ci % 2, ci % 2
        dstap = bf[off * 128:(off + cn) * 128].rearrange("(p c) -> p c", c=cn)
        nc.gpsimd.wait_ge(s_ci[sl4].h, iv[ci])
        if ci >= 2:
            nc.gpsimd.wait_ge(s_co[sl2].h, co_v[ci - 2])
        cv = s_cc.inc(nc.gpsimd.tensor_copy(fout[sl2][:, 0:cn],
                                            fin[sl4][:, 0:cn]), 1)
        cc_v[ci] = cv
        nc.gpsimd.wait_ge(s_cc.h, cv)
        co_v[ci] = s_co[sl2].inc(nc.gpsimd.dma_start(dstap,
                                                     fout[sl2][:, 0:cn]), 16)
        P.ccname[nm] = [(s_co[0], s_co[0].n), (s_co[1], s_co[1].n)]
        if ci + 2 < n_ch:
            iv[ci + 2] = in_dma(ci + 2)
    P.ccsem = s_cc
    P.convdone = (s_cc, s_cc.n, s_co, [s_co[0].n, s_co[1].n])




def build_program(blocks):
    nc = bass.Bass("TRN2", target_bir_lowering=False)
    stack = ExitStack()
    P = Prog(nc, stack)

    Wd = {}
    names = [
        ("L0_attn_x", [T, D]),
        ("L0_attn_W_qkv", [D, 3072]), ("L0_attn_b_qkv", [3072]),
        ("L0_attn_W_o", [D, D]), ("L0_attn_b_o", [D]),
        ("L0_ffn_wqkv", [2, D, 1536]), ("L0_ffn_q_scale", [2, 64]),
        ("L0_ffn_k_scale", [2, 64]), ("L0_ffn_wo", [2, 512, D]),
        ("L0_ffn_ln_g", [2, D]), ("L0_ffn_ln_b", [2, D]),
        ("L0_ffn_w1p", [2, D, 5632]), ("L0_ffn_b1p", [2, 5632]),
        ("L0_ffn_w2p", [2, 2816, D]), ("L0_ffn_b2", [2, D]),
        ("L1_attn_ln_g", [D]), ("L1_attn_ln_b", [D]),
        ("L1_attn_W_in", [D, 2048]), ("L1_attn_conv_w", [D, 1, 4]),
        ("L1_attn_conv_b", [D]), ("L1_attn_dt_w", [D, D]),
        ("L1_attn_dt_b", [D]), ("L1_attn_B_w", [D, 16]),
        ("L1_attn_C_w", [D, 16]), ("L1_attn_log_A", [D]),
        ("L1_attn_D_param", [D]), ("L1_attn_W_out", [D, D]),
        ("L1_ffn_gate_Wt", [D, 8]), ("L1_ffn_gate_b", [8]),
        ("L1_ffn_wi_Wt", [D, 4096]), ("L1_ffn_wi_b", [4096]),
        ("L1_ffn_wo_Wt", [4096, D]), ("L1_ffn_wo_b", [D]),
        ("L1_ffn_lora_At", [D, 32]), ("L1_ffn_lora_Bc", [32, 4096]),
        ("L0_attn_bq_cols", [128, 16]),
        ("L1_ffn_wib_cols", [128, 32]),
        ("L1_attn_logA_cols", [128, 8]), ("L1_attn_dtb_cols", [128, 8]),
        ("L1_attn_D_cols", [128, 8]), ("L1_attn_convb_cols", [128, 8]),
        ("L1_attn_convw_cols", [128, 32]),
        ("c_bmask8", [128, 256]), ("c_cmask", [128, 256]), ("c_tid16", [16, 128]),
        ("c_rot_cs", [128, 2048]), ("c_rot_ss", [128, 2048]),
        ("c_rot_cds", [128, 2048]), ("c_rot_sds", [128, 2048]),
        ("c_locmask", [128, 4096]), ("c_perm", [128, 128]),
        ("L0_ffn_qk_cols", [128, 4]), ("L0_ffn_lngb_cols", [128, 32]),
        ("L0_ffn_b1_cols", [2, 128, 44]),
        ("c_ident", [128, 128]),
    ]
    for nm, shp in names:
        dt = BF16 if nm in ("c_ident", "c_bmask8", "c_cmask", "c_tid16",
                            "c_rot_cs", "c_rot_ss", "c_rot_cds", "c_rot_sds",
                            "c_locmask", "c_perm") else F32
        Wd[nm] = nc.dram_tensor(nm, shp, dt, kind="ExternalInput").ap()
    xout = nc.dram_tensor("x_out", [T, D], F32, kind="ExternalOutput").ap()

    P.dram["mrow"] = nc.dram_tensor("scr_mrow", [16, 2048], BF16,
                                    kind="Internal").ap()
    Wbf = {}
    for nm in CONVERT_WEIGHTS:
        shp = dict(names)[nm]
        Wbf[nm] = nc.dram_tensor(f"bf_{nm}", shp, BF16, kind="Internal").ap()
    P.dram["x0"] = nc.dram_tensor("scr_x0", [T, D], F32, kind="Internal").ap()
    P.dram["z"] = nc.dram_tensor("scr_z", [D, T], BF16, kind="Internal").ap()

    with stack:
        mega = stack.enter_context(nc.sbuf_tensor("mega", [128, ARENA_COLS], F32))
        with nc.Block() as _block:
            carve(P, mega)
            stg = WStager(P)
            s_io = P.sem("io")
            s_io.inc(nc.sync.dma_start(
                P.v["X"],
                Wd["L0_attn_x"].rearrange("(i p) d -> p i d", p=128)), 16)
            s_io.inc(nc.sync.dma_start(P.v["ident"], Wd["c_ident"]), 16)
            s_io.inc(nc.sync.dma_start(P.v["perm"], Wd["c_perm"]), 16)
            nc.vector.memset(P.v["ones_rowb"][:], 1.0)
            for e in (nc.tensor, nc.vector, nc.scalar, nc.sync):
                s_io.wait(e)
            if CONVERT_WEIGHTS:
                emit_convert_weights(P, Wd, Wbf)
                for nm in CONVERT_WEIGHTS:
                    Wd[nm] = Wbf[nm]

            for b in blocks:
                if b == "l0_attn":
                    emit_l0_attn(P, stg, Wd)
                elif b == "l0_ffn":
                    emit_l0_ffn(P, stg, Wd)
                elif b == "l1_attn":
                    emit_l1_attn(P, stg, Wd)
                elif b == "l1_ffn":
                    emit_l1_ffn(P, stg, Wd)

            s_out = P.sem("out")
            s_out.inc(nc.sync.dma_start(
                xout.rearrange("(i p) d -> p i d", p=128), P.v["X"]), 16)
            s_out.wait(nc.sync)
    return nc




def prep_inputs(inputs):
    f32 = np.float32
    g = {k: np.asarray(v, dtype=f32) for k, v in inputs.items()}

    HID, HIDP = 2730, 2816
    com = {}
    com["L0_attn_W_qkv"] = g["L0_attn_W_qkv"]
    com["L0_attn_b_qkv"] = g["L0_attn_b_qkv"]
    com["L0_attn_W_o"] = g["L0_attn_W_o"]
    com["L0_attn_b_o"] = g["L0_attn_b_o"]
    com["L0_ffn_wqkv"] = g["L0_ffn_wqkv"]
    com["L0_ffn_q_scale"] = g["L0_ffn_q_scale"]
    com["L0_ffn_k_scale"] = g["L0_ffn_k_scale"]
    com["L0_ffn_wo"] = g["L0_ffn_wo"]
    com["L0_ffn_ln_g"] = g["L0_ffn_ln_g"]
    com["L0_ffn_ln_b"] = g["L0_ffn_ln_b"]
    w1 = g["L0_ffn_w1"]
    b1 = g["L0_ffn_b1"]
    w1p = np.zeros((2, D, 2 * HIDP), f32)
    b1p = np.zeros((2, 2 * HIDP), f32)
    perm = np.zeros(2 * HIDP, np.int64)
    keep = np.zeros(2 * HIDP, bool)
    for blk in range(22):
        lo, hi = blk * 128, min((blk + 1) * 128, HID)
        n = hi - lo
        pa = 2 * blk * 128
        pg = (2 * blk + 1) * 128
        perm[pa:pa + n] = np.arange(lo, hi)
        keep[pa:pa + n] = True
        perm[pg:pg + n] = HID + np.arange(lo, hi)
        keep[pg:pg + n] = True
    w1p[:, :, keep[: 2 * HIDP]] = w1[:, :, perm[keep]]
    b1p[:, keep] = b1[:, perm[keep]]
    com["L0_ffn_w1p"] = w1p
    com["L0_ffn_b1p"] = b1p
    w2 = g["L0_ffn_w2"]
    w2p = np.zeros((2, HIDP, D), f32)
    w2p[:, :HID, :] = w2
    com["L0_ffn_w2p"] = w2p
    com["L0_ffn_b2"] = g["L0_ffn_b2"]
    for k in ["L1_attn_ln_g", "L1_attn_ln_b", "L1_attn_W_in",
              "L1_attn_conv_b", "L1_attn_dt_w", "L1_attn_dt_b",
              "L1_attn_B_w", "L1_attn_C_w", "L1_attn_log_A",
              "L1_attn_D_param", "L1_attn_W_out"]:
        com[k] = g[k]
    com["L1_attn_conv_w"] = g["L1_attn_conv_w"]
    com["L1_ffn_gate_Wt"] = np.ascontiguousarray(g["L1_ffn_gate_W"].T)
    com["L1_ffn_gate_b"] = g["L1_ffn_gate_b"]
    com["L1_ffn_wi_Wt"] = np.ascontiguousarray(g["L1_ffn_wi_W"].T)
    com["L1_ffn_wi_b"] = g["L1_ffn_wi_b"]
    com["L1_ffn_wo_Wt"] = np.ascontiguousarray(g["L1_ffn_wo_W"].T)
    com["L1_ffn_wo_b"] = g["L1_ffn_wo_b"]
    la = g["L1_ffn_lora_A"]
    com["L1_ffn_lora_At"] = np.ascontiguousarray(
        la.reshape(32, D).T)
    lb = g["L1_ffn_lora_B"]
    com["L1_ffn_lora_Bc"] = np.ascontiguousarray(
        lb.transpose(0, 2, 1).reshape(32, 4096))
    com["L0_attn_bq_cols"] = np.ascontiguousarray(
        g["L0_attn_b_qkv"][:2048].reshape(16, 128).T)
    com["L1_ffn_wib_cols"] = np.ascontiguousarray(
        g["L1_ffn_wi_b"].reshape(32, 128).T)
    com["L1_attn_logA_cols"] = np.ascontiguousarray(
        g["L1_attn_log_A"].reshape(8, 128).T)
    com["L1_attn_dtb_cols"] = np.ascontiguousarray(
        g["L1_attn_dt_b"].reshape(8, 128).T)
    com["L1_attn_D_cols"] = np.ascontiguousarray(
        g["L1_attn_D_param"].reshape(8, 128).T)
    com["L1_attn_convb_cols"] = np.ascontiguousarray(
        g["L1_attn_conv_b"].reshape(8, 128).T)
    com["L1_attn_convw_cols"] = np.ascontiguousarray(
        g["L1_attn_conv_w"][:, 0, :].reshape(8, 128, 4).transpose(1, 0, 2)
        .reshape(128, 32))
    bm = np.zeros((128, 16, 16), BF)
    for p in range(128):
        bm[p, p // 8, :] = 1
    com["c_bmask8"] = bm.reshape(128, 256)
    cm = np.zeros((128, 2, 128), BF)
    for q in range(128):
        c_l_in_half = q // 16
        for h2 in range(2):
            c_l = h2 * 8 + c_l_in_half
            for p in range(128):
                if c_l == p // 8:
                    cm[q, h2, p] = 1
    com["c_cmask"] = cm.reshape(128, 256)
    tid = np.zeros((16, 128), BF)
    for qq in range(128):
        tid[qq % 16, qq] = 1
    com["c_tid16"] = tid
    DH, WIN, NN = 64, 512, 2048
    half_idx = np.arange(0, DH, 2, dtype=np.float64)
    inv_freq = 1.0 / 10000.0 ** (half_idx / DH)
    tt = np.arange(NN, dtype=np.float64)
    freqs = tt[:, None] * inv_freq[None, :]
    freqs = np.concatenate([freqs, freqs], axis=-1)
    basev = (half_idx + 0.4 * DH) / (1.4 * DH)
    power = (tt - NN // 2) / float(WIN // 2)
    scalev = basev[None, :] ** power[:, None]
    scalev = np.concatenate([scalev, scalev], axis=-1)
    cosv, sinv = np.cos(freqs), np.sin(freqs)
    def rep2(m):
        r = np.empty((128, NN))
        r[0:64, :] = m.T
        r[64:128, :] = m.T
        return r.astype(BF)
    com["c_rot_cs"] = rep2(cosv * scalev)
    com["c_rot_ss"] = rep2(sinv * scalev)
    com["c_rot_cds"] = rep2(cosv / scalev)
    com["c_rot_sds"] = rep2(sinv / scalev)
    lm = np.zeros((128, 8, 512), BF)
    for tkt in range(8):
        for p in range(128):
            tk = tkt * 128 + p
            c0, c1 = max(0, tk - 512), min(511, tk)
            if c0 <= c1:
                lm[p, tkt, c0:c1 + 1] = 1
    com["c_locmask"] = lm.reshape(128, 4096)
    pm = np.zeros((128, 128), BF)
    for mcol in range(128):
        base0 = 64 * (mcol // 64)
        ml = mcol % 64
        if ml < 64 // 2:
            pm[base0 + ml + 32, mcol] = -1
        else:
            pm[base0 + ml - 32, mcol] = 1
    com["c_perm"] = pm
    com["L0_ffn_qk_cols"] = np.stack(
        [g["L0_ffn_q_scale"][0], g["L0_ffn_k_scale"][0],
         g["L0_ffn_q_scale"][1], g["L0_ffn_k_scale"][1]],
        axis=1)[np.tile(np.arange(64), 2)].astype(f32)
    lngb = np.zeros((128, 32), f32)
    for l in range(2):
        for gg in range(8):
            lngb[:, l * 8 + gg] = g["L0_ffn_ln_g"][l][gg * 128: (gg + 1) * 128]
            lngb[:, 16 + l * 8 + gg] = g["L0_ffn_ln_b"][l][gg * 128: (gg + 1) * 128]
    com["L0_ffn_lngb_cols"] = lngb
    com["L0_ffn_b1_cols"] = np.ascontiguousarray(
        b1p.reshape(2, 44, 128).transpose(0, 2, 1))
    com["c_ident"] = np.eye(128, dtype=BF)

    x = g["L0_attn_x"]
    in_maps = []
    for c in range(NCORES):
        m = dict(com)
        m["L0_attn_x"] = np.ascontiguousarray(x[c])
        in_maps.append(m)
    return in_maps


def kernel(**inputs):
    in_maps = prep_inputs(inputs)
    nc = build_program(BLOCKS)
    res = run_bass_kernel_spmd(nc, in_maps, core_ids=list(range(NCORES)))
    out = np.stack([np.asarray(r["x_out"], dtype=np.float32)
                    for r in res.results], axis=0)
    return out


def emit_l0_ffn(P, stg, Wd):
    nc = P.nc
    v = P.v
    X = v["X"]
    P.gate_w("L0_ffn_w2p")
    FA, FB, FC, FD = v["FA"], v["FB"], v["FC"], v["FD"]
    XF = FA[:, 0:16392].rearrange("p (g c) -> p g c", g=DG)
    QK = FC[:, 0:16384].rearrange("p (g c) -> p g c", g=DG)
    VA = FD[:, 0:8320].rearrange("p (i h e) -> p i h e", i=TI, h=8)
    ROT = FD[:, 8320:16512]
    AVN = FB[:, 0:8192].rearrange("p (g c) -> p g c", g=4)
    PT = FB[:, 8192:12288].rearrange("p (j c) -> p j c", j=8)
    MSK = FB[:, 12288:16384].rearrange("p (j c) -> p j c", j=8)
    katk = [FA[:, 0:2048], FA[:, 2560:4608]]
    qaug = [FA[:, 2048:2560], FA[:, 4608:5120]]
    dmax = P.dram["mrow"]
    s_dma = P.sem("f0_dma")
    s_rows = P.sem("f0_rows")
    s_pe = P.sem("f0_pe")
    s_ln = P.sem("f0_ln")

    s_sp = P.sem("f0_spill")
    s_sp.inc(nc.sync.dma_start(
        P.dram["x0"].rearrange("(i p) d -> p i d", p=128), X), 16)
    s_dma.inc(nc.sync.dma_start(ROT[:, 0:2048], Wd["c_rot_cs"]), 16)
    s_dma.inc(nc.sync.dma_start(ROT[:, 2048:4096], Wd["c_rot_ss"]), 16)
    s_dma.inc(nc.sync.dma_start(ROT[:, 4096:6144], Wd["c_rot_cds"]), 16)
    s_dma.inc(nc.sync.dma_start(ROT[:, 6144:8192], Wd["c_rot_sds"]), 16)
    s_dma.inc(nc.sync.dma_start(v["ccol"][:, 52:56], Wd["L0_ffn_qk_cols"]), 16)
    s_dma.inc(nc.sync.dma_start(v["bcol"][:, 0:32], Wd["L0_ffn_lngb_cols"]), 16)
    nc.vector.memset(v["ones_c128"], 1.0)
    s_dma.wait(nc.vector)
    s_dma.wait(nc.tensor)
    s_dma.wait(nc.scalar)

    for lay in range(2):
        wq = Wd["L0_ffn_wqkv"].rearrange("l (g p) f -> l p g f", p=128)[lay]
        nc.sync.wait_ge(s_dma.h, s_dma.n)
        mv = s_dma.inc(nc.sync.dma_start(
            MSK.rearrange("p j c -> p (j c)"), Wd["c_locmask"]), 16)
        nc.vector.wait_ge(s_dma.h, mv)

        emit_build_xf(P, X, XF, 1, f"f{lay}")

        def sh(gg, c0, cn):
            off = 1 if gg < 4 else 0
            return XF[:, gg, off + c0: off + c0 + cn]

        def qk_src(m):
            return wq[:, :, m * 128:(m + 1) * 128]

        def qk_rhs(g, tb):
            return sh(g, tb * 512, 512)

        def qk_evac(m, tb, pt):
            return nc.vector.tensor_copy(QK[:, m, tb * 512:(tb + 1) * 512], pt[:])

        emit_proj_fm(P, stg, "fqk", qk_src, 8, qk_rhs, DG, qk_evac,
                     nc.vector)
        P.barrier()

        def v_src(k):
            return wq[:, k, 1024:1536]

        def v_lhs(k, i):
            return sh(k, i * 128, 128)

        def v_evac(i, vb, pt):
            dstv = VA[:, i, :, 0:64]
            return nc.vector.tensor_copy(
                dstv, pt.rearrange("p (h e) -> p h e", h=8))

        emit_proj_tm(P, stg, "fv", v_lhs, v_src, DG, v_evac, nc.vector,
                     n_ob=1)
        nc.vector.memset(VA[:, :, :, 64:65].rearrange("p i h e -> p (i h e)"), 1.0)
        P.barrier()

        s_n = P.sem("fn")
        s_ns = P.sem("fns")
        sqsc = [FB[:, 0:2048], FB[:, 2048:4096]]
        tscr = [FB[:, 4096:4608], FB[:, 4608:5120]]
        t2s = FB[:, 5120:5632]
        t3s = FB[:, 5632:6144]
        rstdbA = [v["STG"][0:1, 1024:1280].bitcast(BF16),
                  v["STG"][0:1, 1536:1792].bitcast(BF16)]
        rstdbB = [v["STG"][0:1, 1280:1536].bitcast(BF16),
                  v["STG"][0:1, 1792:2048].bitcast(BF16)]
        rAs = [v["WBf32"][0:1, 0:512], v["WBf32"][0:1, 512:1024]]
        rBs = [v["WBf32"][32:33, 0:512], v["WBf32"][32:33, 512:1024]]
        epsrow = v["STG"][0:1, 0:256].bitcast(BF16)
        sq_v = {}
        pn_v = {}
        sx_v = {}
        rc_v = {}
        tm_v = {}
        pm_v = {}
        bc_v = {}
        t3_v = {}
        fn_v = {}
        ev24 = s_n.inc(nc.vector.memset(epsrow, 1e-24), 1)
        nc.tensor.wait_ge(s_n.h, ev24)
        with (
            nc.psum_tensor(f"ps_fna0{lay}", [1, 512]) as pnA0,
            nc.psum_tensor(f"ps_fnb0{lay}", [1, 512]) as pnB0,
            nc.psum_tensor(f"ps_fna1{lay}", [1, 512]) as pnA1,
            nc.psum_tensor(f"ps_fnb1{lay}", [1, 512]) as pnB1,
            nc.psum_tensor(f"ps_fra{lay}", [128, 512]) as prnA,
            nc.psum_tensor(f"ps_frb{lay}", [128, 512]) as prnB,
            nc.psum_tensor(f"ps_fp0{lay}", [128, 512]) as prot0,
            nc.psum_tensor(f"ps_fp1{lay}", [128, 512]) as prot1,
        ):
            pnA = [pnA0, pnA1]
            pnB = [pnB0, pnB1]
            prot = [prot0, prot1]
            sq_v[0] = s_ns.inc(nc.scalar.activation(sqsc[0], QK[:, 0, :],
                                                    AF.Square), 1)
            for tile in range(8):
                isq = tile < 4
                scol = v["ccol"][:, 52 + 2 * lay + (0 if isq else 1):
                                 53 + 2 * lay + (0 if isq else 1)]
                for tb in range(4):
                    it = tile * 4 + tb
                    lo = tb * 512
                    qsl = QK[:, tile, lo:lo + 512]
                    ts = tscr[it % 2]
                    cosT = ROT[:, (0 if isq else 4096) + lo:
                               (0 if isq else 4096) + lo + 512]
                    sinT = ROT[:, (2048 if isq else 6144) + lo:
                               (2048 if isq else 6144) + lo + 512]
                    if tb == 1 and tile + 1 < 8:
                        if tile >= 1:
                            nc.scalar.wait_ge(s_pe.h, pn_v[(tile - 1) * 4 + 3])
                        sq_v[tile + 1] = s_ns.inc(nc.scalar.activation(
                            sqsc[(tile + 1) % 2], QK[:, tile + 1, :],
                            AF.Square), 1)
                    if tb == 0:
                        nc.tensor.wait_ge(s_ns.h, sq_v[tile])
                    if it >= 2:
                        nc.tensor.wait_ge(s_ns.h, sx_v[it - 2])
                    nc.tensor.matmul(pnA[it % 2][:], v["ones_c128"][0:64, :],
                                     sqsc[tile % 2][0:64, lo:lo + 512],
                                     start=True, stop=False)
                    nc.tensor.matmul(pnA[it % 2][:], v["ones_rowb"][:, 0:1],
                                     epsrow, start=False, stop=True)
                    nc.tensor.matmul(pnB[it % 2][:],
                                     v["ones_c128"][64:128, :],
                                     sqsc[tile % 2][64:128, lo:lo + 512],
                                     start=True, stop=False,
                                     tile_position=(64, 0))
                    ins = nc.tensor.matmul(pnB[it % 2][:],
                                           v["ones_rowb"][:, 0:1],
                                           epsrow, start=False, stop=True)
                    pn_v[it] = s_pe.inc(ins, 1)
                    nc.scalar.wait_ge(s_pe.h, pn_v[it])
                    if it >= 2:
                        nc.scalar.wait_ge(s_n.h, rc_v[it - 2])
                    s_ns.inc(nc.scalar.activation(rAs[it % 2],
                                                  pnA[it % 2][:], AF.Sqrt), 1)
                    sx_v[it] = s_ns.inc(nc.scalar.activation(
                        rBs[it % 2], pnB[it % 2][:], AF.Sqrt), 1)
                    if it >= 2:
                        nc.vector.wait_ge(s_pe.h, pm_v[it - 2])
                    tm_v[it] = s_n.inc(nc.vector.tensor_scalar_mul(
                        ts, qsl, scol), 1)
                    nc.vector.wait_ge(s_ns.h, sx_v[it])
                    with nc.allow_low_precision(reason="bf16 store of rstd"):
                        nc.vector.reciprocal(rstdbA[it % 2], rAs[it % 2])
                        rc_v[it] = s_n.inc(nc.vector.reciprocal(
                            rstdbB[it % 2], rBs[it % 2]), 1)
                    nc.tensor.wait_ge(s_n.h, tm_v[it])
                    if it >= 2:
                        nc.tensor.wait_ge(s_n.h, t3_v[it - 2])
                    pm_v[it] = s_pe.inc(nc.tensor.matmul(
                        prot[it % 2][:], v["perm"], ts,
                        start=True, stop=True), 1)
                    nc.tensor.wait_ge(s_n.h, rc_v[it])
                    if it >= 1:
                        nc.tensor.wait_ge(s_n.h, fn_v[it - 1])
                    nc.tensor.matmul(prnA[0:64, :], v["ones_rowb"][:, 0:64],
                                     rstdbA[it % 2], start=True, stop=True)
                    bc_v[it] = s_pe.inc(nc.tensor.matmul(
                        prnB[0:64, :], v["ones_rowb"][:, 0:64],
                        rstdbB[it % 2], start=True, stop=True), 1)
                    nc.vector.wait_ge(s_pe.h, pm_v[it])
                    nc.vector.tensor_mul(t2s, ts, cosT)
                    nc.vector.tensor_mul(t3s[0:64, :], prot[it % 2][0:64, :],
                                         sinT[0:64, :])
                    t3_v[it] = s_n.inc(nc.vector.tensor_mul(
                        t3s[64:128, :], prot[it % 2][64:128, :],
                        sinT[64:128, :]), 1)
                    vfence(P)
                    nc.vector.tensor_add(t2s[0:64, :], t2s[0:64, :],
                                         t3s[0:64, :])
                    nc.vector.tensor_add(t2s[64:128, :], t2s[64:128, :],
                                         t3s[64:128, :])
                    nc.vector.wait_ge(s_pe.h, bc_v[it])
                    vfence(P)
                    nc.vector.tensor_tensor(qsl[0:64, :], t2s[0:64, :],
                                            prnA[0:64, :], ALU.mult)
                    fn_v[it] = s_n.inc(nc.vector.tensor_tensor(
                        qsl[64:128, :], t2s[64:128, :], prnB[0:64, :],
                        ALU.mult), 1)
        P.barrier()

        s_red = P.sem("f0red")
        s_mmb = P.sem("f0mmb")
        s_dx = P.sem("f0dx")
        mrt = v["STG"][0:16, 0:128].bitcast(BF16)
        dmax_written = 0
        red_val = {}
        f = 0
        with (
            nc.psum_tensor(f"ps_f1_{lay}0", [128, 1024]) as q0,
            nc.psum_tensor(f"ps_f1_{lay}1", [128, 1024]) as q1,
            nc.psum_tensor(f"ps_f1_{lay}2", [128, 1024]) as q2,
            nc.psum_tensor(f"ps_f1t{lay}", [16, 256], BF16) as ptr,
        ):
            qs = [q0, q1, q2]
            for hp in range(4):
                for w in range(4):
                    klo = (w - 1) * 512 if w > 0 else 0
                    kw = 1024 if w > 0 else 512
                    for sub in range(4):
                        col = w * 4 + sub
                        for par in range(2):
                            pt = qs[f % 3]
                            if f >= 3:
                                nc.tensor.wait_ge(s_red.h, red_val[f - 3])
                            lh = QK[64 * par:64 * par + 64, hp,
                                    w * 512 + sub * 128: w * 512 + sub * 128 + 128]
                            ins = None
                            for tkb in range(kw // 512):
                                rh = QK[64 * par:64 * par + 64, 4 + hp,
                                        klo + tkb * 512: klo + tkb * 512 + 512]
                                ins = nc.tensor.matmul(
                                    pt[:, tkb * 512:(tkb + 1) * 512], lh, rh,
                                    start=True, stop=True,
                                    tile_position=(64 * par, 0))
                            mv = s_mmb.inc(ins, 1)
                            nc.vector.wait_ge(s_mmb.h, mv)
                            tgt = v["mcols"] if par == 0 else v["colsA"]
                            red_val[f] = s_red.inc(nc.vector.reduce_max(
                                tgt[:, col:col + 1], pt[:, 0:kw], axis=AX), 1)
                            f += 1
                vfence(P)
                mcb = v["STG"][:, 256:272].bitcast(BF16)
                cv = s_red.inc(nc.vector.tensor_copy(mcb, v["GF"][:, 2048:2080]), 1)
                nc.tensor.wait_ge(s_red.h, cv)
                t2 = None
                for q in range(2):
                    t2 = nc.tensor.transpose(ptr[:, q * 128:q * 128 + 128],
                                             mcb[:, q * 16:(q + 1) * 16],
                                             v["ident"])
                mv = s_mmb.inc(t2, 1)
                nc.vector.wait_ge(s_mmb.h, mv)
                if hp > 0:
                    nc.vector.wait_ge(s_dx.h, dmax_written)
                ev = s_red.inc(nc.vector.tensor_copy(mrt, ptr[:]), 1)
                nc.sync.wait_ge(s_red.h, ev)
                s_dx.inc(nc.sync.dma_start(
                    dmax[2 * hp, :].rearrange("(i p) -> i p", p=128),
                    mrt[:, 0:128]), 16)
                s_dx.inc(nc.sync.dma_start(
                    dmax[2 * hp + 1, :].rearrange("(i p) -> i p", p=128),
                    mrt[:, 128:256]), 16)
                dmax_written = s_dx.n
        P.barrier()

        s_row = P.sem("f0row")
        s_exp = P.sem("f0exp")
        s_avp = P.sem("f0avp")
        s_ave = P.sem("f0ave")
        s_mmc = P.sem("f0mmc")
        s_msk = P.sem("f0msk")
        s_nrm = P.sem("f0nrm")
        s_mr = [P.sem("f0mr0"), P.sem("f0mr1")]
        mrow = [v["STG"][0:1, 1024:1280].bitcast(BF16),
                v["STG"][0:1, 1280:1536].bitcast(BF16)]
        rs_row = v["STG"][0:1, 0:512]
        rs_rowb = v["STG"][0:1, 512:768].bitcast(BF16)
        av_sb = v["STG"][64:128, 0:256].bitcast(BF16)
        s_dx.wait(nc.sync)
        row0, n0 = s_row.n, s_nrm.n
        av_cnt = {}
        exp_cnt = {}
        msk_cnt = {}
        mmc_k = {}
        last_av = {}
        jr = 0
        with (
            nc.psum_tensor(f"ps_f2a{lay}", [128, 1024]) as p2a,
            nc.psum_tensor(f"ps_f2b{lay}", [128, 1024]) as p2b,
            nc.psum_tensor(f"ps_f2c{lay}", [128, 1024]) as p2c,
            nc.psum_tensor(f"ps_fav{lay}", [128, 512]) as pav,
            nc.psum_tensor(f"ps_frr{lay}", [128, 512]) as prr,
        ):
            p2 = [p2a, p2b, p2c]
            pend = None
            tt_prev = 0

            def f_av(hw2, grp, t0, w2, h2):
                nc.tensor.wait_ge(s_msk.h, msk_cnt[(hw2, grp)])
                if grp * 2 == t0 and hw2 >= 1:
                    nc.tensor.wait_ge(s_nrm.h, n0 + hw2)
                ins = None
                for k2 in range(2):
                    tkt = grp * 2 + k2
                    jt = (w2 - 1) * 4 + tkt
                    ins = nc.tensor.matmul(pav[0:65, :], VA[:, jt, h2, :],
                                           PT[:, tkt, :],
                                           start=(tkt == t0), stop=(tkt == 7))
                av_cnt[(hw2, grp)] = s_avp.inc(ins, 1)
                last_av[grp] = av_cnt[(hw2, grp)]

            def f_norm2(pn):
                nonlocal tt_prev
                g2, rb2, w2, nv, ttp = pn
                if ttp:
                    nc.tensor.wait_ge(s_ave.h, ttp)
                nc.tensor.wait_ge(s_nrm.h, nv)
                pins = nc.tensor.matmul(prr[0:64, :], v["ones_rowb"][:, 0:64],
                                        rs_rowb, start=True, stop=True)
                pv = s_mmc.inc(pins, 1)
                nc.vector.wait_ge(s_mmc.h, pv)
                dsta = AVN[rb2:rb2 + 64, g2, w2 * 512:(w2 + 1) * 512]
                tt_prev = s_ave.inc(nc.vector.tensor_tensor(
                    dsta, av_sb[:, 0:512], prr[0:64, :], ALU.mult), 1)

            rowv = [0, 0]
            rowv[0] = s_mr[0].inc(nc.sync.dma_start(
                mrow[0], dmax[0, 0:512][None, :]), 16)
            for h in range(8):
                g, rb = h // 2, (h % 2) * 64
                for w in range(4):
                    hw = h * 4 + w
                    sl = hw % 2
                    t0 = 4 if w == 0 else 0
                    grps = list(range(t0 // 2, 4))
                    if hw + 1 < 32:
                        nsl = (hw + 1) % 2
                        h2, w2 = (hw + 1) // 4, (hw + 1) % 4
                        if hw >= 1:
                            nc.sync.wait_ge(s_row.h, row0 + hw)
                        rowv[nsl] = s_mr[nsl].inc(nc.sync.dma_start(
                            mrow[nsl], dmax[h2, w2 * 512:(w2 + 1) * 512][None, :]),
                            16)
                    if hw >= 2:
                        nc.vector.wait_ge(s_mmc.h, mmc_k[hw - 2])
                    if hw < 2:
                        nc.vector.memset(katk[sl][64:65, 0:1024], 1.0)
                    klo = (w - 1) * 512 if w > 0 else 0
                    kw = 1024 if w > 0 else 512
                    koff = 0 if w > 0 else 512
                    nc.vector.tensor_copy(katk[sl][0:64, koff:koff + kw],
                                          QK[rb:rb + 64, 4 + g, klo:klo + kw])
                    nc.vector.tensor_copy(qaug[sl][0:64, 0:512],
                                          QK[rb:rb + 64, g,
                                             w * 512:(w + 1) * 512])
                    nc.vector.wait_ge(s_mr[sl].h, rowv[sl])
                    rv = s_row.inc(nc.vector.tensor_scalar_mul(
                        qaug[sl][64:65, 0:512], mrow[sl], -1.0), 1)
                    rhs = qaug[sl][0:65, 0:512]
                    for gi, grp in enumerate(grps):
                        pt = p2[jr % 3]
                        if gi == 0:
                            nc.tensor.wait_ge(s_row.h, rv)
                        if jr >= 3:
                            nc.tensor.wait_ge(s_exp.h, exp_cnt[jr - 3])
                        ins = None
                        for k2 in range(2):
                            tkt = grp * 2 + k2
                            ins = nc.tensor.matmul(
                                pt[:, k2 * 512:(k2 + 1) * 512],
                                katk[sl][0:65, tkt * 128:(tkt + 1) * 128],
                                rhs, start=True, stop=True)
                        mv = s_mmc.inc(ins, 1)
                        if gi == len(grps) - 1:
                            mmc_k[hw] = mv
                        nc.scalar.wait_ge(s_mmc.h, mv)
                        if grp in last_av:
                            nc.scalar.wait_ge(s_avp.h, last_av[grp])
                        dstp = PT[:, grp * 2:grp * 2 + 2, :] \
                            .rearrange("p j c -> p (j c)")
                        exp_cnt[jr] = s_exp.inc(
                            nc.scalar.activation(dstp, pt[:], AF.Exp,
                                                 scale=8.0), 1)
                        nc.vector.wait_ge(s_exp.h, exp_cnt[jr])
                        for k2 in range(2):
                            tkt = grp * 2 + k2
                            mm = nc.vector.tensor_mul(PT[:, tkt, :],
                                                      PT[:, tkt, :],
                                                      MSK[:, tkt, :])
                        msk_cnt[(hw, grp)] = s_msk.inc(mm, 1)
                        jr += 1
                        if gi >= 1:
                            f_av(hw, grps[gi - 1], t0, w, h)
                        if gi == 1 and pend is not None:
                            f_norm2(pend)
                            pend = None
                    f_av(hw, grps[-1], t0, w, h)
                    nc.vector.wait_ge(s_avp.h, av_cnt[(hw, grps[-1])])
                    nc.vector.reciprocal(rs_row, pav[64:65, :])
                    vfence(P)
                    nc.vector.tensor_copy(rs_rowb, rs_row)
                    nv = s_nrm.inc(nc.vector.tensor_copy(av_sb[:, 0:512],
                                                         pav[0:64, :]), 1)
                    pend = (g, rb, w, nv, tt_prev)
            f_norm2(pend)
        P.barrier()

        wo = Wd["L0_ffn_wo"].rearrange("l (k p) f -> l p k f", p=128)[lay]

        def o_src(k):
            return wo[:, k, :]

        def o_lhs(k, i):
            return AVN[:, k, i * 128:(i + 1) * 128]

        def o_evac(i, ob, pt):
            Xsl = X[:, i, ob * 512:(ob + 1) * 512]
            return nc.vector.tensor_tensor(Xsl, Xsl, pt, ALU.add)

        s_sp.wait(nc.vector)
        emit_proj_tm(P, stg, "fo", o_lhs, o_src, 4, o_evac, nc.vector,
                     n_ob=2)
        P.barrier()

        emit_build_xf(P, X, XF, 1, f"g{lay}")
        hn = FB[:, 0:4096].rearrange("p (g c) -> p g c", g=DG)
        H1 = FB[:, 4096:15360].rearrange("p (m c) -> p m c", m=22)
        scr1 = FB[:, 15360:15872]
        scr2 = FB[:, 15872:16384]
        w1 = Wd["L0_ffn_w1p"].rearrange("l (g p) f -> l p g f", p=128)[lay]
        w2 = Wd["L0_ffn_w2p"].rearrange("l (k p) f -> l p k f", p=128)[lay]
        load_row_bf16(P, Wd["L0_ffn_b2"][lay, 0:512], v["brow_bf"][:, 0:512], s_rows)
        load_row_bf16(P, Wd["L0_ffn_b2"][lay, 512:1024], v["brow_bf"][:, 512:1024],
                      s_rows)
        s_rows.wait(nc.tensor)
        bv = s_dma.inc(nc.sync.dma_start(v["ccol"][:, 8:52],
                                         Wd["L0_ffn_b1_cols"][lay]), 16)
        nc.vector.wait_ge(s_dma.h, bv)
        nc.scalar.wait_ge(s_dma.h, bv)
        s_f = P.sem("ff")
        s_fs = P.sem("ffs")
        for tb in range(4):
            with (
                nc.psum_tensor(f"ps_fmu{lay}{tb}", [1, 512]) as pmu,
                nc.psum_tensor(f"ps_fm2{lay}{tb}", [1, 512]) as pm2,
                nc.psum_tensor(f"ps_fb1{lay}{tb}", [128, 512]) as pb1,
                nc.psum_tensor(f"ps_fb2{lay}{tb}", [128, 512]) as pb2,
            ):
                insmu = None
                for g in range(DG):
                    insmu = nc.tensor.matmul(pmu[:], v["ones_c128"],
                                             sh(g, tb * 512, 512),
                                             start=(g == 0), stop=(g == DG - 1))
                m2pe = {}
                for g in range(DG):
                    sq = scr1 if g % 2 == 0 else scr2
                    if g >= 2:
                        nc.scalar.wait_ge(s_pe.h, m2pe[g - 2])
                    qv = s_fs.inc(nc.scalar.activation(sq, sh(g, tb * 512, 512),
                                                       AF.Square), 1)
                    nc.tensor.wait_ge(s_fs.h, qv)
                    insmu = nc.tensor.matmul(pm2[:], v["ones_c128"], sq,
                                             start=(g == 0), stop=(g == DG - 1))
                    m2pe[g] = s_pe.inc(insmu, 1)
                nc.vector.wait_ge(s_pe.h, m2pe[DG - 1])
                r0 = v["STG"][0:1, 0:512]
                r1 = v["STG"][0:1, 512:1024]
                nc.vector.tensor_scalar_mul(r0, pmu[:], -1.0 / 1024.0)
                vfence(P)
                nc.vector.tensor_mul(r1, r0, r0)
                vfence(P)
                nc.vector.scalar_tensor_tensor(r1, pm2[:], 1.0 / 1024.0, r1,
                                               ALU.mult, ALU.subtract)
                vfence(P)
                nc.vector.tensor_single_scalar(r1, r1, 1e-5, ALU.add)
                fv = s_f.inc(nc.vector.engine_nop(), 1)
                s_f.wait(nc.scalar, fv)
                fv = s_fs.inc(nc.scalar.activation(r1, r1, AF.Sqrt), 1)
                s_fs.wait(nc.vector, fv)
                nc.vector.reciprocal(r1, r1)
                vfence(P)
                nmb = v["STG"][0:1, 1024:1280].bitcast(BF16)
                rsb = v["STG"][0:1, 1280:1536].bitcast(BF16)
                nc.vector.tensor_copy(nmb, r0)
                vfence(P)
                fv = s_f.inc(nc.vector.tensor_copy(rsb, r1), 1)
                nc.tensor.wait_ge(s_f.h, fv)
                i1 = nc.tensor.matmul(pb1[:], v["ones_rowb"], nmb,
                                      start=True, stop=True)
                i2 = nc.tensor.matmul(pb2[:], v["ones_rowb"], rsb,
                                      start=True, stop=True)
                pe_bcast = s_pe.inc(i2, 1)
                nc.vector.wait_ge(s_pe.h, pe_bcast)
                for g in range(DG):
                    nc.vector.tensor_add(scr1, sh(g, tb * 512, 512), pb1[:])
                    vfence(P)
                    nc.vector.tensor_mul(scr2, scr1, pb2[:])
                    vfence(P)
                    s_f.inc(nc.vector.tensor_scalar(
                        hn[:, g, :], scr2,
                        v["bcol"][:, lay * 8 + g: lay * 8 + g + 1],
                        v["bcol"][:, 16 + lay * 8 + g: 17 + lay * 8 + g],
                        ALU.mult, ALU.add), 1)
                s_f.wait(nc.tensor)
                nc.sync.wait_ge(s_pe.h, pe_bcast)
            s_w1 = P.sem("fw1")
            w10 = s_w1.n
            with (
                nc.psum_tensor(f"ps_w1a{lay}{tb}", [128, 512]) as pw1a,
                nc.psum_tensor(f"ps_w1b{lay}{tb}", [128, 512]) as pw1b,
            ):
                pws = [pw1a, pw1b]
                for m in range(44):
                    dst = v["WB"][:, (m % 2) * 1024:(m % 2) * 1024 + 1024]
                    stg.stage(w1[:, :, m * 128:(m + 1) * 128], dst, 1024,
                              consumer_sem=s_w1 if m >= 2 else None,
                              consumer_val=w10 + m - 1)
                    stg.wait(nc.tensor)
                    pt = pws[m % 2]
                    if m >= 2:
                        nc.tensor.wait_ge(s_w1.h, s_w1.n - 1)
                    ins = None
                    for g in range(DG):
                        ins = nc.tensor.matmul(pt[:], dst[:, g * 128:(g + 1) * 128],
                                               hn[:, g, :], start=(g == 0),
                                               stop=(g == DG - 1))
                    mv = s_pe.inc(ins, 1)
                    if m % 2 == 0:
                        nc.vector.wait_ge(s_pe.h, mv)
                        nc.vector.wait_ge(s_w1.h, s_w1.n)
                        s_w1.inc(nc.vector.tensor_scalar_add(
                            scr1, pt[:], v["ccol"][:, 8 + m:9 + m]), 1)
                    else:
                        nc.scalar.wait_ge(s_pe.h, mv)
                        gv = s_fs.inc(nc.scalar.activation(
                            scr2, pt[:], AF.Gelu,
                            bias=v["ccol"][:, 8 + m:9 + m]), 1)
                        nc.vector.wait_ge(s_fs.h, gv)
                        nc.vector.wait_ge(s_w1.h, s_w1.n)
                        s_w1.inc(nc.vector.tensor_mul(H1[:, m // 2, :],
                                                      scr1, scr2), 1)
            P.barrier()

            def w2_src(k):
                return w2[:, k, :]

            def w2_lhs(k, i):
                return H1[:, k, (i % 4) * 128:(i % 4) * 128 + 128]

            def w2_evac(i, ob, pt):
                Xsl = X[:, i, ob * 512:(ob + 1) * 512]
                return nc.vector.tensor_tensor(Xsl, Xsl, pt, ALU.add)

            emit_proj_tm(P, stg, "fw2", w2_lhs, w2_src, 22, w2_evac,
                         nc.vector, n_ob=2,
                         bias_row=v["brow_bf"][:, 0:1024], tb_list=(tb,))
            P.barrier()

    s_sp.wait(nc.sync)
    x0d = P.dram["x0"].rearrange("(i p) d -> p i d", p=128)
    x0s = [v["STG"][:, 0:1024], v["STG"][:, 1024:2048]]
    s_x0 = [P.sem("f0x0a"), P.sem("f0x0b")]
    addv = {}
    for i in range(TI):
        sl = i % 2
        if i >= 2:
            nc.sync.wait_ge(s_ln.h, addv[i - 2])
        dv = s_x0[sl].inc(nc.sync.dma_start(x0s[sl], x0d[:, i, :]), 16)
        nc.vector.wait_ge(s_x0[sl].h, dv)
        addv[i] = s_ln.inc(nc.vector.tensor_add(X[:, i, :], X[:, i, :],
                                                x0s[sl]), 1)
    vfence(P)
    emit_ln_all(P, X, FD[:, 0:2048], s_ln)
    P.barrier()


def emit_l1_attn(P, stg, Wd):
    nc = P.nc
    v = P.v
    P.gate_w("L1_attn_W_out")
    X = v["X"]
    FA, FB, FC, FD = v["FA"], v["FB"], v["FC"], v["FD"]
    BT = FA[0:16, 0:2048]
    CT = FA[64:80, 0:2048]
    Bblk = FA[:, 2048:6144].rearrange("p (i q) -> p i q", i=TI)
    Cblk = FA[:, 6144:10240].rearrange("p (i q) -> p i q", i=TI)
    Hall = FA[:, 10240:14352]
    BTt = FA[:, 14352:14608].rearrange("p (i s) -> p i s", i=TI)
    Wqb = FA[:, 14608:15120]
    Evq = FA[:, 15120:15632]
    WTs = FA[:, 15632:15760]
    HTs = FA[:, 15760:16016]
    hspre = FA[:, 16016:16144]
    Eyq = FA[:, 16016:16392]
    xsT = FB[:, 0:16384].rearrange("p (g c) -> p g c", g=DG)
    dtT = FC[:, 0:16384].rearrange("p (g c) -> p g c", g=DG)
    Gbc = FD[:, 0:16384].rearrange("p (j c) -> p j c", j=8)
    Xr = v["X"].rearrange("p i d -> p (i d)")
    va = Xr[:, 0:2048]
    Dc = Xr[:, 2048:2304]
    Sall = Xr[:, 2560:6656]
    yac = Xr[:, 6656:8704]
    zsl = Xr[:, 8704:9728].bitcast(BF16)
    esc = Xr[:, 9728:11776]
    Eb = Xr[:, 11776:12800].bitcast(BF16)
    Pb = Xr[:, 12800:13824].bitcast(BF16)
    PGb = Xr[:, 13824:14848].bitcast(BF16)

    s_dma = P.sem("m_dma")
    s_rows = P.sem("m_rows")
    s_ln = P.sem("m_ln")
    s_sp = P.sem("m_spill")
    s_pe = P.sem("m_pe")

    s_dma.inc(nc.sync.dma_start(v["bcol"][:, 0:8], Wd["L1_attn_logA_cols"]), 16)
    s_dma.inc(nc.sync.dma_start(v["bcol"][:, 8:16], Wd["L1_attn_dtb_cols"]), 16)
    s_dma.inc(nc.sync.dma_start(v["bcol"][:, 16:24], Wd["L1_attn_D_cols"]), 16)
    s_dma.inc(nc.sync.dma_start(v["bcol"][:, 24:32], Wd["L1_attn_convb_cols"]), 16)
    s_dma.inc(nc.sync.dma_start(v["ccol"][:, 8:40], Wd["L1_attn_convw_cols"]), 16)
    Acol = v["ccol"][:, 40:48]
    s_dma.wait(nc.scalar)
    s_ln.inc(nc.scalar.activation(Acol, v["bcol"][:, 0:8], AF.Exp), 1)
    bmask = FA[:, 14608:14864]
    cmask = FA[:, 14864:15120]
    tid16 = FA[64:80, 15120:15248]

    xn_tok = FB[:, 0:16384].rearrange("p (i d) -> p i d", i=TI)
    gb_bc = FD[:, 0:2048]
    with nc.psum_tensor("ps_mgb", [128, 1024]) as pgb:
        for which, wname in ((0, "L1_attn_ln_g"), (1, "L1_attn_ln_b")):
            load_row_bf16(P, Wd[wname][0:512], v["brow_bf"][:, 0:512], s_rows)
            load_row_bf16(P, Wd[wname][512:1024], v["brow_bf"][:, 512:1024], s_rows)
            s_rows.wait(nc.tensor)
            if which == 1:
                s_ln.wait(nc.tensor)
            ins = None
            for q in range(2):
                ins = nc.tensor.matmul(pgb[:, q * 512:(q + 1) * 512],
                                       v["ones_rowb"][:, 0:128],
                                       v["brow_bf"][:, q * 512:(q + 1) * 512],
                                       start=True, stop=True)
            pv = s_pe.inc(ins, 1)
            nc.vector.wait_ge(s_pe.h, pv)
            s_ln.inc(nc.vector.tensor_copy(
                gb_bc[:, which * 1024:(which + 1) * 1024], pgb[:]), 1)
    lc = v["lncol"]
    t1f = v["STG"][:, 0:1024]
    gbcf = gb_bc[:, 0:1024]
    bbcf = gb_bc[:, 1024:2048]
    for i in range(TI):
        Xv = X[:, i, :]
        nc.vector.wait_ge(s_ln.h, s_ln.n)
        nc.vector.reduce_sum(lc[:, 0:1], Xv, axis=AX)
        vfence(P)
        nc.vector.tensor_scalar_mul(lc[:, 1:2], lc[:, 0:1], -1.0 / 1024.0)
        vv = s_ln.inc(nc.vector.engine_nop(), 1)
        s_ln.wait(nc.scalar, vv)
        vv = s_ln.inc(nc.scalar.activation(FD[:, 2560:3584], Xv, AF.Square,
                                           bias=lc[:, 1:2], scale=1.0,
                                           accum_out=lc[:, 2:3]), 1)
        s_ln.wait(nc.vector, vv)
        nc.vector.tensor_scalar(lc[:, 3:4], lc[:, 2:3], 1.0 / 1024.0, 1e-5,
                                ALU.mult, ALU.add)
        vv = s_ln.inc(nc.vector.engine_nop(), 1)
        s_ln.wait(nc.scalar, vv)
        vv = s_ln.inc(nc.scalar.activation(lc[:, 4:5], lc[:, 3:4], AF.Sqrt), 1)
        s_ln.wait(nc.vector, vv)
        nc.vector.reciprocal(lc[:, 5:6], lc[:, 4:5])
        vfence(P)
        nc.vector.tensor_scalar(t1f, Xv, lc[:, 1:2], lc[:, 5:6],
                                ALU.add, ALU.mult)
        vfence(P)
        nc.vector.tensor_mul(t1f, t1f, gbcf)
        vfence(P)
        s_ln.inc(nc.vector.tensor_tensor(xn_tok[:, i, :], t1f, bbcf, ALU.add), 1)
    P.barrier()

    if M_PHASES < 2:
        return
    s_sp.inc(nc.sync.dma_start(
        P.dram["x0"].rearrange("(i p) d -> p i d", p=128), X), 16)
    s_sp.wait(nc.sync)
    xnF = FA[:, 0:16384].rearrange("p (g c) -> p g c", g=DG)
    emit_build_xf(P, xn_tok, xnF, 0, "mx", src_bf16=True)

    win = Wd["L1_attn_W_in"].rearrange("(g p) f -> p g f", p=128)
    zscr = FD[:, 2048:2560]
    s_z = P.sem("m_z")

    def win_src(m):
        return win[:, :, m * 128:(m + 1) * 128]

    def win_rhs(g, tb):
        return xnF[:, g, tb * 512:(tb + 1) * 512]

    def win_evac(m, tb, pt):
        if m < 8:
            return nc.vector.tensor_copy(dtT[:, m, tb * 512:(tb + 1) * 512], pt[:])
        zv = s_z.inc(nc.vector.tensor_copy(zscr, pt[:]), 1)
        nc.scalar.wait_ge(s_z.h, zv)
        dv = s_z.inc(nc.scalar.dma_start(
            P.dram["z"][(m - 8) * 128:(m - 7) * 128, tb * 512:(tb + 1) * 512],
            zscr), 16)
        nc.vector.wait_ge(s_z.h, dv)
        return nc.vector.engine_nop()

    emit_proj_fm(P, stg, "mwin", win_src, 16, win_rhs, DG, win_evac, nc.vector)
    P.barrier()
    if M_PHASES < 3:
        return

    cw = v["ccol"][:, 8:40].rearrange("p (g k) -> p g k", g=DG)
    s_cv = P.sem("m_cv")
    for g in range(DG):
        xp = dtT[:, g, :]
        xs = xsT[:, g, :]
        acc = t1f.bitcast(BF16)
        nc.vector.wait_ge(s_cv.h, s_cv.n)
        nc.vector.tensor_scalar_mul(acc, xp, cw[:, g, 3:4])
        vfence(P)
        for k in range(3):
            sh = 3 - k
            nc.vector.scalar_tensor_tensor(acc[:, sh:2048], xp[:, 0:2048 - sh],
                                           cw[:, g, k:k + 1], acc[:, sh:2048],
                                           ALU.mult, ALU.add)
            vfence(P)
        cvv = s_cv.inc(nc.vector.engine_nop(), 1)
        s_cv.wait(nc.scalar, cvv)
        s_cv.inc(nc.scalar.activation(xs, acc, AF.Silu,
                                      bias=v["bcol"][:, 24 + g:25 + g]), 1)
        s_cv.wait(nc.vector)
    P.barrier()
    if M_PHASES < 4:
        return

    dtw = Wd["L1_attn_dt_w"].rearrange("(g p) f -> p g f", p=128)
    s_dt = P.sem("m_dt")
    spsc = esc[:, 0:512]

    def dt_src(m):
        return dtw[:, :, m * 128:(m + 1) * 128]

    def dt_rhs(g, tb):
        return xsT[:, g, tb * 512:(tb + 1) * 512]

    def dt_evac(m, tb, pt):
        sev = P.sem("pf_ev")
        nc.scalar.wait_ge(sev.h, sev.n)
        dv = s_dt.inc(nc.scalar.activation(spsc, pt[:], AF.Exp,
                                           bias=v["bcol"][:, 8 + m:9 + m]), 1)
        nc.scalar.wait_ge(s_dt.h, dv)
        return nc.scalar.activation(dtT[:, m, tb * 512:(tb + 1) * 512],
                                    spsc, AF.Ln, bias=1.0)

    emit_proj_fm(P, stg, "mdt", dt_src, 8, dt_rhs, DG, dt_evac, nc.scalar)
    P.barrier()
    if M_PHASES < 5:
        return

    s_bc = P.sem("m_bc")
    bcw = v["WB"][:, 0:128].rearrange("p (g s) -> p g s", g=DG)
    ccw = v["WB"][:, 128:256].rearrange("p (g s) -> p g s", g=DG)
    stg.stage(Wd["L1_attn_B_w"].rearrange("(g p) s -> p g s", p=128),
              bcw.rearrange("p g s -> p (g s)"), 128)
    stg.wait(nc.tensor)
    stg.stage(Wd["L1_attn_C_w"].rearrange("(g p) s -> p g s", p=128),
              ccw.rearrange("p g s -> p (g s)"), 128)
    stg.wait(nc.tensor)
    with (
        nc.psum_tensor("ps_mb", [16, 512]) as pb,
        nc.psum_tensor("ps_mc", [16, 512]) as pc,
    ):
        for tb in range(4):
            if tb > 0:
                s_bc.wait(nc.tensor)
            insb = insc = None
            for g in range(DG):
                insb = nc.tensor.matmul(pb[:], bcw[:, g, :],
                                        xsT[:, g, tb * 512:(tb + 1) * 512],
                                        start=(g == 0), stop=(g == DG - 1))
            for g in range(DG):
                insc = nc.tensor.matmul(pc[:], ccw[:, g, :],
                                        xsT[:, g, tb * 512:(tb + 1) * 512],
                                        start=(g == 0), stop=(g == DG - 1))
            pv = s_pe.inc(insc, 1)
            nc.vector.wait_ge(s_pe.h, pv)
            s_bc.inc(nc.vector.tensor_copy(BT[:, tb * 512:(tb + 1) * 512], pb[:]), 1)
            s_bc.inc(nc.vector.tensor_copy(CT[:, tb * 512:(tb + 1) * 512], pc[:]), 1)
    P.barrier()
    prodsc = FA[0:16, 2048:2560]
    ctsc = FA[0:16, 2560:3072]
    onesc16 = v["CS"][0:16, 300:301]
    nc.vector.memset(onesc16, 1.0)
    s_g = P.sem("m_g")
    with (
        nc.psum_tensor("ps_mgr", [1, 512]) as pgr,
        nc.psum_tensor("ps_mgb2", [128, 512]) as pgb2,
    ):
        for j in range(8):
            for tb in range(4):
                lo = tb * 512
                nc.vector.wait_ge(s_g.h, s_g.n)
                nc.vector.tensor_copy(ctsc, CT[:, lo:lo + 512])
                vfence(P)
                if j == 0:
                    gv = s_g.inc(nc.vector.tensor_mul(
                        prodsc, ctsc, BT[:, lo:lo + 512]), 1)
                elif lo == 0:
                    nc.vector.memset(prodsc[:, 0:j], 0.0)
                    gv = s_g.inc(nc.vector.tensor_mul(
                        prodsc[:, j:512], ctsc[:, j:512], BT[:, 0:512 - j]), 1)
                else:
                    gv = s_g.inc(nc.vector.tensor_mul(
                        prodsc, ctsc, BT[:, lo - j:lo + 512 - j]), 1)
                nc.tensor.wait_ge(s_g.h, gv)
                i1 = nc.tensor.matmul(pgr[:], onesc16, prodsc,
                                      start=True, stop=True)
                pv = s_pe.inc(i1, 1)
                nc.vector.wait_ge(s_pe.h, pv)
                gv = s_g.inc(nc.vector.tensor_copy(v["brow_bf"][:, 0:512], pgr[:]), 1)
                nc.tensor.wait_ge(s_g.h, gv)
                i2 = nc.tensor.matmul(pgb2[:], v["ones_rowb"],
                                      v["brow_bf"][:, 0:512], start=True, stop=True)
                pv = s_pe.inc(i2, 1)
                nc.vector.wait_ge(s_pe.h, pv)
                s_g.inc(nc.vector.tensor_copy(Gbc[:, j, lo:lo + 512], pgb2[:]), 1)
    P.barrier()
    s_bb = P.sem("m_bb")
    s_dma.inc(nc.sync.dma_start(bmask, Wd["c_bmask8"]), 16)
    s_dma.inc(nc.sync.dma_start(cmask, Wd["c_cmask"]), 16)
    s_dma.inc(nc.sync.dma_start(tid16, Wd["c_tid16"]), 16)
    s_dma.wait(nc.vector)
    s_dma.wait(nc.tensor)
    with (
        nc.psum_tensor("ps_mt", [128, 512], BF16) as pt4,
        nc.psum_tensor("ps_mtc", [128, 512]) as ptc,
    ):
        for q in range(4):
            if q > 0:
                nc.tensor.wait_ge(s_bb.h, s_bb.n)
            ins = None
            for k in range(4):
                i = q * 4 + k
                ins = nc.tensor.transpose(pt4[:, k * 16:(k + 1) * 16],
                                          BT[:, i * 128:(i + 1) * 128],
                                          v["ident"][0:16, 0:16])
            pv = s_pe.inc(ins, 1)
            nc.vector.wait_ge(s_pe.h, pv)
            s_bb.inc(nc.vector.tensor_copy(
                BTt[:, 4 * q:4 * q + 4, :].rearrange("p i s -> p (i s)"),
                pt4[:, 0:64]), 1)
        nc.vector.wait_ge(s_bb.h, s_bb.n)
        for i in range(TI):
            a1 = bmask.rearrange("p (c s) -> p c s", c=16)
            a2 = BTt[:, i, :].rearrange("p (o s) -> p o s", o=1)
            b1, b2 = bass.broadcast_tensor_aps(a1, a2)
            s_bb.inc(nc.vector.tensor_tensor(
                Bblk[:, i, :].rearrange("p (c s) -> p c s", c=16),
                b1, b2, ALU.mult), 1)
        for i in range(TI):
            for h2 in range(2):
                jj = i * 2 + h2
                pcc = ptc[:, (jj % 4) * 128:(jj % 4) * 128 + 128]
                if jj >= 4:
                    nc.tensor.wait_ge(s_bb.h, s_bb.n)
                i3 = nc.tensor.matmul(pcc, tid16,
                                      CT[:, i * 128:(i + 1) * 128],
                                      start=True, stop=True)
                pv = s_pe.inc(i3, 1)
                nc.vector.wait_ge(s_pe.h, pv)
                s_bb.inc(nc.vector.tensor_tensor(
                    Cblk[:, i, h2 * 128:(h2 + 1) * 128], pcc,
                    cmask[:, h2 * 128:(h2 + 1) * 128], ALU.mult), 1)
    P.barrier()
    if M_PHASES < 6:
        return

    s_sc = P.sem("m_sc")
    s_scs = P.sem("m_scs")
    s_zd = P.sem("m_zd")
    va3 = va.rearrange("p (c r) -> p c r", r=8)
    for g in M_GORDER:
        nc.scalar.wait_ge(s_scs.h, s_scs.n)
        zdv = s_zd.inc(nc.scalar.dma_start(
            zsl, P.dram["z"][g * 128:(g + 1) * 128, :]), 16)
        nc.vector.wait_ge(s_sc.h, s_sc.n)
        nc.vector.wait_ge(s_scs.h, s_scs.n)
        nc.vector.tensor_scalar(va, dtT[:, g, :], Acol[:, g:g + 1], -1.0,
                                ALU.mult, ALU.mult)
        vfence(P)
        nc.vector.tensor_mul(dtT[:, g, :], dtT[:, g, :], xsT[:, g, :])
        ug = dtT[:, g, :]
        ug3 = ug.rearrange("p (c r) -> p c r", r=8)
        vfence(P)
        for pstep in range(1, 8):
            nc.vector.tensor_add(va3[:, :, pstep:pstep + 1],
                                 va3[:, :, pstep:pstep + 1],
                                 va3[:, :, pstep - 1:pstep])
            vfence(P)
        scv = s_sc.inc(nc.vector.engine_nop(), 1)
        s_sc.wait(nc.scalar, scv)
        dcv = s_scs.inc(nc.scalar.activation(Dc, va3[:, :, 7:8], AF.Exp), 1)
        nc.vector.tensor_mul(yac, ug, Gbc[:, 0, :])
        vfence(P)
        esc3 = esc.rearrange("p (c r) -> p c r", r=8)
        Eb3 = Eb.rearrange("p (c r) -> p c r", r=8)
        Pb3 = Pb.rearrange("p (c r) -> p c r", r=8)
        PG3 = PGb.rearrange("p (c r) -> p c r", r=8)
        yac3 = yac.rearrange("p (c r) -> p c r", r=8)
        for j in range(1, 8):
            w = 8 - j
            G3 = Gbc[:, j, :].rearrange("p (c r) -> p c r", r=8)
            nc.vector.tensor_sub(esc3[:, :, j:8], va3[:, :, j:8], va3[:, :, 0:w])
            ev = s_sc.inc(nc.vector.engine_nop(), 1)
            s_sc.wait(nc.scalar, ev)
            ev = s_scs.inc(nc.scalar.activation(Eb3[:, :, j:8], esc3[:, :, j:8],
                                                AF.Exp), 1)
            s_scs.wait(nc.vector, ev)
            nc.vector.tensor_mul(Pb3[:, :, j:8], Eb3[:, :, j:8], ug3[:, :, 0:w])
            vfence(P)
            nc.vector.tensor_mul(PG3[:, :, j:8], Pb3[:, :, j:8], G3[:, :, j:8])
            vfence(P)
            nc.vector.tensor_add(yac3[:, :, j:8], yac3[:, :, j:8], PG3[:, :, j:8])
            vfence(P)
        with (
            nc.psum_tensor(f"ps_ms{g}", [128, 1024]) as pS,
            nc.psum_tensor(f"ps_mw{g}", [128, 512], BF16) as pW,
            nc.psum_tensor(f"ps_mh{g}", [128, 256], BF16) as pH,
            nc.psum_tensor(f"ps_my{g}", [128, 512]) as pY,
        ):
            WTs4 = FA[:, 15632:16144]
            for q in range(4):
                lo = q * 512
                vq3 = va3[:, 64 * q:64 * (q + 1), :]
                b1, b2 = bass.broadcast_tensor_aps(vq3[:, :, 7:8], vq3)
                nc.vector.wait_ge(s_sc.h, s_sc.n)
                nc.vector.wait_ge(s_scs.h, s_scs.n)
                nc.vector.tensor_tensor(
                    esc3[:, 0:64, :], b1, b2, ALU.subtract)
                ev = s_sc.inc(nc.vector.engine_nop(), 1)
                s_sc.wait(nc.scalar, ev)
                ev = s_scs.inc(nc.scalar.activation(Wqb, esc[:, 0:512], AF.Exp), 1)
                s_scs.wait(nc.vector, ev)
                nc.vector.tensor_mul(Wqb, Wqb, ug[:, lo:lo + 512])
                wv = s_sc.inc(nc.vector.engine_nop(), 1)
                nc.tensor.wait_ge(s_sc.h, wv)
                it = None
                for k4 in range(4):
                    it = nc.tensor.transpose(pW[:, k4 * 128:(k4 + 1) * 128],
                                             Wqb[:, k4 * 128:(k4 + 1) * 128],
                                             v["ident"])
                pv = s_pe.inc(it, 1)
                nc.vector.wait_ge(s_pe.h, pv)
                wtv = s_sc.inc(nc.vector.tensor_copy(WTs4, pW[:]), 1)
                nc.tensor.wait_ge(s_sc.h, wtv)
                iS = None
                for k4 in range(4):
                    i = q * 4 + k4
                    iS = nc.tensor.matmul(pS[:, k4 * 256:(k4 + 1) * 256],
                                          WTs4[:, k4 * 128:(k4 + 1) * 128],
                                          Bblk[:, i, :], start=True, stop=True)
                pv = s_pe.inc(iS, 1)
                nc.vector.wait_ge(s_pe.h, pv)
                s_sc.inc(nc.vector.tensor_copy(
                    Sall[:, q * 1024:(q + 1) * 1024], pS[:]), 1)
            s_sc.wait(nc.vector)
            s_scs.wait(nc.vector)
            nc.vector.memset(
                Hall.rearrange("p (s c) -> p s c", s=16)[:, :, 0:1], 0.0)
            vfence(P)
            Hall3 = Hall.rearrange("p (s c) -> p s c", s=16)
            Sall3 = Sall.rearrange("p (a s) -> p a s", s=16)
            for s16 in range(16):
                nc.vector.tensor_tensor_scan(
                    Hall3[:, s16, 1:257], Dc, Sall3[:, :, s16],
                    0.0, ALU.mult, ALU.add)
            hv = s_sc.inc(nc.vector.engine_nop(), 1)
            nc.tensor.wait_ge(s_sc.h, hv)
            nc.vector.wait_ge(s_sc.h, hv)
            hspre2 = FA[:, 16016:16272].rearrange("p (c s) -> p c s", c=16)
            ymm = 0
            for q in range(4):
                lo = q * 512
                nc.scalar.wait_ge(s_sc.h, s_sc.n)
                eqv = s_scs.inc(nc.scalar.activation(Evq, va[:, lo:lo + 512],
                                                     AF.Exp), 1)
                for k4 in range(4):
                    i = q * 4 + k4
                    hsrc = Hall3[:, :, i * 16:i * 16 + 16] \
                        .rearrange("p s c -> p c s")
                    if ymm:
                        nc.vector.wait_ge(s_pe.h, ymm)
                    hcv = s_sc.inc(nc.vector.tensor_copy(hspre2, hsrc), 1)
                    nc.tensor.wait_ge(s_sc.h, hcv)
                    it = None
                    for h2 in range(2):
                        it = nc.tensor.transpose(
                            pH[:, h2 * 128:(h2 + 1) * 128],
                            FA[:, 16016 + h2 * 128:16144 + h2 * 128],
                            v["ident"])
                    pv = s_pe.inc(it, 1)
                    nc.vector.wait_ge(s_pe.h, pv)
                    htv = s_sc.inc(nc.vector.tensor_copy(HTs, pH[:]), 1)
                    nc.tensor.wait_ge(s_sc.h, htv)
                    ins = None
                    for h2 in range(2):
                        ins = nc.tensor.matmul(
                            pY[:, k4 * 128:(k4 + 1) * 128],
                            HTs[:, h2 * 128:(h2 + 1) * 128],
                            Cblk[:, i, h2 * 128:(h2 + 1) * 128],
                            start=(h2 == 0), stop=(h2 == 1))
                    ymm = s_pe.inc(ins, 1)
                nc.vector.wait_ge(s_pe.h, ymm)
                nc.vector.wait_ge(s_scs.h, eqv)
                nc.vector.tensor_mul(Pb[:, 0:512], Evq, pY[:])
                vfence(P)
                cyv = s_sc.inc(nc.vector.tensor_tensor(
                    yac[:, lo:lo + 512], yac[:, lo:lo + 512],
                    Pb[:, 0:512], ALU.add), 1)
                nc.tensor.wait_ge(s_sc.h, cyv)
        nc.vector.wait_ge(s_zd.h, zdv)
        nc.vector.wait_ge(s_sc.h, s_sc.n)
        nc.vector.wait_ge(s_scs.h, s_scs.n)
        nc.vector.tensor_scalar_mul(Eb[:, 0:2048], xsT[:, g, :],
                                    v["bcol"][:, 16 + g:17 + g])
        vfence(P)
        nc.vector.tensor_add(esc, yac, Eb[:, 0:2048])
        zv = s_sc.inc(nc.vector.engine_nop(), 1)
        s_sc.wait(nc.scalar, zv)
        zv = s_scs.inc(nc.scalar.activation(Pb[:, 0:2048], zsl, AF.Silu), 1)
        s_scs.wait(nc.vector, zv)
        s_sc.inc(nc.vector.tensor_mul(xsT[:, g, :], esc, Pb[:, 0:2048]), 1)
    P.barrier()
    if M_PHASES < 7:
        return

    s_sp.inc(nc.sync.dma_start(
        X, P.dram["x0"].rearrange("(i p) d -> p i d", p=128)), 16)
    s_sp.wait(nc.vector)
    s_sp.wait(nc.tensor)
    wout = Wd["L1_attn_W_out"].rearrange("(k p) f -> p k f", p=128)

    def wo_src(k):
        return wout[:, k, :]

    def wo_lhs(k, i):
        return xsT[:, k, i * 128:(i + 1) * 128]

    def wo_evac(i, ob, pt):
        Xsl = X[:, i, ob * 512:(ob + 1) * 512]
        return nc.vector.scalar_tensor_tensor(Xsl, Xsl, 2.0, pt,
                                              ALU.mult, ALU.add)

    emit_proj_tm(P, stg, "mout", wo_lhs, wo_src, DG, wo_evac, nc.vector, n_ob=2)
    P.barrier()
    emit_ln_all(P, X, FD[:, 0:2048], s_ln)
    P.barrier()


def emit_l1_ffn(P, stg, Wd):
    nc = P.nc
    v = P.v
    P.gate_w("L1_ffn_lora_Bc")
    X = v["X"]
    XF = v["FA"][:, 0:16384].rearrange("p (g c) -> p g c", g=DG)
    INTER = v["FB"][:, 0:16384].rearrange("p (m c) -> p m c", m=32)
    OH = v["FC"][:, 6656:6784].rearrange("p (i e) -> p i e", i=TI)
    RM = v["FC"][:, 6784:7296].rearrange("p (i r) -> p i r", i=TI)
    rmT = v["FC"][0:32, 0:2048]
    loraB = v["FC"][0:32, 2048:6144]

    emit_build_xf(P, X, XF, 0, "m")

    s_rows = P.sem("f_rows")
    load_row_bf16(P, Wd["L1_ffn_gate_b"], v["brow_bf"][:, 0:8], s_rows)
    s_rows.wait(nc.tensor)

    s_g = P.sem("f_g")
    s_goh = P.sem("f_goh")
    ohf = v["ccol"][:, 8:16]
    cum = v["ccol"][:, 16:24]
    t1c = v["ccol"][:, 24:32]
    mcol = v["ccol"][:, 32:33]
    zer8 = v["ccol"][:, 40:48]
    nc.vector.memset(zer8, 0.0)
    gwt = Wd["L1_ffn_gate_Wt"].rearrange("(g p) e -> p g e", p=128)
    gdst = v["WB"][:, 0:1024]
    stg.stage(gwt, gdst[:, 0:64].rearrange("p (g e) -> p g e", g=DG), 64)
    stg.wait(nc.tensor)
    gd3 = gdst[:, 0:64].rearrange("p (g e) -> p g e", g=DG)
    with (
        nc.psum_tensor("ps_fg0", [128, 512]) as pg0,
        nc.psum_tensor("ps_fg1", [128, 512]) as pg1,
    ):
        pg = [pg0[:, 0:8], pg1[:, 0:8]]
        for i in range(TI):
            pt = pg[i % 2]
            if i >= 2:
                nc.tensor.wait_ge(s_goh.h, i - 1)
            ins = None
            for g in range(DG):
                ins = nc.tensor.matmul(pt, XF[:, g, i * 128:(i + 1) * 128],
                                       gd3[:, g, :], start=(g == 0), stop=False)
            ins = nc.tensor.matmul(pt, P.v["ones_rowb"],
                                   v["brow_bf"][:, 0:8], start=False, stop=True)
            mv = s_g.inc(ins, 1)
            nc.vector.wait_ge(s_g.h, mv)
            nc.vector.wait_ge(s_goh.h, s_goh.n)
            nc.vector.reduce_max(mcol, pt, axis=AX)
            vfence(P)
            nc.vector.tensor_scalar(ohf, pt, mcol, 1.0, ALU.is_ge, ALU.mult)
            vfence(P)
            nc.vector.tensor_tensor_scan(cum, ohf, zer8, 0.0, ALU.add, ALU.add)
            vfence(P)
            nc.vector.tensor_scalar(t1c, cum, 1.0, 1.0, ALU.is_equal, ALU.mult)
            vfence(P)
            s_goh.inc(nc.vector.tensor_tensor(OH[:, i, :], t1c, ohf, ALU.mult), 1)
    P.barrier()

    lat = Wd["L1_ffn_lora_At"].rearrange("(g p) r -> p g r", p=128)
    s_r = P.sem("f_r")
    s_rm = P.sem("f_rm")
    stg.stage(lat, gdst[:, 64:320].rearrange("p (g r) -> p g r", g=DG), 256)
    la3 = gdst[:, 64:320].rearrange("p (g r) -> p g r", g=DG)
    stg.wait(nc.tensor)
    with (
        nc.psum_tensor("ps_fr0", [128, 512]) as pr0,
        nc.psum_tensor("ps_fr1", [128, 512]) as pr1,
    ):
        pr = [pr0[:, 0:32], pr1[:, 0:32]]
        for i in range(TI):
            pt = pr[i % 2]
            if i >= 2:
                nc.tensor.wait_ge(s_rm.h, i - 1)
            ins = None
            for g in range(DG):
                ins = nc.tensor.matmul(pt, XF[:, g, i * 128:(i + 1) * 128],
                                       la3[:, g, :], start=(g == 0),
                                       stop=(g == DG - 1))
            mv = s_r.inc(ins, 1)
            nc.vector.wait_ge(s_r.h, mv)
            a1 = pt.rearrange("p (e r) -> p e r", e=8)
            a2 = OH[:, i, :].rearrange("p (e o) -> p e o", o=1)
            b1, b2 = bass.broadcast_tensor_aps(a1, a2)
            s_rm.inc(nc.vector.tensor_tensor(
                RM[:, i, :].rearrange("p (e r) -> p e r", e=8), b1, b2,
                ALU.mult), 1)
    s_t = P.sem("f_t")
    s_te = P.sem("f_te")
    with nc.psum_tensor("ps_ft", [32, 512], BF16) as ptr:
        s_rm.wait(nc.tensor)
        for q in range(4):
            if q > 0:
                nc.tensor.wait_ge(s_te.h, q)
            ins = None
            for k in range(4):
                i = q * 4 + k
                ins = nc.tensor.transpose(ptr[:, k * 128:(k + 1) * 128],
                                          RM[:, i, :], v["ident"])
            tv = s_t.inc(ins, 1)
            nc.vector.wait_ge(s_t.h, tv)
            s_te.inc(nc.vector.tensor_copy(
                rmT[:, q * 512:(q + 1) * 512], ptr[:]), 1)
    lb = Wd["L1_ffn_lora_Bc"]
    for q in range(4):
        stg.stage(lb[:, q * 1024:(q + 1) * 1024],
                  loraB[:, q * 1024:(q + 1) * 1024], 1024)
        stg.wait(nc.tensor)
    s_dma = P.sem("f_dma")
    bv = s_dma.inc(nc.sync.dma_start(v["bcol"][:, 0:32],
                                     Wd["L1_ffn_wib_cols"]), 16)
    nc.scalar.wait_ge(s_dma.h, bv)
    load_row_bf16(P, Wd["L1_ffn_wo_b"][0:512], v["brow_bf"][:, 0:512], s_rows)
    load_row_bf16(P, Wd["L1_ffn_wo_b"][512:1024], v["brow_bf"][:, 512:1024], s_rows)
    s_rows.wait(nc.tensor)
    P.barrier()

    wi = Wd["L1_ffn_wi_Wt"].rearrange("(g p) f -> p g f", p=128)
    wo = Wd["L1_ffn_wo_Wt"].rearrange("(k p) f -> p k f", p=128)
    s_ln = P.sem("f_ln")
    SQ = v["FD"][:, 0:1024]
    for tb in range(4):
        def wi_src(m):
            return wi[:, :, m * 128:(m + 1) * 128]

        def wi_rhs(g, _tb):
            return XF[:, g, tb * 512:(tb + 1) * 512]

        def wi_extra(m, _tb, pt):
            return nc.tensor.matmul(pt[:], loraB[:, m * 128:(m + 1) * 128],
                                    rmT[:, tb * 512:(tb + 1) * 512],
                                    start=False, stop=True)

        def wi_evac(m, _tb, pt):
            return nc.scalar.activation(INTER[:, m, :], pt[:], AF.Relu,
                                        bias=v["bcol"][:, m:m + 1], scale=1.0)

        emit_proj_fm(P, stg, "fwi", wi_src, 32, wi_rhs, DG, wi_evac,
                     nc.scalar, n_tb=1, extra_mm=wi_extra)
        P.barrier()

        def wo_src(k):
            return wo[:, k, :]

        def wo_lhs(k, i):
            return INTER[:, k, (i % 4) * 128:(i % 4) * 128 + 128]

        def wo_evac(i, ob, pt):
            Xsl = X[:, i, ob * 512:(ob + 1) * 512]
            return nc.vector.tensor_tensor(Xsl, Xsl, pt, ALU.add)

        emit_proj_tm(P, stg, "fwo", wo_lhs, wo_src, 32, wo_evac, nc.vector,
                     n_ob=2, bias_row=v["brow_bf"][:, 0:1024], tb_list=(tb,))
        P.barrier()
    emit_ln_all(P, X, v["FD"][:, 0:2048], s_ln)
    P.barrier()
